# version 32
# baseline (speedup 1.0000x reference)
import contextlib
import math
import numpy as np
import ml_dtypes
import concourse.bass as bass
import concourse.mybir as mybir
from concourse.bass_utils import run_bass_kernel_spmd

F32 = mybir.dt.float32
BF16 = mybir.dt.bfloat16
I32 = mybir.dt.int32
AF = mybir.ActivationFunctionType
ALU = mybir.AluOpType
AX = mybir.AxisListType

NCORES = 8
D = 2048
DC = D // 128
HE = 8
DH = 64
FH = 16
FDH = 128
FI = 8192
EPS_LN = 1e-5
EPS_RMS = 1e-6
C_MAIN = 6.0
C_FFN = 4.0
MASKVAL = -100000.0
TWO_PI = 2.0 * math.pi
CW_C1 = 6.28125
CW_C2 = TWO_PI - CW_C1
ROPE_THETA = 10000.0
FW_WQ, FW_WK, FW_WV, FW_WO, FW_G, FW_U = 0, 2048, 3072, 4096, 6144, 14336
FW_COLS = 22528
CONSTVALS = [0.0, 1.0, EPS_LN, EPS_RMS, -C_FFN, 0.125, 1.0 / math.sqrt(FDH)]


class CFG:
    def __init__(self, T):
        self.T = T
        self.Tc = T // NCORES
        self.NT = self.Tc // 128
        self.QT = 512
        self.NQT = T // self.QT
        self.NKB = T // 128
        self.NKC = self.NKB // NCORES
        self.NS = self.Tc // 512
        self.trips = [(4 * (t + 1) - 1) // 8 + 1 for t in range(self.NQT)]
        self.slots = [(t, j) for t in range(self.NQT)
                      for j in range(max(0, self.trips[t] - 2), self.trips[t])]
        self.NV = sum(self.trips)
        self.KSEL = T // 8
        self.KC = self.KSEL // NCORES
        self.NKT = self.KC // 128
        self.QHW = min(1024, self.KSEL)
        self.QH = self.KSEL // self.QHW
        self.NCONS = self.QHW // self.KC


def host_tables(cfg, c):
    t = {}
    vb = np.zeros((128, cfg.NV), np.float32)
    v = 0
    for qt in range(cfg.NQT):
        for j in range(cfg.trips[qt]):
            vb[:, v] = -C_MAIN if 8 * j + c < 4 * (qt + 1) else MASKVAL
            v += 1
    t["vbias"] = vb
    p = np.arange(128)[:, None]
    f = np.arange(cfg.QT)[None, :]
    dm = np.zeros((len(cfg.slots), 128, cfg.QT), ml_dtypes.bfloat16)
    for s, (qt, j) in enumerate(cfg.slots):
        dm[s] = ((cfg.QT * qt + f) >= (128 * (8 * j + c) + p)).astype(
            ml_dtypes.bfloat16)
    t["dmask"] = dm
    f2 = np.arange(cfg.QHW)[None, :]
    am = np.zeros((cfg.NKT * cfg.QH, 128, cfg.QHW), ml_dtypes.bfloat16)
    for jj in range(cfg.NKT):
        for h in range(cfg.QH):
            am[jj * cfg.QH + h] = ((cfg.QHW * h + f2) >=
                                   (128 * (cfg.NKT * c + jj) + p)).astype(
                ml_dtypes.bfloat16)
    t["amask"] = am
    mc = np.zeros((NCORES, 1), np.float32)
    mc[:c, 0] = 1.0
    t["maskc"] = mc
    gt = np.zeros((128, cfg.NT), np.float32)
    for tb in range(cfg.NT):
        gt[:, tb] = cfg.Tc * c + 128 * tb + np.arange(128)
    t["gtok"] = gt
    return t


def host_consts():
    t = {}
    t["eye_bf"] = np.eye(128, dtype=ml_dtypes.bfloat16)
    t["eye_f32"] = np.eye(128, dtype=np.float32)
    t["ones_bf"] = np.ones((128, 128), ml_dtypes.bfloat16)
    t["ones_f32"] = np.ones((128, 128), np.float32)
    t["u128s"] = (np.arange(128)[:, None] < np.arange(128)[None, :]).astype(
        np.float32)
    t["u16s"] = (np.arange(16)[:, None] < np.arange(16)[None, :]).astype(np.float32)
    ivf = (1.0 / ROPE_THETA ** (np.arange(0, FDH, 2) / FDH)).astype(np.float32)
    t["invfreq"] = np.concatenate([ivf, ivf])[None, :].astype(np.float32)
    t["iotap1"] = (np.arange(128, dtype=np.float32) + 1.0)[:, None]
    cv = np.zeros((128, len(CONSTVALS)), np.float32)
    for i, v in enumerate(CONSTVALS):
        cv[:, i] = v
    t["cvec"] = cv
    sio = np.zeros((2, 128), np.float32)
    sio[0, 0:64] = 1.0
    sio[1, 64:128] = 1.0
    t["selio"] = sio
    return t


def shard_inputs(cfg, c, inp):
    d = {}
    d["x"] = np.ascontiguousarray(inp["L0_attn_normalized_resid_pre"][0,
                                  cfg.Tc * c:cfg.Tc * (c + 1), :])
    wq4s = inp["L0_attn_W_Q"][4 * c:4 * c + 4]
    d["wq4"] = np.ascontiguousarray(
        wq4s.reshape(4, DC, 128, DH).transpose(2, 1, 0, 3)
        .reshape(128, 4 * DC * DH))
    wkv2 = np.concatenate([inp["L0_attn_W_K"][c], inp["L0_attn_W_V"][c]],
                          axis=1)
    d["wkv"] = np.ascontiguousarray(
        wkv2.reshape(DC, 128, 128).transpose(1, 0, 2).reshape(128, DC * 128))
    d["wo1"] = np.ascontiguousarray(inp["L0_attn_W_O"][c])
    r0 = (D // NCORES) * c
    d["fw1"] = np.ascontiguousarray(np.concatenate(
        [inp["L0_ffn_wq"][r0:r0 + 256], inp["L0_ffn_wk"][r0:r0 + 256],
         inp["L0_ffn_wv"][r0:r0 + 256], inp["L0_ffn_wo"][r0:r0 + 256]],
        axis=1))
    i0 = (FI // NCORES) * c
    d["fgu"] = np.ascontiguousarray(np.concatenate(
        [inp["L0_ffn_w_gate"][:, i0:i0 + FI // NCORES],
         inp["L0_ffn_w_up"][:, i0:i0 + FI // NCORES]], axis=1))
    d["fdn"] = np.ascontiguousarray(
        inp["L0_ffn_w_down"][(FI // NCORES) * c:(FI // NCORES) * (c + 1)])
    d["bq"] = inp["L0_ffn_bq"]
    d["bk"] = inp["L0_ffn_bk"]
    d["bv"] = inp["L0_ffn_bv"]
    d["ln1"] = inp["L0_ffn_ln1_w"]
    d["ln2"] = inp["L0_ffn_ln2_w"]
    d["wr"] = inp["L0_ffn_w_router"]
    d.update(host_tables(cfg, c))
    d.update(host_consts())
    return d


_DRAIN_OPS = {
    "tensor_copy", "tensor_tensor", "tensor_add", "tensor_sub", "tensor_mul",
    "tensor_max", "tensor_scalar", "tensor_single_scalar", "tensor_scalar_add",
    "tensor_scalar_sub", "tensor_scalar_mul", "tensor_scalar_max",
    "tensor_scalar_min", "tensor_relu", "tensor_reduce", "reduce_max",
    "reduce_sum", "scalar_tensor_tensor", "tensor_tensor_scan", "memset",
    "memzero", "reciprocal", "select", "transpose", "bn_stats", "bn_aggr",
    "activation", "square", "sqrt", "mul", "copy", "add", "sign", "iota",
    "affine_select", "affine_mul_reduce",
}


def _ap_range(a):
    t = getattr(a, "tensor", None)
    if t is None:
        return None
    pat = getattr(a, "ap", None)
    off = getattr(a, "offset", None)
    if pat is None or off is None:
        raise ValueError("unparsed AP")
    try:
        dims = [(int(p[0]), int(p[1])) for p in pat]
    except Exception:
        raise ValueError("unparsed AP pattern")
    rowlen, npart = dims[0]
    if rowlen <= 0:
        raise ValueError("odd partition pair")
    plo = int(off) // rowlen
    clo = int(off) % rowlen
    creach = clo
    for s, n_ in dims[1:]:
        creach += (n_ - 1) * abs(s)
    return (t.name, plo, plo + npart, clo, creach + 1)


class DrainWrap:

    def __init__(self, eng):
        self._e = eng
        self._w = []
        self._r = []
        self.always = _AlwaysDrain(self)

    @staticmethod
    def _overlap(r, lst):
        for (n_, pl, ph, cl, ch) in lst:
            if (n_ == r[0] and r[1] < ph and pl < r[2]
                    and r[3] < ch and cl < r[4]):
                return True
        return False

    def _emit(self, a, args, kw, force):
        need = force
        wr = []
        rd = []
        try:
            for x in args[1:]:
                r = _ap_range(x)
                if r is not None:
                    rd.append(r)
                    if self._overlap(r, self._w):
                        need = True
            for k_, x in kw.items():
                r = _ap_range(x)
                if r is None:
                    continue
                if k_ == "accum_out":
                    wr.append(r)
                else:
                    rd.append(r)
                    if self._overlap(r, self._w):
                        need = True
            r0 = _ap_range(args[0])
            if r0 is None:
                raise ValueError("no out ap")
            wr.append(r0)
            for r in wr:
                if (self._overlap(r, self._w)
                        or self._overlap(r, self._r)):
                    need = True
        except ValueError:
            need = True
            wr = None
        if (need or wr is None or len(self._w) > 96
                or len(self._r) > 96):
            self._e.drain()
            self._w = []
            self._r = []
        if wr:
            self._w.extend(wr)
            self._r.extend(rd)
        return a(*args, **kw)

    def __getattr__(self, n):
        a = getattr(self._e, n)
        if n in _DRAIN_OPS:
            def f(*args, **kw):
                return self._emit(a, args, kw, False)
            return f
        if n == "drain":
            def fd(*args, **kw):
                self._w = []
                self._r = []
                return a(*args, **kw)
            return fd
        return a


class _AlwaysDrain:
    def __init__(self, wrap):
        self._wrap = wrap

    def __getattr__(self, n):
        a = getattr(self._wrap._e, n)
        if n in _DRAIN_OPS:
            def f(*args, **kw):
                return self._wrap._emit(a, args, kw, True)
            return f
        return a


class Sem:
    def __init__(self, ctx, nc, name):
        self.h = ctx.enter_context(nc.semaphore(name))
        self.n = 0

    def inc(self, instr, k=1):
        if k == 1:
            instr.then_inc(self.h)
        else:
            instr.then_inc(self.h, k)
        self.n += k
        return instr

    def dma(self, instr):
        return self.inc(instr, 16)

    def wait(self, eng, n=None):
        eng.wait_ge(self.h, self.n if n is None else n)


def build(cfg, stop_after=None):
    nc = bass.Bass("TRN2", target_bir_lowering=False)
    Tc, NT, QT, NQT, NS = cfg.Tc, cfg.NT, cfg.QT, cfg.NQT, cfg.NS
    KC, NKT, QHW, QH = cfg.KC, cfg.NKT, cfg.QHW, cfg.QH

    def EI(name, shape, dtype=F32):
        return nc.dram_tensor(name, list(shape), dtype, kind="ExternalInput")

    x_in = EI("x", [Tc, D])
    wq4 = EI("wq4", [128, 4 * DC * DH])
    wkv = EI("wkv", [128, DC * 128])
    wo1 = EI("wo1", [DH, D])
    fw1 = EI("fw1", [D // NCORES, FW_G])
    fgu = EI("fgu", [D, 2 * (FI // NCORES)])
    fdn = EI("fdn", [FI // NCORES, D])
    bq_i = EI("bq", [D])
    bk_i = EI("bk", [1024])
    bv_i = EI("bv", [1024])
    ln1_i = EI("ln1", [D])
    ln2_i = EI("ln2", [D])
    wr_i = EI("wr", [D])
    vb_i = EI("vbias", [128, cfg.NV])
    dm_i = EI("dmask", [len(cfg.slots), 128, QT], BF16)
    am_i = EI("amask", [NKT * QH, 128, QHW], BF16)
    mc_i = EI("maskc", [NCORES, 1])
    gt_i = EI("gtok", [128, NT])
    ey_bf_i = EI("eye_bf", [128, 128], BF16)
    ey_f_i = EI("eye_f32", [128, 128])
    on_bf_i = EI("ones_bf", [128, 128], BF16)
    on_f_i = EI("ones_f32", [128, 128])
    u128_i = EI("u128s", [128, 128])
    u16_i = EI("u16s", [16, 16])
    ivf_i = EI("invfreq", [1, 128])
    io1_i = EI("iotap1", [128, 1])
    out_t = nc.dram_tensor("out", [Tc, D], F32, kind="ExternalOutput")

    def DT(name, shape, dtype=F32, shared=False):
        return nc.dram_tensor(name, list(shape), dtype, kind="Internal",
                              addr_space="Shared" if shared else "Local")

    a1c = DT("a1c", [D, 192], BF16)
    a1g = DT("a1g", [NCORES * D, 192], BF16)
    a2c = DT("a2c", [DH, D], BF16)
    a2g = DT("a2g", [NCORES * DH, D], BF16)
    fw1cAs = [DT(f"fw1cA{i}", [D // NCORES, 2048], BF16) for i in range(3)]
    fw1gAs = [DT(f"fw1gA{i}", [D, 2048], BF16) for i in range(3)]
    fguc = DT("fguc", [D, 2 * (FI // NCORES)], BF16)
    h2c = DT("h2c", [D, KC], BF16)
    h2g = DT("h2g", [NCORES * D, KC], BF16)
    _spl = (KC // 128) >= 2
    dncA = DT("dncA", [cfg.KSEL // 2 if _spl else cfg.KSEL, D], F32)
    dncB = DT("dncB", [cfg.KSEL // 2, D], F32)
    dnrA = DT("dnrA", [KC // 2 if _spl else KC, D], F32)
    dnrB = DT("dnrB", [KC // 2, D], F32)
    fdnc = DT("fdnc", [FI // NCORES, D], BF16)
    KW = cfg.NKC * 128 // NCORES
    ktc = DT("ktc", [NCORES * 512, KW], BF16)
    kta = DT("kta", [NCORES * 512, KW], BF16)
    vtc = DT("vtc", [Tc, 512], BF16)
    vta = DT("vta", [Tc, 512], BF16)
    qc = DT("qc", [512, Tc], BF16)
    qg = DT("qg", [NCORES * 512, Tc], BF16)
    NZB = Tc // 512
    zlcb = [DT(f"zlc{b}", [NCORES * 520, 512], F32) for b in range(NZB)]
    zlrb = [DT(f"zlr{b}", [520, 512], F32) for b in range(NZB)]
    scc = DT("scc", [Tc, 1], F32)
    scg = DT("scg", [cfg.T, 1], F32)
    cntc = DT("cntc", [1, 1], F32)
    cntg = DT("cntg", [NCORES, 1], F32)
    selhc = DT("selhc", [cfg.KSEL, 2048], F32)
    selhr = DT("selhr", [KC, 2048], F32)
    selxc = DT("selxc", [cfg.KSEL, 4], F32)
    selxr = DT("selxr", [KC, 4], F32)
    fqc = DT("fqc", [D, KC], BF16)
    fqg = DT("fqg", [NCORES * D, KC], BF16)
    FZR = FH // 2 * 128 + FH // 2
    fzlcA = DT("fzlcA", [NCORES * FZR, KC], F32)
    fzlcB = DT("fzlcB", [NCORES * FZR, KC], F32)
    fzlrA = DT("fzlrA", [FZR, KC], F32)
    fzlrB = DT("fzlrB", [FZR, KC], F32)
    gatc = DT("gatc", [KC, D], F32)
    gatg = DT("gatg", [cfg.KSEL, D], F32)
    x1d = DT("x1d", [Tc, D], F32)
    RG = [list(range(NCORES))]

    ctx = contextlib.ExitStack()
    SB = lambda name, shape, dt=F32: ctx.enter_context(
        nc.sbuf_tensor(name, list(shape), dt))

    s_in = Sem(ctx, nc, "s_in")
    s_w = Sem(ctx, nc, "s_w")
    s_cc = Sem(ctx, nc, "s_cc")
    s_pe = Sem(ctx, nc, "s_pe")
    s_dv = Sem(ctx, nc, "s_dv")
    s_ac = Sem(ctx, nc, "s_ac")
    s_gp = Sem(ctx, nc, "s_gp")
    s_st = Sem(ctx, nc, "s_st")
    s_ld = Sem(ctx, nc, "s_ld")
    s_x = Sem(ctx, nc, "s_x")
    s_sc = Sem(ctx, nc, "s_sc")
    s_ex = Sem(ctx, nc, "s_ex")
    s_av = Sem(ctx, nc, "s_av")
    s_ev = Sem(ctx, nc, "s_ev")
    s_out = Sem(ctx, nc, "s_out")
    s_qp = [Sem(ctx, nc, f"s_q{i}") for i in range(6)]
    s_fqp = [Sem(ctx, nc, "s_fq0"), Sem(ctx, nc, "s_fq1")]
    s_mk = Sem(ctx, nc, "s_mk")
    s_sca = Sem(ctx, nc, "s_sca")
    s_gat = Sem(ctx, nc, "s_gat")
    s_wg = Sem(ctx, nc, "s_wg")
    s_sg = Sem(ctx, nc, "s_sg")
    s_ev2 = Sem(ctx, nc, "s_ev2")
    s_zz = Sem(ctx, nc, "s_zz")
    bnd_reg = ctx.enter_context(nc.gpsimd.register("bnd"))
    nc.gpsimd.reg_mov(bnd_reg, cfg.KSEL - 1)

    sync = nc.sync
    pe = nc.tensor
    gps = DrainWrap(nc.gpsimd)
    dve = DrainWrap(nc.vector)
    act = DrainWrap(nc.scalar)
    bar = nc.all_engine_barrier

    def syncpoint():
        for s_ in (s_in, s_w, s_st, s_ld, s_x, s_gp, s_out,
                   s_qp[0], s_qp[1], s_qp[2], s_qp[3], s_qp[4], s_qp[5],
                   s_fqp[0], s_fqp[1], s_sca, s_gat, s_wg, s_sg, s_zz):
            s_.wait(sync)
        bar()

    ey_bf = SB("ey_bf", [128, 128], BF16)
    ey_f = SB("ey_f", [128, 128])
    on_bf = SB("on_bf", [128, 128], BF16)
    on_f = SB("on_f", [128, 128])
    u128 = SB("u128", [128, 128])
    u16 = SB("u16", [16, 16])
    ivf = SB("ivf", [1, 128])
    io1 = SB("io1", [128, 1])
    vb_s = SB("vb_s", [128, cfg.NV])
    mc_s = SB("mc_s", [NCORES, 1])
    gt_s = SB("gt_s", [128, NT])
    cvec = SB("cvec_s", [128, len(CONSTVALS)])
    cvec_i = EI("cvec", [128, len(CONSTVALS)])
    selio = SB("selio_s", [2, 128])
    selio_i = EI("selio", [2, 128])
    for dst, src in [(ey_bf, ey_bf_i), (ey_f, ey_f_i), (on_bf, on_bf_i),
                     (on_f, on_f_i), (u128, u128_i), (u16, u16_i), (ivf, ivf_i),
                     (io1, io1_i), (vb_s, vb_i), (mc_s, mc_i), (gt_s, gt_i),
                     (cvec, cvec_i), (selio, selio_i)]:
        s_in.dma(sync.dma_start(out=dst[:, :], in_=src.ap()))
    for i, v in enumerate(CONSTVALS):
        nc.const_aps.register(F32, v, cvec[:, i:i + 1])
    for eng in (pe, dve, act, gps):
        s_in.wait(eng)
    m1_s = SB("m1_s", [128, NT])
    r1_s = SB("r1_s", [128, NT])
    v1_s = SB("v1_s", [128, NT])
    sc_s = SB("sc_s", [128, NT])
    msk_s = SB("msk_s", [128, NT])
    ka_s = SB("ka_s", [128, NT])
    kb_s = SB("kb_s", [128, NT])
    idx_s = SB("idx_s", [128, NT], I32)
    gate_s = SB("gate_s", [128, NT])

    def ln_stats(xa, sa, sb, sc2, eps):
        dve.reduce_sum(sa[:, :], xa, AX.X)
        s_ac.inc(act.always.activation(pscr[:, :], xa, AF.Square, accum_out=sb[:, :]))
        dve.tensor_scalar_mul(sa[:, :], sa[:, :], 1.0 / D)
        s_ac.wait(dve)
        dve.tensor_scalar_mul(sb[:, :], sb[:, :], 1.0 / D)
        dve.tensor_tensor(sc2[:, :], sa[:, :], sa[:, :], ALU.mult)
        s_dv.inc(dve.always.tensor_sub(sb[:, :], sb[:, :], sc2[:, :]))
        s_dv.wait(act)
        s_ac.inc(act.always.activation(sc2[:, :], sb[:, :], AF.Sqrt, bias=eps))
        s_ac.wait(dve)
        dve.reciprocal(sc2[:, :], sc2[:, :])

    wcctx = contextlib.ExitStack()
    wc = wcctx.enter_context(nc.sbuf_tensor("wc", [128, 2 * 2048], F32))
    wco = wcctx.enter_context(nc.sbuf_tensor("wco", [128, 2 * 2048], BF16))
    with nc.sbuf_tensor("wa", [128, DC * 4 * DH], F32) as wa, \
         nc.sbuf_tensor("wkvs", [128, DC * 128], F32) as wkvs, \
         nc.sbuf_tensor("wob", [128, DC * 192], BF16) as wob:
        s_w.dma(sync.dma_start(out=wa[:, :], in_=wq4.ap()))
        s_w.dma(sync.dma_start(out=wkvs[:, :], in_=wkv.ap()))
        s_w.wait(sync)
        s_w.wait(dve)
        for dc in range(DC):
            o4 = dc * 4 * DH
            oo = dc * 192
            dve.tensor_add(wob[:, oo:oo + DH].bitcast(F32)
                           if False else wa[:, o4:o4 + DH],
                           wa[:, o4:o4 + DH], wa[:, o4 + DH:o4 + 2 * DH])
            dve.tensor_add(wa[:, o4:o4 + DH], wa[:, o4:o4 + DH],
                           wa[:, o4 + 2 * DH:o4 + 3 * DH])
            dve.tensor_add(wa[:, o4:o4 + DH], wa[:, o4:o4 + DH],
                           wa[:, o4 + 3 * DH:o4 + 4 * DH])
            dve.tensor_copy(wob[:, oo:oo + DH], wa[:, o4:o4 + DH])
            s_dv.inc(dve.always.tensor_copy(wob[:, oo + DH:oo + 192],
                                     wkvs[:, 128 * dc:128 * (dc + 1)]))
        s_dv.wait(sync)
        s_st.dma(sync.dma_start(
            out=a1c.ap().rearrange("(dc p) k -> p dc k", p=128),
            in_=wob[:, :].rearrange("p (dc k) -> p dc k", k=192)))
        s_w.dma(sync.dma_start(out=wc[0:DH, 0:D], in_=wo1.ap()))
        s_w.wait(dve)
        s_dv.inc(dve.always.tensor_copy(wco[0:DH, 0:D], wc[0:DH, 0:D]))
        s_dv.wait(sync)
        s_st.dma(sync.dma_start(out=a2c.ap(), in_=wco[0:DH, 0:D]))
        pass
    wk_items = []
    for cs in range(FW_G // 2048):
        for blk in range(2):
            wk_items.append((fw1.ap()[128 * blk:128 * (blk + 1),
                                      2048 * cs:2048 * (cs + 1)],
                             fw1cAs[cs].ap()[128 * blk:128 * (blk + 1), :]))
    for blk in range(D // 128):
        wk_items.append((fgu.ap()[128 * blk:128 * (blk + 1), :],
                         fguc.ap()[128 * blk:128 * (blk + 1), :]))
    for blk in range(FI // NCORES // 128):
        wk_items.append((fdn.ap()[128 * blk:128 * (blk + 1), :],
                         fdnc.ap()[128 * blk:128 * (blk + 1), :]))
    NWI = len(wk_items)
    wk_gathers = [(2 * cs + 1, fw1cAs[cs], fw1gAs[cs])
                  for cs in range(FW_G // 2048)]
    wk_emitted = [False] * NWI
    wk_dvw = [0] * NWI
    wk_stw = [0] * NWI
    wk_swl = [0] * NWI
    dv_p0 = s_dv.n

    def wk_load(n):
        s_wg.wait(gps)
        if n == 0:
            s_dv.wait(gps, dv_p0)
        if n >= 2:
            s_dv.wait(gps, wk_dvw[n - 2])
        s_wg.dma(gps.dma_start(
            out=wc[:, (n % 2) * 2048:(n % 2 + 1) * 2048], in_=wk_items[n][0]))
        wk_swl[n] = s_wg.n

    def wk_pump(n):
        if n + 1 < NWI:
            wk_load(n + 1)
        s_wg.wait(dve, wk_swl[n])
        if n == 0:
            s_cc.wait(dve, cc_attnw)
        if n >= 2:
            s_sg.wait(dve, wk_stw[n - 2])
        s_dv.inc(dve.always.tensor_copy(wco[:, (n % 2) * 2048:(n % 2 + 1) * 2048],
                                 wc[:, (n % 2) * 2048:(n % 2 + 1) * 2048]))
        wk_dvw[n] = s_dv.n
        s_dv.wait(gps, wk_dvw[n])
        s_sg.wait(gps)
        s_sg.dma(gps.dma_start(out=wk_items[n][1],
                               in_=wco[:, (n % 2) * 2048:(n % 2 + 1) * 2048]))
        wk_stw[n] = s_sg.n
        wk_emitted[n] = True

    s_st.wait(gps)
    s_cc.inc(gps.always.collective_compute("AllGather", ALU.bypass, RG,
                                    [a1c.ap().opt()], [a1g.ap().opt()]))
    s_cc.inc(gps.always.collective_compute("AllGather", ALU.bypass, RG,
                                    [a2c.ap().opt()], [a2g.ap().opt()]))
    cc_attnw = s_cc.n
    cc_ffnw = [0]
    zz = wcctx.enter_context(nc.sbuf_tensor("zz", [128, 2052], F32))
    s_gp.inc(gps.always.memset(zz[:, :], 0.0))
    s_gp.wait(sync)
    zz_pend = list(range(cfg.KSEL // 128))

    syncpoint()

    def finish(ret=None):
        s_st.wait(sync)
        s_in.wait(sync)
        s_cc.wait(gps)
        bar()
        return nc

    if stop_after == "P0":
        return finish()

    with contextlib.ExitStack() as p3ctx:
        SB3 = lambda name, shape, dt=F32: p3ctx.enter_context(
            nc.sbuf_tensor(name, list(shape), dt))
        pscr = SB3("pscr", [128, 2048])
        stage = SB3("stage", [128, DC * Tc], BF16)
        p2ctx = contextlib.ExitStack()
        SB2 = lambda name, shape, dt=F32: p2ctx.enter_context(
            nc.sbuf_tensor(name, list(shape), dt))
        ln_t = SB2("ln_t", [128, NT * 2048], BF16)
        with nc.sbuf_tensor("xt", [128, 2 * 2048], F32) as xt, \
             nc.sbuf_tensor("sa", [128, 1], F32) as sa, \
             nc.sbuf_tensor("sb", [128, 1], F32) as sb, \
             nc.sbuf_tensor("sc2", [128, 1], F32) as sc2, \
             nc.psum_tensor("tp0", [128, 128], BF16) as tp0, \
             nc.psum_tensor("tp1", [128, 128], BF16) as tp1, \
             nc.psum_tensor("tp2", [128, 128], BF16) as tp2, \
             nc.psum_tensor("tp3", [128, 128], BF16) as tp3:
            tps = [tp0, tp1, tp2, tp3]
            xld = [0] * NT
            dvl = [0] * (NT * DC + 8)
            k = 0
            ln_done = [0] * NT

            def x_load(tb):
                s_x.wait(sync)
                s_dv.wait(sync)
                s_ac.wait(sync)
                s_x.dma(sync.dma_start(
                    out=xt[:, (tb % 2) * 2048:(tb % 2 + 1) * 2048],
                    in_=x_in.ap()[128 * tb:128 * (tb + 1), :]))
                xld[tb] = s_x.n

            x_load(0)
            for tb in range(NT):
                if tb + 1 < NT:
                    x_load(tb + 1)
                xa = xt[:, (tb % 2) * 2048:(tb % 2 + 1) * 2048]
                s_x.wait(dve, xld[tb])
                s_x.wait(act, xld[tb])
                ln_stats(xa, sa, sb, sc2, EPS_LN)
                s_dv.inc(dve.always.tensor_scalar(ln_t[:, 2048 * tb:2048 * (tb + 1)],
                                           xa, sa[:, :], sc2[:, :],
                                           ALU.subtract, ALU.mult))
                ln_done[tb] = s_dv.n
                s_dv.wait(pe, ln_done[tb])
                for dc in range(DC):
                    pp = tps[k % 4]
                    if k >= 4:
                        s_dv.wait(pe, dvl[k - 4])
                    s_pe.inc(pe.transpose(
                        pp[:, :], ln_t[:, 2048 * tb + 128 * dc:
                                       2048 * tb + 128 * (dc + 1)],
                        ey_bf[:, :]))
                    s_pe.wait(dve)
                    s_dv.inc(dve.always.tensor_copy(
                        stage[:, Tc * dc + 128 * tb:Tc * dc + 128 * (tb + 1)],
                        pp[:, :]))
                    dvl[k] = s_dv.n
                    k += 1
        if stop_after == "P2":
            s_dv.wait(sync)
            return finish()
        syncpoint()
        p2ctx.close()
        s_cc.wait(sync, cc_attnw)
        wq_t = SB3("wq_t", [128, 4 * DC * 128], BF16)
        wk_t = SB3("wk_t", [128, 4 * DC * 128], BF16)
        wv_t = SB3("wv_t", [128, DC * 512], BF16)
        WQW = 4 * DC * 128
        for hp in range(4):
            for g in range(2):
                h = 2 * hp + g
                base = hp * DC * 128 + 64 * g
                asrc = a1g.ap()[D * h:D * (h + 1), :].rearrange(
                    "(dc p) k -> p dc k", p=128)
                s_in.dma(sync.dma_start(
                    out=bass.AP(wq_t, base, [[WQW, 128], [128, DC], [1, 64]]),
                    in_=asrc[:, :, 0:DH]))
                s_in.dma(sync.dma_start(
                    out=bass.AP(wk_t, base, [[WQW, 128], [128, DC], [1, 64]]),
                    in_=asrc[:, :, DH:2 * DH]))
        for h in range(HE):
            s_in.dma(sync.dma_start(
                out=bass.AP(wv_t, 64 * h, [[DC * 512, 128], [512, DC], [1, 64]]),
                in_=a1g.ap()[D * h:D * (h + 1), 2 * DH:3 * DH]
                .rearrange("(dc p) k -> p dc k", p=128)))
        qkt_s = SB3("qkt_s", [128, 8 * Tc], BF16)
        vt_s = SB3("vt_s", [128, NT * 512], BF16)
        with nc.psum_tensor("pj_ps", [128, 512], F32) as pj_ps, \
             nc.psum_tensor("pj_ps2", [128, 512], F32) as pj_ps2, \
             nc.psum_tensor("pj_ps3", [128, 512], F32) as pj_ps3, \
             nc.psum_tensor("pj_ps4", [128, 512], F32) as pj_ps4:
            pjs = [pj_ps, pj_ps2, pj_ps3, pj_ps4]
            from concourse.aot_env import allow_ici_all_to_all
            s_in.wait(pe)
            s_dv.wait(pe)
            k = 0
            dv_base = s_dv.n
            for kind in range(2):
                wt = wq_t if kind == 0 else wk_t
                for hp in range(4):
                    for ns in range(NS):
                        pp = pjs[k % 4]
                        if k >= 4:
                            s_dv.wait(pe, dv_base + k - 3)
                        for dc in range(DC):
                            s_pe.inc(pe.matmul(
                                pp[:, :],
                                wt[:, (hp * DC + dc) * 128:
                                   (hp * DC + dc) * 128 + 128],
                                stage[:, Tc * dc + 512 * ns:
                                      Tc * dc + 512 * (ns + 1)],
                                start=(dc == 0), stop=(dc == DC - 1)))
                        s_pe.wait(dve)
                        s_dv.inc(dve.always.tensor_copy(
                            qkt_s[:, (kind * 4 + hp) * Tc + 512 * ns:
                                  (kind * 4 + hp) * Tc + 512 * (ns + 1)],
                            pp[:, :]))
                        k += 1
                s_dv.wait(sync)
                s_st.wait(sync)
                if kind == 0:
                    for hp in range(4):
                        s_st.dma(sync.dma_start(
                            out=qc.ap()[128 * hp:128 * (hp + 1), :],
                            in_=qkt_s[:, hp * Tc:(hp + 1) * Tc]))
                    s_st.wait(gps)
                    s_cc.inc(gps.always.collective_compute(
                        "AllGather", ALU.bypass, RG,
                        [qc.ap().opt()], [qg.ap().opt()]))
                else:
                    for m in range(cfg.NKC):
                        d_ = m % NCORES
                        half = m // NCORES
                        for hp in range(4):
                            s_st.dma(sync.dma_start(
                                out=ktc.ap()[512 * d_ + 128 * hp:
                                             512 * d_ + 128 * (hp + 1),
                                             128 * half:128 * (half + 1)],
                                in_=qkt_s[:, (4 + hp) * Tc + 128 * m:
                                          (4 + hp) * Tc + 128 * (m + 1)]))
                    s_st.wait(gps)
                    with allow_ici_all_to_all():
                        s_cc.inc(gps.always.collective_compute(
                            "AllToAll", ALU.bypass, RG,
                            [ktc.ap().opt()], [kta.ap().opt()]))
            for tb in range(NT):
                pp = pjs[k % 4]
                s_dv.wait(pe, dv_base + k - 3)
                for dc in range(DC):
                    s_pe.inc(pe.matmul(
                        pp[:, :],
                        stage[:, Tc * dc + 128 * tb:Tc * dc + 128 * (tb + 1)],
                        wv_t[:, 512 * dc:512 * (dc + 1)],
                        start=(dc == 0), stop=(dc == DC - 1)))
                s_pe.wait(dve)
                s_dv.inc(dve.always.tensor_copy(vt_s[:, 512 * tb:512 * (tb + 1)],
                                         pp[:, :]))
                k += 1
        s_dv.wait(sync)
        s_st.wait(sync)
        for m in range(cfg.NKC):
            d_ = m % NCORES
            half = m // NCORES
            s_st.dma(sync.dma_start(
                out=vtc.ap()[(Tc // NCORES) * d_ + 128 * half:
                             (Tc // NCORES) * d_ + 128 * (half + 1), :],
                in_=vt_s[:, 512 * m:512 * (m + 1)]))
        s_st.wait(gps)
        with allow_ici_all_to_all():
            s_cc.inc(gps.always.collective_compute("AllToAll", ALU.bypass, RG,
                                            [vtc.ap().opt()], [vta.ap().opt()]))
        cc_qkv = s_cc.n
    if stop_after == "P3":
        return finish()

    syncpoint()
    with contextlib.ExitStack() as p5ctx:
        SB5 = lambda name, shape, dt=F32: p5ctx.enter_context(
            nc.sbuf_tensor(name, list(shape), dt))
        ktx = SB5("ktx", [128, 4 * cfg.NKC * 128], BF16)
        vres = SB5("vres", [128, cfg.NKC * 512], BF16)
        s_cc.wait(sync, cc_qkv)
        NH2_ = cfg.NKC // 8
        for hp in range(4):
            s_ld.dma(sync.dma_start(
                out=ktx[:, hp * cfg.NKC * 128:(hp + 1) * cfg.NKC * 128]
                .rearrange("p (j2 half k) -> p j2 half k", half=NH2_, k=128),
                in_=kta.ap().rearrange("(j2 x p) (half k) -> x p j2 half k",
                                       x=4, p=128, k=128)[hp]))
        s_ld.dma(sync.dma_start(
            out=vres[:, :].rearrange("p (j2 half k) -> p j2 half k",
                                     half=NH2_, k=512),
            in_=vta.ap().rearrange("(j2 half p) k -> p j2 half k",
                                   half=NH2_, p=128)))
        ld_res = s_ld.n
        qbuf = SB5("qbuf", [128, 24 * QT], BF16)
        mbuf = SB5("mbuf", [128, 12 * QT], BF16)
        pT = SB5("pT", [128, 4 * QT], BF16)
        zst = SB5("zst", [128, 8 * 512])
        lstA = SB5("lstA", [1, 8 * 512])
        lstB = SB5("lstB", [1, 8 * 512])
        with nc.psum_tensor("s_ps", [128, 2 * QT], F32) as s_psA, \
             nc.psum_tensor("s_ps2", [128, 2 * QT], F32) as s_psB, \
             nc.psum_tensor("z_psa", [128, QT], F32) as z_psa, \
             nc.psum_tensor("z_psb", [128, QT], F32) as z_psb, \
             nc.psum_tensor("l_psa", [128, QT], F32) as l_psa, \
             nc.psum_tensor("l_psb", [128, QT], F32) as l_psb:

            def q_load(t, sl):
                o = (QT * t) // Tc
                col = QT * t - Tc * o
                for hp in range(4):
                    s_qp[sl].dma(sync.dma_start(
                        out=qbuf[:, (sl * 4 + hp) * QT:
                                 (sl * 4 + hp + 1) * QT],
                        in_=qg.ap()[512 * o + 128 * hp:512 * o + 128 * (hp + 1),
                                    col:col + QT]))

            def m_load(t, sl):
                base = len([s for s in cfg.slots if s[0] < t])
                for u in range(min(2, cfg.trips[t])):
                    s_qp[sl].dma(sync.dma_start(
                        out=mbuf[:, (sl * 2 + u) * QT:
                                 (sl * 2 + u + 1) * QT],
                        in_=dm_i.ap()[base + u, :, :]))

            NBLK = Tc // QT
            tseq = [t_ for b_ in range(NBLK) for t_ in range(b_, NQT, NBLK)]
            vbase = [0] * NQT
            for t_ in range(1, NQT):
                vbase[t_] = vbase[t_ - 1] + cfg.trips[t_ - 1]
            visits = []
            vend = []
            tslot = {}
            for si_, t_ in enumerate(tseq):
                trips_ = cfg.trips[t_]
                jm0_ = trips_ - min(2, trips_)
                tslot[t_] = si_ % 6
                for hp_ in range(4):
                    for j_ in range(trips_):
                        visits.append((t_, hp_, j_, trips_, jm0_,
                                       vbase[t_] + j_))
                vend.append(len(visits))
            NVIS = len(visits)
            seq_of = {t_: si_ for si_, t_ in enumerate(tseq)}
            ldcount = [0] * NQT
            for pf_ in range(min(5, len(tseq))):
                q_load(tseq[pf_], pf_)
                m_load(tseq[pf_], pf_)
                ldcount[tseq[pf_]] = s_qp[pf_].n
            av_hist = [0] * NVIS
            mk_hist = [0] * NVIS
            sc_hist = [0] * NVIS
            cc_zlb = [0] * NBLK
            wk_gat_next = [0]
            PUMP_EVERY = max(1, (2 * NVIS) // (3 * NWI))
            wk_load(0)

            qp_waited = set()

            def emit_scores(k):
                t_, hp_, j_, trips_, jm0_, v_ = visits[k]
                sp = s_psA if k % 2 == 0 else s_psB
                if t_ not in qp_waited:
                    s_qp[tslot[t_]].wait(pe, ldcount[t_])
                    qp_waited.add(t_)
                if k == 0:
                    s_ld.wait(pe)
                if k >= 2:
                    s_ex.wait(pe, k - 1)
                kof = (hp_ * cfg.NKC + j_) * 128
                qof = (tslot[t_] * 4 + hp_) * QT
                s_sc.inc(pe.matmul(sp[:, 0:QT], ktx[0:64, kof:kof + 128],
                                   qbuf[0:64, qof:qof + QT],
                                   start=True, stop=True, tile_position=(0, 0)))
                s_sc.inc(pe.matmul(sp[:, QT:2 * QT], ktx[64:128, kof:kof + 128],
                                   qbuf[64:128, qof:qof + QT],
                                   start=True, stop=True, tile_position=(64, 0)))
                sc_hist[k] = s_sc.n

            st_hist = {}

            def emit_evac(av_t, g_):
                sl_ = 512 * (g_ % 8)
                s_av.wait(dve, av_t)
                if g_ >= 8:
                    s_st.wait(dve, st_hist[g_ - 8])
                s_ev.inc(dve.tensor_copy(zst[0:64, sl_:sl_ + 512],
                                         z_psa[0:64, :]))
                s_ev.inc(dve.tensor_copy(zst[64:128, sl_:sl_ + 512],
                                         z_psb[64:128, :]))
                s_av.wait(act, av_t)
                if g_ >= 8:
                    s_st.wait(act, st_hist[g_ - 8])
                s_ev2.inc(act.copy(lstA[:, sl_:sl_ + 512], l_psa[0:1, :]))
                s_ev2.inc(act.copy(lstB[:, sl_:sl_ + 512], l_psb[64:65, :]))

            def emit_store(t_p, hp_p, g_):
                sl_ = 512 * (g_ % 8)
                s_ev.wait(sync)
                s_ev2.wait(sync)
                s_st.wait(sync)
                r_ = t_p // NBLK
                zt = zlcb[t_p % NBLK]
                s_st.dma(sync.dma_start(
                    out=zt.ap()[520 * r_ + 128 * hp_p:
                                520 * r_ + 128 * (hp_p + 1), :],
                    in_=zst[:, sl_:sl_ + 512]))
                s_st.dma(sync.dma_start(
                    out=zt.ap()[520 * r_ + 512 + 2 * hp_p:
                                520 * r_ + 513 + 2 * hp_p, :],
                    in_=lstA[:, sl_:sl_ + 512]))
                s_st.dma(sync.dma_start(
                    out=zt.ap()[520 * r_ + 513 + 2 * hp_p:
                                520 * r_ + 514 + 2 * hp_p, :],
                    in_=lstB[:, sl_:sl_ + 512]))
                st_hist[g_] = s_st.n
                if hp_p == 3 and t_p == NQT - NBLK + (t_p % NBLK):
                    b_ = t_p % NBLK
                    s_st.wait(gps)
                    s_cc.inc(gps.always.collective_compute(
                        "ReduceScatter", ALU.add, RG,
                        [zlcb[b_].ap().opt()], [zlrb[b_].ap().opt()]))
                    cc_zlb[b_] = s_cc.n
                    while wk_gat_next[0] < len(wk_gathers):
                        li, cin_, cout_ = wk_gathers[wk_gat_next[0]]
                        if not wk_emitted[li]:
                            break
                        s_sg.wait(gps, wk_stw[li])
                        s_cc.inc(gps.always.collective_compute(
                            "AllGather", ALU.bypass, RG,
                            [cin_.ap().opt()], [cout_.ap().opt()]))
                        cc_ffnw[0] = s_cc.n
                        wk_gat_next[0] += 1

            pend = None
            gcount = 0
            emit_scores(0)
            for k in range(NVIS):
                t, hp, j, trips, jmask0, v = visits[k]
                pj = k % 2
                sp = s_psA if pj == 0 else s_psB
                if j == 0 and hp == 0:
                    si = seq_of[t]
                    if zz_pend:
                        r_z = zz_pend.pop(0)
                        s_zz.dma(sync.dma_start(
                            out=selhc.ap()[128 * r_z:128 * (r_z + 1), :],
                            in_=zz[:, 0:2048]))
                        s_zz.dma(sync.dma_start(
                            out=selxc.ap()[128 * r_z:128 * (r_z + 1), :],
                            in_=zz[:, 0:4]))
                    if si + 5 < len(tseq):
                        tn = tseq[si + 5]
                        if si >= 1:
                            s_sc.wait(sync, 2 * vend[si - 1])
                            s_mk.wait(sync, mk_hist[vend[si - 1] - 1])
                        q_load(tn, (si + 5) % 6)
                        m_load(tn, (si + 5) % 6)
                        ldcount[tn] = s_qp[(si + 5) % 6].n
                if k + 1 < NVIS:
                    emit_scores(k + 1)
                s_sc.wait(act, sc_hist[k])
                if k >= 2:
                    s_av.wait(act, av_hist[k - 2])
                ei = s_ex.inc(act.activation(
                    pT[:, pj * 2 * QT:(pj + 1) * 2 * QT],
                    sp[:, :], AF.Exp, scale=0.125, bias=vb_s[:, v:v + 1]))
                if pend is not None:
                    emit_evac(pend[2], pend[3])
                    emit_store(pend[0], pend[1], pend[3])
                    pend = None
                masked = j >= jmask0
                if masked:
                    u = j - jmask0
                    mof = (tslot[t] * 2 + u) * QT
                    s_ex.wait(dve, k + 1)
                    s_qp[tslot[t]].wait(dve, ldcount[t])
                    s_mk.inc(dve.tensor_mul(
                        pT[:, pj * 2 * QT:pj * 2 * QT + QT],
                        pT[:, pj * 2 * QT:pj * 2 * QT + QT],
                        mbuf[:, mof:mof + QT]))
                    s_mk.inc(dve.tensor_mul(
                        pT[:, pj * 2 * QT + QT:(pj + 1) * 2 * QT],
                        pT[:, pj * 2 * QT + QT:(pj + 1) * 2 * QT],
                        mbuf[:, mof:mof + QT]))
                mk_hist[k] = s_mk.n
                if k % PUMP_EVERY == PUMP_EVERY - 1 and k // PUMP_EVERY < NWI:
                    wk_pump(k // PUMP_EVERY)
                s_ex.wait(pe, k + 1)
                if masked:
                    s_mk.wait(pe, mk_hist[k])
                ev0 = s_ev.n if (j == 0 and k > 0) else None
                ev2 = s_ev2.n if (j == 0 and k > 0) else None
                st_ = (j == 0)
                sp_ = (j == trips - 1)
                if ev0 is not None:
                    s_ev.wait(pe, ev0 - 1)
                s_av.inc(pe.matmul(z_psa[0:64, :],
                                   vres[:, 512 * j + 128 * hp:
                                        512 * j + 128 * hp + 64],
                                   pT[:, pj * 2 * QT:pj * 2 * QT + QT],
                                   start=st_, stop=sp_, tile_position=(0, 0)))
                if ev0 is not None:
                    s_ev.wait(pe, ev0)
                s_av.inc(pe.matmul(z_psb[64:128, :],
                                   vres[:, 512 * j + 128 * hp + 64:
                                        512 * j + 128 * (hp + 1)],
                                   pT[:, pj * 2 * QT + QT:(pj + 1) * 2 * QT],
                                   start=st_, stop=sp_, tile_position=(0, 64)))
                if ev0 is not None:
                    s_ev2.wait(pe, ev2 - 1)
                s_av.inc(pe.matmul(l_psa[0:1, :], on_bf[:, 0:1],
                                   pT[:, pj * 2 * QT:pj * 2 * QT + QT],
                                   start=st_, stop=sp_, tile_position=(0, 0)))
                if ev0 is not None:
                    s_ev2.wait(pe, ev2)
                s_av.inc(pe.matmul(l_psb[64:65, :], on_bf[:, 0:1],
                                   pT[:, pj * 2 * QT + QT:(pj + 1) * 2 * QT],
                                   start=st_, stop=sp_, tile_position=(0, 64)))
                av_hist[k] = s_av.n
                if j == trips - 1:
                    pend = (t, hp, av_hist[k], gcount)
                    gcount += 1
            emit_evac(pend[2], pend[3])
            emit_store(pend[0], pend[1], pend[3])
        s_st.wait(gps)
    wcctx.close()
    if stop_after == "P5":
        return finish()

    syncpoint()
    with contextlib.ExitStack() as p7ctx:
        SB7 = lambda name, shape, dt=F32: p7ctx.enter_context(
            nc.sbuf_tensor(name, list(shape), dt))
        zfm = SB7("zfm", [128, 4 * Tc], BF16)
        with nc.sbuf_tensor("zn_f", [128, Tc], F32) as zn_f, \
             nc.sbuf_tensor("li_ab", [2, Tc], F32) as li_ab, \
             nc.sbuf_tensor("rcp_t", [128, 512], F32) as rcp_t, \
             nc.psum_tensor("lb_ps", [128, 512], F32) as lb_ps:
            ldb = {}
            for hp in range(4):
                s_dv.wait(sync)
                for b_ in range(NZB):
                    if hp == 0:
                        s_cc.wait(sync, cc_zlb[b_])
                    s_ld.wait(sync)
                    s_ld.dma(sync.dma_start(
                        out=zn_f[:, 512 * b_:512 * (b_ + 1)],
                        in_=zlrb[b_].ap()[128 * hp:128 * (hp + 1), :]))
                    s_ld.dma(sync.dma_start(
                        out=li_ab[:, 512 * b_:512 * (b_ + 1)],
                        in_=zlrb[b_].ap()[512 + 2 * hp:514 + 2 * hp, :]))
                    ldb[b_] = s_ld.n
                for ns in range(NS):
                    s_ld.wait(pe, ldb[ns])
                    s_pe.inc(pe.matmul(lb_ps[:, :], selio[0:2, :],
                                       li_ab[:, 512 * ns:512 * (ns + 1)],
                                       start=True, stop=True))
                    s_pe.wait(dve)
                    s_ld.wait(dve, ldb[ns])
                    dve.reciprocal(rcp_t[:, :], lb_ps[:, :])
                    s_dv.inc(dve.always.tensor_mul(
                        zfm[:, hp * Tc + 512 * ns:hp * Tc + 512 * (ns + 1)],
                        zn_f[:, 512 * ns:512 * (ns + 1)], rcp_t[:, :]))
                    s_dv.wait(pe)
        s_dv.wait(sync)
        wo_t = SB7("wo_t", [128, 4 * DC * 128], BF16)
        for hp in range(4):
            s_in.dma(sync.dma_start(
                out=wo_t[:, hp * DC * 128:(hp + 1) * DC * 128],
                in_=a2g.ap()[128 * hp:128 * (hp + 1), :]))
        aotm = SB7("aotm", [128, NT * 2048], BF16)
        with nc.psum_tensor("op_ps", [128, 512], F32) as op_ps, \
             nc.psum_tensor("op_ps2", [128, 512], F32) as op_ps2, \
             nc.psum_tensor("tq0", [128, 128], BF16) as tq0, \
             nc.psum_tensor("tq1", [128, 128], BF16) as tq1, \
             nc.psum_tensor("tq2", [128, 128], BF16) as tq2, \
             nc.psum_tensor("tq3", [128, 128], BF16) as tq3, \
             nc.sbuf_tensor("ao_f", [128, 2 * Tc], BF16) as ao_f:
            tqs = [tq0, tq1, tq2, tq3]
            s_in.wait(pe)
            s_dv.wait(pe)
            k7 = 0
            dvl7 = [0] * (DC * NT + 8)
            aof_done = [0] * DC
            for dc in range(DC):
                asl = (dc % 2) * Tc
                for ns in range(NS):
                    pp = op_ps if ns % 2 == 0 else op_ps2
                    if not (dc == 0 and ns == 0):
                        s_dv.wait(pe, s_dv.n - 1)
                    if dc >= 2 and ns == 0:
                        s_dv.wait(pe, aof_done[dc - 2])
                    for hp in range(4):
                        s_pe.inc(pe.matmul(
                            pp[:, :],
                            wo_t[:, (hp * DC + dc) * 128:
                                 (hp * DC + dc + 1) * 128],
                            zfm[:, hp * Tc + 512 * ns:hp * Tc + 512 * (ns + 1)],
                            start=(hp == 0), stop=(hp == 3)))
                    s_pe.wait(dve)
                    s_dv.inc(dve.always.tensor_copy(
                        ao_f[:, asl + 512 * ns:asl + 512 * (ns + 1)], pp[:, :]))
                s_dv.wait(pe)
                for tb in range(NT):
                    pp2 = tqs[k7 % 4]
                    if k7 >= 4:
                        s_dv.wait(pe, dvl7[k7 - 4])
                    s_pe.inc(pe.transpose(pp2[:, :],
                                          ao_f[:, asl + 128 * tb:
                                               asl + 128 * (tb + 1)],
                                          ey_bf[:, :]))
                    s_pe.wait(dve)
                    s_dv.inc(dve.always.tensor_copy(
                        aotm[:, 2048 * tb + 128 * dc:2048 * tb + 128 * (dc + 1)],
                        pp2[:, :]))
                    dvl7[k7] = s_dv.n
                    k7 += 1
                aof_done[dc] = s_dv.n
        pscr = SB7("pscr7", [128, 2048])
        wr_b = SB7("wr_b", [128, 2048])
        srw_b = SB7("srw_b", [128, 1])
        with nc.sbuf_tensor("wr_r", [1, 2048], F32) as wr_r, \
             nc.psum_tensor("br_ps", [128, 512], F32) as br_ps:
            s_in.wait(sync)
            s_in.dma(sync.dma_start(out=wr_r[:, :], in_=wr_i.ap()[None, :]))
            s_in.wait(pe)
            for ns in range(4):
                s_pe.inc(pe.matmul(br_ps[:, :], on_f[0:1, :],
                                   wr_r[:, 512 * ns:512 * (ns + 1)],
                                   start=True, stop=True))
                s_pe.wait(dve)
                s_dv.inc(dve.always.tensor_copy(wr_b[:, 512 * ns:512 * (ns + 1)],
                                         br_ps[:, :]))
                s_dv.wait(pe)
        s_dv.inc(dve.always.reduce_sum(srw_b[:, :], wr_b[:, :], AX.X))
        with nc.sbuf_tensor("xt2", [128, 2 * 2048], F32) as xt2, \
             nc.sbuf_tensor("at2", [128, 2 * 2048], F32) as at2, \
             nc.sbuf_tensor("x1t", [128, 2 * 2048], F32) as x1t, \
             nc.sbuf_tensor("sa7", [128, 2], F32) as sa, \
             nc.sbuf_tensor("sb7", [128, 2], F32) as sb, \
             nc.sbuf_tensor("sc27", [128, 2], F32) as sc2:
            xh7 = [0] * NT
            dv7 = [0] * NT
            st7 = [0] * NT

            def x7_fetch(tb):
                s_x.wait(sync)
                if tb >= 2:
                    s_dv.wait(sync, dv7[tb - 2])
                s_x.dma(sync.dma_start(
                    out=xt2[:, (tb % 2) * 2048:(tb % 2 + 1) * 2048],
                    in_=x_in.ap()[128 * tb:128 * (tb + 1), :]))
                xh7[tb] = s_x.n

            s_dv.wait(sync)
            s_ac.wait(sync)
            x7_fetch(0)
            x7_fetch(1)
            for tb in range(NT):
                sl = (tb % 2) * 2048
                c = tb % 2
                a2 = at2[:, sl:sl + 2048]
                x1s = x1t[:, sl:sl + 2048]
                sac = sa[:, c:c + 1]
                sbc = sb[:, c:c + 1]
                scc2 = sc2[:, c:c + 1]
                s_dv.inc(dve.always.tensor_copy(
                    a2, aotm[:, 2048 * tb:2048 * (tb + 1)]))
                cta = s_dv.n
                if tb >= 2:
                    s_dv.wait(act, dv7[tb - 2])
                s_dv.wait(act, cta)
                s_ac.inc(act.always.activation(pscr[:, :], a2, AF.Identity,
                                        accum_out=sac))
                s_ac.inc(act.always.activation(pscr[:, :], a2, AF.Square,
                                        accum_out=sbc))
                s_ac.wait(dve)
                dve.tensor_scalar_mul(sac, sac, 1.0 / D)
                dve.tensor_scalar_mul(sbc, sbc, 1.0 / D)
                dve.tensor_tensor(scc2, sac, sac, ALU.mult)
                s_dv.inc(dve.always.tensor_sub(sbc, sbc, scc2))
                s_dv.wait(act)
                s_ac.inc(act.always.activation(scc2, sbc, AF.Sqrt,
                                        bias=EPS_LN))
                s_ac.wait(dve)
                dve.reciprocal(scc2, scc2)
                dve.tensor_scalar(a2, a2, sac, scc2,
                                  ALU.subtract, ALU.mult)
                s_x.wait(dve, xh7[tb])
                if tb >= 2:
                    s_st.wait(dve, st7[tb - 2])
                s_dv.inc(dve.always.tensor_add(
                    x1s, xt2[:, sl:sl + 2048], a2))
                s_dv.wait(sync)
                s_st.wait(sync)
                s_st.dma(sync.dma_start(out=x1d.ap()[128 * tb:128 * (tb + 1), :],
                                        in_=x1s))
                st7[tb] = s_st.n
                ct2 = s_dv.n
                s_dv.wait(act, ct2)
                s_ac.inc(act.always.activation(pscr[:, :], x1s, AF.Identity,
                                        accum_out=sac))
                s_ac.inc(act.always.activation(pscr[:, :], x1s, AF.Square,
                                        accum_out=sbc))
                s_ac.wait(dve)
                dve.tensor_scalar_mul(m1_s[:, tb:tb + 1], sac, 1.0 / D)
                dve.tensor_scalar_mul(sbc, sbc, 1.0 / D)
                dve.tensor_tensor(scc2, m1_s[:, tb:tb + 1],
                                  m1_s[:, tb:tb + 1], ALU.mult)
                s_dv.inc(dve.always.tensor_sub(v1_s[:, tb:tb + 1], sbc, scc2))
                s_dv.wait(act)
                s_ac.inc(act.always.activation(sac, v1_s[:, tb:tb + 1],
                                        AF.Sqrt, bias=EPS_LN))
                s_ac.wait(dve)
                dve.reciprocal(r1_s[:, tb:tb + 1], sac)
                dve.tensor_mul(a2, x1s, wr_b[:, :])
                dve.reduce_sum(sac, a2, AX.X)
                dve.tensor_mul(sbc, m1_s[:, tb:tb + 1], srw_b[:, :])
                dve.tensor_sub(sac, sac, sbc)
                s_dv.inc(dve.always.tensor_mul(sc_s[:, tb:tb + 1], sac,
                                        r1_s[:, tb:tb + 1]))
                dv7[tb] = s_dv.n
                if tb + 2 < NT:
                    x7_fetch(tb + 2)
    if stop_after == "P7":
        return finish()

    syncpoint()
    with contextlib.ExitStack() as p8actx:
        SB8 = lambda name, shape, dt=F32: p8actx.enter_context(
            nc.sbuf_tensor(name, list(shape), dt))
        with nc.psum_tensor("sc_ps", [128, 128], F32) as sc_ps, \
             nc.sbuf_tensor("sc_t", [NT, 128], F32) as sc_t:
            s_dv.wait(pe)
            s_pe.inc(pe.transpose(sc_ps[0:NT, :], sc_s[:, :], ey_f[:, :]))
            s_pe.wait(dve)
            s_dv.inc(dve.always.tensor_copy(sc_t[:, :], sc_ps[0:NT, :]))
            s_dv.wait(sync)
            s_st.dma(sync.dma_start(
                out=scc.ap().rearrange("(tb p) one -> tb (p one)", p=128),
                in_=sc_t[:, :]))
        s_st.wait(gps)
        s_cc.inc(gps.always.collective_compute("AllGather", ALU.bypass, RG,
                                        [scc.ap().opt()], [scg.ap().opt()]))
        s_cc.wait(sync)
        s_st.wait(dve)
        s_st.wait(pe)
        th_lo = SB8("th_lo", [1, 1])
        th_sz = SB8("th_sz", [1, 1])
        thv = SB8("thv", [128, 1])
        cnt_s = SB8("cnt_s", [128, 1])
        nch = cfg.T // 2048
        with nc.sbuf_tensor("bcb", [128, nch * 2048], F32) as bcb, \
             nc.sbuf_tensor("bch", [128, nch * 2048], BF16) as bch, \
             nc.sbuf_tensor("cc1", [128, 1], F32) as cc1, \
             nc.sbuf_tensor("fl", [128, 1], F32) as fl, \
             nc.psum_tensor("tr1_ps", [128, 128], F32) as tr1_ps, \
             nc.sbuf_tensor("flt", [1, 128], F32) as flt, \
             nc.sbuf_tensor("mx", [1, 1], F32) as mx:
            for ch in range(nch):
                s_ld.dma(sync.dma_start(
                    out=bcb[:, 2048 * ch:2048 * (ch + 1)],
                    in_=bass.AP(scg, 2048 * ch, [[0, 128], [1, 2048]])))
            s_ld.wait(dve)
            dve.memset(th_lo[:, :], -16.0)
            s_dv.inc(dve.always.memset(th_sz[:, :], 32.0))
            for rnd in range(5):
                s_dv.wait(pe)
                s_pe.inc(pe.matmul(tr1_ps[:, 0:1], on_f[0:1, :], th_lo[:, :],
                                   start=True, stop=True))
                s_pe.inc(pe.matmul(tr1_ps[:, 1:2], on_f[0:1, :], th_sz[:, :],
                                   start=True, stop=True))
                s_pe.wait(dve)
                dve.tensor_scalar_mul(fl[:, :], tr1_ps[:, 1:2], 1.0 / 130.0)
                dve.tensor_mul(fl[:, :], fl[:, :], io1[:, :])
                dve.tensor_add(thv[:, :], fl[:, :], tr1_ps[:, 0:1])
                dve.tensor_scalar(bch[:, :], bcb[:, :], thv[:, :], None,
                                  ALU.is_gt, op1=ALU.add,
                                  accum_out=cc1[:, :])
                s_dv.inc(dve.always.tensor_copy(cnt_s[:, :], cc1[:, :]))
                dve.tensor_scalar(fl[:, :], cnt_s[:, :],
                                  float(cfg.KSEL) - 0.5, None, ALU.is_gt)
                s_dv.inc(dve.always.tensor_mul(fl[:, :], fl[:, :], io1[:, :]))
                s_dv.wait(pe)
                s_pe.inc(pe.transpose(tr1_ps[0:1, :], fl[:, :], ey_f[:, :]))
                s_pe.wait(dve)
                dve.tensor_copy(flt[:, :], tr1_ps[0:1, :])
                dve.reduce_max(mx[:, :], flt[:, :], AX.X)
                dve.tensor_scalar_mul(mx[:, :], mx[:, :], 1.0 / 130.0)
                dve.tensor_mul(mx[:, :], mx[:, :], th_sz[:, :])
                dve.tensor_add(th_lo[:, :], th_lo[:, :], mx[:, :])
                s_dv.inc(dve.always.tensor_scalar_mul(th_sz[:, :], th_sz[:, :],
                                               1.0 / 130.0))
        with nc.psum_tensor("px_ps", [128, 128], F32) as px_ps, \
             nc.sbuf_tensor("thb", [128, 1], F32) as thb, \
             nc.sbuf_tensor("rs1", [128, 1], F32) as rs1, \
             nc.sbuf_tensor("cs16", [16, 1], F32) as cs16, \
             nc.sbuf_tensor("pre16", [16, 1], F32) as pre16, \
             nc.sbuf_tensor("pre16t", [1, 16], F32) as pre16t, \
             nc.sbuf_tensor("preb", [128, NT], F32) as preb, \
             nc.sbuf_tensor("cnt1", [1, 1], F32) as cnt1, \
             nc.sbuf_tensor("cnt8", [8, 1], F32) as cnt8, \
             nc.sbuf_tensor("base1", [1, 1], F32) as base1, \
             nc.sbuf_tensor("pos_s", [128, NT], F32) as pos_s:
            s_dv.wait(pe)
            s_pe.inc(pe.matmul(px_ps[:, 0:1], on_f[0:1, :], th_lo[:, :],
                               start=True, stop=True))
            s_pe.wait(dve)
            s_dv.inc(dve.always.tensor_copy(thb[:, :], px_ps[:, 0:1]))
            dve.tensor_scalar(msk_s[:, :], sc_s[:, :], thb[:, :], None,
                              ALU.is_gt)
            s_dv.wait(act)
            s_ac.inc(act.always.activation(gate_s[:, :], sc_s[:, :], AF.Sigmoid))
            s_dv.inc(dve.always.reduce_sum(rs1[:, :], msk_s[:, :], AX.X))
            s_dv.wait(pe)
            s_pe.inc(pe.matmul(px_ps[:, 0:NT], u128[:, :], msk_s[:, :],
                               start=True, stop=True))
            s_pe.inc(pe.matmul(px_ps[0:NT, NT:NT + 1], msk_s[:, :], on_f[:, 0:1],
                               start=True, stop=True))
            s_pe.inc(pe.matmul(px_ps[0:1, NT + 1:NT + 2], rs1[:, :], on_f[:, 0:1],
                               start=True, stop=True))
            s_pe.wait(dve)
            dve.tensor_copy(pos_s[:, :], px_ps[:, 0:NT])
            dve.tensor_copy(cs16[0:NT, :], px_ps[0:NT, NT:NT + 1])
            s_dv.inc(dve.always.tensor_copy(cnt1[:, :], px_ps[0:1, NT + 1:NT + 2]))
            s_dv.wait(sync)
            s_st.dma(sync.dma_start(out=cntc.ap(), in_=cnt1[:, :]))
            s_st.wait(gps)
            s_cc.inc(gps.always.collective_compute("AllGather", ALU.bypass, RG,
                                            [cntc.ap().opt()], [cntg.ap().opt()]))
            s_cc.wait(sync)
            s_ld.dma(sync.dma_start(out=cnt8[:, :], in_=cntg.ap()))
            s_ld.wait(dve)
            s_dv.inc(dve.always.tensor_mul(cnt8[:, :], cnt8[:, :], mc_s[:, :]))
            s_dv.wait(pe)
            s_pe.inc(pe.matmul(px_ps[0:1, 0:1], cnt8[:, :], on_f[0:8, 0:1],
                               start=True, stop=True))
            s_pe.inc(pe.matmul(px_ps[0:NT, 1:2], u16[0:NT, 0:NT], cs16[0:NT, :],
                               start=True, stop=True))
            s_pe.wait(dve)
            dve.tensor_copy(base1[:, :], px_ps[0:1, 0:1])
            s_dv.inc(dve.always.tensor_copy(pre16[0:NT, :], px_ps[0:NT, 1:2]))
            s_dv.wait(pe)
            s_pe.inc(pe.transpose(px_ps[0:1, 0:NT], pre16[0:NT, :],
                                  ey_f[0:NT, 0:NT]))
            s_pe.wait(dve)
            s_dv.inc(dve.always.tensor_copy(pre16t[:, 0:NT], px_ps[0:1, 0:NT]))
            s_dv.wait(pe)
            s_pe.inc(pe.matmul(px_ps[:, 0:NT], on_f[0:1, :], pre16t[:, 0:NT],
                               start=True, stop=True))
            s_pe.inc(pe.matmul(px_ps[:, NT:NT + 1], on_f[0:1, :], base1[:, :],
                               start=True, stop=True))
            s_pe.wait(dve)
            dve.tensor_copy(preb[:, :], px_ps[:, 0:NT])
            s_dv.inc(dve.always.tensor_copy(thb[:, :], px_ps[:, NT:NT + 1]))
            dve.tensor_add(pos_s[:, :], pos_s[:, :], preb[:, :])
            dve.tensor_scalar(pos_s[:, :], pos_s[:, :], thb[:, :], None,
                              ALU.add)
            BIGI = 524288.0
            dve.tensor_scalar(pos_s[:, :], pos_s[:, :], -BIGI, None, ALU.add)
            dve.tensor_mul(pos_s[:, :], pos_s[:, :], msk_s[:, :])
            dve.tensor_scalar(pos_s[:, :], pos_s[:, :], BIGI, None, ALU.add)
            s_dv.inc(dve.always.tensor_copy(idx_s[:, :], pos_s[:, :]))
        with nc.sbuf_tensor("x1b", [128, 2 * 2048], F32) as x1b, \
             nc.sbuf_tensor("hrow", [128, 2 * 2048], F32) as hrow, \
             nc.sbuf_tensor("aux2", [128, 4], F32) as aux2:
            s_ac.wait(dve)
            s_pe.wait(sync)
            s_dv.wait(sync)
            xh9 = [0] * NT
            dvh9 = [0] * NT
            sch9 = [0] * NT

            def s9_fetch(tb):
                s_x.wait(sync)
                if tb >= 2:
                    s_dv.wait(sync, dvh9[tb - 2])
                s_x.dma(sync.dma_start(
                    out=x1b[:, (tb % 2) * 2048:(tb % 2 + 1) * 2048],
                    in_=x1d.ap()[128 * tb:128 * (tb + 1), :]))
                xh9[tb] = s_x.n

            s9_fetch(0)
            s9_fetch(1)
            for tb in range(NT):
                sl = (tb % 2) * 2048
                s_x.wait(dve, xh9[tb])
                if tb >= 2:
                    s_sca.wait(dve, sch9[tb - 2])
                dve.tensor_scalar(hrow[:, sl:sl + 2048],
                                  x1b[:, sl:sl + 2048], m1_s[:, tb:tb + 1],
                                  r1_s[:, tb:tb + 1], ALU.subtract, ALU.mult)
                dve.tensor_copy(aux2[:, (tb % 2) * 2:(tb % 2) * 2 + 1],
                                gt_s[:, tb:tb + 1])
                s_dv.inc(dve.always.tensor_copy(
                    aux2[:, (tb % 2) * 2 + 1:(tb % 2) * 2 + 2],
                    gate_s[:, tb:tb + 1]))
                dvh9[tb] = s_dv.n
                s_dv.wait(gps, dvh9[tb])
                if tb == 0:
                    s_zz.wait(gps)
                s_sca.wait(gps)
                s_sca.dma(gps.indirect_dma_start(
                    out=selhc.ap(),
                    out_offset=bass.IndirectOffsetOnAxis(
                        ap=idx_s[:, tb:tb + 1], axis=0),
                    in_=hrow[:, sl:sl + 2048], in_offset=None,
                    bounds_check=bnd_reg, oob_is_err=False))
                s_sca.dma(gps.indirect_dma_start(
                    out=selxc.ap(),
                    out_offset=bass.IndirectOffsetOnAxis(
                        ap=idx_s[:, tb:tb + 1], axis=0),
                    in_=aux2[:, (tb % 2) * 2:(tb % 2) * 2 + 2], in_offset=None,
                    bounds_check=bnd_reg, oob_is_err=False))
                sch9[tb] = s_sca.n
                if tb + 2 < NT:
                    s9_fetch(tb + 2)
        s_sca.wait(gps)
        s_cc.inc(gps.always.collective_compute("ReduceScatter", ALU.add, RG,
                                        [selhc.ap().opt()], [selhr.ap().opt()]))
        s_cc.inc(gps.always.collective_compute("ReduceScatter", ALU.add, RG,
                                        [selxc.ap().opt()], [selxr.ap().opt()]))
        cc_sel = s_cc.n
    if stop_after == "P8a":
        return finish()

    syncpoint()
    with contextlib.ExitStack() as p8ctx:
        SB8b = lambda name, shape, dt=F32: p8ctx.enter_context(
            nc.sbuf_tensor(name, list(shape), dt))
        s_cc.wait(sync, cc_sel)
        s_cc.wait(sync, cc_ffnw[0])
        sel_t = SB8b("sel_t", [128, NKT * 2048])
        posv = SB8b("posv", [1, KC])
        gat_v = SB8b("gat_v", [128, NKT])
        for kt in range(NKT):
            s_ld.dma(sync.dma_start(out=sel_t[:, 2048 * kt:2048 * (kt + 1)],
                                    in_=selhr.ap()[128 * kt:128 * (kt + 1), :]))
            s_ld.dma(sync.dma_start(out=gat_v[:, kt:kt + 1],
                                    in_=selxr.ap()[128 * kt:128 * (kt + 1),
                                                   1:2],
                                    allow_slow_non_contiguous=True))
        s_ld.dma(sync.dma_start(
            out=posv[:, :],
            in_=selxr.ap()[:, 0:1].rearrange("r one -> one r"),
            allow_slow_non_contiguous=True))
        s_ld.wait(sync)
        ln1_b = SB8b("ln1_b", [128, 2048])
        ln2_b = SB8b("ln2_b", [128, 2048])
        with nc.sbuf_tensor("lnra", [1, 2048], F32) as lnra, \
             nc.sbuf_tensor("lnrb", [1, 2048], F32) as lnrb, \
             nc.psum_tensor("lb2_ps", [128, 512], F32) as lb2_ps:
            s_in.dma(sync.dma_start(out=lnra[:, :], in_=ln1_i.ap()[None, :]))
            s_in.dma(sync.dma_start(out=lnrb[:, :], in_=ln2_i.ap()[None, :]))
            s_in.wait(pe)
            for (lr, dst) in [(lnra, ln1_b), (lnrb, ln2_b)]:
                for ns in range(4):
                    s_pe.inc(pe.matmul(lb2_ps[:, :], on_f[0:1, :],
                                       lr[:, 512 * ns:512 * (ns + 1)],
                                       start=True, stop=True))
                    s_pe.wait(dve)
                    s_dv.inc(dve.always.tensor_copy(dst[:, 512 * ns:512 * (ns + 1)],
                                             lb2_ps[:, :]))
                    s_dv.wait(pe)
        s_in.wait(sync)
        for eng in (dve, act, gps):
            s_in.wait(eng)
            s_pe.wait(eng)
        pscr = SB8b("pscr8b", [128, 2048])
        mid = contextlib.ExitStack()
        SBm = lambda name, shape, dt=F32: mid.enter_context(
            nc.sbuf_tensor(name, list(shape), dt))
        h1f = SBm("h1f", [128, DC * KC], BF16)
        with nc.sbuf_tensor("h1t", [128, 2048], BF16) as h1t, \
             nc.sbuf_tensor("ra", [128, 1], F32) as ra, \
             nc.psum_tensor("ht_ps", [128, 128], BF16) as ht_ps:
            for kt in range(NKT):
                s_ld.wait(act)
                s_ld.wait(dve)
                s_ac.inc(act.always.activation(pscr[:, :],
                                        sel_t[:, 2048 * kt:2048 * (kt + 1)],
                                        AF.Square, accum_out=ra[:, :]))
                s_ac.wait(dve)
                s_dv.inc(dve.always.tensor_scalar_mul(ra[:, :], ra[:, :], 1.0 / D))
                s_dv.wait(act)
                s_ac.inc(act.always.activation(ra[:, :], ra[:, :], AF.Sqrt,
                                        bias=EPS_RMS))
                s_ac.wait(dve)
                dve.reciprocal(ra[:, :], ra[:, :])
                s_pe.wait(dve)
                s_dv.inc(dve.always.scalar_tensor_tensor(
                    h1t[:, :], sel_t[:, 2048 * kt:2048 * (kt + 1)], ra[:, :],
                    ln1_b[:, :], ALU.mult, ALU.mult))
                s_dv.wait(pe)
                for dc in range(DC):
                    s_pe.inc(pe.transpose(ht_ps[:, :],
                                          h1t[:, 128 * dc:128 * (dc + 1)],
                                          ey_bf[:, :]))
                    s_pe.wait(dve)
                    s_dv.inc(dve.always.tensor_copy(
                        h1f[:, KC * dc + 128 * kt:KC * dc + 128 * (kt + 1)],
                        ht_ps[:, :]))
                    s_dv.wait(pe)
        cosf = SBm("cosf", [128, KC])
        sinf = SBm("sinf", [128, KC])
        sneg = SBm("sneg", [128, KC])
        with nc.psum_tensor("an_ps", [128, KC], F32) as an_ps, \
             nc.sbuf_tensor("ang", [128, KC], F32) as ang, \
             nc.sbuf_tensor("ang2", [128, KC], F32) as ang2, \
             nc.sbuf_tensor("nn", [128, KC], F32) as nn, \
             nc.sbuf_tensor("nni", [128, KC], I32) as nni:
            s_ld.wait(pe)
            for ns in range(max(1, KC // 512)):
                w = min(512, KC)
                s_pe.inc(pe.matmul(an_ps[:, w * ns:w * (ns + 1)], ivf[:, :],
                                   posv[:, w * ns:w * (ns + 1)],
                                   start=True, stop=True))
            s_pe.wait(dve)
            s_dv.inc(dve.always.tensor_copy(ang[:, :], an_ps[:, :]))

            def range_reduce_sin(src_ap, out_ap):
                dve.tensor_scalar_mul(nn[:, :], src_ap, 1.0 / TWO_PI)
                dve.tensor_scalar(nn[:, :], nn[:, :], 0.5, None, ALU.add)
                dve.tensor_copy(nni[:, :], nn[:, :])
                dve.tensor_copy(nn[:, :], nni[:, :])
                dve.scalar_tensor_tensor(ang2[:, :], nn[:, :], -CW_C1, src_ap,
                                         ALU.mult, ALU.add)
                s_dv.inc(dve.always.scalar_tensor_tensor(ang2[:, :], nn[:, :], -CW_C2,
                                                  ang2[:, :], ALU.mult, ALU.add))
                s_dv.wait(act)
                s_ac.inc(act.always.activation(out_ap, ang2[:, :], AF.Sin))
                s_ac.wait(dve)

            range_reduce_sin(ang[:, :], sinf[:, :])
            dve.tensor_scalar(ang[:, :], ang[:, :], math.pi / 2.0, None,
                              ALU.add)
            range_reduce_sin(ang[:, :], cosf[:, :])
            dve.tensor_scalar_mul(sneg[0:64, :], sinf[0:64, :], -1.0)
            s_dv.inc(dve.always.tensor_copy(sneg[64:128, :], sinf[64:128, :]))
        fqt = SBm("fqt", [128, FH * KC], BF16)
        fkt = SBm("fkt", [128, HE * KC], BF16)
        fvt = SBm("fvt", [128, NKT * 1024], BF16)
        bq_s = SBm("bq_s", [128, DC])
        bkv_s = SBm("bkv_s", [128, 16])
        with nc.sbuf_tensor("btmp", [16, 128], F32) as btmp, \
             nc.psum_tensor("bt_ps", [128, 16], F32) as bt_ps:
            s_dv.wait(pe)
            s_ac.wait(pe)
            s_in.dma(sync.dma_start(
                out=btmp[:, :], in_=bq_i.ap().rearrange("(k p) -> k p", p=128)))
            s_in.wait(pe)
            s_pe.inc(pe.transpose(bt_ps[:, :], btmp[:, :], ey_f[0:16, 0:16]))
            s_pe.wait(dve)
            s_dv.inc(dve.always.tensor_copy(bq_s[:, :], bt_ps[:, :]))
            s_dv.wait(pe)
            s_pe.wait(sync)
            s_in.dma(sync.dma_start(
                out=btmp[0:8, :], in_=bk_i.ap().rearrange("(k p) -> k p", p=128)))
            s_in.wait(pe)
            s_pe.inc(pe.transpose(bt_ps[:, 0:8], btmp[0:8, :], ey_f[0:8, 0:8]))
            s_pe.wait(dve)
            s_dv.inc(dve.always.tensor_copy(bkv_s[:, 0:8], bt_ps[:, 0:8]))
            s_dv.wait(pe)
            s_pe.wait(sync)
            s_in.dma(sync.dma_start(
                out=btmp[0:8, :], in_=bv_i.ap().rearrange("(k p) -> k p", p=128)))
            s_in.wait(pe)
            s_pe.inc(pe.transpose(bt_ps[:, 0:8], btmp[0:8, :], ey_f[0:8, 0:8]))
            s_pe.wait(dve)
            s_dv.inc(dve.always.tensor_copy(bkv_s[:, 8:16], bt_ps[:, 0:8]))
            s_dv.wait(pe)
        with nc.sbuf_tensor("fw_t", [128, 2 * 2048], BF16) as fw_t, \
             nc.psum_tensor("fp_ps", [128, 1024], F32) as fp_ps, \
             nc.psum_tensor("fr_ps", [128, 128], BF16) as fr_ps, \
             nc.sbuf_tensor("prj", [128, KC], F32) as prj, \
             nc.sbuf_tensor("prh", [128, KC], F32) as prh, \
             nc.sbuf_tensor("sq_t", [128, KC], F32) as sq_t, \
             nc.sbuf_tensor("qsw", [128, KC], F32) as qsw, \
             nc.sbuf_tensor("vtmp", [128, KC], BF16) as vtmp:
            s_in.wait(dve)
            calls = ([(FW_WQ, h, bq_s, h, "q") for h in range(FH)]
                     + [(FW_WK, h, bkv_s, h, "k") for h in range(HE)]
                     + [(FW_WV, h, bkv_s, 8 + h, "v") for h in range(HE)])
            NCALL = len(calls)
            ldc = [0] * NCALL
            pe_after = [0] * NCALL
            dv_after = [0] * NCALL

            def load_call(n):
                col0, h = calls[n][0], calls[n][1]
                sl = (n % 2) * 2048
                s_ld.wait(sync)
                if n >= 2:
                    s_pe.wait(sync, pe_after[n - 2])
                gcol = col0 + 128 * h
                s_ld.dma(sync.dma_start(
                    out=fw_t[:, sl:sl + 2048].rearrange("p (dc k) -> p dc k",
                                                        dc=DC),
                    in_=fw1gAs[gcol // 2048]
                    .ap()[:, gcol % 2048:gcol % 2048 + 128]
                    .rearrange("(dc p) k -> p dc k", p=128)))
                ldc[n] = s_ld.n

            def rope_to(dst_ap):
                dve.tensor_copy(qsw[0:64, :], prj[64:128, :])
                dve.tensor_copy(qsw[64:128, :], prj[0:64, :])
                dve.tensor_mul(prh[:, :], prj[:, :], cosf[:, :])
                dve.tensor_mul(sq_t[:, :], qsw[:, :], sneg[:, :])
                s_dv.inc(dve.always.tensor_add(dst_ap, prh[:, :], sq_t[:, :]))

            s_in.wait(sync)
            s_pe.wait(sync)
            s_dv.wait(sync)
            load_call(0)
            for n in range(NCALL):
                col0, h, bias_col, bias_idx, kind = calls[n]
                if n + 1 < NCALL:
                    load_call(n + 1)
                sl = (n % 2) * 2048
                ps = fp_ps[:, (n % 2) * 512:(n % 2) * 512 + KC]
                s_ld.wait(pe, ldc[n])
                if n >= 2:
                    s_dv.wait(pe, dv_after[n - 2])
                for dc in range(DC):
                    s_pe.inc(pe.matmul(ps,
                                       fw_t[:, sl + 128 * dc:sl + 128 * (dc + 1)],
                                       h1f[:, KC * dc:KC * (dc + 1)],
                                       start=(dc == 0), stop=(dc == DC - 1)))
                pe_after[n] = s_pe.n
                s_pe.wait(dve, pe_after[n])
                s_dv.inc(dve.always.tensor_scalar(prj[:, :], ps,
                                           bias_col[:, bias_idx:bias_idx + 1],
                                           None, ALU.add))
                dv_after[n] = s_dv.n
                if kind == "q":
                    rope_to(fqt[:, KC * h:KC * (h + 1)])
                    if h == FH - 1:
                        s_dv.wait(sync)
                        s_st.wait(sync)
                        for h2 in range(FH):
                            s_st.dma(sync.dma_start(
                                out=fqc.ap()[128 * h2:128 * (h2 + 1), :],
                                in_=fqt[:, KC * h2:KC * (h2 + 1)]))
                        s_st.wait(gps)
                        s_cc.inc(gps.always.collective_compute(
                            "AllGather", ALU.bypass, RG,
                            [fqc.ap().opt()], [fqg.ap().opt()]))
                        cc_fq = s_cc.n
                elif kind == "k":
                    rope_to(fkt[:, KC * h:KC * (h + 1)])
                else:
                    s_pe.wait(dve)
                    s_dv.inc(dve.always.tensor_copy(vtmp[:, :], prj[:, :]))
                    s_dv.wait(pe)
                    for kt in range(NKT):
                        s_pe.inc(pe.transpose(fr_ps[:, :],
                                              vtmp[:, 128 * kt:128 * (kt + 1)],
                                              ey_bf[:, :]))
                        s_pe.wait(dve)
                        s_dv.inc(dve.always.tensor_copy(
                            fvt[:, 1024 * kt + 128 * h:
                                1024 * kt + 128 * (h + 1)],
                            fr_ps[:, :]))
                        s_dv.wait(pe)

        syncpoint()
        s_cc.wait(sync, cc_fq)
        fqh = SBm("fqh", [128, 2 * cfg.KSEL], BF16)
        amk = SBm("amk", [128, NKT * QH * QHW], BF16)
        for sidx in range(NKT * QH):
            s_ld.dma(sync.dma_start(out=amk[:, QHW * sidx:QHW * (sidx + 1)],
                                    in_=am_i.ap()[sidx, :, :]))
        s_ld.wait(sync)
        with nc.psum_tensor("fsc_ps", [128, 2 * QHW], F32) as fsc_ps, \
             nc.psum_tensor("fz_ps", [128, QHW], F32) as fz_ps, \
             nc.psum_tensor("fl_ps", [1, QHW], F32) as fl_ps, \
             nc.sbuf_tensor("fpT", [128, 2 * QHW], BF16) as fpT, \
             nc.sbuf_tensor("fzs", [128, QHW], F32) as fzs, \
             nc.sbuf_tensor("fls", [1, QHW], F32) as fls:

            def fq_load(h):
                s_fqp[h % 2].dma(sync.dma_start(
                    out=fqh[:, (h % 2) * cfg.KSEL:(h % 2 + 1) * cfg.KSEL]
                    .rearrange("p (o k) -> p o k", o=NCORES),
                    in_=fqg.ap().rearrange("(o hh p) k -> hh p o k",
                                           o=NCORES, p=128)[h]))

            fvisits = [(h, qh, jj) for h in range(FH) for qh in range(QH)
                       for jj in range(NKT)]
            NFV = len(fvisits)
            ex_base = s_ex.n
            fq_ld = [0] * FH
            fq_load(0)
            fq_ld[0] = s_fqp[0].n
            fmk_hist = [0] * NFV
            fav_hist = [0] * NFV
            fsc_hist = [0] * NFV
            fsc_end = []

            def femit_scores(k):
                h_, qh_, jj_ = fvisits[k]
                kvh_ = h_ // 2
                off = (k % 2) * QHW
                s_fqp[h_ % 2].wait(pe, fq_ld[h_])
                if k >= 2:
                    s_ex.wait(pe, ex_base + k - 1)
                for ns in range(QHW // 512):
                    s_sc.inc(pe.matmul(
                        fsc_ps[:, off + 512 * ns:off + 512 * (ns + 1)],
                        fkt[:, KC * kvh_ + 128 * jj_:KC * kvh_ + 128 * (jj_ + 1)],
                        fqh[:, (h_ % 2) * cfg.KSEL + QHW * qh_ + 512 * ns:
                            (h_ % 2) * cfg.KSEL + QHW * qh_ + 512 * (ns + 1)],
                        start=True, stop=True))
                fsc_hist[k] = s_sc.n

            def femit_evac(av_t):
                s_av.wait(act, av_t)
                s_st.wait(act)
                s_ev2.inc(act.copy(fzs[:, :], fz_ps[:, :]))
                s_ev2.inc(act.copy(fls[:, :], fl_ps[0:1, :]))

            def femit_store(h_p, qh_p):
                s_ev2.wait(sync)
                ft_ = fzlcA if h_p < FH // 2 else fzlcB
                hl_ = h_p % (FH // 2)
                for cons in range(cfg.NCONS):
                    r_ = (QHW * qh_p) // KC + cons
                    s_st.dma(sync.dma_start(
                        out=ft_.ap()[FZR * r_ + 128 * hl_:
                                     FZR * r_ + 128 * (hl_ + 1), :],
                        in_=fzs[:, KC * cons:KC * (cons + 1)]))
                    s_st.dma(sync.dma_start(
                        out=ft_.ap()[FZR * r_ + 1024 + hl_:
                                     FZR * r_ + 1025 + hl_, :],
                        in_=fls[:, KC * cons:KC * (cons + 1)]))
                if h_p == FH // 2 - 1 and qh_p == QH - 1:
                    s_st.wait(gps)
                    s_cc.inc(gps.always.collective_compute(
                        "ReduceScatter", ALU.add, RG,
                        [fzlcA.ap().opt()], [fzlrA.ap().opt()]))
                    cc_fzlh[0] = s_cc.n

            fpend = None
            cc_fzlh = [0, 0]
            femit_scores(0)
            for k in range(NFV):
                h, qh, jj = fvisits[k]
                kvh = h // 2
                off = (k % 2) * QHW
                if qh == 0 and jj == 0 and h + 1 < FH:
                    if h >= 1:
                        s_sc.wait(sync, fsc_end[h - 1])
                    fq_load(h + 1)
                    fq_ld[h + 1] = s_fqp[(h + 1) % 2].n
                if k + 1 < NFV:
                    femit_scores(k + 1)
                s_sc.wait(act, fsc_hist[k])
                if k >= 2:
                    s_av.wait(act, fav_hist[k - 2])
                s_ex.inc(act.activation(fpT[:, off:off + QHW],
                                        fsc_ps[:, off:off + QHW], AF.Exp,
                                        scale=1.0 / math.sqrt(FDH),
                                        bias=-C_FFN))
                if fpend is not None:
                    femit_evac(fpend[2])
                    femit_store(fpend[0], fpend[1])
                    fpend = None
                s_ex.wait(dve, s_ex.n)
                s_ld.wait(dve)
                s_mk.inc(dve.tensor_mul(
                    fpT[:, off:off + QHW], fpT[:, off:off + QHW],
                    amk[:, QHW * (jj * QH + qh):QHW * (jj * QH + qh + 1)]))
                fmk_hist[k] = s_mk.n
                s_mk.wait(pe, fmk_hist[k])
                fev0 = s_ev2.n if (jj == 0 and k > 0) else None
                st_ = (jj == 0)
                sp_ = (jj == NKT - 1)
                for ns in range(QHW // 512):
                    if fev0 is not None:
                        s_ev2.wait(pe, fev0 - 1)
                    s_av.inc(pe.matmul(
                        fz_ps[:, 512 * ns:512 * (ns + 1)],
                        fvt[:, 1024 * jj + 128 * kvh:1024 * jj + 128 * (kvh + 1)],
                        fpT[:, off + 512 * ns:off + 512 * (ns + 1)],
                        start=st_, stop=sp_))
                    if fev0 is not None:
                        s_ev2.wait(pe, fev0)
                        fev0 = None
                    s_av.inc(pe.matmul(
                        fl_ps[0:1, 512 * ns:512 * (ns + 1)], on_bf[:, 0:1],
                        fpT[:, off + 512 * ns:off + 512 * (ns + 1)],
                        start=st_, stop=sp_, tile_position=(0, 0)))
                fav_hist[k] = s_av.n
                if jj == NKT - 1:
                    fpend = (h, qh, fav_hist[k])
                    if qh == QH - 1:
                        fsc_end.append(s_sc.n)
            femit_evac(fpend[2])
            femit_store(fpend[0], fpend[1])
        s_st.wait(gps)
        s_cc.inc(gps.always.collective_compute("ReduceScatter", ALU.add, RG,
                                        [fzlcB.ap().opt()], [fzlrB.ap().opt()]))
        cc_fzlh[1] = s_cc.n
        mid.close()
        fzn = SB8b("fzn", [128, FH * KC], BF16)
        x1f = SB8b("x1f", [128, NKT * 2048])
        woctx = contextlib.ExitStack()
        fw_o = woctx.enter_context(nc.sbuf_tensor("fw_o", [128, 2 * 2048],
                                                  BF16))
        ldc4 = [0] * DC
        pe_af4 = [0] * DC

        def load_o(n):
            sl = (n % 2) * 2048
            s_ld.wait(sync)
            if n >= 2:
                s_pe.wait(sync, pe_af4[n - 2])
            s_ld.dma(sync.dma_start(
                out=fw_o[:, sl:sl + 2048].rearrange("p (zc k) -> p zc k",
                                                    zc=FH),
                in_=fw1gAs[2].ap()[0:2048, 128 * n:128 * (n + 1)]
                .rearrange("(zc p) k -> p zc k", p=128)))
            ldc4[n] = s_ld.n

        load_o(0)
        load_o(1)
        with nc.sbuf_tensor("fzl_t", [128, KC], F32) as fzl_t, \
             nc.sbuf_tensor("fli", [1, KC], F32) as fli, \
             nc.psum_tensor("fb_ps", [128, KC], F32) as fb_ps:
            for h in range(FH):
                fr_ = fzlrA if h < FH // 2 else fzlrB
                hl2 = h % (FH // 2)
                if h == 0:
                    s_cc.wait(sync, cc_fzlh[0])
                if h == FH // 2:
                    s_cc.wait(sync, cc_fzlh[1])
                s_dv.wait(sync)
                s_ld.wait(sync)
                s_ld.dma(sync.dma_start(
                    out=fzl_t[:, :],
                    in_=fr_.ap()[128 * hl2:128 * (hl2 + 1), :]))
                s_ld.dma(sync.dma_start(
                    out=fli[:, :],
                    in_=fr_.ap()[1024 + hl2:1025 + hl2, :]))
                s_ld.wait(sync)
                s_ld.wait(dve)
                s_dv.inc(dve.always.reciprocal(fli[:, :], fli[:, :]))
                s_dv.wait(pe)
                for ns in range(max(1, KC // 512)):
                    w = min(512, KC)
                    s_pe.inc(pe.matmul(fb_ps[:, w * ns:w * (ns + 1)],
                                       on_f[0:1, :], fli[:, w * ns:w * (ns + 1)],
                                       start=True, stop=True))
                s_pe.wait(dve)
                s_dv.inc(dve.always.tensor_mul(fzn[:, KC * h:KC * (h + 1)],
                                        fzl_t[:, :], fb_ps[:, :]))
                s_dv.wait(pe)
        with nc.psum_tensor("fo_ps", [128, KC], F32) as fo_ps, \
             nc.psum_tensor("fo_tr", [128, 128], F32) as fo_tr, \
             nc.sbuf_tensor("ao2", [128, KC], F32) as ao2:
            for dc in range(DC):
                sl = (dc % 2) * 2048
                s_ld.wait(pe, ldc4[dc])
                s_dv.wait(pe)
                for zc in range(FH):
                    s_pe.inc(pe.matmul(fo_ps[:, :],
                                       fw_o[:, sl + 128 * zc:sl + 128 * (zc + 1)],
                                       fzn[:, KC * zc:KC * (zc + 1)],
                                       start=(zc == 0), stop=(zc == FH - 1)))
                pe_af4[dc] = s_pe.n
                if dc + 2 < DC:
                    load_o(dc + 2)
                s_pe.wait(dve)
                s_dv.inc(dve.always.tensor_copy(ao2[:, :], fo_ps[:, :]))
                s_dv.wait(pe)
                for kt in range(NKT):
                    s_pe.inc(pe.transpose(fo_tr[:, :],
                                          ao2[:, 128 * kt:128 * (kt + 1)],
                                          ey_f[:, :]))
                    s_pe.wait(dve)
                    s_dv.inc(dve.always.tensor_copy(
                        x1f[:, 2048 * kt + 128 * dc:2048 * kt + 128 * (dc + 1)],
                        fo_tr[:, :]))
                    s_dv.wait(pe)
        s_ld.wait(sync)
        s_pe.wait(sync)
        woctx.close()
        for kt in range(NKT):
            s_dv.inc(dve.always.tensor_add(x1f[:, 2048 * kt:2048 * (kt + 1)],
                                    x1f[:, 2048 * kt:2048 * (kt + 1)],
                                    sel_t[:, 2048 * kt:2048 * (kt + 1)]))
        h2f = SB8b("h2f", [128, DC * KC], BF16)
        with nc.sbuf_tensor("h2t", [128, 2048], BF16) as h2t, \
             nc.sbuf_tensor("rb", [128, 1], F32) as rb, \
             nc.psum_tensor("h2_ps", [128, 128], BF16) as h2_ps:
            for kt in range(NKT):
                s_dv.wait(act)
                s_ac.inc(act.always.activation(pscr[:, :],
                                        x1f[:, 2048 * kt:2048 * (kt + 1)],
                                        AF.Square, accum_out=rb[:, :]))
                s_ac.wait(dve)
                s_dv.inc(dve.always.tensor_scalar_mul(rb[:, :], rb[:, :], 1.0 / D))
                s_dv.wait(act)
                s_ac.inc(act.always.activation(rb[:, :], rb[:, :], AF.Sqrt,
                                        bias=EPS_RMS))
                s_ac.wait(dve)
                dve.reciprocal(rb[:, :], rb[:, :])
                s_pe.wait(dve)
                s_dv.inc(dve.always.scalar_tensor_tensor(
                    h2t[:, :], x1f[:, 2048 * kt:2048 * (kt + 1)], rb[:, :],
                    ln2_b[:, :], ALU.mult, ALU.mult))
                s_dv.wait(pe)
                for dc in range(DC):
                    s_pe.inc(pe.transpose(h2_ps[:, :],
                                          h2t[:, 128 * dc:128 * (dc + 1)],
                                          ey_bf[:, :]))
                    s_pe.wait(dve)
                    s_dv.inc(dve.always.tensor_copy(
                        h2f[:, KC * dc + 128 * kt:KC * dc + 128 * (kt + 1)],
                        h2_ps[:, :]))
                    s_dv.wait(pe)
        s_dv.wait(sync)
        s_st.dma(sync.dma_start(
            out=h2c.ap().rearrange("(dc p) t -> p dc t", p=128),
            in_=h2f[:, :].rearrange("p (dc t) -> p dc t", t=KC)))
        s_st.wait(gps)
        s_cc.inc(gps.always.collective_compute("AllGather", ALU.bypass, RG,
                                        [h2c.ap().opt()], [h2g.ap().opt()]))
        cc_h2 = s_cc.n
        ITN = FI // NCORES // 128
        KS = cfg.KSEL
        TB2 = KS // 512
        tsb = SB8b("tsb", [128, ITN * KS], BF16)
        with nc.sbuf_tensor("h2a", [128, DC * KS], BF16) as h2a, \
             nc.sbuf_tensor("wgu", [128, 2 * 2048], BF16) as wgu, \
             nc.psum_tensor("g_ps", [128, KS], F32) as g_ps, \
             nc.psum_tensor("u_ps", [128, KS], F32) as u_ps, \
             nc.sbuf_tensor("gsb", [128, KS], BF16) as gsb, \
             nc.sbuf_tensor("usb", [128, KS], BF16) as usb:
            s_dv.wait(sync)
            s_ac.wait(sync)
            bursts = [(kind, it) for it in range(ITN) for kind in ("g", "u")]
            NB = len(bursts)
            ldc2 = [0] * NB
            pe_af2 = [0] * NB

            def load_b(n):
                kind, it = bursts[n]
                base = 0 if kind == "g" else FI // NCORES
                sl = (n % 2) * 2048
                s_ld.wait(sync)
                if n >= 2:
                    s_pe.wait(sync, pe_af2[n - 2])
                s_ld.dma(sync.dma_start(
                    out=wgu[:, sl:sl + 2048].rearrange("p (dc k) -> p dc k",
                                                       dc=DC),
                    in_=fguc.ap()[:, base + 128 * it:base + 128 * (it + 1)]
                    .rearrange("(dc p) k -> p dc k", p=128)))
                ldc2[n] = s_ld.n

            load_b(0)
            load_b(1)
            s_cc.wait(sync, cc_h2)
            s_ld.wait(sync)
            for dc in range(DC):
                s_ld.dma(sync.dma_start(
                    out=h2a[:, KS * dc:KS * (dc + 1)]
                    .rearrange("p (o t) -> p o t", o=NCORES),
                    in_=h2g.ap().rearrange("(o dc2 p) t -> dc2 p o t",
                                           dc2=DC, p=128)[dc]))
            ldh2 = s_ld.n
            s_ld.wait(sync)
            s_ld.wait(pe, ldh2)
            pe_g = 0
            for n in range(NB):
                kind, it = bursts[n]
                sl = (n % 2) * 2048
                s_ld.wait(pe, ldc2[n])
                if kind == "g":
                    s_ac.wait(pe)
                    for dc in range(DC):
                        for tb in range(TB2):
                            s_pe.inc(pe.matmul(
                                g_ps[:, 512 * tb:512 * (tb + 1)],
                                wgu[:, sl + 128 * dc:sl + 128 * (dc + 1)],
                                h2a[:, KS * dc + 512 * tb:
                                    KS * dc + 512 * (tb + 1)],
                                start=(dc == 0), stop=(dc == DC - 1)))
                    pe_g = s_pe.n
                else:
                    s_dv.wait(pe)
                    for dc in range(DC):
                        for tb in range(TB2):
                            s_pe.inc(pe.matmul(
                                u_ps[:, 512 * tb:512 * (tb + 1)],
                                wgu[:, sl + 128 * dc:sl + 128 * (dc + 1)],
                                h2a[:, KS * dc + 512 * tb:
                                    KS * dc + 512 * (tb + 1)],
                                start=(dc == 0), stop=(dc == DC - 1)))
                    pe_u = s_pe.n
                    s_pe.wait(act, pe_g)
                    s_dv.wait(act)
                    s_ac.inc(act.always.activation(gsb[:, :], g_ps[:, :], AF.Silu))
                    s_pe.wait(dve, pe_u)
                    s_ac.wait(dve)
                    dve.tensor_copy(usb[:, :], u_ps[:, :])
                    s_dv.inc(dve.always.tensor_mul(tsb[:, KS * it:KS * (it + 1)],
                                            gsb[:, :], usb[:, :]))
                pe_af2[n] = s_pe.n
                if n + 2 < NB:
                    load_b(n + 2)
        gtd = SB8b("gtd", [128, NKT * 2048])
        with nc.sbuf_tensor("fdn_s", [128, ITN * 2048], BF16) as fdn_s, \
             nc.psum_tensor("dn_psA", [128, 2048], F32) as dn_psA, \
             nc.psum_tensor("dn_psB", [128, 2048], F32) as dn_psB, \
             nc.sbuf_tensor("dnb", [128, 2048], F32) as dnb, \
             nc.sbuf_tensor("dnb2", [128, 2048], F32) as dnb2:
            s_dv.wait(sync)
            s_ac.wait(sync)
            s_pe.wait(sync)
            s_ld.wait(sync)
            for it in range(ITN):
                s_ld.dma(sync.dma_start(
                    out=fdn_s[:, 2048 * it:2048 * (it + 1)],
                    in_=fdnc.ap()[128 * it:128 * (it + 1), :]))
            ldn = s_ld.n
            s_ld.wait(pe, ldn)
            s_dv.wait(pe)
            TT = KS // 128
            SPL = NKT >= 2
            if SPL:
                tts = ([t2 for t2 in range(TT) if t2 % NKT < NKT // 2]
                       + [t2 for t2 in range(TT) if t2 % NKT >= NKT // 2])
            else:
                tts = list(range(TT))
            dn_done = [0] * TT
            st_dn = [0] * TT
            cc_dnh = [0, 0]
            for nn_, tt in enumerate(tts):
                ps = dn_psA if nn_ % 2 == 0 else dn_psB
                bufd = dnb if nn_ % 2 == 0 else dnb2
                if nn_ >= 2:
                    s_ac.wait(pe, dn_done[nn_ - 2])
                for it in range(ITN):
                    for db in range(4):
                        s_pe.inc(pe.matmul(
                            ps[:, 512 * db:512 * (db + 1)],
                            tsb[:, KS * it + 128 * tt:KS * it + 128 * (tt + 1)],
                            fdn_s[:, 2048 * it + 512 * db:
                                  2048 * it + 512 * (db + 1)],
                            start=(it == 0), stop=(it == ITN - 1)))
                s_pe.wait(act)
                if nn_ >= 2:
                    s_st.wait(act, st_dn[nn_ - 2])
                s_ac.inc(act.always.copy(bufd[:, :], ps[:, :]))
                dn_done[nn_] = s_ac.n
                s_ac.wait(sync, dn_done[nn_])
                s_st.wait(sync)
                o_, j_ = tt // NKT, tt % NKT
                if SPL and j_ >= NKT // 2:
                    dt_, ro_ = dncB, (KC // 2) * o_ + 128 * (j_ - NKT // 2)
                else:
                    dt_, ro_ = dncA, (KC // 2 if SPL else KC) * o_ + 128 * j_
                s_st.dma(sync.dma_start(
                    out=dt_.ap()[ro_:ro_ + 128, :], in_=bufd[:, :]))
                st_dn[nn_] = s_st.n
                if SPL and nn_ == TT // 2 - 1:
                    s_st.wait(gps)
                    s_cc.inc(gps.always.collective_compute(
                        "ReduceScatter", ALU.add, RG,
                        [dncA.ap().opt()], [dnrA.ap().opt()]))
                    cc_dnh[0] = s_cc.n
        s_st.wait(gps)
        if SPL:
            s_cc.inc(gps.always.collective_compute(
                "ReduceScatter", ALU.add, RG,
                [dncB.ap().opt()], [dnrB.ap().opt()]))
            cc_dnh[1] = s_cc.n
        else:
            s_cc.inc(gps.always.collective_compute(
                "ReduceScatter", ALU.add, RG,
                [dncA.ap().opt()], [dnrA.ap().opt()]))
            cc_dnh[0] = cc_dnh[1] = s_cc.n
        for kt in range(NKT):
            s_cc.wait(sync, cc_dnh[1 if SPL and kt >= NKT // 2 else 0])
            if SPL and kt >= NKT // 2:
                src_, ro_ = dnrB, 128 * (kt - NKT // 2)
            else:
                src_, ro_ = dnrA, 128 * kt
            s_ld.dma(sync.dma_start(
                out=gtd[:, 2048 * kt:2048 * (kt + 1)],
                in_=src_.ap()[ro_:ro_ + 128, :]))
        s_ld.wait(sync)
        s_ld.wait(dve)
        for kt in range(NKT):
            dve.tensor_add(gtd[:, 2048 * kt:2048 * (kt + 1)],
                           gtd[:, 2048 * kt:2048 * (kt + 1)],
                           x1f[:, 2048 * kt:2048 * (kt + 1)])
            dve.tensor_sub(gtd[:, 2048 * kt:2048 * (kt + 1)],
                           gtd[:, 2048 * kt:2048 * (kt + 1)],
                           sel_t[:, 2048 * kt:2048 * (kt + 1)])
            dve.tensor_scalar(gtd[:, 2048 * kt:2048 * (kt + 1)],
                              gtd[:, 2048 * kt:2048 * (kt + 1)],
                              gat_v[:, kt:kt + 1], None, ALU.mult)
            s_dv.inc(dve.always.tensor_add(gtd[:, 2048 * kt:2048 * (kt + 1)],
                                    gtd[:, 2048 * kt:2048 * (kt + 1)],
                                    sel_t[:, 2048 * kt:2048 * (kt + 1)]))
            s_dv.wait(sync)
            s_st.dma(sync.dma_start(out=gatc.ap()[128 * kt:128 * (kt + 1), :],
                                    in_=gtd[:, 2048 * kt:2048 * (kt + 1)]))
        s_st.wait(gps)
        s_cc.inc(gps.always.collective_compute("AllGather", ALU.bypass, RG,
                                        [gatc.ap().opt()], [gatg.ap().opt()]))
        cc_gat = s_cc.n
    if stop_after == "P8b":
        return finish()

    syncpoint()
    with contextlib.ExitStack() as p9ctx:
        SB9 = lambda name, shape, dt=F32: p9ctx.enter_context(
            nc.sbuf_tensor(name, list(shape), dt))
        pscr = SB9("pscr9", [128, 2048])
        with nc.sbuf_tensor("k1", [128, 1], F32) as k1, \
             nc.sbuf_tensor("k2", [128, 1], F32) as k2:
            for tb in range(NT):
                dve.tensor_mul(k1[:, :], r1_s[:, tb:tb + 1], r1_s[:, tb:tb + 1])
                s_dv.inc(dve.always.tensor_mul(k1[:, :], k1[:, :], v1_s[:, tb:tb + 1]))
                s_dv.wait(act)
                s_ac.inc(act.always.activation(k2[:, :], k1[:, :], AF.Sqrt,
                                        bias=EPS_LN))
                s_ac.wait(dve)
                dve.reciprocal(k2[:, :], k2[:, :])
                dve.tensor_mul(k2[:, :], k2[:, :], r1_s[:, tb:tb + 1])
                dve.tensor_scalar(k1[:, :], msk_s[:, tb:tb + 1], -1.0, None,
                                  ALU.mult)
                dve.tensor_scalar(k1[:, :], k1[:, :], 1.0, None, ALU.add)
                dve.tensor_mul(k1[:, :], k1[:, :], k2[:, :])
                dve.tensor_scalar(ka_s[:, tb:tb + 1], k1[:, :], 1.0, None,
                                  ALU.add)
                s_dv.inc(dve.always.tensor_mul(kb_s[:, tb:tb + 1], m1_s[:, tb:tb + 1],
                                        k1[:, :]))
        with nc.sbuf_tensor("x1c", [128, 2 * 2048], F32) as x1c, \
             nc.sbuf_tensor("gg", [128, 2 * 2048], F32) as gg, \
             nc.sbuf_tensor("oo", [128, 2 * 2048], F32) as oo, \
             nc.sbuf_tensor("ga", [128, 1], F32) as ga, \
             nc.sbuf_tensor("gb", [128, 1], F32) as gb, \
             nc.sbuf_tensor("gc", [128, 1], F32) as gc:
            gat_h = [0] * NT
            x_h = [0] * NT
            dv_h = [0] * NT
            out_h = [0] * NT

            def p9_pre(tb):
                sl = (tb % 2) * 2048
                s_x.wait(sync)
                if tb >= 2:
                    s_dv.wait(sync, dv_h[tb - 2])
                s_x.dma(sync.dma_start(
                    out=x1c[:, sl:sl + 2048],
                    in_=x1d.ap()[128 * tb:128 * (tb + 1), :]))
                x_h[tb] = s_x.n
                if tb >= 2:
                    s_dv.wait(gps, dv_h[tb - 2])
                s_gp.inc(gps.always.memset(gg[:, sl:sl + 2048], 0.0))
                s_gp.wait(gps)

            def p9_gather(tb):
                sl = (tb % 2) * 2048
                s_gat.wait(gps)
                s_gat.dma(gps.indirect_dma_start(
                    out=gg[:, sl:sl + 2048], out_offset=None,
                    in_=gatg.ap(),
                    in_offset=bass.IndirectOffsetOnAxis(
                        ap=idx_s[:, tb:tb + 1], axis=0),
                    bounds_check=bnd_reg, oob_is_err=False))
                gat_h[tb] = s_gat.n

            def p9_fetch(tb):
                p9_pre(tb)
                p9_gather(tb)

            s_ac.wait(sync)
            s_dv.wait(sync)
            p9_pre(0)
            p9_pre(1)
            s_cc.wait(sync, cc_gat)
            s_cc.wait(gps, cc_gat)
            p9_gather(0)
            p9_gather(1)
            for tb in range(NT):
                sl = (tb % 2) * 2048
                s_gat.wait(dve, gat_h[tb])
                s_gat.wait(act, gat_h[tb])
                s_x.wait(dve, x_h[tb])
                if tb >= 1:
                    s_dv.wait(act, dv_h[tb - 1])
                s_ac.inc(act.always.activation(pscr[:, :], gg[:, sl:sl + 2048],
                                        AF.Identity, accum_out=ga[:, :]))
                s_ac.inc(act.always.activation(pscr[:, :], gg[:, sl:sl + 2048],
                                        AF.Square, accum_out=gb[:, :]))
                s_ac.wait(dve)
                dve.tensor_scalar_mul(ga[:, :], ga[:, :], 1.0 / D)
                dve.tensor_scalar_mul(gb[:, :], gb[:, :], 1.0 / D)
                dve.tensor_tensor(gc[:, :], ga[:, :], ga[:, :], ALU.mult)
                s_dv.inc(dve.always.tensor_sub(gb[:, :], gb[:, :], gc[:, :]))
                s_dv.wait(act)
                s_ac.inc(act.always.activation(gb[:, :], gb[:, :], AF.Sqrt,
                                        bias=EPS_LN))
                s_ac.wait(dve)
                dve.reciprocal(gb[:, :], gb[:, :])
                dve.tensor_mul(gb[:, :], gb[:, :], msk_s[:, tb:tb + 1])
                dve.tensor_scalar(gg[:, sl:sl + 2048], gg[:, sl:sl + 2048],
                                  ga[:, :], gb[:, :],
                                  ALU.subtract, ALU.mult)
                if tb >= 2:
                    s_out.wait(dve, out_h[tb - 2])
                dve.tensor_scalar(oo[:, sl:sl + 2048], x1c[:, sl:sl + 2048],
                                  ka_s[:, tb:tb + 1], kb_s[:, tb:tb + 1],
                                  ALU.mult, ALU.subtract)
                s_dv.inc(dve.always.tensor_add(
                    oo[:, sl:sl + 2048], oo[:, sl:sl + 2048],
                    gg[:, sl:sl + 2048]))
                dv_h[tb] = s_dv.n
                s_dv.wait(sync, dv_h[tb])
                s_out.wait(sync)
                s_out.dma(sync.dma_start(
                    out=out_t.ap()[128 * tb:128 * (tb + 1), :],
                    in_=oo[:, sl:sl + 2048]))
                out_h[tb] = s_out.n
                if tb + 2 < NT:
                    p9_fetch(tb + 2)
    s_out.wait(sync)
    bar()
    return nc


_CACHE = {}


def kernel(**inputs):
    T = inputs["L0_attn_normalized_resid_pre"].shape[1]
    cfg = CFG(T)
    if T not in _CACHE:
        _CACHE[T] = build(cfg)
    nc = _CACHE[T]
    in_maps = [shard_inputs(cfg, c, inputs) for c in range(NCORES)]
    res = run_bass_kernel_spmd(nc, in_maps, core_ids=list(range(NCORES)))
    out = np.concatenate([res.results[c]["out"] for c in range(NCORES)], axis=0)
    return out[None, :, :]


# version 33
# speedup vs baseline: 1.0174x; 1.0174x over previous
import contextlib
import math
import numpy as np
import ml_dtypes
import concourse.bass as bass
import concourse.mybir as mybir
from concourse.bass_utils import run_bass_kernel_spmd

F32 = mybir.dt.float32
BF16 = mybir.dt.bfloat16
I32 = mybir.dt.int32
AF = mybir.ActivationFunctionType
ALU = mybir.AluOpType
AX = mybir.AxisListType

NCORES = 8
D = 2048
DC = D // 128
HE = 8
DH = 64
FH = 16
FDH = 128
FI = 8192
EPS_LN = 1e-5
EPS_RMS = 1e-6
C_MAIN = 6.0
C_FFN = 4.0
MASKVAL = -100000.0
TWO_PI = 2.0 * math.pi
CW_C1 = 6.28125
CW_C2 = TWO_PI - CW_C1
ROPE_THETA = 10000.0
FW_WQ, FW_WK, FW_WV, FW_WO, FW_G, FW_U = 0, 2048, 3072, 4096, 6144, 14336
FW_COLS = 22528
CONSTVALS = [0.0, 1.0, EPS_LN, EPS_RMS, -C_FFN, 0.125, 1.0 / math.sqrt(FDH)]


class CFG:
    def __init__(self, T):
        self.T = T
        self.Tc = T // NCORES
        self.NT = self.Tc // 128
        self.QT = 512
        self.NQT = T // self.QT
        self.NKB = T // 128
        self.NKC = self.NKB // NCORES
        self.NS = self.Tc // 512
        self.trips = [(4 * (t + 1) - 1) // 8 + 1 for t in range(self.NQT)]
        self.slots = [(t, j) for t in range(self.NQT)
                      for j in range(max(0, self.trips[t] - 2), self.trips[t])]
        self.NV = sum(self.trips)
        self.KSEL = T // 8
        self.KC = self.KSEL // NCORES
        self.NKT = self.KC // 128
        self.QHW = min(1024, self.KSEL)
        self.QH = self.KSEL // self.QHW
        self.NCONS = self.QHW // self.KC


def host_tables(cfg, c):
    t = {}
    vb = np.zeros((128, cfg.NV), np.float32)
    v = 0
    for qt in range(cfg.NQT):
        for j in range(cfg.trips[qt]):
            vb[:, v] = -C_MAIN if 8 * j + c < 4 * (qt + 1) else MASKVAL
            v += 1
    t["vbias"] = vb
    p = np.arange(128)[:, None]
    f = np.arange(cfg.QT)[None, :]
    dm = np.zeros((len(cfg.slots), 128, cfg.QT), ml_dtypes.bfloat16)
    for s, (qt, j) in enumerate(cfg.slots):
        dm[s] = ((cfg.QT * qt + f) >= (128 * (8 * j + c) + p)).astype(
            ml_dtypes.bfloat16)
    t["dmask"] = dm
    f2 = np.arange(cfg.QHW)[None, :]
    am = np.zeros((cfg.NKT * cfg.QH, 128, cfg.QHW), ml_dtypes.bfloat16)
    for jj in range(cfg.NKT):
        for h in range(cfg.QH):
            am[jj * cfg.QH + h] = ((cfg.QHW * h + f2) >=
                                   (128 * (cfg.NKT * c + jj) + p)).astype(
                ml_dtypes.bfloat16)
    t["amask"] = am
    mc = np.zeros((NCORES, 1), np.float32)
    mc[:c, 0] = 1.0
    t["maskc"] = mc
    gt = np.zeros((128, cfg.NT), np.float32)
    for tb in range(cfg.NT):
        gt[:, tb] = cfg.Tc * c + 128 * tb + np.arange(128)
    t["gtok"] = gt
    return t


def host_consts():
    t = {}
    t["eye_bf"] = np.eye(128, dtype=ml_dtypes.bfloat16)
    t["eye_f32"] = np.eye(128, dtype=np.float32)
    t["ones_bf"] = np.ones((128, 128), ml_dtypes.bfloat16)
    t["ones_f32"] = np.ones((128, 128), np.float32)
    t["u128s"] = (np.arange(128)[:, None] < np.arange(128)[None, :]).astype(
        np.float32)
    t["u16s"] = (np.arange(16)[:, None] < np.arange(16)[None, :]).astype(np.float32)
    ivf = (1.0 / ROPE_THETA ** (np.arange(0, FDH, 2) / FDH)).astype(np.float32)
    t["invfreq"] = np.concatenate([ivf, ivf])[None, :].astype(np.float32)
    t["iotap1"] = (np.arange(128, dtype=np.float32) + 1.0)[:, None]
    cv = np.zeros((128, len(CONSTVALS)), np.float32)
    for i, v in enumerate(CONSTVALS):
        cv[:, i] = v
    t["cvec"] = cv
    sio = np.zeros((2, 128), np.float32)
    sio[0, 0:64] = 1.0
    sio[1, 64:128] = 1.0
    t["selio"] = sio
    return t


def shard_inputs(cfg, c, inp):
    d = {}
    d["x"] = np.ascontiguousarray(inp["L0_attn_normalized_resid_pre"][0,
                                  cfg.Tc * c:cfg.Tc * (c + 1), :])
    wq4s = inp["L0_attn_W_Q"][4 * c:4 * c + 4]
    d["wq4"] = np.ascontiguousarray(
        wq4s.reshape(4, DC, 128, DH).transpose(2, 1, 0, 3)
        .reshape(128, 4 * DC * DH))
    wkv2 = np.concatenate([inp["L0_attn_W_K"][c], inp["L0_attn_W_V"][c]],
                          axis=1)
    d["wkv"] = np.ascontiguousarray(
        wkv2.reshape(DC, 128, 128).transpose(1, 0, 2).reshape(128, DC * 128))
    d["wo1"] = np.ascontiguousarray(inp["L0_attn_W_O"][c])
    r0 = (D // NCORES) * c
    d["fw1"] = np.ascontiguousarray(np.concatenate(
        [inp["L0_ffn_wq"][r0:r0 + 256], inp["L0_ffn_wk"][r0:r0 + 256],
         inp["L0_ffn_wv"][r0:r0 + 256], inp["L0_ffn_wo"][r0:r0 + 256]],
        axis=1))
    i0 = (FI // NCORES) * c
    d["fgu"] = np.ascontiguousarray(np.concatenate(
        [inp["L0_ffn_w_gate"][:, i0:i0 + FI // NCORES],
         inp["L0_ffn_w_up"][:, i0:i0 + FI // NCORES]], axis=1))
    d["fdn"] = np.ascontiguousarray(
        inp["L0_ffn_w_down"][(FI // NCORES) * c:(FI // NCORES) * (c + 1)])
    d["bq"] = inp["L0_ffn_bq"]
    d["bk"] = inp["L0_ffn_bk"]
    d["bv"] = inp["L0_ffn_bv"]
    d["ln1"] = inp["L0_ffn_ln1_w"]
    d["ln2"] = inp["L0_ffn_ln2_w"]
    d["wr"] = inp["L0_ffn_w_router"]
    d.update(host_tables(cfg, c))
    d.update(host_consts())
    return d


_DRAIN_OPS = {
    "tensor_copy", "tensor_tensor", "tensor_add", "tensor_sub", "tensor_mul",
    "tensor_max", "tensor_scalar", "tensor_single_scalar", "tensor_scalar_add",
    "tensor_scalar_sub", "tensor_scalar_mul", "tensor_scalar_max",
    "tensor_scalar_min", "tensor_relu", "tensor_reduce", "reduce_max",
    "reduce_sum", "scalar_tensor_tensor", "tensor_tensor_scan", "memset",
    "memzero", "reciprocal", "select", "transpose", "bn_stats", "bn_aggr",
    "activation", "square", "sqrt", "mul", "copy", "add", "sign", "iota",
    "affine_select", "affine_mul_reduce",
}


def _ap_range(a):
    t = getattr(a, "tensor", None)
    if t is None:
        return None
    pat = getattr(a, "ap", None)
    off = getattr(a, "offset", None)
    if pat is None or off is None:
        raise ValueError("unparsed AP")
    try:
        dims = [(int(p[0]), int(p[1])) for p in pat]
    except Exception:
        raise ValueError("unparsed AP pattern")
    rowlen, npart = dims[0]
    if rowlen <= 0:
        raise ValueError("odd partition pair")
    plo = int(off) // rowlen
    clo = int(off) % rowlen
    creach = clo
    for s, n_ in dims[1:]:
        creach += (n_ - 1) * abs(s)
    return (t.name, plo, plo + npart, clo, creach + 1)


class DrainWrap:

    def __init__(self, eng):
        self._e = eng
        self._w = []
        self._r = []
        self.always = _AlwaysDrain(self)

    @staticmethod
    def _overlap(r, lst):
        for (n_, pl, ph, cl, ch) in lst:
            if (n_ == r[0] and r[1] < ph and pl < r[2]
                    and r[3] < ch and cl < r[4]):
                return True
        return False

    def _emit(self, a, args, kw, force):
        need = force
        wr = []
        rd = []
        try:
            for x in args[1:]:
                r = _ap_range(x)
                if r is not None:
                    rd.append(r)
                    if self._overlap(r, self._w):
                        need = True
            for k_, x in kw.items():
                r = _ap_range(x)
                if r is None:
                    continue
                if k_ == "accum_out":
                    wr.append(r)
                else:
                    rd.append(r)
                    if self._overlap(r, self._w):
                        need = True
            r0 = _ap_range(args[0])
            if r0 is None:
                raise ValueError("no out ap")
            wr.append(r0)
            for r in wr:
                if (self._overlap(r, self._w)
                        or self._overlap(r, self._r)):
                    need = True
        except ValueError:
            need = True
            wr = None
        if (need or wr is None or len(self._w) > 96
                or len(self._r) > 96):
            self._e.drain()
            self._w = []
            self._r = []
        if wr:
            self._w.extend(wr)
            self._r.extend(rd)
        return a(*args, **kw)

    def __getattr__(self, n):
        a = getattr(self._e, n)
        if n in _DRAIN_OPS:
            def f(*args, **kw):
                return self._emit(a, args, kw, False)
            return f
        if n == "drain":
            def fd(*args, **kw):
                self._w = []
                self._r = []
                return a(*args, **kw)
            return fd
        return a


class _AlwaysDrain:
    def __init__(self, wrap):
        self._wrap = wrap

    def __getattr__(self, n):
        a = getattr(self._wrap._e, n)
        if n in _DRAIN_OPS:
            def f(*args, **kw):
                return self._wrap._emit(a, args, kw, True)
            return f
        return a


class Sem:
    def __init__(self, ctx, nc, name):
        self.h = ctx.enter_context(nc.semaphore(name))
        self.n = 0

    def inc(self, instr, k=1):
        if k == 1:
            instr.then_inc(self.h)
        else:
            instr.then_inc(self.h, k)
        self.n += k
        return instr

    def dma(self, instr):
        return self.inc(instr, 16)

    def wait(self, eng, n=None):
        eng.wait_ge(self.h, self.n if n is None else n)


def build(cfg, stop_after=None):
    nc = bass.Bass("TRN2", target_bir_lowering=False)
    Tc, NT, QT, NQT, NS = cfg.Tc, cfg.NT, cfg.QT, cfg.NQT, cfg.NS
    KC, NKT, QHW, QH = cfg.KC, cfg.NKT, cfg.QHW, cfg.QH

    def EI(name, shape, dtype=F32):
        return nc.dram_tensor(name, list(shape), dtype, kind="ExternalInput")

    x_in = EI("x", [Tc, D])
    wq4 = EI("wq4", [128, 4 * DC * DH])
    wkv = EI("wkv", [128, DC * 128])
    wo1 = EI("wo1", [DH, D])
    fw1 = EI("fw1", [D // NCORES, FW_G])
    fgu = EI("fgu", [D, 2 * (FI // NCORES)])
    fdn = EI("fdn", [FI // NCORES, D])
    bq_i = EI("bq", [D])
    bk_i = EI("bk", [1024])
    bv_i = EI("bv", [1024])
    ln1_i = EI("ln1", [D])
    ln2_i = EI("ln2", [D])
    wr_i = EI("wr", [D])
    vb_i = EI("vbias", [128, cfg.NV])
    dm_i = EI("dmask", [len(cfg.slots), 128, QT], BF16)
    am_i = EI("amask", [NKT * QH, 128, QHW], BF16)
    mc_i = EI("maskc", [NCORES, 1])
    gt_i = EI("gtok", [128, NT])
    ey_bf_i = EI("eye_bf", [128, 128], BF16)
    ey_f_i = EI("eye_f32", [128, 128])
    on_bf_i = EI("ones_bf", [128, 128], BF16)
    on_f_i = EI("ones_f32", [128, 128])
    u128_i = EI("u128s", [128, 128])
    u16_i = EI("u16s", [16, 16])
    ivf_i = EI("invfreq", [1, 128])
    io1_i = EI("iotap1", [128, 1])
    out_t = nc.dram_tensor("out", [Tc, D], F32, kind="ExternalOutput")

    def DT(name, shape, dtype=F32, shared=False):
        return nc.dram_tensor(name, list(shape), dtype, kind="Internal",
                              addr_space="Shared" if shared else "Local")

    a1c = DT("a1c", [D, 192], BF16)
    a1g = DT("a1g", [NCORES * D, 192], BF16)
    a2c = DT("a2c", [DH, D], BF16)
    a2g = DT("a2g", [NCORES * DH, D], BF16)
    fw1cAs = [DT(f"fw1cA{i}", [D // NCORES, 2048], BF16) for i in range(3)]
    fw1gAs = [DT(f"fw1gA{i}", [D, 2048], BF16) for i in range(3)]
    fguc = DT("fguc", [D, 2 * (FI // NCORES)], BF16)
    h2c = DT("h2c", [D, KC], BF16)
    h2g = DT("h2g", [NCORES * D, KC], BF16)
    _spl = (KC // 128) >= 2
    dncA = DT("dncA", [cfg.KSEL // 2 if _spl else cfg.KSEL, D], F32)
    dncB = DT("dncB", [cfg.KSEL // 2, D], F32)
    dnrA = DT("dnrA", [KC // 2 if _spl else KC, D], F32)
    dnrB = DT("dnrB", [KC // 2, D], F32)
    fdnc = DT("fdnc", [FI // NCORES, D], BF16)
    KW = cfg.NKC * 128 // NCORES
    ktc = DT("ktc", [NCORES * 512, KW], BF16)
    kta = DT("kta", [NCORES * 512, KW], BF16)
    vtc = DT("vtc", [Tc, 512], BF16)
    vta = DT("vta", [Tc, 512], BF16)
    qc = DT("qc", [512, Tc], BF16)
    qg = DT("qg", [NCORES * 512, Tc], BF16)
    NZB = Tc // 512
    zlcb = [DT(f"zlc{b}", [NCORES * 520, 512], F32) for b in range(NZB)]
    zlrb = [DT(f"zlr{b}", [520, 512], F32) for b in range(NZB)]
    scc = DT("scc", [Tc, 1], F32)
    scg = DT("scg", [cfg.T, 1], F32)
    cntc = DT("cntc", [1, 1], F32)
    cntg = DT("cntg", [NCORES, 1], F32)
    selhc = DT("selhc", [cfg.KSEL, 2048], F32)
    selhr = DT("selhr", [KC, 2048], F32)
    selxc = DT("selxc", [cfg.KSEL, 4], F32)
    selxr = DT("selxr", [KC, 4], F32)
    fqc = DT("fqc", [D, KC], BF16)
    fqg = DT("fqg", [NCORES * D, KC], BF16)
    FZR = FH // 2 * 128 + FH // 2
    fzlcA = DT("fzlcA", [NCORES * FZR, KC], F32)
    fzlcB = DT("fzlcB", [NCORES * FZR, KC], F32)
    fzlrA = DT("fzlrA", [FZR, KC], F32)
    fzlrB = DT("fzlrB", [FZR, KC], F32)
    gatc = DT("gatc", [KC, D], F32)
    gatg = DT("gatg", [cfg.KSEL, D], F32)
    x1d = DT("x1d", [Tc, D], F32)
    RG = [list(range(NCORES))]

    ctx = contextlib.ExitStack()
    SB = lambda name, shape, dt=F32: ctx.enter_context(
        nc.sbuf_tensor(name, list(shape), dt))

    s_in = Sem(ctx, nc, "s_in")
    s_w = Sem(ctx, nc, "s_w")
    s_cc = Sem(ctx, nc, "s_cc")
    s_pe = Sem(ctx, nc, "s_pe")
    s_dv = Sem(ctx, nc, "s_dv")
    s_ac = Sem(ctx, nc, "s_ac")
    s_gp = Sem(ctx, nc, "s_gp")
    s_st = Sem(ctx, nc, "s_st")
    s_ld = Sem(ctx, nc, "s_ld")
    s_x = Sem(ctx, nc, "s_x")
    s_sc = Sem(ctx, nc, "s_sc")
    s_ex = Sem(ctx, nc, "s_ex")
    s_av = Sem(ctx, nc, "s_av")
    s_ev = Sem(ctx, nc, "s_ev")
    s_out = Sem(ctx, nc, "s_out")
    s_qp = [Sem(ctx, nc, f"s_q{i}") for i in range(6)]
    s_fqp = [Sem(ctx, nc, "s_fq0"), Sem(ctx, nc, "s_fq1")]
    s_mk = Sem(ctx, nc, "s_mk")
    s_sca = Sem(ctx, nc, "s_sca")
    s_gat = Sem(ctx, nc, "s_gat")
    s_wg = Sem(ctx, nc, "s_wg")
    s_sg = Sem(ctx, nc, "s_sg")
    s_ev2 = Sem(ctx, nc, "s_ev2")
    s_zz = Sem(ctx, nc, "s_zz")
    bnd_reg = ctx.enter_context(nc.gpsimd.register("bnd"))
    nc.gpsimd.reg_mov(bnd_reg, cfg.KSEL - 1)

    sync = nc.sync
    pe = nc.tensor
    gps = DrainWrap(nc.gpsimd)
    dve = DrainWrap(nc.vector)
    act = DrainWrap(nc.scalar)
    bar = nc.all_engine_barrier

    def syncpoint():
        for s_ in (s_in, s_w, s_st, s_ld, s_x, s_gp, s_out,
                   s_qp[0], s_qp[1], s_qp[2], s_qp[3], s_qp[4], s_qp[5],
                   s_fqp[0], s_fqp[1], s_sca, s_gat, s_wg, s_sg, s_zz):
            s_.wait(sync)
        bar()

    ey_bf = SB("ey_bf", [128, 128], BF16)
    ey_f = SB("ey_f", [128, 128])
    on_bf = SB("on_bf", [128, 128], BF16)
    on_f = SB("on_f", [128, 128])
    u128 = SB("u128", [128, 128])
    u16 = SB("u16", [16, 16])
    ivf = SB("ivf", [1, 128])
    io1 = SB("io1", [128, 1])
    vb_s = SB("vb_s", [128, cfg.NV])
    mc_s = SB("mc_s", [NCORES, 1])
    gt_s = SB("gt_s", [128, NT])
    cvec = SB("cvec_s", [128, len(CONSTVALS)])
    cvec_i = EI("cvec", [128, len(CONSTVALS)])
    selio = SB("selio_s", [2, 128])
    selio_i = EI("selio", [2, 128])
    for dst, src in [(ey_bf, ey_bf_i), (ey_f, ey_f_i), (on_bf, on_bf_i),
                     (on_f, on_f_i), (u128, u128_i), (u16, u16_i), (ivf, ivf_i),
                     (io1, io1_i), (vb_s, vb_i), (mc_s, mc_i), (gt_s, gt_i),
                     (cvec, cvec_i), (selio, selio_i)]:
        s_in.dma(sync.dma_start(out=dst[:, :], in_=src.ap()))
    for i, v in enumerate(CONSTVALS):
        nc.const_aps.register(F32, v, cvec[:, i:i + 1])
    for eng in (pe, dve, act, gps):
        s_in.wait(eng)
    m1_s = SB("m1_s", [128, NT])
    r1_s = SB("r1_s", [128, NT])
    v1_s = SB("v1_s", [128, NT])
    sc_s = SB("sc_s", [128, NT])
    msk_s = SB("msk_s", [128, NT])
    ka_s = SB("ka_s", [128, NT])
    kb_s = SB("kb_s", [128, NT])
    idx_s = SB("idx_s", [128, NT], I32)
    gate_s = SB("gate_s", [128, NT])

    def ln_stats(xa, sa, sb, sc2, eps):
        dve.reduce_sum(sa[:, :], xa, AX.X)
        s_ac.inc(act.always.activation(pscr[:, :], xa, AF.Square, accum_out=sb[:, :]))
        dve.tensor_scalar_mul(sa[:, :], sa[:, :], 1.0 / D)
        s_ac.wait(dve)
        dve.tensor_scalar_mul(sb[:, :], sb[:, :], 1.0 / D)
        dve.tensor_tensor(sc2[:, :], sa[:, :], sa[:, :], ALU.mult)
        s_dv.inc(dve.always.tensor_sub(sb[:, :], sb[:, :], sc2[:, :]))
        s_dv.wait(act)
        s_ac.inc(act.always.activation(sc2[:, :], sb[:, :], AF.Sqrt, bias=eps))
        s_ac.wait(dve)
        dve.reciprocal(sc2[:, :], sc2[:, :])

    wcctx = contextlib.ExitStack()
    wc = wcctx.enter_context(nc.sbuf_tensor("wc", [128, 2 * 2048], F32))
    wco = wcctx.enter_context(nc.sbuf_tensor("wco", [128, 2 * 2048], BF16))
    with nc.sbuf_tensor("wa", [128, DC * 4 * DH], F32) as wa, \
         nc.sbuf_tensor("wkvs", [128, DC * 128], F32) as wkvs, \
         nc.sbuf_tensor("wob", [128, DC * 192], BF16) as wob:
        s_w.dma(sync.dma_start(out=wa[:, :], in_=wq4.ap()))
        s_w.dma(sync.dma_start(out=wkvs[:, :], in_=wkv.ap()))
        s_w.wait(sync)
        s_w.wait(dve)
        for dc in range(DC):
            o4 = dc * 4 * DH
            oo = dc * 192
            dve.tensor_add(wob[:, oo:oo + DH].bitcast(F32)
                           if False else wa[:, o4:o4 + DH],
                           wa[:, o4:o4 + DH], wa[:, o4 + DH:o4 + 2 * DH])
            dve.tensor_add(wa[:, o4:o4 + DH], wa[:, o4:o4 + DH],
                           wa[:, o4 + 2 * DH:o4 + 3 * DH])
            dve.tensor_add(wa[:, o4:o4 + DH], wa[:, o4:o4 + DH],
                           wa[:, o4 + 3 * DH:o4 + 4 * DH])
            dve.tensor_copy(wob[:, oo:oo + DH], wa[:, o4:o4 + DH])
            s_dv.inc(dve.always.tensor_copy(wob[:, oo + DH:oo + 192],
                                     wkvs[:, 128 * dc:128 * (dc + 1)]))
        s_dv.wait(sync)
        s_st.dma(sync.dma_start(
            out=a1c.ap().rearrange("(dc p) k -> p dc k", p=128),
            in_=wob[:, :].rearrange("p (dc k) -> p dc k", k=192)))
        s_w.dma(sync.dma_start(out=wc[0:DH, 0:D], in_=wo1.ap()))
        s_w.wait(dve)
        s_dv.inc(dve.always.tensor_copy(wco[0:DH, 0:D], wc[0:DH, 0:D]))
        s_dv.wait(sync)
        s_st.dma(sync.dma_start(out=a2c.ap(), in_=wco[0:DH, 0:D]))
        pass
    wk_items = []
    for cs in range(FW_G // 2048):
        for blk in range(2):
            wk_items.append((fw1.ap()[128 * blk:128 * (blk + 1),
                                      2048 * cs:2048 * (cs + 1)],
                             fw1cAs[cs].ap()[128 * blk:128 * (blk + 1), :]))
    for blk in range(D // 128):
        wk_items.append((fgu.ap()[128 * blk:128 * (blk + 1), :],
                         fguc.ap()[128 * blk:128 * (blk + 1), :]))
    for blk in range(FI // NCORES // 128):
        wk_items.append((fdn.ap()[128 * blk:128 * (blk + 1), :],
                         fdnc.ap()[128 * blk:128 * (blk + 1), :]))
    NWI = len(wk_items)
    wk_gathers = [(2 * cs + 1, fw1cAs[cs], fw1gAs[cs])
                  for cs in range(FW_G // 2048)]
    wk_emitted = [False] * NWI
    wk_dvw = [0] * NWI
    wk_stw = [0] * NWI
    wk_swl = [0] * NWI
    dv_p0 = s_dv.n

    def wk_load(n):
        s_wg.wait(gps)
        if n == 0:
            s_dv.wait(gps, dv_p0)
        if n >= 2:
            s_dv.wait(gps, wk_dvw[n - 2])
        s_wg.dma(gps.dma_start(
            out=wc[:, (n % 2) * 2048:(n % 2 + 1) * 2048], in_=wk_items[n][0]))
        wk_swl[n] = s_wg.n

    def wk_pump(n):
        if n + 1 < NWI:
            wk_load(n + 1)
        s_wg.wait(dve, wk_swl[n])
        if n == 0:
            s_cc.wait(dve, cc_attnw)
        if n >= 2:
            s_sg.wait(dve, wk_stw[n - 2])
        s_dv.inc(dve.always.tensor_copy(wco[:, (n % 2) * 2048:(n % 2 + 1) * 2048],
                                 wc[:, (n % 2) * 2048:(n % 2 + 1) * 2048]))
        wk_dvw[n] = s_dv.n
        s_dv.wait(gps, wk_dvw[n])
        s_sg.wait(gps)
        s_sg.dma(gps.dma_start(out=wk_items[n][1],
                               in_=wco[:, (n % 2) * 2048:(n % 2 + 1) * 2048]))
        wk_stw[n] = s_sg.n
        wk_emitted[n] = True

    s_st.wait(gps)
    s_cc.inc(gps.always.collective_compute("AllGather", ALU.bypass, RG,
                                    [a1c.ap().opt()], [a1g.ap().opt()]))
    s_cc.inc(gps.always.collective_compute("AllGather", ALU.bypass, RG,
                                    [a2c.ap().opt()], [a2g.ap().opt()]))
    cc_attnw = s_cc.n
    cc_ffnw = [0]
    zz = wcctx.enter_context(nc.sbuf_tensor("zz", [128, 2052], F32))
    s_gp.inc(gps.always.memset(zz[:, :], 0.0))
    s_gp.wait(sync)
    zz_pend = list(range(cfg.KSEL // 128))

    syncpoint()

    def finish(ret=None):
        s_st.wait(sync)
        s_in.wait(sync)
        s_cc.wait(gps)
        bar()
        return nc

    if stop_after == "P0":
        return finish()

    with contextlib.ExitStack() as p3ctx:
        SB3 = lambda name, shape, dt=F32: p3ctx.enter_context(
            nc.sbuf_tensor(name, list(shape), dt))
        pscr = SB3("pscr", [128, 2048])
        stage = SB3("stage", [128, DC * Tc], BF16)
        p2ctx = contextlib.ExitStack()
        SB2 = lambda name, shape, dt=F32: p2ctx.enter_context(
            nc.sbuf_tensor(name, list(shape), dt))
        ln_t = SB2("ln_t", [128, NT * 2048], BF16)
        with nc.sbuf_tensor("xt", [128, 2 * 2048], F32) as xt, \
             nc.sbuf_tensor("sa", [128, 1], F32) as sa, \
             nc.sbuf_tensor("sb", [128, 1], F32) as sb, \
             nc.sbuf_tensor("sc2", [128, 1], F32) as sc2, \
             nc.psum_tensor("tp0", [128, 128], BF16) as tp0, \
             nc.psum_tensor("tp1", [128, 128], BF16) as tp1, \
             nc.psum_tensor("tp2", [128, 128], BF16) as tp2, \
             nc.psum_tensor("tp3", [128, 128], BF16) as tp3:
            tps = [tp0, tp1, tp2, tp3]
            xld = [0] * NT
            dvl = [0] * (NT * DC + 8)
            k = 0
            ln_done = [0] * NT

            def x_load(tb):
                s_x.wait(sync)
                s_dv.wait(sync)
                s_ac.wait(sync)
                s_x.dma(sync.dma_start(
                    out=xt[:, (tb % 2) * 2048:(tb % 2 + 1) * 2048],
                    in_=x_in.ap()[128 * tb:128 * (tb + 1), :]))
                xld[tb] = s_x.n

            x_load(0)
            for tb in range(NT):
                if tb + 1 < NT:
                    x_load(tb + 1)
                xa = xt[:, (tb % 2) * 2048:(tb % 2 + 1) * 2048]
                s_x.wait(dve, xld[tb])
                s_x.wait(act, xld[tb])
                ln_stats(xa, sa, sb, sc2, EPS_LN)
                s_dv.inc(dve.always.tensor_scalar(ln_t[:, 2048 * tb:2048 * (tb + 1)],
                                           xa, sa[:, :], sc2[:, :],
                                           ALU.subtract, ALU.mult))
                ln_done[tb] = s_dv.n
                s_dv.wait(pe, ln_done[tb])
                for dc in range(DC):
                    pp = tps[k % 4]
                    if k >= 4:
                        s_dv.wait(pe, dvl[k - 4])
                    s_pe.inc(pe.transpose(
                        pp[:, :], ln_t[:, 2048 * tb + 128 * dc:
                                       2048 * tb + 128 * (dc + 1)],
                        ey_bf[:, :]))
                    s_pe.wait(dve)
                    s_dv.inc(dve.always.tensor_copy(
                        stage[:, Tc * dc + 128 * tb:Tc * dc + 128 * (tb + 1)],
                        pp[:, :]))
                    dvl[k] = s_dv.n
                    k += 1
        if stop_after == "P2":
            s_dv.wait(sync)
            return finish()
        syncpoint()
        p2ctx.close()
        s_cc.wait(sync, cc_attnw)
        wq_t = SB3("wq_t", [128, 4 * DC * 128], BF16)
        wk_t = SB3("wk_t", [128, 4 * DC * 128], BF16)
        wv_t = SB3("wv_t", [128, DC * 512], BF16)
        WQW = 4 * DC * 128
        for hp in range(4):
            for g in range(2):
                h = 2 * hp + g
                base = hp * DC * 128 + 64 * g
                asrc = a1g.ap()[D * h:D * (h + 1), :].rearrange(
                    "(dc p) k -> p dc k", p=128)
                s_in.dma(sync.dma_start(
                    out=bass.AP(wq_t, base, [[WQW, 128], [128, DC], [1, 64]]),
                    in_=asrc[:, :, 0:DH]))
                s_in.dma(sync.dma_start(
                    out=bass.AP(wk_t, base, [[WQW, 128], [128, DC], [1, 64]]),
                    in_=asrc[:, :, DH:2 * DH]))
        for h in range(HE):
            s_in.dma(sync.dma_start(
                out=bass.AP(wv_t, 64 * h, [[DC * 512, 128], [512, DC], [1, 64]]),
                in_=a1g.ap()[D * h:D * (h + 1), 2 * DH:3 * DH]
                .rearrange("(dc p) k -> p dc k", p=128)))
        qkt_s = SB3("qkt_s", [128, 8 * Tc], BF16)
        vt_s = SB3("vt_s", [128, NT * 512], BF16)
        with nc.psum_tensor("pj_ps", [128, 512], F32) as pj_ps, \
             nc.psum_tensor("pj_ps2", [128, 512], F32) as pj_ps2, \
             nc.psum_tensor("pj_ps3", [128, 512], F32) as pj_ps3, \
             nc.psum_tensor("pj_ps4", [128, 512], F32) as pj_ps4:
            pjs = [pj_ps, pj_ps2, pj_ps3, pj_ps4]
            from concourse.aot_env import allow_ici_all_to_all
            s_in.wait(pe)
            s_dv.wait(pe)
            k = 0
            dv_base = s_dv.n
            for kind in range(2):
                wt = wq_t if kind == 0 else wk_t
                for hp in range(4):
                    for ns in range(NS):
                        pp = pjs[k % 4]
                        if k >= 4:
                            s_dv.wait(pe, dv_base + k - 3)
                        for dc in range(DC):
                            s_pe.inc(pe.matmul(
                                pp[:, :],
                                wt[:, (hp * DC + dc) * 128:
                                   (hp * DC + dc) * 128 + 128],
                                stage[:, Tc * dc + 512 * ns:
                                      Tc * dc + 512 * (ns + 1)],
                                start=(dc == 0), stop=(dc == DC - 1)))
                        s_pe.wait(dve)
                        s_dv.inc(dve.always.tensor_copy(
                            qkt_s[:, (kind * 4 + hp) * Tc + 512 * ns:
                                  (kind * 4 + hp) * Tc + 512 * (ns + 1)],
                            pp[:, :]))
                        k += 1
                s_dv.wait(sync)
                s_st.wait(sync)
                if kind == 0:
                    for hp in range(4):
                        s_st.dma(sync.dma_start(
                            out=qc.ap()[128 * hp:128 * (hp + 1), :],
                            in_=qkt_s[:, hp * Tc:(hp + 1) * Tc]))
                    s_st.wait(gps)
                    s_cc.inc(gps.always.collective_compute(
                        "AllGather", ALU.bypass, RG,
                        [qc.ap().opt()], [qg.ap().opt()]))
                else:
                    for m in range(cfg.NKC):
                        d_ = m % NCORES
                        half = m // NCORES
                        for hp in range(4):
                            s_st.dma(sync.dma_start(
                                out=ktc.ap()[512 * d_ + 128 * hp:
                                             512 * d_ + 128 * (hp + 1),
                                             128 * half:128 * (half + 1)],
                                in_=qkt_s[:, (4 + hp) * Tc + 128 * m:
                                          (4 + hp) * Tc + 128 * (m + 1)]))
                    s_st.wait(gps)
                    with allow_ici_all_to_all():
                        s_cc.inc(gps.always.collective_compute(
                            "AllToAll", ALU.bypass, RG,
                            [ktc.ap().opt()], [kta.ap().opt()]))
            for tb in range(NT):
                pp = pjs[k % 4]
                s_dv.wait(pe, dv_base + k - 3)
                for dc in range(DC):
                    s_pe.inc(pe.matmul(
                        pp[:, :],
                        stage[:, Tc * dc + 128 * tb:Tc * dc + 128 * (tb + 1)],
                        wv_t[:, 512 * dc:512 * (dc + 1)],
                        start=(dc == 0), stop=(dc == DC - 1)))
                s_pe.wait(dve)
                s_dv.inc(dve.always.tensor_copy(vt_s[:, 512 * tb:512 * (tb + 1)],
                                         pp[:, :]))
                k += 1
        s_dv.wait(sync)
        s_st.wait(sync)
        for m in range(cfg.NKC):
            d_ = m % NCORES
            half = m // NCORES
            s_st.dma(sync.dma_start(
                out=vtc.ap()[(Tc // NCORES) * d_ + 128 * half:
                             (Tc // NCORES) * d_ + 128 * (half + 1), :],
                in_=vt_s[:, 512 * m:512 * (m + 1)]))
        s_st.wait(gps)
        with allow_ici_all_to_all():
            s_cc.inc(gps.always.collective_compute("AllToAll", ALU.bypass, RG,
                                            [vtc.ap().opt()], [vta.ap().opt()]))
        cc_qkv = s_cc.n
    if stop_after == "P3":
        return finish()

    syncpoint()
    with contextlib.ExitStack() as p5ctx:
        SB5 = lambda name, shape, dt=F32: p5ctx.enter_context(
            nc.sbuf_tensor(name, list(shape), dt))
        ktx = SB5("ktx", [128, 4 * cfg.NKC * 128], BF16)
        vres = SB5("vres", [128, cfg.NKC * 512], BF16)
        s_cc.wait(sync, cc_qkv)
        NH2_ = cfg.NKC // 8
        for hp in range(4):
            s_ld.dma(sync.dma_start(
                out=ktx[:, hp * cfg.NKC * 128:(hp + 1) * cfg.NKC * 128]
                .rearrange("p (j2 half k) -> p j2 half k", half=NH2_, k=128),
                in_=kta.ap().rearrange("(j2 x p) (half k) -> x p j2 half k",
                                       x=4, p=128, k=128)[hp]))
        s_ld.dma(sync.dma_start(
            out=vres[:, :].rearrange("p (j2 half k) -> p j2 half k",
                                     half=NH2_, k=512),
            in_=vta.ap().rearrange("(j2 half p) k -> p j2 half k",
                                   half=NH2_, p=128)))
        ld_res = s_ld.n
        qbuf = SB5("qbuf", [128, 24 * QT], BF16)
        mbuf = SB5("mbuf", [128, 12 * QT], BF16)
        pT = SB5("pT", [128, 4 * QT], BF16)
        zst = SB5("zst", [128, 8 * 512])
        lstA = SB5("lstA", [1, 8 * 512])
        lstB = SB5("lstB", [1, 8 * 512])
        with nc.psum_tensor("s_ps", [128, 2 * QT], F32) as s_psA, \
             nc.psum_tensor("s_ps2", [128, 2 * QT], F32) as s_psB, \
             nc.psum_tensor("z_psa", [128, QT], F32) as z_psa, \
             nc.psum_tensor("z_psb", [128, QT], F32) as z_psb, \
             nc.psum_tensor("l_psa", [128, QT], F32) as l_psa, \
             nc.psum_tensor("l_psb", [128, QT], F32) as l_psb:

            def q_load(t, sl):
                o = (QT * t) // Tc
                col = QT * t - Tc * o
                for hp in range(4):
                    s_qp[sl].dma(sync.dma_start(
                        out=qbuf[:, (sl * 4 + hp) * QT:
                                 (sl * 4 + hp + 1) * QT],
                        in_=qg.ap()[512 * o + 128 * hp:512 * o + 128 * (hp + 1),
                                    col:col + QT]))

            def m_load(t, sl):
                base = len([s for s in cfg.slots if s[0] < t])
                for u in range(min(2, cfg.trips[t])):
                    s_qp[sl].dma(sync.dma_start(
                        out=mbuf[:, (sl * 2 + u) * QT:
                                 (sl * 2 + u + 1) * QT],
                        in_=dm_i.ap()[base + u, :, :]))

            NBLK = Tc // QT
            tseq = [t_ for b_ in range(NBLK) for t_ in range(b_, NQT, NBLK)]
            vbase = [0] * NQT
            for t_ in range(1, NQT):
                vbase[t_] = vbase[t_ - 1] + cfg.trips[t_ - 1]
            visits = []
            vend = []
            tslot = {}
            for si_, t_ in enumerate(tseq):
                trips_ = cfg.trips[t_]
                jm0_ = trips_ - min(2, trips_)
                tslot[t_] = si_ % 6
                for hp_ in range(4):
                    for j_ in range(trips_):
                        visits.append((t_, hp_, j_, trips_, jm0_,
                                       vbase[t_] + j_))
                vend.append(len(visits))
            NVIS = len(visits)
            seq_of = {t_: si_ for si_, t_ in enumerate(tseq)}
            ldcount = [0] * NQT
            for pf_ in range(min(5, len(tseq))):
                q_load(tseq[pf_], pf_)
                m_load(tseq[pf_], pf_)
                ldcount[tseq[pf_]] = s_qp[pf_].n
            av_hist = [0] * NVIS
            mk_hist = [0] * NVIS
            sc_hist = [0] * NVIS
            cc_zlb = [0] * NBLK
            wk_gat_next = [0]
            PUMP_EVERY = max(1, (2 * NVIS) // (3 * NWI))
            wk_load(0)

            qp_waited = set()

            def emit_scores(k):
                t_, hp_, j_, trips_, jm0_, v_ = visits[k]
                sp = s_psA if k % 2 == 0 else s_psB
                if t_ not in qp_waited:
                    s_qp[tslot[t_]].wait(pe, ldcount[t_])
                    qp_waited.add(t_)
                if k == 0:
                    s_ld.wait(pe)
                if k >= 2:
                    s_ex.wait(pe, k - 1)
                kof = (hp_ * cfg.NKC + j_) * 128
                qof = (tslot[t_] * 4 + hp_) * QT
                s_sc.inc(pe.matmul(sp[:, 0:QT], ktx[0:64, kof:kof + 128],
                                   qbuf[0:64, qof:qof + QT],
                                   start=True, stop=True, tile_position=(0, 0)))
                s_sc.inc(pe.matmul(sp[:, QT:2 * QT], ktx[64:128, kof:kof + 128],
                                   qbuf[64:128, qof:qof + QT],
                                   start=True, stop=True, tile_position=(64, 0)))
                sc_hist[k] = s_sc.n

            st_hist = {}

            def emit_evac(av_t, g_):
                sl_ = 512 * (g_ % 8)
                s_av.wait(dve, av_t)
                if g_ >= 8:
                    s_st.wait(dve, st_hist[g_ - 8])
                s_ev.inc(dve.tensor_copy(zst[0:64, sl_:sl_ + 512],
                                         z_psa[0:64, :]))
                s_ev.inc(dve.tensor_copy(zst[64:128, sl_:sl_ + 512],
                                         z_psb[64:128, :]))
                s_av.wait(act, av_t)
                if g_ >= 8:
                    s_st.wait(act, st_hist[g_ - 8])
                s_ev2.inc(act.copy(lstA[:, sl_:sl_ + 512], l_psa[0:1, :]))
                s_ev2.inc(act.copy(lstB[:, sl_:sl_ + 512], l_psb[64:65, :]))

            def emit_store(t_p, hp_p, g_):
                sl_ = 512 * (g_ % 8)
                s_ev.wait(sync)
                s_ev2.wait(sync)
                s_st.wait(sync)
                r_ = t_p // NBLK
                zt = zlcb[t_p % NBLK]
                s_st.dma(sync.dma_start(
                    out=zt.ap()[520 * r_ + 128 * hp_p:
                                520 * r_ + 128 * (hp_p + 1), :],
                    in_=zst[:, sl_:sl_ + 512]))
                s_st.dma(sync.dma_start(
                    out=zt.ap()[520 * r_ + 512 + 2 * hp_p:
                                520 * r_ + 513 + 2 * hp_p, :],
                    in_=lstA[:, sl_:sl_ + 512]))
                s_st.dma(sync.dma_start(
                    out=zt.ap()[520 * r_ + 513 + 2 * hp_p:
                                520 * r_ + 514 + 2 * hp_p, :],
                    in_=lstB[:, sl_:sl_ + 512]))
                st_hist[g_] = s_st.n
                if hp_p == 3 and t_p == NQT - NBLK + (t_p % NBLK):
                    b_ = t_p % NBLK
                    s_st.wait(gps)
                    s_cc.inc(gps.always.collective_compute(
                        "ReduceScatter", ALU.add, RG,
                        [zlcb[b_].ap().opt()], [zlrb[b_].ap().opt()]))
                    cc_zlb[b_] = s_cc.n
                    while wk_gat_next[0] < len(wk_gathers):
                        li, cin_, cout_ = wk_gathers[wk_gat_next[0]]
                        if not wk_emitted[li]:
                            break
                        s_sg.wait(gps, wk_stw[li])
                        s_cc.inc(gps.always.collective_compute(
                            "AllGather", ALU.bypass, RG,
                            [cin_.ap().opt()], [cout_.ap().opt()]))
                        cc_ffnw[0] = s_cc.n
                        wk_gat_next[0] += 1

            pend = None
            gcount = 0
            emit_scores(0)
            for k in range(NVIS):
                t, hp, j, trips, jmask0, v = visits[k]
                pj = k % 2
                sp = s_psA if pj == 0 else s_psB
                if j == 0 and hp == 0:
                    si = seq_of[t]
                    if zz_pend:
                        r_z = zz_pend.pop(0)
                        s_zz.dma(sync.dma_start(
                            out=selhc.ap()[128 * r_z:128 * (r_z + 1), :],
                            in_=zz[:, 0:2048]))
                        s_zz.dma(sync.dma_start(
                            out=selxc.ap()[128 * r_z:128 * (r_z + 1), :],
                            in_=zz[:, 0:4]))
                    if si + 5 < len(tseq):
                        tn = tseq[si + 5]
                        if si >= 1:
                            s_sc.wait(sync, 2 * vend[si - 1])
                            s_mk.wait(sync, mk_hist[vend[si - 1] - 1])
                        q_load(tn, (si + 5) % 6)
                        m_load(tn, (si + 5) % 6)
                        ldcount[tn] = s_qp[(si + 5) % 6].n
                if k + 1 < NVIS:
                    emit_scores(k + 1)
                s_sc.wait(act, sc_hist[k])
                if k >= 2:
                    s_av.wait(act, av_hist[k - 2])
                ei = s_ex.inc(act.activation(
                    pT[:, pj * 2 * QT:(pj + 1) * 2 * QT],
                    sp[:, :], AF.Exp, scale=0.125, bias=vb_s[:, v:v + 1]))
                if pend is not None:
                    emit_evac(pend[2], pend[3])
                    emit_store(pend[0], pend[1], pend[3])
                    pend = None
                masked = j >= jmask0
                if masked:
                    u = j - jmask0
                    mof = (tslot[t] * 2 + u) * QT
                    s_ex.wait(dve, k + 1)
                    s_qp[tslot[t]].wait(dve, ldcount[t])
                    s_mk.inc(dve.tensor_mul(
                        pT[:, pj * 2 * QT:pj * 2 * QT + QT],
                        pT[:, pj * 2 * QT:pj * 2 * QT + QT],
                        mbuf[:, mof:mof + QT]))
                    s_mk.inc(dve.tensor_mul(
                        pT[:, pj * 2 * QT + QT:(pj + 1) * 2 * QT],
                        pT[:, pj * 2 * QT + QT:(pj + 1) * 2 * QT],
                        mbuf[:, mof:mof + QT]))
                mk_hist[k] = s_mk.n
                if k % PUMP_EVERY == PUMP_EVERY - 1 and k // PUMP_EVERY < NWI:
                    wk_pump(k // PUMP_EVERY)
                s_ex.wait(pe, k + 1)
                if masked:
                    s_mk.wait(pe, mk_hist[k])
                ev0 = s_ev.n if (j == 0 and k > 0) else None
                ev2 = s_ev2.n if (j == 0 and k > 0) else None
                st_ = (j == 0)
                sp_ = (j == trips - 1)
                if ev0 is not None:
                    s_ev.wait(pe, ev0 - 1)
                s_av.inc(pe.matmul(z_psa[0:64, :],
                                   vres[:, 512 * j + 128 * hp:
                                        512 * j + 128 * hp + 64],
                                   pT[:, pj * 2 * QT:pj * 2 * QT + QT],
                                   start=st_, stop=sp_, tile_position=(0, 0)))
                if ev0 is not None:
                    s_ev.wait(pe, ev0)
                s_av.inc(pe.matmul(z_psb[64:128, :],
                                   vres[:, 512 * j + 128 * hp + 64:
                                        512 * j + 128 * (hp + 1)],
                                   pT[:, pj * 2 * QT + QT:(pj + 1) * 2 * QT],
                                   start=st_, stop=sp_, tile_position=(0, 64)))
                if ev0 is not None:
                    s_ev2.wait(pe, ev2 - 1)
                s_av.inc(pe.matmul(l_psa[0:1, :], on_bf[:, 0:1],
                                   pT[:, pj * 2 * QT:pj * 2 * QT + QT],
                                   start=st_, stop=sp_, tile_position=(0, 0)))
                if ev0 is not None:
                    s_ev2.wait(pe, ev2)
                s_av.inc(pe.matmul(l_psb[64:65, :], on_bf[:, 0:1],
                                   pT[:, pj * 2 * QT + QT:(pj + 1) * 2 * QT],
                                   start=st_, stop=sp_, tile_position=(0, 64)))
                av_hist[k] = s_av.n
                if j == trips - 1:
                    pend = (t, hp, av_hist[k], gcount)
                    gcount += 1
            emit_evac(pend[2], pend[3])
            emit_store(pend[0], pend[1], pend[3])
        s_st.wait(gps)
    wcctx.close()
    if stop_after == "P5":
        return finish()

    syncpoint()
    with contextlib.ExitStack() as p7ctx:
        SB7 = lambda name, shape, dt=F32: p7ctx.enter_context(
            nc.sbuf_tensor(name, list(shape), dt))
        zfm = SB7("zfm", [128, 4 * Tc], BF16)
        with nc.sbuf_tensor("zn_f", [128, Tc], F32) as zn_f, \
             nc.sbuf_tensor("li_ab", [2, Tc], F32) as li_ab, \
             nc.sbuf_tensor("rcp_t", [128, 512], F32) as rcp_t, \
             nc.psum_tensor("lb_ps", [128, 512], F32) as lb_ps:
            ldb = {}
            for hp in range(4):
                s_dv.wait(sync)
                for b_ in range(NZB):
                    if hp == 0:
                        s_cc.wait(sync, cc_zlb[b_])
                    s_ld.wait(sync)
                    s_ld.dma(sync.dma_start(
                        out=zn_f[:, 512 * b_:512 * (b_ + 1)],
                        in_=zlrb[b_].ap()[128 * hp:128 * (hp + 1), :]))
                    s_ld.dma(sync.dma_start(
                        out=li_ab[:, 512 * b_:512 * (b_ + 1)],
                        in_=zlrb[b_].ap()[512 + 2 * hp:514 + 2 * hp, :]))
                    ldb[b_] = s_ld.n
                for ns in range(NS):
                    s_ld.wait(pe, ldb[ns])
                    s_pe.inc(pe.matmul(lb_ps[:, :], selio[0:2, :],
                                       li_ab[:, 512 * ns:512 * (ns + 1)],
                                       start=True, stop=True))
                    s_pe.wait(dve)
                    s_ld.wait(dve, ldb[ns])
                    dve.reciprocal(rcp_t[:, :], lb_ps[:, :])
                    s_dv.inc(dve.always.tensor_mul(
                        zfm[:, hp * Tc + 512 * ns:hp * Tc + 512 * (ns + 1)],
                        zn_f[:, 512 * ns:512 * (ns + 1)], rcp_t[:, :]))
                    s_dv.wait(pe)
        s_dv.wait(sync)
        wo_t = SB7("wo_t", [128, 4 * DC * 128], BF16)
        for hp in range(4):
            s_in.dma(sync.dma_start(
                out=wo_t[:, hp * DC * 128:(hp + 1) * DC * 128],
                in_=a2g.ap()[128 * hp:128 * (hp + 1), :]))
        pscr = SB7("pscr7", [128, 2048])
        wr_b = SB7("wr_b", [128, 2048])
        srw_b = SB7("srw_b", [128, 1])
        with nc.sbuf_tensor("wr_r", [1, 2048], F32) as wr_r, \
             nc.psum_tensor("br_ps", [128, 512], F32) as br_ps:
            s_in.wait(sync)
            s_in.dma(sync.dma_start(out=wr_r[:, :], in_=wr_i.ap()[None, :]))
            s_in.wait(pe)
            for ns in range(4):
                s_pe.inc(pe.matmul(br_ps[:, :], on_f[0:1, :],
                                   wr_r[:, 512 * ns:512 * (ns + 1)],
                                   start=True, stop=True))
                s_pe.wait(dve)
                s_dv.inc(dve.always.tensor_copy(wr_b[:, 512 * ns:512 * (ns + 1)],
                                         br_ps[:, :]))
                s_dv.wait(pe)
        s_dv.inc(dve.always.reduce_sum(srw_b[:, :], wr_b[:, :], AX.X))
        with nc.sbuf_tensor("xt2", [128, 2 * 2048], F32) as xt2, \
             nc.sbuf_tensor("at2", [128, 2 * 2048], F32) as at2, \
             nc.sbuf_tensor("x1t", [128, 2 * 2048], F32) as x1t, \
             nc.psum_tensor("ao_psA", [128, 2048], F32) as ao_psA, \
             nc.psum_tensor("ao_psB", [128, 2048], F32) as ao_psB, \
             nc.sbuf_tensor("sa7", [128, 2], F32) as sa, \
             nc.sbuf_tensor("sb7", [128, 2], F32) as sb, \
             nc.sbuf_tensor("sc27", [128, 2], F32) as sc2:
            s_in.wait(pe)
            s_dv.wait(pe)
            xh7 = [0] * NT
            dv7 = [0] * NT
            st7 = [0] * NT
            pe7 = [0] * NT
            cta7 = [0] * NT

            def x7_fetch(tb):
                s_x.wait(sync)
                if tb >= 2:
                    s_dv.wait(sync, dv7[tb - 2])
                s_x.dma(sync.dma_start(
                    out=xt2[:, (tb % 2) * 2048:(tb % 2 + 1) * 2048],
                    in_=x_in.ap()[128 * tb:128 * (tb + 1), :]))
                xh7[tb] = s_x.n

            s_dv.wait(sync)
            s_ac.wait(sync)
            x7_fetch(0)
            x7_fetch(1)
            for tb in range(NT):
                sl = (tb % 2) * 2048
                c = tb % 2
                a2 = at2[:, sl:sl + 2048]
                x1s = x1t[:, sl:sl + 2048]
                sac = sa[:, c:c + 1]
                sbc = sb[:, c:c + 1]
                scc2 = sc2[:, c:c + 1]
                pp7 = ao_psA if tb % 2 == 0 else ao_psB
                if tb >= 2:
                    s_dv.wait(pe, cta7[tb - 2])
                for hp in range(4):
                    for dn in range(4):
                        s_pe.inc(pe.matmul(
                            pp7[:, 512 * dn:512 * (dn + 1)],
                            zfm[:, hp * Tc + 128 * tb:hp * Tc + 128 * (tb + 1)],
                            wo_t[:, hp * DC * 128 + 512 * dn:
                                 hp * DC * 128 + 512 * (dn + 1)],
                            start=(hp == 0), stop=(hp == 3)))
                pe7[tb] = s_pe.n
                s_pe.wait(dve, pe7[tb])
                s_dv.inc(dve.always.tensor_copy(a2, pp7[:, :]))
                cta = s_dv.n
                cta7[tb] = cta
                if tb >= 2:
                    s_dv.wait(act, dv7[tb - 2])
                s_dv.wait(act, cta)
                s_ac.inc(act.always.activation(pscr[:, :], a2, AF.Identity,
                                        accum_out=sac))
                s_ac.inc(act.always.activation(pscr[:, :], a2, AF.Square,
                                        accum_out=sbc))
                s_ac.wait(dve)
                dve.tensor_scalar_mul(sac, sac, 1.0 / D)
                dve.tensor_scalar_mul(sbc, sbc, 1.0 / D)
                dve.tensor_tensor(scc2, sac, sac, ALU.mult)
                s_dv.inc(dve.always.tensor_sub(sbc, sbc, scc2))
                s_dv.wait(act)
                s_ac.inc(act.always.activation(scc2, sbc, AF.Sqrt,
                                        bias=EPS_LN))
                s_ac.wait(dve)
                dve.reciprocal(scc2, scc2)
                dve.tensor_scalar(a2, a2, sac, scc2,
                                  ALU.subtract, ALU.mult)
                s_x.wait(dve, xh7[tb])
                if tb >= 2:
                    s_st.wait(dve, st7[tb - 2])
                s_dv.inc(dve.always.tensor_add(
                    x1s, xt2[:, sl:sl + 2048], a2))
                s_dv.wait(sync)
                s_st.wait(sync)
                s_st.dma(sync.dma_start(out=x1d.ap()[128 * tb:128 * (tb + 1), :],
                                        in_=x1s))
                st7[tb] = s_st.n
                ct2 = s_dv.n
                s_dv.wait(act, ct2)
                s_ac.inc(act.always.activation(pscr[:, :], x1s, AF.Identity,
                                        accum_out=sac))
                s_ac.inc(act.always.activation(pscr[:, :], x1s, AF.Square,
                                        accum_out=sbc))
                s_ac.wait(dve)
                dve.tensor_scalar_mul(m1_s[:, tb:tb + 1], sac, 1.0 / D)
                dve.tensor_scalar_mul(sbc, sbc, 1.0 / D)
                dve.tensor_tensor(scc2, m1_s[:, tb:tb + 1],
                                  m1_s[:, tb:tb + 1], ALU.mult)
                s_dv.inc(dve.always.tensor_sub(v1_s[:, tb:tb + 1], sbc, scc2))
                s_dv.wait(act)
                s_ac.inc(act.always.activation(sac, v1_s[:, tb:tb + 1],
                                        AF.Sqrt, bias=EPS_LN))
                s_ac.wait(dve)
                dve.reciprocal(r1_s[:, tb:tb + 1], sac)
                dve.tensor_mul(a2, x1s, wr_b[:, :])
                dve.reduce_sum(sac, a2, AX.X)
                dve.tensor_mul(sbc, m1_s[:, tb:tb + 1], srw_b[:, :])
                dve.tensor_sub(sac, sac, sbc)
                s_dv.inc(dve.always.tensor_mul(sc_s[:, tb:tb + 1], sac,
                                        r1_s[:, tb:tb + 1]))
                dv7[tb] = s_dv.n
                if tb + 2 < NT:
                    x7_fetch(tb + 2)
    if stop_after == "P7":
        return finish()

    syncpoint()
    with contextlib.ExitStack() as p8actx:
        SB8 = lambda name, shape, dt=F32: p8actx.enter_context(
            nc.sbuf_tensor(name, list(shape), dt))
        with nc.psum_tensor("sc_ps", [128, 128], F32) as sc_ps, \
             nc.sbuf_tensor("sc_t", [NT, 128], F32) as sc_t:
            s_dv.wait(pe)
            s_pe.inc(pe.transpose(sc_ps[0:NT, :], sc_s[:, :], ey_f[:, :]))
            s_pe.wait(dve)
            s_dv.inc(dve.always.tensor_copy(sc_t[:, :], sc_ps[0:NT, :]))
            s_dv.wait(sync)
            s_st.dma(sync.dma_start(
                out=scc.ap().rearrange("(tb p) one -> tb (p one)", p=128),
                in_=sc_t[:, :]))
        s_st.wait(gps)
        s_cc.inc(gps.always.collective_compute("AllGather", ALU.bypass, RG,
                                        [scc.ap().opt()], [scg.ap().opt()]))
        s_cc.wait(sync)
        s_st.wait(dve)
        s_st.wait(pe)
        th_lo = SB8("th_lo", [1, 1])
        th_sz = SB8("th_sz", [1, 1])
        thv = SB8("thv", [128, 1])
        cnt_s = SB8("cnt_s", [128, 1])
        nch = cfg.T // 2048
        with nc.sbuf_tensor("bcb", [128, nch * 2048], F32) as bcb, \
             nc.sbuf_tensor("bch", [128, nch * 2048], BF16) as bch, \
             nc.sbuf_tensor("cc1", [128, 1], F32) as cc1, \
             nc.sbuf_tensor("fl", [128, 1], F32) as fl, \
             nc.psum_tensor("tr1_ps", [128, 128], F32) as tr1_ps, \
             nc.sbuf_tensor("flt", [1, 128], F32) as flt, \
             nc.sbuf_tensor("mx", [1, 1], F32) as mx:
            for ch in range(nch):
                s_ld.dma(sync.dma_start(
                    out=bcb[:, 2048 * ch:2048 * (ch + 1)],
                    in_=bass.AP(scg, 2048 * ch, [[0, 128], [1, 2048]])))
            s_ld.wait(dve)
            dve.memset(th_lo[:, :], -16.0)
            s_dv.inc(dve.always.memset(th_sz[:, :], 32.0))
            for rnd in range(5):
                s_dv.wait(pe)
                s_pe.inc(pe.matmul(tr1_ps[:, 0:1], on_f[0:1, :], th_lo[:, :],
                                   start=True, stop=True))
                s_pe.inc(pe.matmul(tr1_ps[:, 1:2], on_f[0:1, :], th_sz[:, :],
                                   start=True, stop=True))
                s_pe.wait(dve)
                dve.tensor_scalar_mul(fl[:, :], tr1_ps[:, 1:2], 1.0 / 130.0)
                dve.tensor_mul(fl[:, :], fl[:, :], io1[:, :])
                dve.tensor_add(thv[:, :], fl[:, :], tr1_ps[:, 0:1])
                dve.tensor_scalar(bch[:, :], bcb[:, :], thv[:, :], None,
                                  ALU.is_gt, op1=ALU.add,
                                  accum_out=cc1[:, :])
                s_dv.inc(dve.always.tensor_copy(cnt_s[:, :], cc1[:, :]))
                dve.tensor_scalar(fl[:, :], cnt_s[:, :],
                                  float(cfg.KSEL) - 0.5, None, ALU.is_gt)
                s_dv.inc(dve.always.tensor_mul(fl[:, :], fl[:, :], io1[:, :]))
                s_dv.wait(pe)
                s_pe.inc(pe.transpose(tr1_ps[0:1, :], fl[:, :], ey_f[:, :]))
                s_pe.wait(dve)
                dve.tensor_copy(flt[:, :], tr1_ps[0:1, :])
                dve.reduce_max(mx[:, :], flt[:, :], AX.X)
                dve.tensor_scalar_mul(mx[:, :], mx[:, :], 1.0 / 130.0)
                dve.tensor_mul(mx[:, :], mx[:, :], th_sz[:, :])
                dve.tensor_add(th_lo[:, :], th_lo[:, :], mx[:, :])
                s_dv.inc(dve.always.tensor_scalar_mul(th_sz[:, :], th_sz[:, :],
                                               1.0 / 130.0))
        with nc.psum_tensor("px_ps", [128, 128], F32) as px_ps, \
             nc.sbuf_tensor("thb", [128, 1], F32) as thb, \
             nc.sbuf_tensor("rs1", [128, 1], F32) as rs1, \
             nc.sbuf_tensor("cs16", [16, 1], F32) as cs16, \
             nc.sbuf_tensor("pre16", [16, 1], F32) as pre16, \
             nc.sbuf_tensor("pre16t", [1, 16], F32) as pre16t, \
             nc.sbuf_tensor("preb", [128, NT], F32) as preb, \
             nc.sbuf_tensor("cnt1", [1, 1], F32) as cnt1, \
             nc.sbuf_tensor("cnt8", [8, 1], F32) as cnt8, \
             nc.sbuf_tensor("base1", [1, 1], F32) as base1, \
             nc.sbuf_tensor("pos_s", [128, NT], F32) as pos_s:
            s_dv.wait(pe)
            s_pe.inc(pe.matmul(px_ps[:, 0:1], on_f[0:1, :], th_lo[:, :],
                               start=True, stop=True))
            s_pe.wait(dve)
            s_dv.inc(dve.always.tensor_copy(thb[:, :], px_ps[:, 0:1]))
            dve.tensor_scalar(msk_s[:, :], sc_s[:, :], thb[:, :], None,
                              ALU.is_gt)
            s_dv.wait(act)
            s_ac.inc(act.always.activation(gate_s[:, :], sc_s[:, :], AF.Sigmoid))
            s_dv.inc(dve.always.reduce_sum(rs1[:, :], msk_s[:, :], AX.X))
            s_dv.wait(pe)
            s_pe.inc(pe.matmul(px_ps[:, 0:NT], u128[:, :], msk_s[:, :],
                               start=True, stop=True))
            s_pe.inc(pe.matmul(px_ps[0:NT, NT:NT + 1], msk_s[:, :], on_f[:, 0:1],
                               start=True, stop=True))
            s_pe.inc(pe.matmul(px_ps[0:1, NT + 1:NT + 2], rs1[:, :], on_f[:, 0:1],
                               start=True, stop=True))
            s_pe.wait(dve)
            dve.tensor_copy(pos_s[:, :], px_ps[:, 0:NT])
            dve.tensor_copy(cs16[0:NT, :], px_ps[0:NT, NT:NT + 1])
            s_dv.inc(dve.always.tensor_copy(cnt1[:, :], px_ps[0:1, NT + 1:NT + 2]))
            s_dv.wait(sync)
            s_st.dma(sync.dma_start(out=cntc.ap(), in_=cnt1[:, :]))
            s_st.wait(gps)
            s_cc.inc(gps.always.collective_compute("AllGather", ALU.bypass, RG,
                                            [cntc.ap().opt()], [cntg.ap().opt()]))
            s_cc.wait(sync)
            s_ld.dma(sync.dma_start(out=cnt8[:, :], in_=cntg.ap()))
            s_ld.wait(dve)
            s_dv.inc(dve.always.tensor_mul(cnt8[:, :], cnt8[:, :], mc_s[:, :]))
            s_dv.wait(pe)
            s_pe.inc(pe.matmul(px_ps[0:1, 0:1], cnt8[:, :], on_f[0:8, 0:1],
                               start=True, stop=True))
            s_pe.inc(pe.matmul(px_ps[0:NT, 1:2], u16[0:NT, 0:NT], cs16[0:NT, :],
                               start=True, stop=True))
            s_pe.wait(dve)
            dve.tensor_copy(base1[:, :], px_ps[0:1, 0:1])
            s_dv.inc(dve.always.tensor_copy(pre16[0:NT, :], px_ps[0:NT, 1:2]))
            s_dv.wait(pe)
            s_pe.inc(pe.transpose(px_ps[0:1, 0:NT], pre16[0:NT, :],
                                  ey_f[0:NT, 0:NT]))
            s_pe.wait(dve)
            s_dv.inc(dve.always.tensor_copy(pre16t[:, 0:NT], px_ps[0:1, 0:NT]))
            s_dv.wait(pe)
            s_pe.inc(pe.matmul(px_ps[:, 0:NT], on_f[0:1, :], pre16t[:, 0:NT],
                               start=True, stop=True))
            s_pe.inc(pe.matmul(px_ps[:, NT:NT + 1], on_f[0:1, :], base1[:, :],
                               start=True, stop=True))
            s_pe.wait(dve)
            dve.tensor_copy(preb[:, :], px_ps[:, 0:NT])
            s_dv.inc(dve.always.tensor_copy(thb[:, :], px_ps[:, NT:NT + 1]))
            dve.tensor_add(pos_s[:, :], pos_s[:, :], preb[:, :])
            dve.tensor_scalar(pos_s[:, :], pos_s[:, :], thb[:, :], None,
                              ALU.add)
            BIGI = 524288.0
            dve.tensor_scalar(pos_s[:, :], pos_s[:, :], -BIGI, None, ALU.add)
            dve.tensor_mul(pos_s[:, :], pos_s[:, :], msk_s[:, :])
            dve.tensor_scalar(pos_s[:, :], pos_s[:, :], BIGI, None, ALU.add)
            s_dv.inc(dve.always.tensor_copy(idx_s[:, :], pos_s[:, :]))
        with nc.sbuf_tensor("x1b", [128, 2 * 2048], F32) as x1b, \
             nc.sbuf_tensor("hrow", [128, 2 * 2048], F32) as hrow, \
             nc.sbuf_tensor("aux2", [128, 4], F32) as aux2:
            s_ac.wait(dve)
            s_pe.wait(sync)
            s_dv.wait(sync)
            xh9 = [0] * NT
            dvh9 = [0] * NT
            sch9 = [0] * NT

            def s9_fetch(tb):
                s_x.wait(sync)
                if tb >= 2:
                    s_dv.wait(sync, dvh9[tb - 2])
                s_x.dma(sync.dma_start(
                    out=x1b[:, (tb % 2) * 2048:(tb % 2 + 1) * 2048],
                    in_=x1d.ap()[128 * tb:128 * (tb + 1), :]))
                xh9[tb] = s_x.n

            s9_fetch(0)
            s9_fetch(1)
            for tb in range(NT):
                sl = (tb % 2) * 2048
                s_x.wait(dve, xh9[tb])
                if tb >= 2:
                    s_sca.wait(dve, sch9[tb - 2])
                dve.tensor_scalar(hrow[:, sl:sl + 2048],
                                  x1b[:, sl:sl + 2048], m1_s[:, tb:tb + 1],
                                  r1_s[:, tb:tb + 1], ALU.subtract, ALU.mult)
                dve.tensor_copy(aux2[:, (tb % 2) * 2:(tb % 2) * 2 + 1],
                                gt_s[:, tb:tb + 1])
                s_dv.inc(dve.always.tensor_copy(
                    aux2[:, (tb % 2) * 2 + 1:(tb % 2) * 2 + 2],
                    gate_s[:, tb:tb + 1]))
                dvh9[tb] = s_dv.n
                s_dv.wait(gps, dvh9[tb])
                if tb == 0:
                    s_zz.wait(gps)
                s_sca.wait(gps)
                s_sca.dma(gps.indirect_dma_start(
                    out=selhc.ap(),
                    out_offset=bass.IndirectOffsetOnAxis(
                        ap=idx_s[:, tb:tb + 1], axis=0),
                    in_=hrow[:, sl:sl + 2048], in_offset=None,
                    bounds_check=bnd_reg, oob_is_err=False))
                s_sca.dma(gps.indirect_dma_start(
                    out=selxc.ap(),
                    out_offset=bass.IndirectOffsetOnAxis(
                        ap=idx_s[:, tb:tb + 1], axis=0),
                    in_=aux2[:, (tb % 2) * 2:(tb % 2) * 2 + 2], in_offset=None,
                    bounds_check=bnd_reg, oob_is_err=False))
                sch9[tb] = s_sca.n
                if tb + 2 < NT:
                    s9_fetch(tb + 2)
        s_sca.wait(gps)
        s_cc.inc(gps.always.collective_compute("ReduceScatter", ALU.add, RG,
                                        [selhc.ap().opt()], [selhr.ap().opt()]))
        s_cc.inc(gps.always.collective_compute("ReduceScatter", ALU.add, RG,
                                        [selxc.ap().opt()], [selxr.ap().opt()]))
        cc_sel = s_cc.n
    if stop_after == "P8a":
        return finish()

    syncpoint()
    with contextlib.ExitStack() as p8ctx:
        SB8b = lambda name, shape, dt=F32: p8ctx.enter_context(
            nc.sbuf_tensor(name, list(shape), dt))
        s_cc.wait(sync, cc_sel)
        s_cc.wait(sync, cc_ffnw[0])
        sel_t = SB8b("sel_t", [128, NKT * 2048])
        posv = SB8b("posv", [1, KC])
        gat_v = SB8b("gat_v", [128, NKT])
        for kt in range(NKT):
            s_ld.dma(sync.dma_start(out=sel_t[:, 2048 * kt:2048 * (kt + 1)],
                                    in_=selhr.ap()[128 * kt:128 * (kt + 1), :]))
            s_ld.dma(sync.dma_start(out=gat_v[:, kt:kt + 1],
                                    in_=selxr.ap()[128 * kt:128 * (kt + 1),
                                                   1:2],
                                    allow_slow_non_contiguous=True))
        s_ld.dma(sync.dma_start(
            out=posv[:, :],
            in_=selxr.ap()[:, 0:1].rearrange("r one -> one r"),
            allow_slow_non_contiguous=True))
        s_ld.wait(sync)
        ln1_b = SB8b("ln1_b", [128, 2048])
        ln2_b = SB8b("ln2_b", [128, 2048])
        with nc.sbuf_tensor("lnra", [1, 2048], F32) as lnra, \
             nc.sbuf_tensor("lnrb", [1, 2048], F32) as lnrb, \
             nc.psum_tensor("lb2_ps", [128, 512], F32) as lb2_ps:
            s_in.dma(sync.dma_start(out=lnra[:, :], in_=ln1_i.ap()[None, :]))
            s_in.dma(sync.dma_start(out=lnrb[:, :], in_=ln2_i.ap()[None, :]))
            s_in.wait(pe)
            for (lr, dst) in [(lnra, ln1_b), (lnrb, ln2_b)]:
                for ns in range(4):
                    s_pe.inc(pe.matmul(lb2_ps[:, :], on_f[0:1, :],
                                       lr[:, 512 * ns:512 * (ns + 1)],
                                       start=True, stop=True))
                    s_pe.wait(dve)
                    s_dv.inc(dve.always.tensor_copy(dst[:, 512 * ns:512 * (ns + 1)],
                                             lb2_ps[:, :]))
                    s_dv.wait(pe)
        s_in.wait(sync)
        for eng in (dve, act, gps):
            s_in.wait(eng)
            s_pe.wait(eng)
        pscr = SB8b("pscr8b", [128, 2048])
        mid = contextlib.ExitStack()
        SBm = lambda name, shape, dt=F32: mid.enter_context(
            nc.sbuf_tensor(name, list(shape), dt))
        h1f = SBm("h1f", [128, DC * KC], BF16)
        with nc.sbuf_tensor("h1t", [128, 2048], BF16) as h1t, \
             nc.sbuf_tensor("ra", [128, 1], F32) as ra, \
             nc.psum_tensor("ht_ps", [128, 128], BF16) as ht_ps:
            for kt in range(NKT):
                s_ld.wait(act)
                s_ld.wait(dve)
                s_ac.inc(act.always.activation(pscr[:, :],
                                        sel_t[:, 2048 * kt:2048 * (kt + 1)],
                                        AF.Square, accum_out=ra[:, :]))
                s_ac.wait(dve)
                s_dv.inc(dve.always.tensor_scalar_mul(ra[:, :], ra[:, :], 1.0 / D))
                s_dv.wait(act)
                s_ac.inc(act.always.activation(ra[:, :], ra[:, :], AF.Sqrt,
                                        bias=EPS_RMS))
                s_ac.wait(dve)
                dve.reciprocal(ra[:, :], ra[:, :])
                s_pe.wait(dve)
                s_dv.inc(dve.always.scalar_tensor_tensor(
                    h1t[:, :], sel_t[:, 2048 * kt:2048 * (kt + 1)], ra[:, :],
                    ln1_b[:, :], ALU.mult, ALU.mult))
                s_dv.wait(pe)
                for dc in range(DC):
                    s_pe.inc(pe.transpose(ht_ps[:, :],
                                          h1t[:, 128 * dc:128 * (dc + 1)],
                                          ey_bf[:, :]))
                    s_pe.wait(dve)
                    s_dv.inc(dve.always.tensor_copy(
                        h1f[:, KC * dc + 128 * kt:KC * dc + 128 * (kt + 1)],
                        ht_ps[:, :]))
                    s_dv.wait(pe)
        cosf = SBm("cosf", [128, KC])
        sinf = SBm("sinf", [128, KC])
        sneg = SBm("sneg", [128, KC])
        with nc.psum_tensor("an_ps", [128, KC], F32) as an_ps, \
             nc.sbuf_tensor("ang", [128, KC], F32) as ang, \
             nc.sbuf_tensor("ang2", [128, KC], F32) as ang2, \
             nc.sbuf_tensor("nn", [128, KC], F32) as nn, \
             nc.sbuf_tensor("nni", [128, KC], I32) as nni:
            s_ld.wait(pe)
            for ns in range(max(1, KC // 512)):
                w = min(512, KC)
                s_pe.inc(pe.matmul(an_ps[:, w * ns:w * (ns + 1)], ivf[:, :],
                                   posv[:, w * ns:w * (ns + 1)],
                                   start=True, stop=True))
            s_pe.wait(dve)
            s_dv.inc(dve.always.tensor_copy(ang[:, :], an_ps[:, :]))

            def range_reduce_sin(src_ap, out_ap):
                dve.tensor_scalar_mul(nn[:, :], src_ap, 1.0 / TWO_PI)
                dve.tensor_scalar(nn[:, :], nn[:, :], 0.5, None, ALU.add)
                dve.tensor_copy(nni[:, :], nn[:, :])
                dve.tensor_copy(nn[:, :], nni[:, :])
                dve.scalar_tensor_tensor(ang2[:, :], nn[:, :], -CW_C1, src_ap,
                                         ALU.mult, ALU.add)
                s_dv.inc(dve.always.scalar_tensor_tensor(ang2[:, :], nn[:, :], -CW_C2,
                                                  ang2[:, :], ALU.mult, ALU.add))
                s_dv.wait(act)
                s_ac.inc(act.always.activation(out_ap, ang2[:, :], AF.Sin))
                s_ac.wait(dve)

            range_reduce_sin(ang[:, :], sinf[:, :])
            dve.tensor_scalar(ang[:, :], ang[:, :], math.pi / 2.0, None,
                              ALU.add)
            range_reduce_sin(ang[:, :], cosf[:, :])
            dve.tensor_scalar_mul(sneg[0:64, :], sinf[0:64, :], -1.0)
            s_dv.inc(dve.always.tensor_copy(sneg[64:128, :], sinf[64:128, :]))
        fqt = SBm("fqt", [128, FH * KC], BF16)
        fkt = SBm("fkt", [128, HE * KC], BF16)
        fvt = SBm("fvt", [128, NKT * 1024], BF16)
        bq_s = SBm("bq_s", [128, DC])
        bkv_s = SBm("bkv_s", [128, 16])
        with nc.sbuf_tensor("btmp", [16, 128], F32) as btmp, \
             nc.psum_tensor("bt_ps", [128, 16], F32) as bt_ps:
            s_dv.wait(pe)
            s_ac.wait(pe)
            s_in.dma(sync.dma_start(
                out=btmp[:, :], in_=bq_i.ap().rearrange("(k p) -> k p", p=128)))
            s_in.wait(pe)
            s_pe.inc(pe.transpose(bt_ps[:, :], btmp[:, :], ey_f[0:16, 0:16]))
            s_pe.wait(dve)
            s_dv.inc(dve.always.tensor_copy(bq_s[:, :], bt_ps[:, :]))
            s_dv.wait(pe)
            s_pe.wait(sync)
            s_in.dma(sync.dma_start(
                out=btmp[0:8, :], in_=bk_i.ap().rearrange("(k p) -> k p", p=128)))
            s_in.wait(pe)
            s_pe.inc(pe.transpose(bt_ps[:, 0:8], btmp[0:8, :], ey_f[0:8, 0:8]))
            s_pe.wait(dve)
            s_dv.inc(dve.always.tensor_copy(bkv_s[:, 0:8], bt_ps[:, 0:8]))
            s_dv.wait(pe)
            s_pe.wait(sync)
            s_in.dma(sync.dma_start(
                out=btmp[0:8, :], in_=bv_i.ap().rearrange("(k p) -> k p", p=128)))
            s_in.wait(pe)
            s_pe.inc(pe.transpose(bt_ps[:, 0:8], btmp[0:8, :], ey_f[0:8, 0:8]))
            s_pe.wait(dve)
            s_dv.inc(dve.always.tensor_copy(bkv_s[:, 8:16], bt_ps[:, 0:8]))
            s_dv.wait(pe)
        with nc.sbuf_tensor("fw_t", [128, 2 * 2048], BF16) as fw_t, \
             nc.psum_tensor("fp_ps", [128, 1024], F32) as fp_ps, \
             nc.psum_tensor("fr_ps", [128, 128], BF16) as fr_ps, \
             nc.sbuf_tensor("prj", [128, KC], F32) as prj, \
             nc.sbuf_tensor("prh", [128, KC], F32) as prh, \
             nc.sbuf_tensor("sq_t", [128, KC], F32) as sq_t, \
             nc.sbuf_tensor("qsw", [128, KC], F32) as qsw, \
             nc.sbuf_tensor("vtmp", [128, KC], BF16) as vtmp:
            s_in.wait(dve)
            calls = ([(FW_WQ, h, bq_s, h, "q") for h in range(FH)]
                     + [(FW_WK, h, bkv_s, h, "k") for h in range(HE)]
                     + [(FW_WV, h, bkv_s, 8 + h, "v") for h in range(HE)])
            NCALL = len(calls)
            ldc = [0] * NCALL
            pe_after = [0] * NCALL
            dv_after = [0] * NCALL

            def load_call(n):
                col0, h = calls[n][0], calls[n][1]
                sl = (n % 2) * 2048
                s_ld.wait(sync)
                if n >= 2:
                    s_pe.wait(sync, pe_after[n - 2])
                gcol = col0 + 128 * h
                s_ld.dma(sync.dma_start(
                    out=fw_t[:, sl:sl + 2048].rearrange("p (dc k) -> p dc k",
                                                        dc=DC),
                    in_=fw1gAs[gcol // 2048]
                    .ap()[:, gcol % 2048:gcol % 2048 + 128]
                    .rearrange("(dc p) k -> p dc k", p=128)))
                ldc[n] = s_ld.n

            def rope_to(dst_ap):
                dve.tensor_copy(qsw[0:64, :], prj[64:128, :])
                dve.tensor_copy(qsw[64:128, :], prj[0:64, :])
                dve.tensor_mul(prh[:, :], prj[:, :], cosf[:, :])
                dve.tensor_mul(sq_t[:, :], qsw[:, :], sneg[:, :])
                s_dv.inc(dve.always.tensor_add(dst_ap, prh[:, :], sq_t[:, :]))

            s_in.wait(sync)
            s_pe.wait(sync)
            s_dv.wait(sync)
            load_call(0)
            for n in range(NCALL):
                col0, h, bias_col, bias_idx, kind = calls[n]
                if n + 1 < NCALL:
                    load_call(n + 1)
                sl = (n % 2) * 2048
                ps = fp_ps[:, (n % 2) * 512:(n % 2) * 512 + KC]
                s_ld.wait(pe, ldc[n])
                if n >= 2:
                    s_dv.wait(pe, dv_after[n - 2])
                for dc in range(DC):
                    s_pe.inc(pe.matmul(ps,
                                       fw_t[:, sl + 128 * dc:sl + 128 * (dc + 1)],
                                       h1f[:, KC * dc:KC * (dc + 1)],
                                       start=(dc == 0), stop=(dc == DC - 1)))
                pe_after[n] = s_pe.n
                s_pe.wait(dve, pe_after[n])
                s_dv.inc(dve.always.tensor_scalar(prj[:, :], ps,
                                           bias_col[:, bias_idx:bias_idx + 1],
                                           None, ALU.add))
                dv_after[n] = s_dv.n
                if kind == "q":
                    rope_to(fqt[:, KC * h:KC * (h + 1)])
                    if h == FH - 1:
                        s_dv.wait(sync)
                        s_st.wait(sync)
                        for h2 in range(FH):
                            s_st.dma(sync.dma_start(
                                out=fqc.ap()[128 * h2:128 * (h2 + 1), :],
                                in_=fqt[:, KC * h2:KC * (h2 + 1)]))
                        s_st.wait(gps)
                        s_cc.inc(gps.always.collective_compute(
                            "AllGather", ALU.bypass, RG,
                            [fqc.ap().opt()], [fqg.ap().opt()]))
                        cc_fq = s_cc.n
                elif kind == "k":
                    rope_to(fkt[:, KC * h:KC * (h + 1)])
                else:
                    s_pe.wait(dve)
                    s_dv.inc(dve.always.tensor_copy(vtmp[:, :], prj[:, :]))
                    s_dv.wait(pe)
                    for kt in range(NKT):
                        s_pe.inc(pe.transpose(fr_ps[:, :],
                                              vtmp[:, 128 * kt:128 * (kt + 1)],
                                              ey_bf[:, :]))
                        s_pe.wait(dve)
                        s_dv.inc(dve.always.tensor_copy(
                            fvt[:, 1024 * kt + 128 * h:
                                1024 * kt + 128 * (h + 1)],
                            fr_ps[:, :]))
                        s_dv.wait(pe)

        syncpoint()
        s_cc.wait(sync, cc_fq)
        fqh = SBm("fqh", [128, 2 * cfg.KSEL], BF16)
        amk = SBm("amk", [128, NKT * QH * QHW], BF16)
        for sidx in range(NKT * QH):
            s_ld.dma(sync.dma_start(out=amk[:, QHW * sidx:QHW * (sidx + 1)],
                                    in_=am_i.ap()[sidx, :, :]))
        s_ld.wait(sync)
        with nc.psum_tensor("fsc_ps", [128, 2 * QHW], F32) as fsc_ps, \
             nc.psum_tensor("fz_ps", [128, QHW], F32) as fz_ps, \
             nc.psum_tensor("fl_ps", [1, QHW], F32) as fl_ps, \
             nc.sbuf_tensor("fpT", [128, 2 * QHW], BF16) as fpT, \
             nc.sbuf_tensor("fzs", [128, QHW], F32) as fzs, \
             nc.sbuf_tensor("fls", [1, QHW], F32) as fls:

            def fq_load(h):
                s_fqp[h % 2].dma(sync.dma_start(
                    out=fqh[:, (h % 2) * cfg.KSEL:(h % 2 + 1) * cfg.KSEL]
                    .rearrange("p (o k) -> p o k", o=NCORES),
                    in_=fqg.ap().rearrange("(o hh p) k -> hh p o k",
                                           o=NCORES, p=128)[h]))

            fvisits = [(h, qh, jj) for h in range(FH) for qh in range(QH)
                       for jj in range(NKT)]
            NFV = len(fvisits)
            ex_base = s_ex.n
            fq_ld = [0] * FH
            fq_load(0)
            fq_ld[0] = s_fqp[0].n
            fmk_hist = [0] * NFV
            fav_hist = [0] * NFV
            fsc_hist = [0] * NFV
            fsc_end = []

            def femit_scores(k):
                h_, qh_, jj_ = fvisits[k]
                kvh_ = h_ // 2
                off = (k % 2) * QHW
                s_fqp[h_ % 2].wait(pe, fq_ld[h_])
                if k >= 2:
                    s_ex.wait(pe, ex_base + k - 1)
                for ns in range(QHW // 512):
                    s_sc.inc(pe.matmul(
                        fsc_ps[:, off + 512 * ns:off + 512 * (ns + 1)],
                        fkt[:, KC * kvh_ + 128 * jj_:KC * kvh_ + 128 * (jj_ + 1)],
                        fqh[:, (h_ % 2) * cfg.KSEL + QHW * qh_ + 512 * ns:
                            (h_ % 2) * cfg.KSEL + QHW * qh_ + 512 * (ns + 1)],
                        start=True, stop=True))
                fsc_hist[k] = s_sc.n

            def femit_evac(av_t):
                s_av.wait(act, av_t)
                s_st.wait(act)
                s_ev2.inc(act.copy(fzs[:, :], fz_ps[:, :]))
                s_ev2.inc(act.copy(fls[:, :], fl_ps[0:1, :]))

            def femit_store(h_p, qh_p):
                s_ev2.wait(sync)
                ft_ = fzlcA if h_p < FH // 2 else fzlcB
                hl_ = h_p % (FH // 2)
                for cons in range(cfg.NCONS):
                    r_ = (QHW * qh_p) // KC + cons
                    s_st.dma(sync.dma_start(
                        out=ft_.ap()[FZR * r_ + 128 * hl_:
                                     FZR * r_ + 128 * (hl_ + 1), :],
                        in_=fzs[:, KC * cons:KC * (cons + 1)]))
                    s_st.dma(sync.dma_start(
                        out=ft_.ap()[FZR * r_ + 1024 + hl_:
                                     FZR * r_ + 1025 + hl_, :],
                        in_=fls[:, KC * cons:KC * (cons + 1)]))
                if h_p == FH // 2 - 1 and qh_p == QH - 1:
                    s_st.wait(gps)
                    s_cc.inc(gps.always.collective_compute(
                        "ReduceScatter", ALU.add, RG,
                        [fzlcA.ap().opt()], [fzlrA.ap().opt()]))
                    cc_fzlh[0] = s_cc.n

            fpend = None
            cc_fzlh = [0, 0]
            femit_scores(0)
            for k in range(NFV):
                h, qh, jj = fvisits[k]
                kvh = h // 2
                off = (k % 2) * QHW
                if qh == 0 and jj == 0 and h + 1 < FH:
                    if h >= 1:
                        s_sc.wait(sync, fsc_end[h - 1])
                    fq_load(h + 1)
                    fq_ld[h + 1] = s_fqp[(h + 1) % 2].n
                if k + 1 < NFV:
                    femit_scores(k + 1)
                s_sc.wait(act, fsc_hist[k])
                if k >= 2:
                    s_av.wait(act, fav_hist[k - 2])
                s_ex.inc(act.activation(fpT[:, off:off + QHW],
                                        fsc_ps[:, off:off + QHW], AF.Exp,
                                        scale=1.0 / math.sqrt(FDH),
                                        bias=-C_FFN))
                if fpend is not None:
                    femit_evac(fpend[2])
                    femit_store(fpend[0], fpend[1])
                    fpend = None
                s_ex.wait(dve, s_ex.n)
                s_ld.wait(dve)
                s_mk.inc(dve.tensor_mul(
                    fpT[:, off:off + QHW], fpT[:, off:off + QHW],
                    amk[:, QHW * (jj * QH + qh):QHW * (jj * QH + qh + 1)]))
                fmk_hist[k] = s_mk.n
                s_mk.wait(pe, fmk_hist[k])
                fev0 = s_ev2.n if (jj == 0 and k > 0) else None
                st_ = (jj == 0)
                sp_ = (jj == NKT - 1)
                for ns in range(QHW // 512):
                    if fev0 is not None:
                        s_ev2.wait(pe, fev0 - 1)
                    s_av.inc(pe.matmul(
                        fz_ps[:, 512 * ns:512 * (ns + 1)],
                        fvt[:, 1024 * jj + 128 * kvh:1024 * jj + 128 * (kvh + 1)],
                        fpT[:, off + 512 * ns:off + 512 * (ns + 1)],
                        start=st_, stop=sp_))
                    if fev0 is not None:
                        s_ev2.wait(pe, fev0)
                        fev0 = None
                    s_av.inc(pe.matmul(
                        fl_ps[0:1, 512 * ns:512 * (ns + 1)], on_bf[:, 0:1],
                        fpT[:, off + 512 * ns:off + 512 * (ns + 1)],
                        start=st_, stop=sp_, tile_position=(0, 0)))
                fav_hist[k] = s_av.n
                if jj == NKT - 1:
                    fpend = (h, qh, fav_hist[k])
                    if qh == QH - 1:
                        fsc_end.append(s_sc.n)
            femit_evac(fpend[2])
            femit_store(fpend[0], fpend[1])
        s_st.wait(gps)
        s_cc.inc(gps.always.collective_compute("ReduceScatter", ALU.add, RG,
                                        [fzlcB.ap().opt()], [fzlrB.ap().opt()]))
        cc_fzlh[1] = s_cc.n
        mid.close()
        fzn = SB8b("fzn", [128, FH * KC], BF16)
        x1f = SB8b("x1f", [128, NKT * 2048])
        woctx = contextlib.ExitStack()
        fw_o = woctx.enter_context(nc.sbuf_tensor("fw_o", [128, 2 * 2048],
                                                  BF16))
        ldc4 = [0] * DC
        pe_af4 = [0] * DC

        def load_o(n):
            sl = (n % 2) * 2048
            s_ld.wait(sync)
            if n >= 2:
                s_pe.wait(sync, pe_af4[n - 2])
            s_ld.dma(sync.dma_start(
                out=fw_o[:, sl:sl + 2048].rearrange("p (zc k) -> p zc k",
                                                    zc=FH),
                in_=fw1gAs[2].ap()[0:2048, 128 * n:128 * (n + 1)]
                .rearrange("(zc p) k -> p zc k", p=128)))
            ldc4[n] = s_ld.n

        load_o(0)
        load_o(1)
        with nc.sbuf_tensor("fzl_t", [128, KC], F32) as fzl_t, \
             nc.sbuf_tensor("fli", [1, KC], F32) as fli, \
             nc.psum_tensor("fb_ps", [128, KC], F32) as fb_ps:
            for h in range(FH):
                fr_ = fzlrA if h < FH // 2 else fzlrB
                hl2 = h % (FH // 2)
                if h == 0:
                    s_cc.wait(sync, cc_fzlh[0])
                if h == FH // 2:
                    s_cc.wait(sync, cc_fzlh[1])
                s_dv.wait(sync)
                s_ld.wait(sync)
                s_ld.dma(sync.dma_start(
                    out=fzl_t[:, :],
                    in_=fr_.ap()[128 * hl2:128 * (hl2 + 1), :]))
                s_ld.dma(sync.dma_start(
                    out=fli[:, :],
                    in_=fr_.ap()[1024 + hl2:1025 + hl2, :]))
                s_ld.wait(sync)
                s_ld.wait(dve)
                s_dv.inc(dve.always.reciprocal(fli[:, :], fli[:, :]))
                s_dv.wait(pe)
                for ns in range(max(1, KC // 512)):
                    w = min(512, KC)
                    s_pe.inc(pe.matmul(fb_ps[:, w * ns:w * (ns + 1)],
                                       on_f[0:1, :], fli[:, w * ns:w * (ns + 1)],
                                       start=True, stop=True))
                s_pe.wait(dve)
                s_dv.inc(dve.always.tensor_mul(fzn[:, KC * h:KC * (h + 1)],
                                        fzl_t[:, :], fb_ps[:, :]))
                s_dv.wait(pe)
        with nc.psum_tensor("fo_ps", [128, KC], F32) as fo_ps, \
             nc.psum_tensor("fo_tr", [128, 128], F32) as fo_tr, \
             nc.sbuf_tensor("ao2", [128, KC], F32) as ao2:
            for dc in range(DC):
                sl = (dc % 2) * 2048
                s_ld.wait(pe, ldc4[dc])
                s_dv.wait(pe)
                for zc in range(FH):
                    s_pe.inc(pe.matmul(fo_ps[:, :],
                                       fw_o[:, sl + 128 * zc:sl + 128 * (zc + 1)],
                                       fzn[:, KC * zc:KC * (zc + 1)],
                                       start=(zc == 0), stop=(zc == FH - 1)))
                pe_af4[dc] = s_pe.n
                if dc + 2 < DC:
                    load_o(dc + 2)
                s_pe.wait(dve)
                s_dv.inc(dve.always.tensor_copy(ao2[:, :], fo_ps[:, :]))
                s_dv.wait(pe)
                for kt in range(NKT):
                    s_pe.inc(pe.transpose(fo_tr[:, :],
                                          ao2[:, 128 * kt:128 * (kt + 1)],
                                          ey_f[:, :]))
                    s_pe.wait(dve)
                    s_dv.inc(dve.always.tensor_copy(
                        x1f[:, 2048 * kt + 128 * dc:2048 * kt + 128 * (dc + 1)],
                        fo_tr[:, :]))
                    s_dv.wait(pe)
        s_ld.wait(sync)
        s_pe.wait(sync)
        woctx.close()
        for kt in range(NKT):
            s_dv.inc(dve.always.tensor_add(x1f[:, 2048 * kt:2048 * (kt + 1)],
                                    x1f[:, 2048 * kt:2048 * (kt + 1)],
                                    sel_t[:, 2048 * kt:2048 * (kt + 1)]))
        h2f = SB8b("h2f", [128, DC * KC], BF16)
        with nc.sbuf_tensor("h2t", [128, 2048], BF16) as h2t, \
             nc.sbuf_tensor("rb", [128, 1], F32) as rb, \
             nc.psum_tensor("h2_ps", [128, 128], BF16) as h2_ps:
            for kt in range(NKT):
                s_dv.wait(act)
                s_ac.inc(act.always.activation(pscr[:, :],
                                        x1f[:, 2048 * kt:2048 * (kt + 1)],
                                        AF.Square, accum_out=rb[:, :]))
                s_ac.wait(dve)
                s_dv.inc(dve.always.tensor_scalar_mul(rb[:, :], rb[:, :], 1.0 / D))
                s_dv.wait(act)
                s_ac.inc(act.always.activation(rb[:, :], rb[:, :], AF.Sqrt,
                                        bias=EPS_RMS))
                s_ac.wait(dve)
                dve.reciprocal(rb[:, :], rb[:, :])
                s_pe.wait(dve)
                s_dv.inc(dve.always.scalar_tensor_tensor(
                    h2t[:, :], x1f[:, 2048 * kt:2048 * (kt + 1)], rb[:, :],
                    ln2_b[:, :], ALU.mult, ALU.mult))
                s_dv.wait(pe)
                for dc in range(DC):
                    s_pe.inc(pe.transpose(h2_ps[:, :],
                                          h2t[:, 128 * dc:128 * (dc + 1)],
                                          ey_bf[:, :]))
                    s_pe.wait(dve)
                    s_dv.inc(dve.always.tensor_copy(
                        h2f[:, KC * dc + 128 * kt:KC * dc + 128 * (kt + 1)],
                        h2_ps[:, :]))
                    s_dv.wait(pe)
        s_dv.wait(sync)
        s_st.dma(sync.dma_start(
            out=h2c.ap().rearrange("(dc p) t -> p dc t", p=128),
            in_=h2f[:, :].rearrange("p (dc t) -> p dc t", t=KC)))
        s_st.wait(gps)
        s_cc.inc(gps.always.collective_compute("AllGather", ALU.bypass, RG,
                                        [h2c.ap().opt()], [h2g.ap().opt()]))
        cc_h2 = s_cc.n
        ITN = FI // NCORES // 128
        KS = cfg.KSEL
        TB2 = KS // 512
        tsb = SB8b("tsb", [128, ITN * KS], BF16)
        with nc.sbuf_tensor("h2a", [128, DC * KS], BF16) as h2a, \
             nc.sbuf_tensor("wgu", [128, 2 * 2048], BF16) as wgu, \
             nc.psum_tensor("g_ps", [128, KS], F32) as g_ps, \
             nc.psum_tensor("u_ps", [128, KS], F32) as u_ps, \
             nc.sbuf_tensor("gsb", [128, KS], BF16) as gsb, \
             nc.sbuf_tensor("usb", [128, KS], BF16) as usb:
            s_dv.wait(sync)
            s_ac.wait(sync)
            bursts = [(kind, it) for it in range(ITN) for kind in ("g", "u")]
            NB = len(bursts)
            ldc2 = [0] * NB
            pe_af2 = [0] * NB

            def load_b(n):
                kind, it = bursts[n]
                base = 0 if kind == "g" else FI // NCORES
                sl = (n % 2) * 2048
                s_ld.wait(sync)
                if n >= 2:
                    s_pe.wait(sync, pe_af2[n - 2])
                s_ld.dma(sync.dma_start(
                    out=wgu[:, sl:sl + 2048].rearrange("p (dc k) -> p dc k",
                                                       dc=DC),
                    in_=fguc.ap()[:, base + 128 * it:base + 128 * (it + 1)]
                    .rearrange("(dc p) k -> p dc k", p=128)))
                ldc2[n] = s_ld.n

            load_b(0)
            load_b(1)
            s_cc.wait(sync, cc_h2)
            s_ld.wait(sync)
            for dc in range(DC):
                s_ld.dma(sync.dma_start(
                    out=h2a[:, KS * dc:KS * (dc + 1)]
                    .rearrange("p (o t) -> p o t", o=NCORES),
                    in_=h2g.ap().rearrange("(o dc2 p) t -> dc2 p o t",
                                           dc2=DC, p=128)[dc]))
            ldh2 = s_ld.n
            s_ld.wait(sync)
            s_ld.wait(pe, ldh2)
            pe_g = 0
            for n in range(NB):
                kind, it = bursts[n]
                sl = (n % 2) * 2048
                s_ld.wait(pe, ldc2[n])
                if kind == "g":
                    s_ac.wait(pe)
                    for dc in range(DC):
                        for tb in range(TB2):
                            s_pe.inc(pe.matmul(
                                g_ps[:, 512 * tb:512 * (tb + 1)],
                                wgu[:, sl + 128 * dc:sl + 128 * (dc + 1)],
                                h2a[:, KS * dc + 512 * tb:
                                    KS * dc + 512 * (tb + 1)],
                                start=(dc == 0), stop=(dc == DC - 1)))
                    pe_g = s_pe.n
                else:
                    s_dv.wait(pe)
                    for dc in range(DC):
                        for tb in range(TB2):
                            s_pe.inc(pe.matmul(
                                u_ps[:, 512 * tb:512 * (tb + 1)],
                                wgu[:, sl + 128 * dc:sl + 128 * (dc + 1)],
                                h2a[:, KS * dc + 512 * tb:
                                    KS * dc + 512 * (tb + 1)],
                                start=(dc == 0), stop=(dc == DC - 1)))
                    pe_u = s_pe.n
                    s_pe.wait(act, pe_g)
                    s_dv.wait(act)
                    s_ac.inc(act.always.activation(gsb[:, :], g_ps[:, :], AF.Silu))
                    s_pe.wait(dve, pe_u)
                    s_ac.wait(dve)
                    dve.tensor_copy(usb[:, :], u_ps[:, :])
                    s_dv.inc(dve.always.tensor_mul(tsb[:, KS * it:KS * (it + 1)],
                                            gsb[:, :], usb[:, :]))
                pe_af2[n] = s_pe.n
                if n + 2 < NB:
                    load_b(n + 2)
        gtd = SB8b("gtd", [128, NKT * 2048])
        with nc.sbuf_tensor("fdn_s", [128, ITN * 2048], BF16) as fdn_s, \
             nc.psum_tensor("dn_psA", [128, 2048], F32) as dn_psA, \
             nc.psum_tensor("dn_psB", [128, 2048], F32) as dn_psB, \
             nc.sbuf_tensor("dnb", [128, 2048], F32) as dnb, \
             nc.sbuf_tensor("dnb2", [128, 2048], F32) as dnb2:
            s_dv.wait(sync)
            s_ac.wait(sync)
            s_pe.wait(sync)
            s_ld.wait(sync)
            for it in range(ITN):
                s_ld.dma(sync.dma_start(
                    out=fdn_s[:, 2048 * it:2048 * (it + 1)],
                    in_=fdnc.ap()[128 * it:128 * (it + 1), :]))
            ldn = s_ld.n
            s_ld.wait(pe, ldn)
            s_dv.wait(pe)
            TT = KS // 128
            SPL = NKT >= 2
            if SPL:
                tts = ([t2 for t2 in range(TT) if t2 % NKT < NKT // 2]
                       + [t2 for t2 in range(TT) if t2 % NKT >= NKT // 2])
            else:
                tts = list(range(TT))
            dn_done = [0] * TT
            st_dn = [0] * TT
            cc_dnh = [0, 0]
            for nn_, tt in enumerate(tts):
                ps = dn_psA if nn_ % 2 == 0 else dn_psB
                bufd = dnb if nn_ % 2 == 0 else dnb2
                if nn_ >= 2:
                    s_ac.wait(pe, dn_done[nn_ - 2])
                for it in range(ITN):
                    for db in range(4):
                        s_pe.inc(pe.matmul(
                            ps[:, 512 * db:512 * (db + 1)],
                            tsb[:, KS * it + 128 * tt:KS * it + 128 * (tt + 1)],
                            fdn_s[:, 2048 * it + 512 * db:
                                  2048 * it + 512 * (db + 1)],
                            start=(it == 0), stop=(it == ITN - 1)))
                s_pe.wait(act)
                if nn_ >= 2:
                    s_st.wait(act, st_dn[nn_ - 2])
                s_ac.inc(act.always.copy(bufd[:, :], ps[:, :]))
                dn_done[nn_] = s_ac.n
                s_ac.wait(sync, dn_done[nn_])
                s_st.wait(sync)
                o_, j_ = tt // NKT, tt % NKT
                if SPL and j_ >= NKT // 2:
                    dt_, ro_ = dncB, (KC // 2) * o_ + 128 * (j_ - NKT // 2)
                else:
                    dt_, ro_ = dncA, (KC // 2 if SPL else KC) * o_ + 128 * j_
                s_st.dma(sync.dma_start(
                    out=dt_.ap()[ro_:ro_ + 128, :], in_=bufd[:, :]))
                st_dn[nn_] = s_st.n
                if SPL and nn_ == TT // 2 - 1:
                    s_st.wait(gps)
                    s_cc.inc(gps.always.collective_compute(
                        "ReduceScatter", ALU.add, RG,
                        [dncA.ap().opt()], [dnrA.ap().opt()]))
                    cc_dnh[0] = s_cc.n
        s_st.wait(gps)
        if SPL:
            s_cc.inc(gps.always.collective_compute(
                "ReduceScatter", ALU.add, RG,
                [dncB.ap().opt()], [dnrB.ap().opt()]))
            cc_dnh[1] = s_cc.n
        else:
            s_cc.inc(gps.always.collective_compute(
                "ReduceScatter", ALU.add, RG,
                [dncA.ap().opt()], [dnrA.ap().opt()]))
            cc_dnh[0] = cc_dnh[1] = s_cc.n
        for kt in range(NKT):
            s_cc.wait(sync, cc_dnh[1 if SPL and kt >= NKT // 2 else 0])
            if SPL and kt >= NKT // 2:
                src_, ro_ = dnrB, 128 * (kt - NKT // 2)
            else:
                src_, ro_ = dnrA, 128 * kt
            s_ld.dma(sync.dma_start(
                out=gtd[:, 2048 * kt:2048 * (kt + 1)],
                in_=src_.ap()[ro_:ro_ + 128, :]))
        s_ld.wait(sync)
        s_ld.wait(dve)
        for kt in range(NKT):
            dve.tensor_add(gtd[:, 2048 * kt:2048 * (kt + 1)],
                           gtd[:, 2048 * kt:2048 * (kt + 1)],
                           x1f[:, 2048 * kt:2048 * (kt + 1)])
            dve.tensor_sub(gtd[:, 2048 * kt:2048 * (kt + 1)],
                           gtd[:, 2048 * kt:2048 * (kt + 1)],
                           sel_t[:, 2048 * kt:2048 * (kt + 1)])
            dve.tensor_scalar(gtd[:, 2048 * kt:2048 * (kt + 1)],
                              gtd[:, 2048 * kt:2048 * (kt + 1)],
                              gat_v[:, kt:kt + 1], None, ALU.mult)
            s_dv.inc(dve.always.tensor_add(gtd[:, 2048 * kt:2048 * (kt + 1)],
                                    gtd[:, 2048 * kt:2048 * (kt + 1)],
                                    sel_t[:, 2048 * kt:2048 * (kt + 1)]))
            s_dv.wait(sync)
            s_st.dma(sync.dma_start(out=gatc.ap()[128 * kt:128 * (kt + 1), :],
                                    in_=gtd[:, 2048 * kt:2048 * (kt + 1)]))
        s_st.wait(gps)
        s_cc.inc(gps.always.collective_compute("AllGather", ALU.bypass, RG,
                                        [gatc.ap().opt()], [gatg.ap().opt()]))
        cc_gat = s_cc.n
    if stop_after == "P8b":
        return finish()

    syncpoint()
    with contextlib.ExitStack() as p9ctx:
        SB9 = lambda name, shape, dt=F32: p9ctx.enter_context(
            nc.sbuf_tensor(name, list(shape), dt))
        pscr = SB9("pscr9", [128, 2048])
        with nc.sbuf_tensor("k1", [128, 1], F32) as k1, \
             nc.sbuf_tensor("k2", [128, 1], F32) as k2:
            for tb in range(NT):
                dve.tensor_mul(k1[:, :], r1_s[:, tb:tb + 1], r1_s[:, tb:tb + 1])
                s_dv.inc(dve.always.tensor_mul(k1[:, :], k1[:, :], v1_s[:, tb:tb + 1]))
                s_dv.wait(act)
                s_ac.inc(act.always.activation(k2[:, :], k1[:, :], AF.Sqrt,
                                        bias=EPS_LN))
                s_ac.wait(dve)
                dve.reciprocal(k2[:, :], k2[:, :])
                dve.tensor_mul(k2[:, :], k2[:, :], r1_s[:, tb:tb + 1])
                dve.tensor_scalar(k1[:, :], msk_s[:, tb:tb + 1], -1.0, None,
                                  ALU.mult)
                dve.tensor_scalar(k1[:, :], k1[:, :], 1.0, None, ALU.add)
                dve.tensor_mul(k1[:, :], k1[:, :], k2[:, :])
                dve.tensor_scalar(ka_s[:, tb:tb + 1], k1[:, :], 1.0, None,
                                  ALU.add)
                s_dv.inc(dve.always.tensor_mul(kb_s[:, tb:tb + 1], m1_s[:, tb:tb + 1],
                                        k1[:, :]))
        with nc.sbuf_tensor("x1c", [128, 2 * 2048], F32) as x1c, \
             nc.sbuf_tensor("gg", [128, 2 * 2048], F32) as gg, \
             nc.sbuf_tensor("oo", [128, 2 * 2048], F32) as oo, \
             nc.sbuf_tensor("ga", [128, 1], F32) as ga, \
             nc.sbuf_tensor("gb", [128, 1], F32) as gb, \
             nc.sbuf_tensor("gc", [128, 1], F32) as gc:
            gat_h = [0] * NT
            x_h = [0] * NT
            dv_h = [0] * NT
            out_h = [0] * NT

            def p9_pre(tb):
                sl = (tb % 2) * 2048
                s_x.wait(sync)
                if tb >= 2:
                    s_dv.wait(sync, dv_h[tb - 2])
                s_x.dma(sync.dma_start(
                    out=x1c[:, sl:sl + 2048],
                    in_=x1d.ap()[128 * tb:128 * (tb + 1), :]))
                x_h[tb] = s_x.n
                if tb >= 2:
                    s_dv.wait(gps, dv_h[tb - 2])
                s_gp.inc(gps.always.memset(gg[:, sl:sl + 2048], 0.0))
                s_gp.wait(gps)

            def p9_gather(tb):
                sl = (tb % 2) * 2048
                s_gat.wait(gps)
                s_gat.dma(gps.indirect_dma_start(
                    out=gg[:, sl:sl + 2048], out_offset=None,
                    in_=gatg.ap(),
                    in_offset=bass.IndirectOffsetOnAxis(
                        ap=idx_s[:, tb:tb + 1], axis=0),
                    bounds_check=bnd_reg, oob_is_err=False))
                gat_h[tb] = s_gat.n

            def p9_fetch(tb):
                p9_pre(tb)
                p9_gather(tb)

            s_ac.wait(sync)
            s_dv.wait(sync)
            p9_pre(0)
            p9_pre(1)
            s_cc.wait(sync, cc_gat)
            s_cc.wait(gps, cc_gat)
            p9_gather(0)
            p9_gather(1)
            for tb in range(NT):
                sl = (tb % 2) * 2048
                s_gat.wait(dve, gat_h[tb])
                s_gat.wait(act, gat_h[tb])
                s_x.wait(dve, x_h[tb])
                if tb >= 1:
                    s_dv.wait(act, dv_h[tb - 1])
                s_ac.inc(act.always.activation(pscr[:, :], gg[:, sl:sl + 2048],
                                        AF.Identity, accum_out=ga[:, :]))
                s_ac.inc(act.always.activation(pscr[:, :], gg[:, sl:sl + 2048],
                                        AF.Square, accum_out=gb[:, :]))
                s_ac.wait(dve)
                dve.tensor_scalar_mul(ga[:, :], ga[:, :], 1.0 / D)
                dve.tensor_scalar_mul(gb[:, :], gb[:, :], 1.0 / D)
                dve.tensor_tensor(gc[:, :], ga[:, :], ga[:, :], ALU.mult)
                s_dv.inc(dve.always.tensor_sub(gb[:, :], gb[:, :], gc[:, :]))
                s_dv.wait(act)
                s_ac.inc(act.always.activation(gb[:, :], gb[:, :], AF.Sqrt,
                                        bias=EPS_LN))
                s_ac.wait(dve)
                dve.reciprocal(gb[:, :], gb[:, :])
                dve.tensor_mul(gb[:, :], gb[:, :], msk_s[:, tb:tb + 1])
                dve.tensor_scalar(gg[:, sl:sl + 2048], gg[:, sl:sl + 2048],
                                  ga[:, :], gb[:, :],
                                  ALU.subtract, ALU.mult)
                if tb >= 2:
                    s_out.wait(dve, out_h[tb - 2])
                dve.tensor_scalar(oo[:, sl:sl + 2048], x1c[:, sl:sl + 2048],
                                  ka_s[:, tb:tb + 1], kb_s[:, tb:tb + 1],
                                  ALU.mult, ALU.subtract)
                s_dv.inc(dve.always.tensor_add(
                    oo[:, sl:sl + 2048], oo[:, sl:sl + 2048],
                    gg[:, sl:sl + 2048]))
                dv_h[tb] = s_dv.n
                s_dv.wait(sync, dv_h[tb])
                s_out.wait(sync)
                s_out.dma(sync.dma_start(
                    out=out_t.ap()[128 * tb:128 * (tb + 1), :],
                    in_=oo[:, sl:sl + 2048]))
                out_h[tb] = s_out.n
                if tb + 2 < NT:
                    p9_fetch(tb + 2)
    s_out.wait(sync)
    bar()
    return nc


_CACHE = {}


def kernel(**inputs):
    T = inputs["L0_attn_normalized_resid_pre"].shape[1]
    cfg = CFG(T)
    if T not in _CACHE:
        _CACHE[T] = build(cfg)
    nc = _CACHE[T]
    in_maps = [shard_inputs(cfg, c, inputs) for c in range(NCORES)]
    res = run_bass_kernel_spmd(nc, in_maps, core_ids=list(range(NCORES)))
    out = np.concatenate([res.results[c]["out"] for c in range(NCORES)], axis=0)
    return out[None, :, :]


# version 34
# speedup vs baseline: 1.0252x; 1.0077x over previous
import contextlib
import math
import numpy as np
import ml_dtypes
import concourse.bass as bass
import concourse.mybir as mybir
from concourse.bass_utils import run_bass_kernel_spmd

F32 = mybir.dt.float32
BF16 = mybir.dt.bfloat16
I32 = mybir.dt.int32
AF = mybir.ActivationFunctionType
ALU = mybir.AluOpType
AX = mybir.AxisListType

NCORES = 8
D = 2048
DC = D // 128
HE = 8
DH = 64
FH = 16
FDH = 128
FI = 8192
EPS_LN = 1e-5
EPS_RMS = 1e-6
C_MAIN = 6.0
C_FFN = 4.0
MASKVAL = -100000.0
TWO_PI = 2.0 * math.pi
CW_C1 = 6.28125
CW_C2 = TWO_PI - CW_C1
ROPE_THETA = 10000.0
FW_WQ, FW_WK, FW_WV, FW_WO, FW_G, FW_U = 0, 2048, 3072, 4096, 6144, 14336
FW_COLS = 22528
CONSTVALS = [0.0, 1.0, EPS_LN, EPS_RMS, -C_FFN, 0.125, 1.0 / math.sqrt(FDH)]


class CFG:
    def __init__(self, T):
        self.T = T
        self.Tc = T // NCORES
        self.NT = self.Tc // 128
        self.QT = 512
        self.NQT = T // self.QT
        self.NKB = T // 128
        self.NKC = self.NKB // NCORES
        self.NS = self.Tc // 512
        self.trips = [(4 * (t + 1) - 1) // 8 + 1 for t in range(self.NQT)]
        self.slots = [(t, j) for t in range(self.NQT)
                      for j in range(max(0, self.trips[t] - 2), self.trips[t])]
        self.NV = sum(self.trips)
        self.KSEL = T // 8
        self.KC = self.KSEL // NCORES
        self.NKT = self.KC // 128
        self.QHW = min(1024, self.KSEL)
        self.QH = self.KSEL // self.QHW
        self.NCONS = self.QHW // self.KC


def host_tables(cfg, c):
    t = {}
    vb = np.zeros((128, cfg.NV), np.float32)
    v = 0
    for qt in range(cfg.NQT):
        for j in range(cfg.trips[qt]):
            vb[:, v] = -C_MAIN if 8 * j + c < 4 * (qt + 1) else MASKVAL
            v += 1
    t["vbias"] = vb
    p = np.arange(128)[:, None]
    f = np.arange(cfg.QT)[None, :]
    dm = np.zeros((len(cfg.slots), 128, cfg.QT), ml_dtypes.bfloat16)
    for s, (qt, j) in enumerate(cfg.slots):
        dm[s] = ((cfg.QT * qt + f) >= (128 * (8 * j + c) + p)).astype(
            ml_dtypes.bfloat16)
    t["dmask"] = dm
    f2 = np.arange(cfg.QHW)[None, :]
    am = np.zeros((cfg.NKT * cfg.QH, 128, cfg.QHW), ml_dtypes.bfloat16)
    for jj in range(cfg.NKT):
        for h in range(cfg.QH):
            am[jj * cfg.QH + h] = ((cfg.QHW * h + f2) >=
                                   (128 * (cfg.NKT * c + jj) + p)).astype(
                ml_dtypes.bfloat16)
    t["amask"] = am
    mc = np.zeros((NCORES, 1), np.float32)
    mc[:c, 0] = 1.0
    t["maskc"] = mc
    gt = np.zeros((128, cfg.NT), np.float32)
    for tb in range(cfg.NT):
        gt[:, tb] = cfg.Tc * c + 128 * tb + np.arange(128)
    t["gtok"] = gt
    return t


def host_consts():
    t = {}
    t["eye_bf"] = np.eye(128, dtype=ml_dtypes.bfloat16)
    t["eye_f32"] = np.eye(128, dtype=np.float32)
    t["ones_bf"] = np.ones((128, 128), ml_dtypes.bfloat16)
    t["ones_f32"] = np.ones((128, 128), np.float32)
    t["u128s"] = (np.arange(128)[:, None] < np.arange(128)[None, :]).astype(
        np.float32)
    t["u16s"] = (np.arange(16)[:, None] < np.arange(16)[None, :]).astype(np.float32)
    ivf = (1.0 / ROPE_THETA ** (np.arange(0, FDH, 2) / FDH)).astype(np.float32)
    t["invfreq"] = np.concatenate([ivf, ivf])[None, :].astype(np.float32)
    t["iotap1"] = (np.arange(128, dtype=np.float32) + 1.0)[:, None]
    cv = np.zeros((128, len(CONSTVALS)), np.float32)
    for i, v in enumerate(CONSTVALS):
        cv[:, i] = v
    t["cvec"] = cv
    sio = np.zeros((2, 128), np.float32)
    sio[0, 0:64] = 1.0
    sio[1, 64:128] = 1.0
    t["selio"] = sio
    return t


def shard_inputs(cfg, c, inp):
    d = {}
    d["x"] = np.ascontiguousarray(inp["L0_attn_normalized_resid_pre"][0,
                                  cfg.Tc * c:cfg.Tc * (c + 1), :])
    wq4s = inp["L0_attn_W_Q"][4 * c:4 * c + 4]
    d["wq4"] = np.ascontiguousarray(
        wq4s.reshape(4, DC, 128, DH).transpose(2, 1, 0, 3)
        .reshape(128, 4 * DC * DH))
    wkv2 = np.concatenate([inp["L0_attn_W_K"][c], inp["L0_attn_W_V"][c]],
                          axis=1)
    d["wkv"] = np.ascontiguousarray(
        wkv2.reshape(DC, 128, 128).transpose(1, 0, 2).reshape(128, DC * 128))
    d["wo1"] = np.ascontiguousarray(inp["L0_attn_W_O"][c])
    r0 = (D // NCORES) * c
    d["fw1"] = np.ascontiguousarray(np.concatenate(
        [inp["L0_ffn_wq"][r0:r0 + 256], inp["L0_ffn_wk"][r0:r0 + 256],
         inp["L0_ffn_wv"][r0:r0 + 256], inp["L0_ffn_wo"][r0:r0 + 256]],
        axis=1))
    i0 = (FI // NCORES) * c
    d["fgu"] = np.ascontiguousarray(np.concatenate(
        [inp["L0_ffn_w_gate"][:, i0:i0 + FI // NCORES],
         inp["L0_ffn_w_up"][:, i0:i0 + FI // NCORES]], axis=1))
    d["fdn"] = np.ascontiguousarray(
        inp["L0_ffn_w_down"][(FI // NCORES) * c:(FI // NCORES) * (c + 1)])
    d["bq"] = inp["L0_ffn_bq"]
    d["bk"] = inp["L0_ffn_bk"]
    d["bv"] = inp["L0_ffn_bv"]
    d["ln1"] = inp["L0_ffn_ln1_w"]
    d["ln2"] = inp["L0_ffn_ln2_w"]
    d["wr"] = inp["L0_ffn_w_router"]
    d.update(host_tables(cfg, c))
    d.update(host_consts())
    return d


_DRAIN_OPS = {
    "tensor_copy", "tensor_tensor", "tensor_add", "tensor_sub", "tensor_mul",
    "tensor_max", "tensor_scalar", "tensor_single_scalar", "tensor_scalar_add",
    "tensor_scalar_sub", "tensor_scalar_mul", "tensor_scalar_max",
    "tensor_scalar_min", "tensor_relu", "tensor_reduce", "reduce_max",
    "reduce_sum", "scalar_tensor_tensor", "tensor_tensor_scan", "memset",
    "memzero", "reciprocal", "select", "transpose", "bn_stats", "bn_aggr",
    "activation", "square", "sqrt", "mul", "copy", "add", "sign", "iota",
    "affine_select", "affine_mul_reduce",
}


def _ap_range(a):
    t = getattr(a, "tensor", None)
    if t is None:
        return None
    pat = getattr(a, "ap", None)
    off = getattr(a, "offset", None)
    if pat is None or off is None:
        raise ValueError("unparsed AP")
    try:
        dims = [(int(p[0]), int(p[1])) for p in pat]
    except Exception:
        raise ValueError("unparsed AP pattern")
    rowlen, npart = dims[0]
    if rowlen <= 0:
        raise ValueError("odd partition pair")
    plo = int(off) // rowlen
    clo = int(off) % rowlen
    creach = clo
    for s, n_ in dims[1:]:
        creach += (n_ - 1) * abs(s)
    return (t.name, plo, plo + npart, clo, creach + 1)


class DrainWrap:

    def __init__(self, eng):
        self._e = eng
        self._w = []
        self._r = []
        self.always = _AlwaysDrain(self)

    @staticmethod
    def _overlap(r, lst):
        for (n_, pl, ph, cl, ch) in lst:
            if (n_ == r[0] and r[1] < ph and pl < r[2]
                    and r[3] < ch and cl < r[4]):
                return True
        return False

    def _emit(self, a, args, kw, force):
        need = force
        wr = []
        rd = []
        try:
            for x in args[1:]:
                r = _ap_range(x)
                if r is not None:
                    rd.append(r)
                    if self._overlap(r, self._w):
                        need = True
            for k_, x in kw.items():
                r = _ap_range(x)
                if r is None:
                    continue
                if k_ == "accum_out":
                    wr.append(r)
                else:
                    rd.append(r)
                    if self._overlap(r, self._w):
                        need = True
            r0 = _ap_range(args[0])
            if r0 is None:
                raise ValueError("no out ap")
            wr.append(r0)
            for r in wr:
                if (self._overlap(r, self._w)
                        or self._overlap(r, self._r)):
                    need = True
        except ValueError:
            need = True
            wr = None
        if (need or wr is None or len(self._w) > 96
                or len(self._r) > 96):
            self._e.drain()
            self._w = []
            self._r = []
        if wr:
            self._w.extend(wr)
            self._r.extend(rd)
        return a(*args, **kw)

    def __getattr__(self, n):
        a = getattr(self._e, n)
        if n in _DRAIN_OPS:
            def f(*args, **kw):
                return self._emit(a, args, kw, False)
            return f
        if n == "drain":
            def fd(*args, **kw):
                self._w = []
                self._r = []
                return a(*args, **kw)
            return fd
        return a


class _AlwaysDrain:
    def __init__(self, wrap):
        self._wrap = wrap

    def __getattr__(self, n):
        a = getattr(self._wrap._e, n)
        if n in _DRAIN_OPS:
            def f(*args, **kw):
                return self._wrap._emit(a, args, kw, True)
            return f
        return a


class Sem:
    def __init__(self, ctx, nc, name):
        self.h = ctx.enter_context(nc.semaphore(name))
        self.n = 0

    def inc(self, instr, k=1):
        if k == 1:
            instr.then_inc(self.h)
        else:
            instr.then_inc(self.h, k)
        self.n += k
        return instr

    def dma(self, instr):
        return self.inc(instr, 16)

    def wait(self, eng, n=None):
        eng.wait_ge(self.h, self.n if n is None else n)


def build(cfg, stop_after=None):
    nc = bass.Bass("TRN2", target_bir_lowering=False)
    Tc, NT, QT, NQT, NS = cfg.Tc, cfg.NT, cfg.QT, cfg.NQT, cfg.NS
    KC, NKT, QHW, QH = cfg.KC, cfg.NKT, cfg.QHW, cfg.QH

    def EI(name, shape, dtype=F32):
        return nc.dram_tensor(name, list(shape), dtype, kind="ExternalInput")

    x_in = EI("x", [Tc, D])
    wq4 = EI("wq4", [128, 4 * DC * DH])
    wkv = EI("wkv", [128, DC * 128])
    wo1 = EI("wo1", [DH, D])
    fw1 = EI("fw1", [D // NCORES, FW_G])
    fgu = EI("fgu", [D, 2 * (FI // NCORES)])
    fdn = EI("fdn", [FI // NCORES, D])
    bq_i = EI("bq", [D])
    bk_i = EI("bk", [1024])
    bv_i = EI("bv", [1024])
    ln1_i = EI("ln1", [D])
    ln2_i = EI("ln2", [D])
    wr_i = EI("wr", [D])
    vb_i = EI("vbias", [128, cfg.NV])
    dm_i = EI("dmask", [len(cfg.slots), 128, QT], BF16)
    am_i = EI("amask", [NKT * QH, 128, QHW], BF16)
    mc_i = EI("maskc", [NCORES, 1])
    gt_i = EI("gtok", [128, NT])
    ey_bf_i = EI("eye_bf", [128, 128], BF16)
    ey_f_i = EI("eye_f32", [128, 128])
    on_bf_i = EI("ones_bf", [128, 128], BF16)
    on_f_i = EI("ones_f32", [128, 128])
    u128_i = EI("u128s", [128, 128])
    u16_i = EI("u16s", [16, 16])
    ivf_i = EI("invfreq", [1, 128])
    io1_i = EI("iotap1", [128, 1])
    out_t = nc.dram_tensor("out", [Tc, D], F32, kind="ExternalOutput")

    def DT(name, shape, dtype=F32, shared=False):
        return nc.dram_tensor(name, list(shape), dtype, kind="Internal",
                              addr_space="Shared" if shared else "Local")

    a1c = DT("a1c", [D, 192], BF16)
    a1g = DT("a1g", [NCORES * D, 192], BF16)
    a2c = DT("a2c", [DH, D], BF16)
    a2g = DT("a2g", [NCORES * DH, D], BF16)
    fw1cAs = [DT(f"fw1cA{i}", [D // NCORES, 2048], BF16) for i in range(3)]
    fw1gAs = [DT(f"fw1gA{i}", [D, 2048], BF16) for i in range(3)]
    fguc = DT("fguc", [D, 2 * (FI // NCORES)], BF16)
    h2c = DT("h2c", [D, KC], BF16)
    h2g = DT("h2g", [NCORES * D, KC], BF16)
    _spl = (KC // 128) >= 2
    dncA = DT("dncA", [cfg.KSEL // 2 if _spl else cfg.KSEL, D], F32)
    dncB = DT("dncB", [cfg.KSEL // 2, D], F32)
    dnrA = DT("dnrA", [KC // 2 if _spl else KC, D], F32)
    dnrB = DT("dnrB", [KC // 2, D], F32)
    fdnc = DT("fdnc", [FI // NCORES, D], BF16)
    KW = cfg.NKC * 128 // NCORES
    ktc = DT("ktc", [NCORES * 512, KW], BF16)
    kta = DT("kta", [NCORES * 512, KW], BF16)
    vtc = DT("vtc", [Tc, 512], BF16)
    vta = DT("vta", [Tc, 512], BF16)
    qc = DT("qc", [512, Tc], BF16)
    qg = DT("qg", [NCORES * 512, Tc], BF16)
    NZB = Tc // 512
    zlcb = [DT(f"zlc{b}", [NCORES * 520, 512], F32) for b in range(NZB)]
    zlrb = [DT(f"zlr{b}", [520, 512], F32) for b in range(NZB)]
    scc = DT("scc", [Tc, 1], F32)
    scg = DT("scg", [cfg.T, 1], F32)
    cntc = DT("cntc", [1, 1], F32)
    cntg = DT("cntg", [NCORES, 1], F32)
    selhc = DT("selhc", [cfg.KSEL, 2048], F32)
    selhr = DT("selhr", [KC, 2048], F32)
    selxc = DT("selxc", [cfg.KSEL, 4], F32)
    selxr = DT("selxr", [KC, 4], F32)
    fqc = DT("fqc", [D, KC], BF16)
    fqg = DT("fqg", [NCORES * D, KC], BF16)
    FZR = FH // 2 * 128 + FH // 2
    fzlcA = DT("fzlcA", [NCORES * FZR, KC], F32)
    fzlcB = DT("fzlcB", [NCORES * FZR, KC], F32)
    fzlrA = DT("fzlrA", [FZR, KC], F32)
    fzlrB = DT("fzlrB", [FZR, KC], F32)
    gatc = DT("gatc", [KC, D], F32)
    gatg = DT("gatg", [cfg.KSEL, D], F32)
    x1d = DT("x1d", [Tc, D], F32)
    RG = [list(range(NCORES))]

    ctx = contextlib.ExitStack()
    SB = lambda name, shape, dt=F32: ctx.enter_context(
        nc.sbuf_tensor(name, list(shape), dt))

    s_in = Sem(ctx, nc, "s_in")
    s_w = Sem(ctx, nc, "s_w")
    s_cc = Sem(ctx, nc, "s_cc")
    s_pe = Sem(ctx, nc, "s_pe")
    s_dv = Sem(ctx, nc, "s_dv")
    s_ac = Sem(ctx, nc, "s_ac")
    s_gp = Sem(ctx, nc, "s_gp")
    s_st = Sem(ctx, nc, "s_st")
    s_ld = Sem(ctx, nc, "s_ld")
    s_x = Sem(ctx, nc, "s_x")
    s_sc = Sem(ctx, nc, "s_sc")
    s_ex = Sem(ctx, nc, "s_ex")
    s_av = Sem(ctx, nc, "s_av")
    s_ev = Sem(ctx, nc, "s_ev")
    s_out = Sem(ctx, nc, "s_out")
    s_qp = [Sem(ctx, nc, f"s_q{i}") for i in range(6)]
    s_fqp = [Sem(ctx, nc, "s_fq0"), Sem(ctx, nc, "s_fq1")]
    s_mk = Sem(ctx, nc, "s_mk")
    s_sca = Sem(ctx, nc, "s_sca")
    s_gat = Sem(ctx, nc, "s_gat")
    s_wg = Sem(ctx, nc, "s_wg")
    s_sg = Sem(ctx, nc, "s_sg")
    s_ev2 = Sem(ctx, nc, "s_ev2")
    s_zz = Sem(ctx, nc, "s_zz")
    bnd_reg = ctx.enter_context(nc.gpsimd.register("bnd"))
    nc.gpsimd.reg_mov(bnd_reg, cfg.KSEL - 1)

    sync = nc.sync
    pe = nc.tensor
    gps = DrainWrap(nc.gpsimd)
    dve = DrainWrap(nc.vector)
    act = DrainWrap(nc.scalar)
    bar = nc.all_engine_barrier

    def syncpoint():
        for s_ in (s_in, s_w, s_st, s_ld, s_x, s_gp, s_out,
                   s_qp[0], s_qp[1], s_qp[2], s_qp[3], s_qp[4], s_qp[5],
                   s_fqp[0], s_fqp[1], s_sca, s_gat, s_wg, s_sg, s_zz):
            s_.wait(sync)
        bar()

    ey_bf = SB("ey_bf", [128, 128], BF16)
    ey_f = SB("ey_f", [128, 128])
    on_bf = SB("on_bf", [128, 128], BF16)
    on_f = SB("on_f", [128, 128])
    u128 = SB("u128", [128, 128])
    u16 = SB("u16", [16, 16])
    ivf = SB("ivf", [1, 128])
    io1 = SB("io1", [128, 1])
    vb_s = SB("vb_s", [128, cfg.NV])
    mc_s = SB("mc_s", [NCORES, 1])
    gt_s = SB("gt_s", [128, NT])
    cvec = SB("cvec_s", [128, len(CONSTVALS)])
    cvec_i = EI("cvec", [128, len(CONSTVALS)])
    selio = SB("selio_s", [2, 128])
    selio_i = EI("selio", [2, 128])
    for dst, src in [(ey_bf, ey_bf_i), (ey_f, ey_f_i), (on_bf, on_bf_i),
                     (on_f, on_f_i), (u128, u128_i), (u16, u16_i), (ivf, ivf_i),
                     (io1, io1_i), (vb_s, vb_i), (mc_s, mc_i), (gt_s, gt_i),
                     (cvec, cvec_i), (selio, selio_i)]:
        s_in.dma(sync.dma_start(out=dst[:, :], in_=src.ap()))
    for i, v in enumerate(CONSTVALS):
        nc.const_aps.register(F32, v, cvec[:, i:i + 1])
    for eng in (pe, dve, act, gps):
        s_in.wait(eng)
    m1_s = SB("m1_s", [128, NT])
    r1_s = SB("r1_s", [128, NT])
    v1_s = SB("v1_s", [128, NT])
    sc_s = SB("sc_s", [128, NT])
    msk_s = SB("msk_s", [128, NT])
    ka_s = SB("ka_s", [128, NT])
    kb_s = SB("kb_s", [128, NT])
    idx_s = SB("idx_s", [128, NT], I32)
    gate_s = SB("gate_s", [128, NT])

    def ln_stats(xa, sa, sb, sc2, eps):
        dve.reduce_sum(sa[:, :], xa, AX.X)
        s_ac.inc(act.always.activation(pscr[:, :], xa, AF.Square, accum_out=sb[:, :]))
        dve.tensor_scalar_mul(sa[:, :], sa[:, :], 1.0 / D)
        s_ac.wait(dve)
        dve.tensor_scalar_mul(sb[:, :], sb[:, :], 1.0 / D)
        dve.tensor_tensor(sc2[:, :], sa[:, :], sa[:, :], ALU.mult)
        s_dv.inc(dve.always.tensor_sub(sb[:, :], sb[:, :], sc2[:, :]))
        s_dv.wait(act)
        s_ac.inc(act.always.activation(sc2[:, :], sb[:, :], AF.Sqrt, bias=eps))
        s_ac.wait(dve)
        dve.reciprocal(sc2[:, :], sc2[:, :])

    wcctx = contextlib.ExitStack()
    wc = wcctx.enter_context(nc.sbuf_tensor("wc", [128, 2 * 2048], F32))
    wco = wcctx.enter_context(nc.sbuf_tensor("wco", [128, 2 * 2048], BF16))
    with nc.sbuf_tensor("wa", [128, DC * 4 * DH], F32) as wa, \
         nc.sbuf_tensor("wkvs", [128, DC * 128], F32) as wkvs, \
         nc.sbuf_tensor("wob", [128, DC * 192], BF16) as wob:
        s_w.dma(sync.dma_start(out=wa[:, :], in_=wq4.ap()))
        s_w.dma(sync.dma_start(out=wkvs[:, :], in_=wkv.ap()))
        s_w.wait(sync)
        s_w.wait(dve)
        for dc in range(DC):
            o4 = dc * 4 * DH
            oo = dc * 192
            dve.tensor_add(wob[:, oo:oo + DH].bitcast(F32)
                           if False else wa[:, o4:o4 + DH],
                           wa[:, o4:o4 + DH], wa[:, o4 + DH:o4 + 2 * DH])
            dve.tensor_add(wa[:, o4:o4 + DH], wa[:, o4:o4 + DH],
                           wa[:, o4 + 2 * DH:o4 + 3 * DH])
            dve.tensor_add(wa[:, o4:o4 + DH], wa[:, o4:o4 + DH],
                           wa[:, o4 + 3 * DH:o4 + 4 * DH])
            dve.tensor_copy(wob[:, oo:oo + DH], wa[:, o4:o4 + DH])
            s_dv.inc(dve.always.tensor_copy(wob[:, oo + DH:oo + 192],
                                     wkvs[:, 128 * dc:128 * (dc + 1)]))
        s_dv.wait(sync)
        s_st.dma(sync.dma_start(
            out=a1c.ap().rearrange("(dc p) k -> p dc k", p=128),
            in_=wob[:, :].rearrange("p (dc k) -> p dc k", k=192)))
        s_w.dma(sync.dma_start(out=wc[0:DH, 0:D], in_=wo1.ap()))
        s_w.wait(dve)
        s_dv.inc(dve.always.tensor_copy(wco[0:DH, 0:D], wc[0:DH, 0:D]))
        s_dv.wait(sync)
        s_st.dma(sync.dma_start(out=a2c.ap(), in_=wco[0:DH, 0:D]))
        pass
    wk_items = []
    for cs in range(FW_G // 2048):
        for blk in range(2):
            wk_items.append((fw1.ap()[128 * blk:128 * (blk + 1),
                                      2048 * cs:2048 * (cs + 1)],
                             fw1cAs[cs].ap()[128 * blk:128 * (blk + 1), :]))
    for blk in range(D // 128):
        wk_items.append((fgu.ap()[128 * blk:128 * (blk + 1), :],
                         fguc.ap()[128 * blk:128 * (blk + 1), :]))
    for blk in range(FI // NCORES // 128):
        wk_items.append((fdn.ap()[128 * blk:128 * (blk + 1), :],
                         fdnc.ap()[128 * blk:128 * (blk + 1), :]))
    NWI = len(wk_items)
    wk_gathers = [(2 * cs + 1, fw1cAs[cs], fw1gAs[cs])
                  for cs in range(FW_G // 2048)]
    wk_emitted = [False] * NWI
    wk_dvw = [0] * NWI
    wk_stw = [0] * NWI
    wk_swl = [0] * NWI
    dv_p0 = s_dv.n

    def wk_load(n):
        s_wg.wait(gps)
        if n == 0:
            s_dv.wait(gps, dv_p0)
        if n >= 2:
            s_dv.wait(gps, wk_dvw[n - 2])
        s_wg.dma(gps.dma_start(
            out=wc[:, (n % 2) * 2048:(n % 2 + 1) * 2048], in_=wk_items[n][0]))
        wk_swl[n] = s_wg.n

    def wk_pump(n):
        if n + 1 < NWI:
            wk_load(n + 1)
        s_wg.wait(dve, wk_swl[n])
        if n == 0:
            s_cc.wait(dve, cc_attnw)
        if n >= 2:
            s_sg.wait(dve, wk_stw[n - 2])
        s_dv.inc(dve.always.tensor_copy(wco[:, (n % 2) * 2048:(n % 2 + 1) * 2048],
                                 wc[:, (n % 2) * 2048:(n % 2 + 1) * 2048]))
        wk_dvw[n] = s_dv.n
        s_dv.wait(gps, wk_dvw[n])
        s_sg.wait(gps)
        s_sg.dma(gps.dma_start(out=wk_items[n][1],
                               in_=wco[:, (n % 2) * 2048:(n % 2 + 1) * 2048]))
        wk_stw[n] = s_sg.n
        wk_emitted[n] = True

    s_st.wait(gps)
    s_cc.inc(gps.always.collective_compute("AllGather", ALU.bypass, RG,
                                    [a1c.ap().opt()], [a1g.ap().opt()]))
    s_cc.inc(gps.always.collective_compute("AllGather", ALU.bypass, RG,
                                    [a2c.ap().opt()], [a2g.ap().opt()]))
    cc_attnw = s_cc.n
    cc_ffnw = [0]
    zz = wcctx.enter_context(nc.sbuf_tensor("zz", [128, 2052], F32))
    s_gp.inc(gps.always.memset(zz[:, :], 0.0))
    s_gp.wait(sync)
    zz_pend = list(range(cfg.KSEL // 128))

    syncpoint()

    def finish(ret=None):
        s_st.wait(sync)
        s_in.wait(sync)
        s_cc.wait(gps)
        bar()
        return nc

    if stop_after == "P0":
        return finish()

    with contextlib.ExitStack() as p3ctx:
        SB3 = lambda name, shape, dt=F32: p3ctx.enter_context(
            nc.sbuf_tensor(name, list(shape), dt))
        pscr = SB3("pscr", [128, 2048])
        stage = SB3("stage", [128, DC * Tc], BF16)
        p2ctx = contextlib.ExitStack()
        SB2 = lambda name, shape, dt=F32: p2ctx.enter_context(
            nc.sbuf_tensor(name, list(shape), dt))
        ln_t = SB2("ln_t", [128, NT * 2048], BF16)
        with nc.sbuf_tensor("xt", [128, 2 * 2048], F32) as xt, \
             nc.sbuf_tensor("sa", [128, 1], F32) as sa, \
             nc.sbuf_tensor("sb", [128, 1], F32) as sb, \
             nc.sbuf_tensor("sc2", [128, 1], F32) as sc2, \
             nc.psum_tensor("tp0", [128, 128], BF16) as tp0, \
             nc.psum_tensor("tp1", [128, 128], BF16) as tp1, \
             nc.psum_tensor("tp2", [128, 128], BF16) as tp2, \
             nc.psum_tensor("tp3", [128, 128], BF16) as tp3:
            tps = [tp0, tp1, tp2, tp3]
            xld = [0] * NT
            dvl = [0] * (NT * DC + 8)
            k = 0
            ln_done = [0] * NT

            def x_load(tb):
                s_x.wait(sync)
                s_dv.wait(sync)
                s_ac.wait(sync)
                s_x.dma(sync.dma_start(
                    out=xt[:, (tb % 2) * 2048:(tb % 2 + 1) * 2048],
                    in_=x_in.ap()[128 * tb:128 * (tb + 1), :]))
                xld[tb] = s_x.n

            x_load(0)
            for tb in range(NT):
                if tb + 1 < NT:
                    x_load(tb + 1)
                xa = xt[:, (tb % 2) * 2048:(tb % 2 + 1) * 2048]
                s_x.wait(dve, xld[tb])
                s_x.wait(act, xld[tb])
                ln_stats(xa, sa, sb, sc2, EPS_LN)
                s_dv.inc(dve.always.tensor_scalar(ln_t[:, 2048 * tb:2048 * (tb + 1)],
                                           xa, sa[:, :], sc2[:, :],
                                           ALU.subtract, ALU.mult))
                ln_done[tb] = s_dv.n
                s_dv.wait(pe, ln_done[tb])
                for dc in range(DC):
                    pp = tps[k % 4]
                    if k >= 4:
                        s_ac.wait(pe, dvl[k - 4])
                    s_pe.inc(pe.transpose(
                        pp[:, :], ln_t[:, 2048 * tb + 128 * dc:
                                       2048 * tb + 128 * (dc + 1)],
                        ey_bf[:, :]))
                    s_pe.wait(act)
                    s_ac.inc(act.always.copy(
                        stage[:, Tc * dc + 128 * tb:Tc * dc + 128 * (tb + 1)],
                        pp[:, :]))
                    dvl[k] = s_ac.n
                    k += 1
        if stop_after == "P2":
            s_dv.wait(sync)
            return finish()
        syncpoint()
        p2ctx.close()
        s_cc.wait(sync, cc_attnw)
        wq_t = SB3("wq_t", [128, 4 * DC * 128], BF16)
        wk_t = SB3("wk_t", [128, 4 * DC * 128], BF16)
        wv_t = SB3("wv_t", [128, DC * 512], BF16)
        WQW = 4 * DC * 128
        for hp in range(4):
            for g in range(2):
                h = 2 * hp + g
                base = hp * DC * 128 + 64 * g
                asrc = a1g.ap()[D * h:D * (h + 1), :].rearrange(
                    "(dc p) k -> p dc k", p=128)
                s_in.dma(sync.dma_start(
                    out=bass.AP(wq_t, base, [[WQW, 128], [128, DC], [1, 64]]),
                    in_=asrc[:, :, 0:DH]))
                s_in.dma(sync.dma_start(
                    out=bass.AP(wk_t, base, [[WQW, 128], [128, DC], [1, 64]]),
                    in_=asrc[:, :, DH:2 * DH]))
        for h in range(HE):
            s_in.dma(sync.dma_start(
                out=bass.AP(wv_t, 64 * h, [[DC * 512, 128], [512, DC], [1, 64]]),
                in_=a1g.ap()[D * h:D * (h + 1), 2 * DH:3 * DH]
                .rearrange("(dc p) k -> p dc k", p=128)))
        qkt_s = SB3("qkt_s", [128, 8 * Tc], BF16)
        vt_s = SB3("vt_s", [128, NT * 512], BF16)
        with nc.psum_tensor("pj_ps", [128, 512], F32) as pj_ps, \
             nc.psum_tensor("pj_ps2", [128, 512], F32) as pj_ps2, \
             nc.psum_tensor("pj_ps3", [128, 512], F32) as pj_ps3, \
             nc.psum_tensor("pj_ps4", [128, 512], F32) as pj_ps4:
            pjs = [pj_ps, pj_ps2, pj_ps3, pj_ps4]
            from concourse.aot_env import allow_ici_all_to_all
            s_in.wait(pe)
            s_dv.wait(pe)
            k = 0
            dv_base = s_dv.n
            for kind in range(2):
                wt = wq_t if kind == 0 else wk_t
                for hp in range(4):
                    for ns in range(NS):
                        pp = pjs[k % 4]
                        if k >= 4:
                            s_dv.wait(pe, dv_base + k - 3)
                        for dc in range(DC):
                            s_pe.inc(pe.matmul(
                                pp[:, :],
                                wt[:, (hp * DC + dc) * 128:
                                   (hp * DC + dc) * 128 + 128],
                                stage[:, Tc * dc + 512 * ns:
                                      Tc * dc + 512 * (ns + 1)],
                                start=(dc == 0), stop=(dc == DC - 1)))
                        s_pe.wait(dve)
                        s_dv.inc(dve.always.tensor_copy(
                            qkt_s[:, (kind * 4 + hp) * Tc + 512 * ns:
                                  (kind * 4 + hp) * Tc + 512 * (ns + 1)],
                            pp[:, :]))
                        k += 1
                s_dv.wait(sync)
                s_st.wait(sync)
                if kind == 0:
                    for hp in range(4):
                        s_st.dma(sync.dma_start(
                            out=qc.ap()[128 * hp:128 * (hp + 1), :],
                            in_=qkt_s[:, hp * Tc:(hp + 1) * Tc]))
                    s_st.wait(gps)
                    s_cc.inc(gps.always.collective_compute(
                        "AllGather", ALU.bypass, RG,
                        [qc.ap().opt()], [qg.ap().opt()]))
                else:
                    for m in range(cfg.NKC):
                        d_ = m % NCORES
                        half = m // NCORES
                        for hp in range(4):
                            s_st.dma(sync.dma_start(
                                out=ktc.ap()[512 * d_ + 128 * hp:
                                             512 * d_ + 128 * (hp + 1),
                                             128 * half:128 * (half + 1)],
                                in_=qkt_s[:, (4 + hp) * Tc + 128 * m:
                                          (4 + hp) * Tc + 128 * (m + 1)]))
                    s_st.wait(gps)
                    with allow_ici_all_to_all():
                        s_cc.inc(gps.always.collective_compute(
                            "AllToAll", ALU.bypass, RG,
                            [ktc.ap().opt()], [kta.ap().opt()]))
            for tb in range(NT):
                pp = pjs[k % 4]
                s_dv.wait(pe, dv_base + k - 3)
                for dc in range(DC):
                    s_pe.inc(pe.matmul(
                        pp[:, :],
                        stage[:, Tc * dc + 128 * tb:Tc * dc + 128 * (tb + 1)],
                        wv_t[:, 512 * dc:512 * (dc + 1)],
                        start=(dc == 0), stop=(dc == DC - 1)))
                s_pe.wait(dve)
                s_dv.inc(dve.always.tensor_copy(vt_s[:, 512 * tb:512 * (tb + 1)],
                                         pp[:, :]))
                k += 1
        s_dv.wait(sync)
        s_st.wait(sync)
        for m in range(cfg.NKC):
            d_ = m % NCORES
            half = m // NCORES
            s_st.dma(sync.dma_start(
                out=vtc.ap()[(Tc // NCORES) * d_ + 128 * half:
                             (Tc // NCORES) * d_ + 128 * (half + 1), :],
                in_=vt_s[:, 512 * m:512 * (m + 1)]))
        s_st.wait(gps)
        with allow_ici_all_to_all():
            s_cc.inc(gps.always.collective_compute("AllToAll", ALU.bypass, RG,
                                            [vtc.ap().opt()], [vta.ap().opt()]))
        cc_qkv = s_cc.n
    if stop_after == "P3":
        return finish()

    syncpoint()
    with contextlib.ExitStack() as p5ctx:
        SB5 = lambda name, shape, dt=F32: p5ctx.enter_context(
            nc.sbuf_tensor(name, list(shape), dt))
        ktx = SB5("ktx", [128, 4 * cfg.NKC * 128], BF16)
        vres = SB5("vres", [128, cfg.NKC * 512], BF16)
        s_cc.wait(sync, cc_qkv)
        NH2_ = cfg.NKC // 8
        for hp in range(4):
            s_ld.dma(sync.dma_start(
                out=ktx[:, hp * cfg.NKC * 128:(hp + 1) * cfg.NKC * 128]
                .rearrange("p (j2 half k) -> p j2 half k", half=NH2_, k=128),
                in_=kta.ap().rearrange("(j2 x p) (half k) -> x p j2 half k",
                                       x=4, p=128, k=128)[hp]))
        s_ld.dma(sync.dma_start(
            out=vres[:, :].rearrange("p (j2 half k) -> p j2 half k",
                                     half=NH2_, k=512),
            in_=vta.ap().rearrange("(j2 half p) k -> p j2 half k",
                                   half=NH2_, p=128)))
        ld_res = s_ld.n
        qbuf = SB5("qbuf", [128, 24 * QT], BF16)
        mbuf = SB5("mbuf", [128, 12 * QT], BF16)
        pT = SB5("pT", [128, 4 * QT], BF16)
        zst = SB5("zst", [128, 8 * 512])
        lstA = SB5("lstA", [1, 8 * 512])
        lstB = SB5("lstB", [1, 8 * 512])
        with nc.psum_tensor("s_ps", [128, 2 * QT], F32) as s_psA, \
             nc.psum_tensor("s_ps2", [128, 2 * QT], F32) as s_psB, \
             nc.psum_tensor("z_psa", [128, QT], F32) as z_psa, \
             nc.psum_tensor("z_psb", [128, QT], F32) as z_psb, \
             nc.psum_tensor("l_psa", [128, QT], F32) as l_psa, \
             nc.psum_tensor("l_psb", [128, QT], F32) as l_psb:

            def q_load(t, sl):
                o = (QT * t) // Tc
                col = QT * t - Tc * o
                for hp in range(4):
                    s_qp[sl].dma(sync.dma_start(
                        out=qbuf[:, (sl * 4 + hp) * QT:
                                 (sl * 4 + hp + 1) * QT],
                        in_=qg.ap()[512 * o + 128 * hp:512 * o + 128 * (hp + 1),
                                    col:col + QT]))

            def m_load(t, sl):
                base = len([s for s in cfg.slots if s[0] < t])
                for u in range(min(2, cfg.trips[t])):
                    s_qp[sl].dma(sync.dma_start(
                        out=mbuf[:, (sl * 2 + u) * QT:
                                 (sl * 2 + u + 1) * QT],
                        in_=dm_i.ap()[base + u, :, :]))

            NBLK = Tc // QT
            tseq = [t_ for b_ in range(NBLK) for t_ in range(b_, NQT, NBLK)]
            vbase = [0] * NQT
            for t_ in range(1, NQT):
                vbase[t_] = vbase[t_ - 1] + cfg.trips[t_ - 1]
            visits = []
            vend = []
            tslot = {}
            for si_, t_ in enumerate(tseq):
                trips_ = cfg.trips[t_]
                jm0_ = trips_ - min(2, trips_)
                tslot[t_] = si_ % 6
                for hp_ in range(4):
                    for j_ in range(trips_):
                        visits.append((t_, hp_, j_, trips_, jm0_,
                                       vbase[t_] + j_))
                vend.append(len(visits))
            NVIS = len(visits)
            seq_of = {t_: si_ for si_, t_ in enumerate(tseq)}
            ldcount = [0] * NQT
            for pf_ in range(min(5, len(tseq))):
                q_load(tseq[pf_], pf_)
                m_load(tseq[pf_], pf_)
                ldcount[tseq[pf_]] = s_qp[pf_].n
            av_hist = [0] * NVIS
            mk_hist = [0] * NVIS
            sc_hist = [0] * NVIS
            cc_zlb = [0] * NBLK
            wk_gat_next = [0]
            PUMP_EVERY = max(1, (2 * NVIS) // (3 * NWI))
            wk_load(0)

            qp_waited = set()

            def emit_scores(k):
                t_, hp_, j_, trips_, jm0_, v_ = visits[k]
                sp = s_psA if k % 2 == 0 else s_psB
                if t_ not in qp_waited:
                    s_qp[tslot[t_]].wait(pe, ldcount[t_])
                    qp_waited.add(t_)
                if k == 0:
                    s_ld.wait(pe)
                if k >= 2:
                    s_ex.wait(pe, k - 1)
                kof = (hp_ * cfg.NKC + j_) * 128
                qof = (tslot[t_] * 4 + hp_) * QT
                s_sc.inc(pe.matmul(sp[:, 0:QT], ktx[0:64, kof:kof + 128],
                                   qbuf[0:64, qof:qof + QT],
                                   start=True, stop=True, tile_position=(0, 0)))
                s_sc.inc(pe.matmul(sp[:, QT:2 * QT], ktx[64:128, kof:kof + 128],
                                   qbuf[64:128, qof:qof + QT],
                                   start=True, stop=True, tile_position=(64, 0)))
                sc_hist[k] = s_sc.n

            st_hist = {}

            def emit_evac(av_t, g_):
                sl_ = 512 * (g_ % 8)
                s_av.wait(dve, av_t)
                if g_ >= 8:
                    s_st.wait(dve, st_hist[g_ - 8])
                s_ev.inc(dve.tensor_copy(zst[0:64, sl_:sl_ + 512],
                                         z_psa[0:64, :]))
                s_ev.inc(dve.tensor_copy(zst[64:128, sl_:sl_ + 512],
                                         z_psb[64:128, :]))
                s_av.wait(act, av_t)
                if g_ >= 8:
                    s_st.wait(act, st_hist[g_ - 8])
                s_ev2.inc(act.copy(lstA[:, sl_:sl_ + 512], l_psa[0:1, :]))
                s_ev2.inc(act.copy(lstB[:, sl_:sl_ + 512], l_psb[64:65, :]))

            def emit_store(t_p, hp_p, g_):
                sl_ = 512 * (g_ % 8)
                s_ev.wait(sync)
                s_ev2.wait(sync)
                s_st.wait(sync)
                r_ = t_p // NBLK
                zt = zlcb[t_p % NBLK]
                s_st.dma(sync.dma_start(
                    out=zt.ap()[520 * r_ + 128 * hp_p:
                                520 * r_ + 128 * (hp_p + 1), :],
                    in_=zst[:, sl_:sl_ + 512]))
                s_st.dma(sync.dma_start(
                    out=zt.ap()[520 * r_ + 512 + 2 * hp_p:
                                520 * r_ + 513 + 2 * hp_p, :],
                    in_=lstA[:, sl_:sl_ + 512]))
                s_st.dma(sync.dma_start(
                    out=zt.ap()[520 * r_ + 513 + 2 * hp_p:
                                520 * r_ + 514 + 2 * hp_p, :],
                    in_=lstB[:, sl_:sl_ + 512]))
                st_hist[g_] = s_st.n
                if hp_p == 3 and t_p == NQT - NBLK + (t_p % NBLK):
                    b_ = t_p % NBLK
                    s_st.wait(gps)
                    s_cc.inc(gps.always.collective_compute(
                        "ReduceScatter", ALU.add, RG,
                        [zlcb[b_].ap().opt()], [zlrb[b_].ap().opt()]))
                    cc_zlb[b_] = s_cc.n
                    while wk_gat_next[0] < len(wk_gathers):
                        li, cin_, cout_ = wk_gathers[wk_gat_next[0]]
                        if not wk_emitted[li]:
                            break
                        s_sg.wait(gps, wk_stw[li])
                        s_cc.inc(gps.always.collective_compute(
                            "AllGather", ALU.bypass, RG,
                            [cin_.ap().opt()], [cout_.ap().opt()]))
                        cc_ffnw[0] = s_cc.n
                        wk_gat_next[0] += 1

            pend = None
            gcount = 0
            emit_scores(0)
            for k in range(NVIS):
                t, hp, j, trips, jmask0, v = visits[k]
                pj = k % 2
                sp = s_psA if pj == 0 else s_psB
                if j == 0 and hp == 0:
                    si = seq_of[t]
                    if zz_pend:
                        r_z = zz_pend.pop(0)
                        s_zz.dma(sync.dma_start(
                            out=selhc.ap()[128 * r_z:128 * (r_z + 1), :],
                            in_=zz[:, 0:2048]))
                        s_zz.dma(sync.dma_start(
                            out=selxc.ap()[128 * r_z:128 * (r_z + 1), :],
                            in_=zz[:, 0:4]))
                    if si + 5 < len(tseq):
                        tn = tseq[si + 5]
                        if si >= 1:
                            s_sc.wait(sync, 2 * vend[si - 1])
                            s_mk.wait(sync, mk_hist[vend[si - 1] - 1])
                        q_load(tn, (si + 5) % 6)
                        m_load(tn, (si + 5) % 6)
                        ldcount[tn] = s_qp[(si + 5) % 6].n
                if k + 1 < NVIS:
                    emit_scores(k + 1)
                s_sc.wait(act, sc_hist[k])
                if k >= 2:
                    s_av.wait(act, av_hist[k - 2])
                ei = s_ex.inc(act.activation(
                    pT[:, pj * 2 * QT:(pj + 1) * 2 * QT],
                    sp[:, :], AF.Exp, scale=0.125, bias=vb_s[:, v:v + 1]))
                if pend is not None:
                    emit_evac(pend[2], pend[3])
                    emit_store(pend[0], pend[1], pend[3])
                    pend = None
                masked = j >= jmask0
                if masked:
                    u = j - jmask0
                    mof = (tslot[t] * 2 + u) * QT
                    s_ex.wait(dve, k + 1)
                    s_qp[tslot[t]].wait(dve, ldcount[t])
                    s_mk.inc(dve.tensor_mul(
                        pT[:, pj * 2 * QT:pj * 2 * QT + QT],
                        pT[:, pj * 2 * QT:pj * 2 * QT + QT],
                        mbuf[:, mof:mof + QT]))
                    s_mk.inc(dve.tensor_mul(
                        pT[:, pj * 2 * QT + QT:(pj + 1) * 2 * QT],
                        pT[:, pj * 2 * QT + QT:(pj + 1) * 2 * QT],
                        mbuf[:, mof:mof + QT]))
                mk_hist[k] = s_mk.n
                if k % PUMP_EVERY == PUMP_EVERY - 1 and k // PUMP_EVERY < NWI:
                    wk_pump(k // PUMP_EVERY)
                s_ex.wait(pe, k + 1)
                if masked:
                    s_mk.wait(pe, mk_hist[k])
                ev0 = s_ev.n if (j == 0 and k > 0) else None
                ev2 = s_ev2.n if (j == 0 and k > 0) else None
                st_ = (j == 0)
                sp_ = (j == trips - 1)
                if ev0 is not None:
                    s_ev.wait(pe, ev0 - 1)
                s_av.inc(pe.matmul(z_psa[0:64, :],
                                   vres[:, 512 * j + 128 * hp:
                                        512 * j + 128 * hp + 64],
                                   pT[:, pj * 2 * QT:pj * 2 * QT + QT],
                                   start=st_, stop=sp_, tile_position=(0, 0)))
                if ev0 is not None:
                    s_ev.wait(pe, ev0)
                s_av.inc(pe.matmul(z_psb[64:128, :],
                                   vres[:, 512 * j + 128 * hp + 64:
                                        512 * j + 128 * (hp + 1)],
                                   pT[:, pj * 2 * QT + QT:(pj + 1) * 2 * QT],
                                   start=st_, stop=sp_, tile_position=(0, 64)))
                if ev0 is not None:
                    s_ev2.wait(pe, ev2 - 1)
                s_av.inc(pe.matmul(l_psa[0:1, :], on_bf[:, 0:1],
                                   pT[:, pj * 2 * QT:pj * 2 * QT + QT],
                                   start=st_, stop=sp_, tile_position=(0, 0)))
                if ev0 is not None:
                    s_ev2.wait(pe, ev2)
                s_av.inc(pe.matmul(l_psb[64:65, :], on_bf[:, 0:1],
                                   pT[:, pj * 2 * QT + QT:(pj + 1) * 2 * QT],
                                   start=st_, stop=sp_, tile_position=(0, 64)))
                av_hist[k] = s_av.n
                if j == trips - 1:
                    pend = (t, hp, av_hist[k], gcount)
                    gcount += 1
            emit_evac(pend[2], pend[3])
            emit_store(pend[0], pend[1], pend[3])
        s_st.wait(gps)
    wcctx.close()
    if stop_after == "P5":
        return finish()

    syncpoint()
    with contextlib.ExitStack() as p7ctx:
        SB7 = lambda name, shape, dt=F32: p7ctx.enter_context(
            nc.sbuf_tensor(name, list(shape), dt))
        zfm = SB7("zfm", [128, 4 * Tc], BF16)
        with nc.sbuf_tensor("zn_f", [128, Tc], F32) as zn_f, \
             nc.sbuf_tensor("li_ab", [2, Tc], F32) as li_ab, \
             nc.sbuf_tensor("rcp_t", [128, 512], F32) as rcp_t, \
             nc.psum_tensor("lb_ps", [128, 512], F32) as lb_ps:
            ldb = {}
            for hp in range(4):
                s_dv.wait(sync)
                for b_ in range(NZB):
                    if hp == 0:
                        s_cc.wait(sync, cc_zlb[b_])
                    s_ld.wait(sync)
                    s_ld.dma(sync.dma_start(
                        out=zn_f[:, 512 * b_:512 * (b_ + 1)],
                        in_=zlrb[b_].ap()[128 * hp:128 * (hp + 1), :]))
                    s_ld.dma(sync.dma_start(
                        out=li_ab[:, 512 * b_:512 * (b_ + 1)],
                        in_=zlrb[b_].ap()[512 + 2 * hp:514 + 2 * hp, :]))
                    ldb[b_] = s_ld.n
                for ns in range(NS):
                    s_ld.wait(pe, ldb[ns])
                    s_pe.inc(pe.matmul(lb_ps[:, :], selio[0:2, :],
                                       li_ab[:, 512 * ns:512 * (ns + 1)],
                                       start=True, stop=True))
                    s_pe.wait(dve)
                    s_ld.wait(dve, ldb[ns])
                    dve.reciprocal(rcp_t[:, :], lb_ps[:, :])
                    s_dv.inc(dve.always.tensor_mul(
                        zfm[:, hp * Tc + 512 * ns:hp * Tc + 512 * (ns + 1)],
                        zn_f[:, 512 * ns:512 * (ns + 1)], rcp_t[:, :]))
                    s_dv.wait(pe)
        s_dv.wait(sync)
        wo_t = SB7("wo_t", [128, 4 * DC * 128], BF16)
        for hp in range(4):
            s_in.dma(sync.dma_start(
                out=wo_t[:, hp * DC * 128:(hp + 1) * DC * 128],
                in_=a2g.ap()[128 * hp:128 * (hp + 1), :]))
        pscr = SB7("pscr7", [128, 2048])
        wr_b = SB7("wr_b", [128, 2048])
        srw_b = SB7("srw_b", [128, 1])
        with nc.sbuf_tensor("wr_r", [1, 2048], F32) as wr_r, \
             nc.psum_tensor("br_ps", [128, 512], F32) as br_ps:
            s_in.wait(sync)
            s_in.dma(sync.dma_start(out=wr_r[:, :], in_=wr_i.ap()[None, :]))
            s_in.wait(pe)
            for ns in range(4):
                s_pe.inc(pe.matmul(br_ps[:, :], on_f[0:1, :],
                                   wr_r[:, 512 * ns:512 * (ns + 1)],
                                   start=True, stop=True))
                s_pe.wait(dve)
                s_dv.inc(dve.always.tensor_copy(wr_b[:, 512 * ns:512 * (ns + 1)],
                                         br_ps[:, :]))
                s_dv.wait(pe)
        s_dv.inc(dve.always.reduce_sum(srw_b[:, :], wr_b[:, :], AX.X))
        with nc.sbuf_tensor("xt2", [128, 2 * 2048], F32) as xt2, \
             nc.sbuf_tensor("at2", [128, 2 * 2048], F32) as at2, \
             nc.sbuf_tensor("x1t", [128, 2 * 2048], F32) as x1t, \
             nc.psum_tensor("ao_psA", [128, 2048], F32) as ao_psA, \
             nc.psum_tensor("ao_psB", [128, 2048], F32) as ao_psB, \
             nc.sbuf_tensor("sa7", [128, 2], F32) as sa, \
             nc.sbuf_tensor("sb7", [128, 2], F32) as sb, \
             nc.sbuf_tensor("sc27", [128, 2], F32) as sc2:
            s_in.wait(pe)
            s_dv.wait(pe)
            xh7 = [0] * NT
            dv7 = [0] * NT
            st7 = [0] * NT
            pe7 = [0] * NT
            cta7 = [0] * NT

            def x7_fetch(tb):
                s_x.wait(sync)
                if tb >= 2:
                    s_dv.wait(sync, dv7[tb - 2])
                s_x.dma(sync.dma_start(
                    out=xt2[:, (tb % 2) * 2048:(tb % 2 + 1) * 2048],
                    in_=x_in.ap()[128 * tb:128 * (tb + 1), :]))
                xh7[tb] = s_x.n

            s_dv.wait(sync)
            s_ac.wait(sync)
            x7_fetch(0)
            x7_fetch(1)
            for tb in range(NT):
                sl = (tb % 2) * 2048
                c = tb % 2
                a2 = at2[:, sl:sl + 2048]
                x1s = x1t[:, sl:sl + 2048]
                sac = sa[:, c:c + 1]
                sbc = sb[:, c:c + 1]
                scc2 = sc2[:, c:c + 1]
                pp7 = ao_psA if tb % 2 == 0 else ao_psB
                if tb >= 2:
                    s_dv.wait(pe, cta7[tb - 2])
                for hp in range(4):
                    for dn in range(4):
                        s_pe.inc(pe.matmul(
                            pp7[:, 512 * dn:512 * (dn + 1)],
                            zfm[:, hp * Tc + 128 * tb:hp * Tc + 128 * (tb + 1)],
                            wo_t[:, hp * DC * 128 + 512 * dn:
                                 hp * DC * 128 + 512 * (dn + 1)],
                            start=(hp == 0), stop=(hp == 3)))
                pe7[tb] = s_pe.n
                s_pe.wait(dve, pe7[tb])
                s_dv.inc(dve.always.tensor_copy(a2, pp7[:, :]))
                cta = s_dv.n
                cta7[tb] = cta
                if tb >= 2:
                    s_dv.wait(act, dv7[tb - 2])
                s_dv.wait(act, cta)
                s_ac.inc(act.always.activation(pscr[:, :], a2, AF.Identity,
                                        accum_out=sac))
                s_ac.inc(act.always.activation(pscr[:, :], a2, AF.Square,
                                        accum_out=sbc))
                s_ac.wait(dve)
                dve.tensor_scalar_mul(sac, sac, 1.0 / D)
                dve.tensor_scalar_mul(sbc, sbc, 1.0 / D)
                dve.tensor_tensor(scc2, sac, sac, ALU.mult)
                s_dv.inc(dve.always.tensor_sub(sbc, sbc, scc2))
                s_dv.wait(act)
                s_ac.inc(act.always.activation(scc2, sbc, AF.Sqrt,
                                        bias=EPS_LN))
                s_ac.wait(dve)
                dve.reciprocal(scc2, scc2)
                dve.tensor_scalar(a2, a2, sac, scc2,
                                  ALU.subtract, ALU.mult)
                s_x.wait(dve, xh7[tb])
                if tb >= 2:
                    s_st.wait(dve, st7[tb - 2])
                s_dv.inc(dve.always.tensor_add(
                    x1s, xt2[:, sl:sl + 2048], a2))
                s_dv.wait(sync)
                s_st.wait(sync)
                s_st.dma(sync.dma_start(out=x1d.ap()[128 * tb:128 * (tb + 1), :],
                                        in_=x1s))
                st7[tb] = s_st.n
                ct2 = s_dv.n
                s_dv.wait(act, ct2)
                s_ac.inc(act.always.activation(pscr[:, :], x1s, AF.Identity,
                                        accum_out=sac))
                s_ac.inc(act.always.activation(pscr[:, :], x1s, AF.Square,
                                        accum_out=sbc))
                s_ac.wait(dve)
                dve.tensor_scalar_mul(m1_s[:, tb:tb + 1], sac, 1.0 / D)
                dve.tensor_scalar_mul(sbc, sbc, 1.0 / D)
                dve.tensor_tensor(scc2, m1_s[:, tb:tb + 1],
                                  m1_s[:, tb:tb + 1], ALU.mult)
                s_dv.inc(dve.always.tensor_sub(v1_s[:, tb:tb + 1], sbc, scc2))
                s_dv.wait(act)
                s_ac.inc(act.always.activation(sac, v1_s[:, tb:tb + 1],
                                        AF.Sqrt, bias=EPS_LN))
                s_ac.wait(dve)
                dve.reciprocal(r1_s[:, tb:tb + 1], sac)
                dve.tensor_mul(a2, x1s, wr_b[:, :])
                dve.reduce_sum(sac, a2, AX.X)
                dve.tensor_mul(sbc, m1_s[:, tb:tb + 1], srw_b[:, :])
                dve.tensor_sub(sac, sac, sbc)
                s_dv.inc(dve.always.tensor_mul(sc_s[:, tb:tb + 1], sac,
                                        r1_s[:, tb:tb + 1]))
                dv7[tb] = s_dv.n
                if tb + 2 < NT:
                    x7_fetch(tb + 2)
    if stop_after == "P7":
        return finish()

    syncpoint()
    with contextlib.ExitStack() as p8actx:
        SB8 = lambda name, shape, dt=F32: p8actx.enter_context(
            nc.sbuf_tensor(name, list(shape), dt))
        with nc.psum_tensor("sc_ps", [128, 128], F32) as sc_ps, \
             nc.sbuf_tensor("sc_t", [NT, 128], F32) as sc_t:
            s_dv.wait(pe)
            s_pe.inc(pe.transpose(sc_ps[0:NT, :], sc_s[:, :], ey_f[:, :]))
            s_pe.wait(dve)
            s_dv.inc(dve.always.tensor_copy(sc_t[:, :], sc_ps[0:NT, :]))
            s_dv.wait(sync)
            s_st.dma(sync.dma_start(
                out=scc.ap().rearrange("(tb p) one -> tb (p one)", p=128),
                in_=sc_t[:, :]))
        s_st.wait(gps)
        s_cc.inc(gps.always.collective_compute("AllGather", ALU.bypass, RG,
                                        [scc.ap().opt()], [scg.ap().opt()]))
        s_cc.wait(sync)
        s_st.wait(dve)
        s_st.wait(pe)
        th_lo = SB8("th_lo", [1, 1])
        th_sz = SB8("th_sz", [1, 1])
        thv = SB8("thv", [128, 1])
        cnt_s = SB8("cnt_s", [128, 1])
        nch = cfg.T // 2048
        with nc.sbuf_tensor("bcb", [128, nch * 2048], F32) as bcb, \
             nc.sbuf_tensor("bch", [128, nch * 2048], BF16) as bch, \
             nc.sbuf_tensor("cc1", [128, 1], F32) as cc1, \
             nc.sbuf_tensor("fl", [128, 1], F32) as fl, \
             nc.psum_tensor("tr1_ps", [128, 128], F32) as tr1_ps, \
             nc.sbuf_tensor("flt", [1, 128], F32) as flt, \
             nc.sbuf_tensor("mx", [1, 1], F32) as mx:
            for ch in range(nch):
                s_ld.dma(sync.dma_start(
                    out=bcb[:, 2048 * ch:2048 * (ch + 1)],
                    in_=bass.AP(scg, 2048 * ch, [[0, 128], [1, 2048]])))
            s_ld.wait(dve)
            dve.memset(th_lo[:, :], -16.0)
            s_dv.inc(dve.always.memset(th_sz[:, :], 32.0))
            for rnd in range(5):
                s_dv.wait(pe)
                s_pe.inc(pe.matmul(tr1_ps[:, 0:1], on_f[0:1, :], th_lo[:, :],
                                   start=True, stop=True))
                s_pe.inc(pe.matmul(tr1_ps[:, 1:2], on_f[0:1, :], th_sz[:, :],
                                   start=True, stop=True))
                s_pe.wait(dve)
                dve.tensor_scalar_mul(fl[:, :], tr1_ps[:, 1:2], 1.0 / 130.0)
                dve.tensor_mul(fl[:, :], fl[:, :], io1[:, :])
                dve.tensor_add(thv[:, :], fl[:, :], tr1_ps[:, 0:1])
                dve.tensor_scalar(bch[:, :], bcb[:, :], thv[:, :], None,
                                  ALU.is_gt, op1=ALU.add,
                                  accum_out=cc1[:, :])
                s_dv.inc(dve.always.tensor_copy(cnt_s[:, :], cc1[:, :]))
                dve.tensor_scalar(fl[:, :], cnt_s[:, :],
                                  float(cfg.KSEL) - 0.5, None, ALU.is_gt)
                s_dv.inc(dve.always.tensor_mul(fl[:, :], fl[:, :], io1[:, :]))
                s_dv.wait(pe)
                s_pe.inc(pe.transpose(tr1_ps[0:1, :], fl[:, :], ey_f[:, :]))
                s_pe.wait(dve)
                dve.tensor_copy(flt[:, :], tr1_ps[0:1, :])
                dve.reduce_max(mx[:, :], flt[:, :], AX.X)
                dve.tensor_scalar_mul(mx[:, :], mx[:, :], 1.0 / 130.0)
                dve.tensor_mul(mx[:, :], mx[:, :], th_sz[:, :])
                dve.tensor_add(th_lo[:, :], th_lo[:, :], mx[:, :])
                s_dv.inc(dve.always.tensor_scalar_mul(th_sz[:, :], th_sz[:, :],
                                               1.0 / 130.0))
        with nc.psum_tensor("px_ps", [128, 128], F32) as px_ps, \
             nc.sbuf_tensor("thb", [128, 1], F32) as thb, \
             nc.sbuf_tensor("rs1", [128, 1], F32) as rs1, \
             nc.sbuf_tensor("cs16", [16, 1], F32) as cs16, \
             nc.sbuf_tensor("pre16", [16, 1], F32) as pre16, \
             nc.sbuf_tensor("pre16t", [1, 16], F32) as pre16t, \
             nc.sbuf_tensor("preb", [128, NT], F32) as preb, \
             nc.sbuf_tensor("cnt1", [1, 1], F32) as cnt1, \
             nc.sbuf_tensor("cnt8", [8, 1], F32) as cnt8, \
             nc.sbuf_tensor("base1", [1, 1], F32) as base1, \
             nc.sbuf_tensor("pos_s", [128, NT], F32) as pos_s:
            s_dv.wait(pe)
            s_pe.inc(pe.matmul(px_ps[:, 0:1], on_f[0:1, :], th_lo[:, :],
                               start=True, stop=True))
            s_pe.wait(dve)
            s_dv.inc(dve.always.tensor_copy(thb[:, :], px_ps[:, 0:1]))
            dve.tensor_scalar(msk_s[:, :], sc_s[:, :], thb[:, :], None,
                              ALU.is_gt)
            s_dv.wait(act)
            s_ac.inc(act.always.activation(gate_s[:, :], sc_s[:, :], AF.Sigmoid))
            s_dv.inc(dve.always.reduce_sum(rs1[:, :], msk_s[:, :], AX.X))
            s_dv.wait(pe)
            s_pe.inc(pe.matmul(px_ps[:, 0:NT], u128[:, :], msk_s[:, :],
                               start=True, stop=True))
            s_pe.inc(pe.matmul(px_ps[0:NT, NT:NT + 1], msk_s[:, :], on_f[:, 0:1],
                               start=True, stop=True))
            s_pe.inc(pe.matmul(px_ps[0:1, NT + 1:NT + 2], rs1[:, :], on_f[:, 0:1],
                               start=True, stop=True))
            s_pe.wait(dve)
            dve.tensor_copy(pos_s[:, :], px_ps[:, 0:NT])
            dve.tensor_copy(cs16[0:NT, :], px_ps[0:NT, NT:NT + 1])
            s_dv.inc(dve.always.tensor_copy(cnt1[:, :], px_ps[0:1, NT + 1:NT + 2]))
            s_dv.wait(sync)
            s_st.dma(sync.dma_start(out=cntc.ap(), in_=cnt1[:, :]))
            s_st.wait(gps)
            s_cc.inc(gps.always.collective_compute("AllGather", ALU.bypass, RG,
                                            [cntc.ap().opt()], [cntg.ap().opt()]))
            s_cc.wait(sync)
            s_ld.dma(sync.dma_start(out=cnt8[:, :], in_=cntg.ap()))
            s_ld.wait(dve)
            s_dv.inc(dve.always.tensor_mul(cnt8[:, :], cnt8[:, :], mc_s[:, :]))
            s_dv.wait(pe)
            s_pe.inc(pe.matmul(px_ps[0:1, 0:1], cnt8[:, :], on_f[0:8, 0:1],
                               start=True, stop=True))
            s_pe.inc(pe.matmul(px_ps[0:NT, 1:2], u16[0:NT, 0:NT], cs16[0:NT, :],
                               start=True, stop=True))
            s_pe.wait(dve)
            dve.tensor_copy(base1[:, :], px_ps[0:1, 0:1])
            s_dv.inc(dve.always.tensor_copy(pre16[0:NT, :], px_ps[0:NT, 1:2]))
            s_dv.wait(pe)
            s_pe.inc(pe.transpose(px_ps[0:1, 0:NT], pre16[0:NT, :],
                                  ey_f[0:NT, 0:NT]))
            s_pe.wait(dve)
            s_dv.inc(dve.always.tensor_copy(pre16t[:, 0:NT], px_ps[0:1, 0:NT]))
            s_dv.wait(pe)
            s_pe.inc(pe.matmul(px_ps[:, 0:NT], on_f[0:1, :], pre16t[:, 0:NT],
                               start=True, stop=True))
            s_pe.inc(pe.matmul(px_ps[:, NT:NT + 1], on_f[0:1, :], base1[:, :],
                               start=True, stop=True))
            s_pe.wait(dve)
            dve.tensor_copy(preb[:, :], px_ps[:, 0:NT])
            s_dv.inc(dve.always.tensor_copy(thb[:, :], px_ps[:, NT:NT + 1]))
            dve.tensor_add(pos_s[:, :], pos_s[:, :], preb[:, :])
            dve.tensor_scalar(pos_s[:, :], pos_s[:, :], thb[:, :], None,
                              ALU.add)
            BIGI = 524288.0
            dve.tensor_scalar(pos_s[:, :], pos_s[:, :], -BIGI, None, ALU.add)
            dve.tensor_mul(pos_s[:, :], pos_s[:, :], msk_s[:, :])
            dve.tensor_scalar(pos_s[:, :], pos_s[:, :], BIGI, None, ALU.add)
            s_dv.inc(dve.always.tensor_copy(idx_s[:, :], pos_s[:, :]))
        with nc.sbuf_tensor("x1b", [128, 2 * 2048], F32) as x1b, \
             nc.sbuf_tensor("hrow", [128, 2 * 2048], F32) as hrow, \
             nc.sbuf_tensor("aux2", [128, 4], F32) as aux2:
            s_ac.wait(dve)
            s_pe.wait(sync)
            s_dv.wait(sync)
            xh9 = [0] * NT
            dvh9 = [0] * NT
            sch9 = [0] * NT

            def s9_fetch(tb):
                s_x.wait(sync)
                if tb >= 2:
                    s_dv.wait(sync, dvh9[tb - 2])
                s_x.dma(sync.dma_start(
                    out=x1b[:, (tb % 2) * 2048:(tb % 2 + 1) * 2048],
                    in_=x1d.ap()[128 * tb:128 * (tb + 1), :]))
                xh9[tb] = s_x.n

            s9_fetch(0)
            s9_fetch(1)
            for tb in range(NT):
                sl = (tb % 2) * 2048
                s_x.wait(dve, xh9[tb])
                if tb >= 2:
                    s_sca.wait(dve, sch9[tb - 2])
                dve.tensor_scalar(hrow[:, sl:sl + 2048],
                                  x1b[:, sl:sl + 2048], m1_s[:, tb:tb + 1],
                                  r1_s[:, tb:tb + 1], ALU.subtract, ALU.mult)
                dve.tensor_copy(aux2[:, (tb % 2) * 2:(tb % 2) * 2 + 1],
                                gt_s[:, tb:tb + 1])
                s_dv.inc(dve.always.tensor_copy(
                    aux2[:, (tb % 2) * 2 + 1:(tb % 2) * 2 + 2],
                    gate_s[:, tb:tb + 1]))
                dvh9[tb] = s_dv.n
                s_dv.wait(gps, dvh9[tb])
                if tb == 0:
                    s_zz.wait(gps)
                s_sca.wait(gps)
                s_sca.dma(gps.indirect_dma_start(
                    out=selhc.ap(),
                    out_offset=bass.IndirectOffsetOnAxis(
                        ap=idx_s[:, tb:tb + 1], axis=0),
                    in_=hrow[:, sl:sl + 2048], in_offset=None,
                    bounds_check=bnd_reg, oob_is_err=False))
                s_sca.dma(gps.indirect_dma_start(
                    out=selxc.ap(),
                    out_offset=bass.IndirectOffsetOnAxis(
                        ap=idx_s[:, tb:tb + 1], axis=0),
                    in_=aux2[:, (tb % 2) * 2:(tb % 2) * 2 + 2], in_offset=None,
                    bounds_check=bnd_reg, oob_is_err=False))
                sch9[tb] = s_sca.n
                if tb + 2 < NT:
                    s9_fetch(tb + 2)
        s_sca.wait(gps)
        s_cc.inc(gps.always.collective_compute("ReduceScatter", ALU.add, RG,
                                        [selhc.ap().opt()], [selhr.ap().opt()]))
        s_cc.inc(gps.always.collective_compute("ReduceScatter", ALU.add, RG,
                                        [selxc.ap().opt()], [selxr.ap().opt()]))
        cc_sel = s_cc.n
    if stop_after == "P8a":
        return finish()

    syncpoint()
    with contextlib.ExitStack() as p8ctx:
        SB8b = lambda name, shape, dt=F32: p8ctx.enter_context(
            nc.sbuf_tensor(name, list(shape), dt))
        s_cc.wait(sync, cc_sel)
        s_cc.wait(sync, cc_ffnw[0])
        sel_t = SB8b("sel_t", [128, NKT * 2048])
        posv = SB8b("posv", [1, KC])
        gat_v = SB8b("gat_v", [128, NKT])
        for kt in range(NKT):
            s_ld.dma(sync.dma_start(out=sel_t[:, 2048 * kt:2048 * (kt + 1)],
                                    in_=selhr.ap()[128 * kt:128 * (kt + 1), :]))
            s_ld.dma(sync.dma_start(out=gat_v[:, kt:kt + 1],
                                    in_=selxr.ap()[128 * kt:128 * (kt + 1),
                                                   1:2],
                                    allow_slow_non_contiguous=True))
        s_ld.dma(sync.dma_start(
            out=posv[:, :],
            in_=selxr.ap()[:, 0:1].rearrange("r one -> one r"),
            allow_slow_non_contiguous=True))
        s_ld.wait(sync)
        ln1_b = SB8b("ln1_b", [128, 2048])
        ln2_b = SB8b("ln2_b", [128, 2048])
        with nc.sbuf_tensor("lnra", [1, 2048], F32) as lnra, \
             nc.sbuf_tensor("lnrb", [1, 2048], F32) as lnrb, \
             nc.psum_tensor("lb2_ps", [128, 512], F32) as lb2_ps:
            s_in.dma(sync.dma_start(out=lnra[:, :], in_=ln1_i.ap()[None, :]))
            s_in.dma(sync.dma_start(out=lnrb[:, :], in_=ln2_i.ap()[None, :]))
            s_in.wait(pe)
            for (lr, dst) in [(lnra, ln1_b), (lnrb, ln2_b)]:
                for ns in range(4):
                    s_pe.inc(pe.matmul(lb2_ps[:, :], on_f[0:1, :],
                                       lr[:, 512 * ns:512 * (ns + 1)],
                                       start=True, stop=True))
                    s_pe.wait(dve)
                    s_dv.inc(dve.always.tensor_copy(dst[:, 512 * ns:512 * (ns + 1)],
                                             lb2_ps[:, :]))
                    s_dv.wait(pe)
        s_in.wait(sync)
        for eng in (dve, act, gps):
            s_in.wait(eng)
            s_pe.wait(eng)
        pscr = SB8b("pscr8b", [128, 2048])
        mid = contextlib.ExitStack()
        SBm = lambda name, shape, dt=F32: mid.enter_context(
            nc.sbuf_tensor(name, list(shape), dt))
        h1f = SBm("h1f", [128, DC * KC], BF16)
        with nc.sbuf_tensor("h1t", [128, 2048], BF16) as h1t, \
             nc.sbuf_tensor("ra", [128, 1], F32) as ra, \
             nc.psum_tensor("ht_ps", [128, 128], BF16) as ht_ps:
            for kt in range(NKT):
                s_ld.wait(act)
                s_ld.wait(dve)
                s_ac.inc(act.always.activation(pscr[:, :],
                                        sel_t[:, 2048 * kt:2048 * (kt + 1)],
                                        AF.Square, accum_out=ra[:, :]))
                s_ac.wait(dve)
                s_dv.inc(dve.always.tensor_scalar_mul(ra[:, :], ra[:, :], 1.0 / D))
                s_dv.wait(act)
                s_ac.inc(act.always.activation(ra[:, :], ra[:, :], AF.Sqrt,
                                        bias=EPS_RMS))
                s_ac.wait(dve)
                dve.reciprocal(ra[:, :], ra[:, :])
                s_pe.wait(dve)
                s_dv.inc(dve.always.scalar_tensor_tensor(
                    h1t[:, :], sel_t[:, 2048 * kt:2048 * (kt + 1)], ra[:, :],
                    ln1_b[:, :], ALU.mult, ALU.mult))
                s_dv.wait(pe)
                for dc in range(DC):
                    s_pe.inc(pe.transpose(ht_ps[:, :],
                                          h1t[:, 128 * dc:128 * (dc + 1)],
                                          ey_bf[:, :]))
                    s_pe.wait(dve)
                    s_dv.inc(dve.always.tensor_copy(
                        h1f[:, KC * dc + 128 * kt:KC * dc + 128 * (kt + 1)],
                        ht_ps[:, :]))
                    s_dv.wait(pe)
        cosf = SBm("cosf", [128, KC])
        sinf = SBm("sinf", [128, KC])
        sneg = SBm("sneg", [128, KC])
        with nc.psum_tensor("an_ps", [128, KC], F32) as an_ps, \
             nc.sbuf_tensor("ang", [128, KC], F32) as ang, \
             nc.sbuf_tensor("ang2", [128, KC], F32) as ang2, \
             nc.sbuf_tensor("nn", [128, KC], F32) as nn, \
             nc.sbuf_tensor("nni", [128, KC], I32) as nni:
            s_ld.wait(pe)
            for ns in range(max(1, KC // 512)):
                w = min(512, KC)
                s_pe.inc(pe.matmul(an_ps[:, w * ns:w * (ns + 1)], ivf[:, :],
                                   posv[:, w * ns:w * (ns + 1)],
                                   start=True, stop=True))
            s_pe.wait(dve)
            s_dv.inc(dve.always.tensor_copy(ang[:, :], an_ps[:, :]))

            def range_reduce_sin(src_ap, out_ap):
                dve.tensor_scalar_mul(nn[:, :], src_ap, 1.0 / TWO_PI)
                dve.tensor_scalar(nn[:, :], nn[:, :], 0.5, None, ALU.add)
                dve.tensor_copy(nni[:, :], nn[:, :])
                dve.tensor_copy(nn[:, :], nni[:, :])
                dve.scalar_tensor_tensor(ang2[:, :], nn[:, :], -CW_C1, src_ap,
                                         ALU.mult, ALU.add)
                s_dv.inc(dve.always.scalar_tensor_tensor(ang2[:, :], nn[:, :], -CW_C2,
                                                  ang2[:, :], ALU.mult, ALU.add))
                s_dv.wait(act)
                s_ac.inc(act.always.activation(out_ap, ang2[:, :], AF.Sin))
                s_ac.wait(dve)

            range_reduce_sin(ang[:, :], sinf[:, :])
            dve.tensor_scalar(ang[:, :], ang[:, :], math.pi / 2.0, None,
                              ALU.add)
            range_reduce_sin(ang[:, :], cosf[:, :])
            dve.tensor_scalar_mul(sneg[0:64, :], sinf[0:64, :], -1.0)
            s_dv.inc(dve.always.tensor_copy(sneg[64:128, :], sinf[64:128, :]))
        fqt = SBm("fqt", [128, FH * KC], BF16)
        fkt = SBm("fkt", [128, HE * KC], BF16)
        fvt = SBm("fvt", [128, NKT * 1024], BF16)
        bq_s = SBm("bq_s", [128, DC])
        bkv_s = SBm("bkv_s", [128, 16])
        with nc.sbuf_tensor("btmp", [16, 128], F32) as btmp, \
             nc.psum_tensor("bt_ps", [128, 16], F32) as bt_ps:
            s_dv.wait(pe)
            s_ac.wait(pe)
            s_in.dma(sync.dma_start(
                out=btmp[:, :], in_=bq_i.ap().rearrange("(k p) -> k p", p=128)))
            s_in.wait(pe)
            s_pe.inc(pe.transpose(bt_ps[:, :], btmp[:, :], ey_f[0:16, 0:16]))
            s_pe.wait(dve)
            s_dv.inc(dve.always.tensor_copy(bq_s[:, :], bt_ps[:, :]))
            s_dv.wait(pe)
            s_pe.wait(sync)
            s_in.dma(sync.dma_start(
                out=btmp[0:8, :], in_=bk_i.ap().rearrange("(k p) -> k p", p=128)))
            s_in.wait(pe)
            s_pe.inc(pe.transpose(bt_ps[:, 0:8], btmp[0:8, :], ey_f[0:8, 0:8]))
            s_pe.wait(dve)
            s_dv.inc(dve.always.tensor_copy(bkv_s[:, 0:8], bt_ps[:, 0:8]))
            s_dv.wait(pe)
            s_pe.wait(sync)
            s_in.dma(sync.dma_start(
                out=btmp[0:8, :], in_=bv_i.ap().rearrange("(k p) -> k p", p=128)))
            s_in.wait(pe)
            s_pe.inc(pe.transpose(bt_ps[:, 0:8], btmp[0:8, :], ey_f[0:8, 0:8]))
            s_pe.wait(dve)
            s_dv.inc(dve.always.tensor_copy(bkv_s[:, 8:16], bt_ps[:, 0:8]))
            s_dv.wait(pe)
        with nc.sbuf_tensor("fw_t", [128, 2 * 2048], BF16) as fw_t, \
             nc.psum_tensor("fp_ps", [128, 1024], F32) as fp_ps, \
             nc.psum_tensor("fr_ps", [128, 128], BF16) as fr_ps, \
             nc.sbuf_tensor("prj", [128, KC], F32) as prj, \
             nc.sbuf_tensor("prh", [128, KC], F32) as prh, \
             nc.sbuf_tensor("sq_t", [128, KC], F32) as sq_t, \
             nc.sbuf_tensor("qsw", [128, KC], F32) as qsw, \
             nc.sbuf_tensor("vtmp", [128, KC], BF16) as vtmp:
            s_in.wait(dve)
            calls = ([(FW_WQ, h, bq_s, h, "q") for h in range(FH)]
                     + [(FW_WK, h, bkv_s, h, "k") for h in range(HE)]
                     + [(FW_WV, h, bkv_s, 8 + h, "v") for h in range(HE)])
            NCALL = len(calls)
            ldc = [0] * NCALL
            pe_after = [0] * NCALL
            dv_after = [0] * NCALL

            def load_call(n):
                col0, h = calls[n][0], calls[n][1]
                sl = (n % 2) * 2048
                s_ld.wait(sync)
                if n >= 2:
                    s_pe.wait(sync, pe_after[n - 2])
                gcol = col0 + 128 * h
                s_ld.dma(sync.dma_start(
                    out=fw_t[:, sl:sl + 2048].rearrange("p (dc k) -> p dc k",
                                                        dc=DC),
                    in_=fw1gAs[gcol // 2048]
                    .ap()[:, gcol % 2048:gcol % 2048 + 128]
                    .rearrange("(dc p) k -> p dc k", p=128)))
                ldc[n] = s_ld.n

            def rope_to(dst_ap):
                dve.tensor_copy(qsw[0:64, :], prj[64:128, :])
                dve.tensor_copy(qsw[64:128, :], prj[0:64, :])
                dve.tensor_mul(prh[:, :], prj[:, :], cosf[:, :])
                dve.tensor_mul(sq_t[:, :], qsw[:, :], sneg[:, :])
                s_dv.inc(dve.always.tensor_add(dst_ap, prh[:, :], sq_t[:, :]))

            s_in.wait(sync)
            s_pe.wait(sync)
            s_dv.wait(sync)
            load_call(0)
            for n in range(NCALL):
                col0, h, bias_col, bias_idx, kind = calls[n]
                if n + 1 < NCALL:
                    load_call(n + 1)
                sl = (n % 2) * 2048
                ps = fp_ps[:, (n % 2) * 512:(n % 2) * 512 + KC]
                s_ld.wait(pe, ldc[n])
                if n >= 2:
                    s_dv.wait(pe, dv_after[n - 2])
                for dc in range(DC):
                    s_pe.inc(pe.matmul(ps,
                                       fw_t[:, sl + 128 * dc:sl + 128 * (dc + 1)],
                                       h1f[:, KC * dc:KC * (dc + 1)],
                                       start=(dc == 0), stop=(dc == DC - 1)))
                pe_after[n] = s_pe.n
                s_pe.wait(dve, pe_after[n])
                s_dv.inc(dve.always.tensor_scalar(prj[:, :], ps,
                                           bias_col[:, bias_idx:bias_idx + 1],
                                           None, ALU.add))
                dv_after[n] = s_dv.n
                if kind == "q":
                    rope_to(fqt[:, KC * h:KC * (h + 1)])
                    if h == FH - 1:
                        s_dv.wait(sync)
                        s_st.wait(sync)
                        for h2 in range(FH):
                            s_st.dma(sync.dma_start(
                                out=fqc.ap()[128 * h2:128 * (h2 + 1), :],
                                in_=fqt[:, KC * h2:KC * (h2 + 1)]))
                        s_st.wait(gps)
                        s_cc.inc(gps.always.collective_compute(
                            "AllGather", ALU.bypass, RG,
                            [fqc.ap().opt()], [fqg.ap().opt()]))
                        cc_fq = s_cc.n
                elif kind == "k":
                    rope_to(fkt[:, KC * h:KC * (h + 1)])
                else:
                    s_pe.wait(dve)
                    s_dv.inc(dve.always.tensor_copy(vtmp[:, :], prj[:, :]))
                    s_dv.wait(pe)
                    for kt in range(NKT):
                        s_pe.inc(pe.transpose(fr_ps[:, :],
                                              vtmp[:, 128 * kt:128 * (kt + 1)],
                                              ey_bf[:, :]))
                        s_pe.wait(dve)
                        s_dv.inc(dve.always.tensor_copy(
                            fvt[:, 1024 * kt + 128 * h:
                                1024 * kt + 128 * (h + 1)],
                            fr_ps[:, :]))
                        s_dv.wait(pe)

        syncpoint()
        s_cc.wait(sync, cc_fq)
        fqh = SBm("fqh", [128, 2 * cfg.KSEL], BF16)
        amk = SBm("amk", [128, NKT * QH * QHW], BF16)
        for sidx in range(NKT * QH):
            s_ld.dma(sync.dma_start(out=amk[:, QHW * sidx:QHW * (sidx + 1)],
                                    in_=am_i.ap()[sidx, :, :]))
        s_ld.wait(sync)
        with nc.psum_tensor("fsc_ps", [128, 2 * QHW], F32) as fsc_ps, \
             nc.psum_tensor("fz_ps", [128, QHW], F32) as fz_ps, \
             nc.psum_tensor("fl_ps", [1, QHW], F32) as fl_ps, \
             nc.sbuf_tensor("fpT", [128, 2 * QHW], BF16) as fpT, \
             nc.sbuf_tensor("fzs", [128, QHW], F32) as fzs, \
             nc.sbuf_tensor("fls", [1, QHW], F32) as fls:

            def fq_load(h):
                s_fqp[h % 2].dma(sync.dma_start(
                    out=fqh[:, (h % 2) * cfg.KSEL:(h % 2 + 1) * cfg.KSEL]
                    .rearrange("p (o k) -> p o k", o=NCORES),
                    in_=fqg.ap().rearrange("(o hh p) k -> hh p o k",
                                           o=NCORES, p=128)[h]))

            fvisits = [(h, qh, jj) for h in range(FH) for qh in range(QH)
                       for jj in range(NKT)]
            NFV = len(fvisits)
            ex_base = s_ex.n
            fq_ld = [0] * FH
            fq_load(0)
            fq_ld[0] = s_fqp[0].n
            fmk_hist = [0] * NFV
            fav_hist = [0] * NFV
            fsc_hist = [0] * NFV
            fsc_end = []

            def femit_scores(k):
                h_, qh_, jj_ = fvisits[k]
                kvh_ = h_ // 2
                off = (k % 2) * QHW
                s_fqp[h_ % 2].wait(pe, fq_ld[h_])
                if k >= 2:
                    s_ex.wait(pe, ex_base + k - 1)
                for ns in range(QHW // 512):
                    s_sc.inc(pe.matmul(
                        fsc_ps[:, off + 512 * ns:off + 512 * (ns + 1)],
                        fkt[:, KC * kvh_ + 128 * jj_:KC * kvh_ + 128 * (jj_ + 1)],
                        fqh[:, (h_ % 2) * cfg.KSEL + QHW * qh_ + 512 * ns:
                            (h_ % 2) * cfg.KSEL + QHW * qh_ + 512 * (ns + 1)],
                        start=True, stop=True))
                fsc_hist[k] = s_sc.n

            def femit_evac(av_t):
                s_av.wait(act, av_t)
                s_st.wait(act)
                s_ev2.inc(act.copy(fzs[:, :], fz_ps[:, :]))
                s_ev2.inc(act.copy(fls[:, :], fl_ps[0:1, :]))

            def femit_store(h_p, qh_p):
                s_ev2.wait(sync)
                ft_ = fzlcA if h_p < FH // 2 else fzlcB
                hl_ = h_p % (FH // 2)
                for cons in range(cfg.NCONS):
                    r_ = (QHW * qh_p) // KC + cons
                    s_st.dma(sync.dma_start(
                        out=ft_.ap()[FZR * r_ + 128 * hl_:
                                     FZR * r_ + 128 * (hl_ + 1), :],
                        in_=fzs[:, KC * cons:KC * (cons + 1)]))
                    s_st.dma(sync.dma_start(
                        out=ft_.ap()[FZR * r_ + 1024 + hl_:
                                     FZR * r_ + 1025 + hl_, :],
                        in_=fls[:, KC * cons:KC * (cons + 1)]))
                if h_p == FH // 2 - 1 and qh_p == QH - 1:
                    s_st.wait(gps)
                    s_cc.inc(gps.always.collective_compute(
                        "ReduceScatter", ALU.add, RG,
                        [fzlcA.ap().opt()], [fzlrA.ap().opt()]))
                    cc_fzlh[0] = s_cc.n

            fpend = None
            cc_fzlh = [0, 0]
            femit_scores(0)
            for k in range(NFV):
                h, qh, jj = fvisits[k]
                kvh = h // 2
                off = (k % 2) * QHW
                if qh == 0 and jj == 0 and h + 1 < FH:
                    if h >= 1:
                        s_sc.wait(sync, fsc_end[h - 1])
                    fq_load(h + 1)
                    fq_ld[h + 1] = s_fqp[(h + 1) % 2].n
                if k + 1 < NFV:
                    femit_scores(k + 1)
                s_sc.wait(act, fsc_hist[k])
                if k >= 2:
                    s_av.wait(act, fav_hist[k - 2])
                s_ex.inc(act.activation(fpT[:, off:off + QHW],
                                        fsc_ps[:, off:off + QHW], AF.Exp,
                                        scale=1.0 / math.sqrt(FDH),
                                        bias=-C_FFN))
                if fpend is not None:
                    femit_evac(fpend[2])
                    femit_store(fpend[0], fpend[1])
                    fpend = None
                s_ex.wait(dve, s_ex.n)
                s_ld.wait(dve)
                s_mk.inc(dve.tensor_mul(
                    fpT[:, off:off + QHW], fpT[:, off:off + QHW],
                    amk[:, QHW * (jj * QH + qh):QHW * (jj * QH + qh + 1)]))
                fmk_hist[k] = s_mk.n
                s_mk.wait(pe, fmk_hist[k])
                fev0 = s_ev2.n if (jj == 0 and k > 0) else None
                st_ = (jj == 0)
                sp_ = (jj == NKT - 1)
                for ns in range(QHW // 512):
                    if fev0 is not None:
                        s_ev2.wait(pe, fev0 - 1)
                    s_av.inc(pe.matmul(
                        fz_ps[:, 512 * ns:512 * (ns + 1)],
                        fvt[:, 1024 * jj + 128 * kvh:1024 * jj + 128 * (kvh + 1)],
                        fpT[:, off + 512 * ns:off + 512 * (ns + 1)],
                        start=st_, stop=sp_))
                    if fev0 is not None:
                        s_ev2.wait(pe, fev0)
                        fev0 = None
                    s_av.inc(pe.matmul(
                        fl_ps[0:1, 512 * ns:512 * (ns + 1)], on_bf[:, 0:1],
                        fpT[:, off + 512 * ns:off + 512 * (ns + 1)],
                        start=st_, stop=sp_, tile_position=(0, 0)))
                fav_hist[k] = s_av.n
                if jj == NKT - 1:
                    fpend = (h, qh, fav_hist[k])
                    if qh == QH - 1:
                        fsc_end.append(s_sc.n)
            femit_evac(fpend[2])
            femit_store(fpend[0], fpend[1])
        s_st.wait(gps)
        s_cc.inc(gps.always.collective_compute("ReduceScatter", ALU.add, RG,
                                        [fzlcB.ap().opt()], [fzlrB.ap().opt()]))
        cc_fzlh[1] = s_cc.n
        mid.close()
        fzn = SB8b("fzn", [128, FH * KC], BF16)
        x1f = SB8b("x1f", [128, NKT * 2048])
        woctx = contextlib.ExitStack()
        fw_o = woctx.enter_context(nc.sbuf_tensor("fw_o", [128, 2 * 2048],
                                                  BF16))
        ldc4 = [0] * DC
        pe_af4 = [0] * DC

        def load_o(n):
            sl = (n % 2) * 2048
            s_ld.wait(sync)
            if n >= 2:
                s_pe.wait(sync, pe_af4[n - 2])
            s_ld.dma(sync.dma_start(
                out=fw_o[:, sl:sl + 2048].rearrange("p (zc k) -> p zc k",
                                                    zc=FH),
                in_=fw1gAs[2].ap()[0:2048, 128 * n:128 * (n + 1)]
                .rearrange("(zc p) k -> p zc k", p=128)))
            ldc4[n] = s_ld.n

        load_o(0)
        load_o(1)
        with nc.sbuf_tensor("fzl_t", [128, KC], F32) as fzl_t, \
             nc.sbuf_tensor("fli", [1, KC], F32) as fli, \
             nc.psum_tensor("fb_ps", [128, KC], F32) as fb_ps:
            for h in range(FH):
                fr_ = fzlrA if h < FH // 2 else fzlrB
                hl2 = h % (FH // 2)
                if h == 0:
                    s_cc.wait(sync, cc_fzlh[0])
                if h == FH // 2:
                    s_cc.wait(sync, cc_fzlh[1])
                s_dv.wait(sync)
                s_ld.wait(sync)
                s_ld.dma(sync.dma_start(
                    out=fzl_t[:, :],
                    in_=fr_.ap()[128 * hl2:128 * (hl2 + 1), :]))
                s_ld.dma(sync.dma_start(
                    out=fli[:, :],
                    in_=fr_.ap()[1024 + hl2:1025 + hl2, :]))
                s_ld.wait(sync)
                s_ld.wait(dve)
                s_dv.inc(dve.always.reciprocal(fli[:, :], fli[:, :]))
                s_dv.wait(pe)
                for ns in range(max(1, KC // 512)):
                    w = min(512, KC)
                    s_pe.inc(pe.matmul(fb_ps[:, w * ns:w * (ns + 1)],
                                       on_f[0:1, :], fli[:, w * ns:w * (ns + 1)],
                                       start=True, stop=True))
                s_pe.wait(dve)
                s_dv.inc(dve.always.tensor_mul(fzn[:, KC * h:KC * (h + 1)],
                                        fzl_t[:, :], fb_ps[:, :]))
                s_dv.wait(pe)
        with nc.psum_tensor("fo_ps", [128, KC], F32) as fo_ps, \
             nc.psum_tensor("fo_tr", [128, 128], F32) as fo_tr, \
             nc.sbuf_tensor("ao2", [128, KC], F32) as ao2:
            for dc in range(DC):
                sl = (dc % 2) * 2048
                s_ld.wait(pe, ldc4[dc])
                s_dv.wait(pe)
                for zc in range(FH):
                    s_pe.inc(pe.matmul(fo_ps[:, :],
                                       fw_o[:, sl + 128 * zc:sl + 128 * (zc + 1)],
                                       fzn[:, KC * zc:KC * (zc + 1)],
                                       start=(zc == 0), stop=(zc == FH - 1)))
                pe_af4[dc] = s_pe.n
                if dc + 2 < DC:
                    load_o(dc + 2)
                s_pe.wait(dve)
                s_dv.inc(dve.always.tensor_copy(ao2[:, :], fo_ps[:, :]))
                s_dv.wait(pe)
                for kt in range(NKT):
                    s_pe.inc(pe.transpose(fo_tr[:, :],
                                          ao2[:, 128 * kt:128 * (kt + 1)],
                                          ey_f[:, :]))
                    s_pe.wait(dve)
                    s_dv.inc(dve.always.tensor_copy(
                        x1f[:, 2048 * kt + 128 * dc:2048 * kt + 128 * (dc + 1)],
                        fo_tr[:, :]))
                    s_dv.wait(pe)
        s_ld.wait(sync)
        s_pe.wait(sync)
        woctx.close()
        for kt in range(NKT):
            s_dv.inc(dve.always.tensor_add(x1f[:, 2048 * kt:2048 * (kt + 1)],
                                    x1f[:, 2048 * kt:2048 * (kt + 1)],
                                    sel_t[:, 2048 * kt:2048 * (kt + 1)]))
        h2f = SB8b("h2f", [128, DC * KC], BF16)
        with nc.sbuf_tensor("h2t", [128, 2048], BF16) as h2t, \
             nc.sbuf_tensor("rb", [128, 1], F32) as rb, \
             nc.psum_tensor("h2_ps", [128, 128], BF16) as h2_ps:
            for kt in range(NKT):
                s_dv.wait(act)
                s_ac.inc(act.always.activation(pscr[:, :],
                                        x1f[:, 2048 * kt:2048 * (kt + 1)],
                                        AF.Square, accum_out=rb[:, :]))
                s_ac.wait(dve)
                s_dv.inc(dve.always.tensor_scalar_mul(rb[:, :], rb[:, :], 1.0 / D))
                s_dv.wait(act)
                s_ac.inc(act.always.activation(rb[:, :], rb[:, :], AF.Sqrt,
                                        bias=EPS_RMS))
                s_ac.wait(dve)
                dve.reciprocal(rb[:, :], rb[:, :])
                s_pe.wait(dve)
                s_dv.inc(dve.always.scalar_tensor_tensor(
                    h2t[:, :], x1f[:, 2048 * kt:2048 * (kt + 1)], rb[:, :],
                    ln2_b[:, :], ALU.mult, ALU.mult))
                s_dv.wait(pe)
                for dc in range(DC):
                    s_pe.inc(pe.transpose(h2_ps[:, :],
                                          h2t[:, 128 * dc:128 * (dc + 1)],
                                          ey_bf[:, :]))
                    s_pe.wait(dve)
                    s_dv.inc(dve.always.tensor_copy(
                        h2f[:, KC * dc + 128 * kt:KC * dc + 128 * (kt + 1)],
                        h2_ps[:, :]))
                    s_dv.wait(pe)
        s_dv.wait(sync)
        s_st.dma(sync.dma_start(
            out=h2c.ap().rearrange("(dc p) t -> p dc t", p=128),
            in_=h2f[:, :].rearrange("p (dc t) -> p dc t", t=KC)))
        s_st.wait(gps)
        s_cc.inc(gps.always.collective_compute("AllGather", ALU.bypass, RG,
                                        [h2c.ap().opt()], [h2g.ap().opt()]))
        cc_h2 = s_cc.n
        ITN = FI // NCORES // 128
        KS = cfg.KSEL
        TB2 = KS // 512
        tsb = SB8b("tsb", [128, ITN * KS], BF16)
        with nc.sbuf_tensor("h2a", [128, DC * KS], BF16) as h2a, \
             nc.sbuf_tensor("wgu", [128, 2 * 2048], BF16) as wgu, \
             nc.psum_tensor("g_ps", [128, KS], F32) as g_ps, \
             nc.psum_tensor("u_ps", [128, KS], F32) as u_ps, \
             nc.sbuf_tensor("gsb", [128, KS], BF16) as gsb, \
             nc.sbuf_tensor("usb", [128, KS], BF16) as usb:
            s_dv.wait(sync)
            s_ac.wait(sync)
            bursts = [(kind, it) for it in range(ITN) for kind in ("g", "u")]
            NB = len(bursts)
            ldc2 = [0] * NB
            pe_af2 = [0] * NB

            def load_b(n):
                kind, it = bursts[n]
                base = 0 if kind == "g" else FI // NCORES
                sl = (n % 2) * 2048
                s_ld.wait(sync)
                if n >= 2:
                    s_pe.wait(sync, pe_af2[n - 2])
                s_ld.dma(sync.dma_start(
                    out=wgu[:, sl:sl + 2048].rearrange("p (dc k) -> p dc k",
                                                       dc=DC),
                    in_=fguc.ap()[:, base + 128 * it:base + 128 * (it + 1)]
                    .rearrange("(dc p) k -> p dc k", p=128)))
                ldc2[n] = s_ld.n

            load_b(0)
            load_b(1)
            s_cc.wait(sync, cc_h2)
            s_ld.wait(sync)
            for dc in range(DC):
                s_ld.dma(sync.dma_start(
                    out=h2a[:, KS * dc:KS * (dc + 1)]
                    .rearrange("p (o t) -> p o t", o=NCORES),
                    in_=h2g.ap().rearrange("(o dc2 p) t -> dc2 p o t",
                                           dc2=DC, p=128)[dc]))
            ldh2 = s_ld.n
            s_ld.wait(sync)
            s_ld.wait(pe, ldh2)
            pe_g = 0
            for n in range(NB):
                kind, it = bursts[n]
                sl = (n % 2) * 2048
                s_ld.wait(pe, ldc2[n])
                if kind == "g":
                    s_ac.wait(pe)
                    for dc in range(DC):
                        for tb in range(TB2):
                            s_pe.inc(pe.matmul(
                                g_ps[:, 512 * tb:512 * (tb + 1)],
                                wgu[:, sl + 128 * dc:sl + 128 * (dc + 1)],
                                h2a[:, KS * dc + 512 * tb:
                                    KS * dc + 512 * (tb + 1)],
                                start=(dc == 0), stop=(dc == DC - 1)))
                    pe_g = s_pe.n
                else:
                    s_dv.wait(pe)
                    for dc in range(DC):
                        for tb in range(TB2):
                            s_pe.inc(pe.matmul(
                                u_ps[:, 512 * tb:512 * (tb + 1)],
                                wgu[:, sl + 128 * dc:sl + 128 * (dc + 1)],
                                h2a[:, KS * dc + 512 * tb:
                                    KS * dc + 512 * (tb + 1)],
                                start=(dc == 0), stop=(dc == DC - 1)))
                    pe_u = s_pe.n
                    s_pe.wait(act, pe_g)
                    s_dv.wait(act)
                    s_ac.inc(act.always.activation(gsb[:, :], g_ps[:, :], AF.Silu))
                    s_pe.wait(dve, pe_u)
                    s_ac.wait(dve)
                    dve.tensor_copy(usb[:, :], u_ps[:, :])
                    s_dv.inc(dve.always.tensor_mul(tsb[:, KS * it:KS * (it + 1)],
                                            gsb[:, :], usb[:, :]))
                pe_af2[n] = s_pe.n
                if n + 2 < NB:
                    load_b(n + 2)
        gtd = SB8b("gtd", [128, NKT * 2048])
        with nc.sbuf_tensor("fdn_s", [128, ITN * 2048], BF16) as fdn_s, \
             nc.psum_tensor("dn_psA", [128, 2048], F32) as dn_psA, \
             nc.psum_tensor("dn_psB", [128, 2048], F32) as dn_psB, \
             nc.sbuf_tensor("dnb", [128, 2048], F32) as dnb, \
             nc.sbuf_tensor("dnb2", [128, 2048], F32) as dnb2:
            s_dv.wait(sync)
            s_ac.wait(sync)
            s_pe.wait(sync)
            s_ld.wait(sync)
            for it in range(ITN):
                s_ld.dma(sync.dma_start(
                    out=fdn_s[:, 2048 * it:2048 * (it + 1)],
                    in_=fdnc.ap()[128 * it:128 * (it + 1), :]))
            ldn = s_ld.n
            s_ld.wait(pe, ldn)
            s_dv.wait(pe)
            TT = KS // 128
            SPL = NKT >= 2
            if SPL:
                tts = ([t2 for t2 in range(TT) if t2 % NKT < NKT // 2]
                       + [t2 for t2 in range(TT) if t2 % NKT >= NKT // 2])
            else:
                tts = list(range(TT))
            dn_done = [0] * TT
            st_dn = [0] * TT
            cc_dnh = [0, 0]
            for nn_, tt in enumerate(tts):
                ps = dn_psA if nn_ % 2 == 0 else dn_psB
                bufd = dnb if nn_ % 2 == 0 else dnb2
                if nn_ >= 2:
                    s_ac.wait(pe, dn_done[nn_ - 2])
                for it in range(ITN):
                    for db in range(4):
                        s_pe.inc(pe.matmul(
                            ps[:, 512 * db:512 * (db + 1)],
                            tsb[:, KS * it + 128 * tt:KS * it + 128 * (tt + 1)],
                            fdn_s[:, 2048 * it + 512 * db:
                                  2048 * it + 512 * (db + 1)],
                            start=(it == 0), stop=(it == ITN - 1)))
                s_pe.wait(act)
                if nn_ >= 2:
                    s_st.wait(act, st_dn[nn_ - 2])
                s_ac.inc(act.always.copy(bufd[:, :], ps[:, :]))
                dn_done[nn_] = s_ac.n
                s_ac.wait(sync, dn_done[nn_])
                s_st.wait(sync)
                o_, j_ = tt // NKT, tt % NKT
                if SPL and j_ >= NKT // 2:
                    dt_, ro_ = dncB, (KC // 2) * o_ + 128 * (j_ - NKT // 2)
                else:
                    dt_, ro_ = dncA, (KC // 2 if SPL else KC) * o_ + 128 * j_
                s_st.dma(sync.dma_start(
                    out=dt_.ap()[ro_:ro_ + 128, :], in_=bufd[:, :]))
                st_dn[nn_] = s_st.n
                if SPL and nn_ == TT // 2 - 1:
                    s_st.wait(gps)
                    s_cc.inc(gps.always.collective_compute(
                        "ReduceScatter", ALU.add, RG,
                        [dncA.ap().opt()], [dnrA.ap().opt()]))
                    cc_dnh[0] = s_cc.n
        s_st.wait(gps)
        if SPL:
            s_cc.inc(gps.always.collective_compute(
                "ReduceScatter", ALU.add, RG,
                [dncB.ap().opt()], [dnrB.ap().opt()]))
            cc_dnh[1] = s_cc.n
        else:
            s_cc.inc(gps.always.collective_compute(
                "ReduceScatter", ALU.add, RG,
                [dncA.ap().opt()], [dnrA.ap().opt()]))
            cc_dnh[0] = cc_dnh[1] = s_cc.n
        for kt in range(NKT):
            s_cc.wait(sync, cc_dnh[1 if SPL and kt >= NKT // 2 else 0])
            if SPL and kt >= NKT // 2:
                src_, ro_ = dnrB, 128 * (kt - NKT // 2)
            else:
                src_, ro_ = dnrA, 128 * kt
            s_ld.dma(sync.dma_start(
                out=gtd[:, 2048 * kt:2048 * (kt + 1)],
                in_=src_.ap()[ro_:ro_ + 128, :]))
        s_ld.wait(sync)
        s_ld.wait(dve)
        for kt in range(NKT):
            dve.tensor_add(gtd[:, 2048 * kt:2048 * (kt + 1)],
                           gtd[:, 2048 * kt:2048 * (kt + 1)],
                           x1f[:, 2048 * kt:2048 * (kt + 1)])
            dve.tensor_sub(gtd[:, 2048 * kt:2048 * (kt + 1)],
                           gtd[:, 2048 * kt:2048 * (kt + 1)],
                           sel_t[:, 2048 * kt:2048 * (kt + 1)])
            dve.tensor_scalar(gtd[:, 2048 * kt:2048 * (kt + 1)],
                              gtd[:, 2048 * kt:2048 * (kt + 1)],
                              gat_v[:, kt:kt + 1], None, ALU.mult)
            s_dv.inc(dve.always.tensor_add(gtd[:, 2048 * kt:2048 * (kt + 1)],
                                    gtd[:, 2048 * kt:2048 * (kt + 1)],
                                    sel_t[:, 2048 * kt:2048 * (kt + 1)]))
            s_dv.wait(sync)
            s_st.dma(sync.dma_start(out=gatc.ap()[128 * kt:128 * (kt + 1), :],
                                    in_=gtd[:, 2048 * kt:2048 * (kt + 1)]))
        s_st.wait(gps)
        s_cc.inc(gps.always.collective_compute("AllGather", ALU.bypass, RG,
                                        [gatc.ap().opt()], [gatg.ap().opt()]))
        cc_gat = s_cc.n
    if stop_after == "P8b":
        return finish()

    syncpoint()
    with contextlib.ExitStack() as p9ctx:
        SB9 = lambda name, shape, dt=F32: p9ctx.enter_context(
            nc.sbuf_tensor(name, list(shape), dt))
        pscr = SB9("pscr9", [128, 2048])
        with nc.sbuf_tensor("k1", [128, 1], F32) as k1, \
             nc.sbuf_tensor("k2", [128, 1], F32) as k2:
            for tb in range(NT):
                dve.tensor_mul(k1[:, :], r1_s[:, tb:tb + 1], r1_s[:, tb:tb + 1])
                s_dv.inc(dve.always.tensor_mul(k1[:, :], k1[:, :], v1_s[:, tb:tb + 1]))
                s_dv.wait(act)
                s_ac.inc(act.always.activation(k2[:, :], k1[:, :], AF.Sqrt,
                                        bias=EPS_LN))
                s_ac.wait(dve)
                dve.reciprocal(k2[:, :], k2[:, :])
                dve.tensor_mul(k2[:, :], k2[:, :], r1_s[:, tb:tb + 1])
                dve.tensor_scalar(k1[:, :], msk_s[:, tb:tb + 1], -1.0, None,
                                  ALU.mult)
                dve.tensor_scalar(k1[:, :], k1[:, :], 1.0, None, ALU.add)
                dve.tensor_mul(k1[:, :], k1[:, :], k2[:, :])
                dve.tensor_scalar(ka_s[:, tb:tb + 1], k1[:, :], 1.0, None,
                                  ALU.add)
                s_dv.inc(dve.always.tensor_mul(kb_s[:, tb:tb + 1], m1_s[:, tb:tb + 1],
                                        k1[:, :]))
        with nc.sbuf_tensor("x1c", [128, 2 * 2048], F32) as x1c, \
             nc.sbuf_tensor("gg", [128, 2 * 2048], F32) as gg, \
             nc.sbuf_tensor("oo", [128, 2 * 2048], F32) as oo, \
             nc.sbuf_tensor("ga", [128, 1], F32) as ga, \
             nc.sbuf_tensor("gb", [128, 1], F32) as gb, \
             nc.sbuf_tensor("gc", [128, 1], F32) as gc:
            gat_h = [0] * NT
            x_h = [0] * NT
            dv_h = [0] * NT
            out_h = [0] * NT

            def p9_pre(tb):
                sl = (tb % 2) * 2048
                s_x.wait(sync)
                if tb >= 2:
                    s_dv.wait(sync, dv_h[tb - 2])
                s_x.dma(sync.dma_start(
                    out=x1c[:, sl:sl + 2048],
                    in_=x1d.ap()[128 * tb:128 * (tb + 1), :]))
                x_h[tb] = s_x.n
                if tb >= 2:
                    s_dv.wait(gps, dv_h[tb - 2])
                s_gp.inc(gps.always.memset(gg[:, sl:sl + 2048], 0.0))
                s_gp.wait(gps)

            def p9_gather(tb):
                sl = (tb % 2) * 2048
                s_gat.wait(gps)
                s_gat.dma(gps.indirect_dma_start(
                    out=gg[:, sl:sl + 2048], out_offset=None,
                    in_=gatg.ap(),
                    in_offset=bass.IndirectOffsetOnAxis(
                        ap=idx_s[:, tb:tb + 1], axis=0),
                    bounds_check=bnd_reg, oob_is_err=False))
                gat_h[tb] = s_gat.n

            def p9_fetch(tb):
                p9_pre(tb)
                p9_gather(tb)

            s_ac.wait(sync)
            s_dv.wait(sync)
            p9_pre(0)
            p9_pre(1)
            s_cc.wait(sync, cc_gat)
            s_cc.wait(gps, cc_gat)
            p9_gather(0)
            p9_gather(1)
            for tb in range(NT):
                sl = (tb % 2) * 2048
                s_gat.wait(dve, gat_h[tb])
                s_gat.wait(act, gat_h[tb])
                s_x.wait(dve, x_h[tb])
                if tb >= 1:
                    s_dv.wait(act, dv_h[tb - 1])
                s_ac.inc(act.always.activation(pscr[:, :], gg[:, sl:sl + 2048],
                                        AF.Identity, accum_out=ga[:, :]))
                s_ac.inc(act.always.activation(pscr[:, :], gg[:, sl:sl + 2048],
                                        AF.Square, accum_out=gb[:, :]))
                s_ac.wait(dve)
                dve.tensor_scalar_mul(ga[:, :], ga[:, :], 1.0 / D)
                dve.tensor_scalar_mul(gb[:, :], gb[:, :], 1.0 / D)
                dve.tensor_tensor(gc[:, :], ga[:, :], ga[:, :], ALU.mult)
                s_dv.inc(dve.always.tensor_sub(gb[:, :], gb[:, :], gc[:, :]))
                s_dv.wait(act)
                s_ac.inc(act.always.activation(gb[:, :], gb[:, :], AF.Sqrt,
                                        bias=EPS_LN))
                s_ac.wait(dve)
                dve.reciprocal(gb[:, :], gb[:, :])
                dve.tensor_mul(gb[:, :], gb[:, :], msk_s[:, tb:tb + 1])
                dve.tensor_scalar(gg[:, sl:sl + 2048], gg[:, sl:sl + 2048],
                                  ga[:, :], gb[:, :],
                                  ALU.subtract, ALU.mult)
                if tb >= 2:
                    s_out.wait(dve, out_h[tb - 2])
                dve.tensor_scalar(oo[:, sl:sl + 2048], x1c[:, sl:sl + 2048],
                                  ka_s[:, tb:tb + 1], kb_s[:, tb:tb + 1],
                                  ALU.mult, ALU.subtract)
                s_dv.inc(dve.always.tensor_add(
                    oo[:, sl:sl + 2048], oo[:, sl:sl + 2048],
                    gg[:, sl:sl + 2048]))
                dv_h[tb] = s_dv.n
                s_dv.wait(sync, dv_h[tb])
                s_out.wait(sync)
                s_out.dma(sync.dma_start(
                    out=out_t.ap()[128 * tb:128 * (tb + 1), :],
                    in_=oo[:, sl:sl + 2048]))
                out_h[tb] = s_out.n
                if tb + 2 < NT:
                    p9_fetch(tb + 2)
    s_out.wait(sync)
    bar()
    return nc


_CACHE = {}


def kernel(**inputs):
    T = inputs["L0_attn_normalized_resid_pre"].shape[1]
    cfg = CFG(T)
    if T not in _CACHE:
        _CACHE[T] = build(cfg)
    nc = _CACHE[T]
    in_maps = [shard_inputs(cfg, c, inputs) for c in range(NCORES)]
    res = run_bass_kernel_spmd(nc, in_maps, core_ids=list(range(NCORES)))
    out = np.concatenate([res.results[c]["out"] for c in range(NCORES)], axis=0)
    return out[None, :, :]


# version 35
# speedup vs baseline: 1.0348x; 1.0094x over previous
import contextlib
import math
import numpy as np
import ml_dtypes
import concourse.bass as bass
import concourse.mybir as mybir
from concourse.bass_utils import run_bass_kernel_spmd

F32 = mybir.dt.float32
BF16 = mybir.dt.bfloat16
I32 = mybir.dt.int32
AF = mybir.ActivationFunctionType
ALU = mybir.AluOpType
AX = mybir.AxisListType

NCORES = 8
D = 2048
DC = D // 128
HE = 8
DH = 64
FH = 16
FDH = 128
FI = 8192
EPS_LN = 1e-5
EPS_RMS = 1e-6
C_MAIN = 6.0
C_FFN = 4.0
MASKVAL = -100000.0
TWO_PI = 2.0 * math.pi
CW_C1 = 6.28125
CW_C2 = TWO_PI - CW_C1
ROPE_THETA = 10000.0
FW_WQ, FW_WK, FW_WV, FW_WO, FW_G, FW_U = 0, 2048, 3072, 4096, 6144, 14336
FW_COLS = 22528
CONSTVALS = [0.0, 1.0, EPS_LN, EPS_RMS, -C_FFN, 0.125, 1.0 / math.sqrt(FDH)]


class CFG:
    def __init__(self, T):
        self.T = T
        self.Tc = T // NCORES
        self.NT = self.Tc // 128
        self.QT = 512
        self.NQT = T // self.QT
        self.NKB = T // 128
        self.NKC = self.NKB // NCORES
        self.NS = self.Tc // 512
        self.trips = [(4 * (t + 1) - 1) // 8 + 1 for t in range(self.NQT)]
        self.slots = [(t, j) for t in range(self.NQT)
                      for j in range(max(0, self.trips[t] - 2), self.trips[t])]
        self.NV = sum(self.trips)
        self.KSEL = T // 8
        self.KC = self.KSEL // NCORES
        self.NKT = self.KC // 128
        self.QHW = min(1024, self.KSEL)
        self.QH = self.KSEL // self.QHW
        self.NCONS = self.QHW // self.KC


def host_tables(cfg, c):
    t = {}
    vb = np.zeros((128, cfg.NV), np.float32)
    v = 0
    for qt in range(cfg.NQT):
        for j in range(cfg.trips[qt]):
            vb[:, v] = -C_MAIN if 8 * j + c < 4 * (qt + 1) else MASKVAL
            v += 1
    t["vbias"] = vb
    p = np.arange(128)[:, None]
    f = np.arange(cfg.QT)[None, :]
    dm = np.zeros((len(cfg.slots), 128, cfg.QT), ml_dtypes.bfloat16)
    for s, (qt, j) in enumerate(cfg.slots):
        dm[s] = ((cfg.QT * qt + f) >= (128 * (8 * j + c) + p)).astype(
            ml_dtypes.bfloat16)
    t["dmask"] = dm
    f2 = np.arange(cfg.QHW)[None, :]
    am = np.zeros((cfg.NKT * cfg.QH, 128, cfg.QHW), ml_dtypes.bfloat16)
    for jj in range(cfg.NKT):
        for h in range(cfg.QH):
            am[jj * cfg.QH + h] = ((cfg.QHW * h + f2) >=
                                   (128 * (cfg.NKT * c + jj) + p)).astype(
                ml_dtypes.bfloat16)
    t["amask"] = am
    mc = np.zeros((NCORES, 1), np.float32)
    mc[:c, 0] = 1.0
    t["maskc"] = mc
    gt = np.zeros((128, cfg.NT), np.float32)
    for tb in range(cfg.NT):
        gt[:, tb] = cfg.Tc * c + 128 * tb + np.arange(128)
    t["gtok"] = gt
    return t


def host_consts():
    t = {}
    t["eye_bf"] = np.eye(128, dtype=ml_dtypes.bfloat16)
    t["eye_f32"] = np.eye(128, dtype=np.float32)
    t["ones_bf"] = np.ones((128, 128), ml_dtypes.bfloat16)
    t["ones_f32"] = np.ones((128, 128), np.float32)
    t["u128s"] = (np.arange(128)[:, None] < np.arange(128)[None, :]).astype(
        np.float32)
    t["u16s"] = (np.arange(16)[:, None] < np.arange(16)[None, :]).astype(np.float32)
    ivf = (1.0 / ROPE_THETA ** (np.arange(0, FDH, 2) / FDH)).astype(np.float32)
    t["invfreq"] = np.concatenate([ivf, ivf])[None, :].astype(np.float32)
    t["iotap1"] = (np.arange(128, dtype=np.float32) + 1.0)[:, None]
    cv = np.zeros((128, len(CONSTVALS)), np.float32)
    for i, v in enumerate(CONSTVALS):
        cv[:, i] = v
    t["cvec"] = cv
    sio = np.zeros((2, 128), np.float32)
    sio[0, 0:64] = 1.0
    sio[1, 64:128] = 1.0
    t["selio"] = sio
    return t


def shard_inputs(cfg, c, inp):
    d = {}
    d["x"] = np.ascontiguousarray(inp["L0_attn_normalized_resid_pre"][0,
                                  cfg.Tc * c:cfg.Tc * (c + 1), :])
    wq4s = inp["L0_attn_W_Q"][4 * c:4 * c + 4]
    d["wq4"] = np.ascontiguousarray(
        wq4s.reshape(4, DC, 128, DH).transpose(2, 1, 0, 3)
        .reshape(128, 4 * DC * DH))
    wkv2 = np.concatenate([inp["L0_attn_W_K"][c], inp["L0_attn_W_V"][c]],
                          axis=1)
    d["wkv"] = np.ascontiguousarray(
        wkv2.reshape(DC, 128, 128).transpose(1, 0, 2).reshape(128, DC * 128))
    d["wo1"] = np.ascontiguousarray(inp["L0_attn_W_O"][c])
    r0 = (D // NCORES) * c
    d["fw1"] = np.ascontiguousarray(np.concatenate(
        [inp["L0_ffn_wq"][r0:r0 + 256], inp["L0_ffn_wk"][r0:r0 + 256],
         inp["L0_ffn_wv"][r0:r0 + 256], inp["L0_ffn_wo"][r0:r0 + 256]],
        axis=1))
    i0 = (FI // NCORES) * c
    d["fgu"] = np.ascontiguousarray(np.concatenate(
        [inp["L0_ffn_w_gate"][:, i0:i0 + FI // NCORES],
         inp["L0_ffn_w_up"][:, i0:i0 + FI // NCORES]], axis=1))
    d["fdn"] = np.ascontiguousarray(
        inp["L0_ffn_w_down"][(FI // NCORES) * c:(FI // NCORES) * (c + 1)])
    d["bq"] = inp["L0_ffn_bq"]
    d["bk"] = inp["L0_ffn_bk"]
    d["bv"] = inp["L0_ffn_bv"]
    d["ln1"] = inp["L0_ffn_ln1_w"]
    d["ln2"] = inp["L0_ffn_ln2_w"]
    d["wr"] = inp["L0_ffn_w_router"]
    d.update(host_tables(cfg, c))
    d.update(host_consts())
    return d


_DRAIN_OPS = {
    "tensor_copy", "tensor_tensor", "tensor_add", "tensor_sub", "tensor_mul",
    "tensor_max", "tensor_scalar", "tensor_single_scalar", "tensor_scalar_add",
    "tensor_scalar_sub", "tensor_scalar_mul", "tensor_scalar_max",
    "tensor_scalar_min", "tensor_relu", "tensor_reduce", "reduce_max",
    "reduce_sum", "scalar_tensor_tensor", "tensor_tensor_scan", "memset",
    "memzero", "reciprocal", "select", "transpose", "bn_stats", "bn_aggr",
    "activation", "square", "sqrt", "mul", "copy", "add", "sign", "iota",
    "affine_select", "affine_mul_reduce",
}


def _ap_range(a):
    t = getattr(a, "tensor", None)
    if t is None:
        return None
    pat = getattr(a, "ap", None)
    off = getattr(a, "offset", None)
    if pat is None or off is None:
        raise ValueError("unparsed AP")
    try:
        dims = [(int(p[0]), int(p[1])) for p in pat]
    except Exception:
        raise ValueError("unparsed AP pattern")
    rowlen, npart = dims[0]
    if rowlen <= 0:
        raise ValueError("odd partition pair")
    plo = int(off) // rowlen
    clo = int(off) % rowlen
    creach = clo
    for s, n_ in dims[1:]:
        creach += (n_ - 1) * abs(s)
    return (t.name, plo, plo + npart, clo, creach + 1)


class DrainWrap:

    def __init__(self, eng):
        self._e = eng
        self._w = []
        self._r = []
        self.always = _AlwaysDrain(self)

    @staticmethod
    def _overlap(r, lst):
        for (n_, pl, ph, cl, ch) in lst:
            if (n_ == r[0] and r[1] < ph and pl < r[2]
                    and r[3] < ch and cl < r[4]):
                return True
        return False

    def _emit(self, a, args, kw, force):
        need = force
        wr = []
        rd = []
        try:
            for x in args[1:]:
                r = _ap_range(x)
                if r is not None:
                    rd.append(r)
                    if self._overlap(r, self._w):
                        need = True
            for k_, x in kw.items():
                r = _ap_range(x)
                if r is None:
                    continue
                if k_ == "accum_out":
                    wr.append(r)
                else:
                    rd.append(r)
                    if self._overlap(r, self._w):
                        need = True
            r0 = _ap_range(args[0])
            if r0 is None:
                raise ValueError("no out ap")
            wr.append(r0)
            for r in wr:
                if (self._overlap(r, self._w)
                        or self._overlap(r, self._r)):
                    need = True
        except ValueError:
            need = True
            wr = None
        if (need or wr is None or len(self._w) > 96
                or len(self._r) > 96):
            self._e.drain()
            self._w = []
            self._r = []
        if wr:
            self._w.extend(wr)
            self._r.extend(rd)
        return a(*args, **kw)

    def __getattr__(self, n):
        a = getattr(self._e, n)
        if n in _DRAIN_OPS:
            def f(*args, **kw):
                return self._emit(a, args, kw, False)
            return f
        if n == "drain":
            def fd(*args, **kw):
                self._w = []
                self._r = []
                return a(*args, **kw)
            return fd
        return a


class _AlwaysDrain:
    def __init__(self, wrap):
        self._wrap = wrap

    def __getattr__(self, n):
        a = getattr(self._wrap._e, n)
        if n in _DRAIN_OPS:
            def f(*args, **kw):
                return self._wrap._emit(a, args, kw, True)
            return f
        return a


class Sem:
    def __init__(self, ctx, nc, name):
        self.h = ctx.enter_context(nc.semaphore(name))
        self.n = 0

    def inc(self, instr, k=1):
        if k == 1:
            instr.then_inc(self.h)
        else:
            instr.then_inc(self.h, k)
        self.n += k
        return instr

    def dma(self, instr):
        return self.inc(instr, 16)

    def wait(self, eng, n=None):
        eng.wait_ge(self.h, self.n if n is None else n)


def build(cfg, stop_after=None):
    nc = bass.Bass("TRN2", target_bir_lowering=False)
    Tc, NT, QT, NQT, NS = cfg.Tc, cfg.NT, cfg.QT, cfg.NQT, cfg.NS
    KC, NKT, QHW, QH = cfg.KC, cfg.NKT, cfg.QHW, cfg.QH

    def EI(name, shape, dtype=F32):
        return nc.dram_tensor(name, list(shape), dtype, kind="ExternalInput")

    x_in = EI("x", [Tc, D])
    wq4 = EI("wq4", [128, 4 * DC * DH])
    wkv = EI("wkv", [128, DC * 128])
    wo1 = EI("wo1", [DH, D])
    fw1 = EI("fw1", [D // NCORES, FW_G])
    fgu = EI("fgu", [D, 2 * (FI // NCORES)])
    fdn = EI("fdn", [FI // NCORES, D])
    bq_i = EI("bq", [D])
    bk_i = EI("bk", [1024])
    bv_i = EI("bv", [1024])
    ln1_i = EI("ln1", [D])
    ln2_i = EI("ln2", [D])
    wr_i = EI("wr", [D])
    vb_i = EI("vbias", [128, cfg.NV])
    dm_i = EI("dmask", [len(cfg.slots), 128, QT], BF16)
    am_i = EI("amask", [NKT * QH, 128, QHW], BF16)
    mc_i = EI("maskc", [NCORES, 1])
    gt_i = EI("gtok", [128, NT])
    ey_bf_i = EI("eye_bf", [128, 128], BF16)
    ey_f_i = EI("eye_f32", [128, 128])
    on_bf_i = EI("ones_bf", [128, 128], BF16)
    on_f_i = EI("ones_f32", [128, 128])
    u128_i = EI("u128s", [128, 128])
    u16_i = EI("u16s", [16, 16])
    ivf_i = EI("invfreq", [1, 128])
    io1_i = EI("iotap1", [128, 1])
    out_t = nc.dram_tensor("out", [Tc, D], F32, kind="ExternalOutput")

    def DT(name, shape, dtype=F32, shared=False):
        return nc.dram_tensor(name, list(shape), dtype, kind="Internal",
                              addr_space="Shared" if shared else "Local")

    a1c = DT("a1c", [D, 192], BF16)
    a1g = DT("a1g", [NCORES * D, 192], BF16)
    a2c = DT("a2c", [DH, D], BF16)
    a2g = DT("a2g", [NCORES * DH, D], BF16)
    fw1cAs = [DT(f"fw1cA{i}", [D // NCORES, 2048], BF16) for i in range(3)]
    fw1gAs = [DT(f"fw1gA{i}", [D, 2048], BF16) for i in range(3)]
    fguc = DT("fguc", [D, 2 * (FI // NCORES)], BF16)
    h2c = DT("h2c", [D, KC], BF16)
    h2g = DT("h2g", [NCORES * D, KC], BF16)
    _spl = (KC // 128) >= 2
    dncA = DT("dncA", [cfg.KSEL // 2 if _spl else cfg.KSEL, D], F32)
    dncB = DT("dncB", [cfg.KSEL // 2, D], F32)
    dnrA = DT("dnrA", [KC // 2 if _spl else KC, D], F32)
    dnrB = DT("dnrB", [KC // 2, D], F32)
    fdnc = DT("fdnc", [FI // NCORES, D], BF16)
    KW = cfg.NKC * 128 // NCORES
    ktc = DT("ktc", [NCORES * 512, KW], BF16)
    kta = DT("kta", [NCORES * 512, KW], BF16)
    vtc = DT("vtc", [Tc, 512], BF16)
    vta = DT("vta", [Tc, 512], BF16)
    qc = DT("qc", [512, Tc], BF16)
    qg = DT("qg", [NCORES * 512, Tc], BF16)
    NZB = Tc // 512
    zlcb = [DT(f"zlc{b}", [NCORES * 520, 512], F32) for b in range(NZB)]
    zlrb = [DT(f"zlr{b}", [520, 512], F32) for b in range(NZB)]
    scc = DT("scc", [Tc, 1], F32)
    scg = DT("scg", [cfg.T, 1], F32)
    cntc = DT("cntc", [1, 1], F32)
    cntg = DT("cntg", [NCORES, 1], F32)
    selhc = DT("selhc", [cfg.KSEL, 2048], F32)
    selhr = DT("selhr", [KC, 2048], F32)
    selxc = DT("selxc", [cfg.KSEL, 4], F32)
    selxr = DT("selxr", [KC, 4], F32)
    fqc = DT("fqc", [D, KC], BF16)
    fqg = DT("fqg", [NCORES * D, KC], BF16)
    FZR = FH // 2 * 128 + FH // 2
    fzlcA = DT("fzlcA", [NCORES * FZR, KC], F32)
    fzlcB = DT("fzlcB", [NCORES * FZR, KC], F32)
    fzlrA = DT("fzlrA", [FZR, KC], F32)
    fzlrB = DT("fzlrB", [FZR, KC], F32)
    gatc = DT("gatc", [KC, D], F32)
    gatg = DT("gatg", [cfg.KSEL, D], F32)
    x1d = DT("x1d", [Tc, D], F32)
    RG = [list(range(NCORES))]

    ctx = contextlib.ExitStack()
    SB = lambda name, shape, dt=F32: ctx.enter_context(
        nc.sbuf_tensor(name, list(shape), dt))

    s_in = Sem(ctx, nc, "s_in")
    s_w = Sem(ctx, nc, "s_w")
    s_cc = Sem(ctx, nc, "s_cc")
    s_pe = Sem(ctx, nc, "s_pe")
    s_dv = Sem(ctx, nc, "s_dv")
    s_ac = Sem(ctx, nc, "s_ac")
    s_gp = Sem(ctx, nc, "s_gp")
    s_st = Sem(ctx, nc, "s_st")
    s_ld = Sem(ctx, nc, "s_ld")
    s_x = Sem(ctx, nc, "s_x")
    s_sc = Sem(ctx, nc, "s_sc")
    s_ex = Sem(ctx, nc, "s_ex")
    s_av = Sem(ctx, nc, "s_av")
    s_ev = Sem(ctx, nc, "s_ev")
    s_out = Sem(ctx, nc, "s_out")
    s_qp = [Sem(ctx, nc, f"s_q{i}") for i in range(6)]
    s_fqp = [Sem(ctx, nc, "s_fq0"), Sem(ctx, nc, "s_fq1")]
    s_mk = Sem(ctx, nc, "s_mk")
    s_sca = Sem(ctx, nc, "s_sca")
    s_gat = Sem(ctx, nc, "s_gat")
    s_wg = Sem(ctx, nc, "s_wg")
    s_sg = Sem(ctx, nc, "s_sg")
    s_ev2 = Sem(ctx, nc, "s_ev2")
    s_zz = Sem(ctx, nc, "s_zz")
    bnd_reg = ctx.enter_context(nc.gpsimd.register("bnd"))
    nc.gpsimd.reg_mov(bnd_reg, cfg.KSEL - 1)

    sync = nc.sync
    pe = nc.tensor
    gps = DrainWrap(nc.gpsimd)
    dve = DrainWrap(nc.vector)
    act = DrainWrap(nc.scalar)
    bar = nc.all_engine_barrier

    def syncpoint():
        for s_ in (s_in, s_w, s_st, s_ld, s_x, s_gp, s_out,
                   s_qp[0], s_qp[1], s_qp[2], s_qp[3], s_qp[4], s_qp[5],
                   s_fqp[0], s_fqp[1], s_sca, s_gat, s_wg, s_sg, s_zz):
            s_.wait(sync)
        bar()

    ey_bf = SB("ey_bf", [128, 128], BF16)
    ey_f = SB("ey_f", [128, 128])
    on_bf = SB("on_bf", [128, 128], BF16)
    on_f = SB("on_f", [128, 128])
    u128 = SB("u128", [128, 128])
    u16 = SB("u16", [16, 16])
    ivf = SB("ivf", [1, 128])
    io1 = SB("io1", [128, 1])
    vb_s = SB("vb_s", [128, cfg.NV])
    mc_s = SB("mc_s", [NCORES, 1])
    gt_s = SB("gt_s", [128, NT])
    cvec = SB("cvec_s", [128, len(CONSTVALS)])
    cvec_i = EI("cvec", [128, len(CONSTVALS)])
    selio = SB("selio_s", [2, 128])
    selio_i = EI("selio", [2, 128])
    for dst, src in [(ey_bf, ey_bf_i), (ey_f, ey_f_i), (on_bf, on_bf_i),
                     (on_f, on_f_i), (u128, u128_i), (u16, u16_i), (ivf, ivf_i),
                     (io1, io1_i), (vb_s, vb_i), (mc_s, mc_i), (gt_s, gt_i),
                     (cvec, cvec_i), (selio, selio_i)]:
        s_in.dma(sync.dma_start(out=dst[:, :], in_=src.ap()))
    for i, v in enumerate(CONSTVALS):
        nc.const_aps.register(F32, v, cvec[:, i:i + 1])
    for eng in (pe, dve, act, gps):
        s_in.wait(eng)
    m1_s = SB("m1_s", [128, NT])
    r1_s = SB("r1_s", [128, NT])
    v1_s = SB("v1_s", [128, NT])
    sc_s = SB("sc_s", [128, NT])
    msk_s = SB("msk_s", [128, NT])
    ka_s = SB("ka_s", [128, NT])
    kb_s = SB("kb_s", [128, NT])
    idx_s = SB("idx_s", [128, NT], I32)
    gate_s = SB("gate_s", [128, NT])

    def ln_stats(xa, sa, sb, sc2, eps):
        dve.reduce_sum(sa[:, :], xa, AX.X)
        s_ac.inc(act.always.activation(pscr[:, :], xa, AF.Square, accum_out=sb[:, :]))
        dve.tensor_scalar_mul(sa[:, :], sa[:, :], 1.0 / D)
        s_ac.wait(dve)
        dve.tensor_scalar_mul(sb[:, :], sb[:, :], 1.0 / D)
        dve.tensor_tensor(sc2[:, :], sa[:, :], sa[:, :], ALU.mult)
        s_dv.inc(dve.always.tensor_sub(sb[:, :], sb[:, :], sc2[:, :]))
        s_dv.wait(act)
        s_ac.inc(act.always.activation(sc2[:, :], sb[:, :], AF.Sqrt, bias=eps))
        s_ac.wait(dve)
        dve.reciprocal(sc2[:, :], sc2[:, :])

    wcctx = contextlib.ExitStack()
    wc = wcctx.enter_context(nc.sbuf_tensor("wc", [128, 2 * 2048], F32))
    wco = wcctx.enter_context(nc.sbuf_tensor("wco", [128, 2 * 2048], BF16))
    with nc.sbuf_tensor("wa", [128, DC * 4 * DH], F32) as wa, \
         nc.sbuf_tensor("wkvs", [128, DC * 128], F32) as wkvs, \
         nc.sbuf_tensor("wob", [128, DC * 192], BF16) as wob:
        s_w.dma(sync.dma_start(out=wa[:, :], in_=wq4.ap()))
        s_w.dma(sync.dma_start(out=wkvs[:, :], in_=wkv.ap()))
        s_w.wait(sync)
        s_w.wait(dve)
        for dc in range(DC):
            o4 = dc * 4 * DH
            oo = dc * 192
            dve.tensor_add(wob[:, oo:oo + DH].bitcast(F32)
                           if False else wa[:, o4:o4 + DH],
                           wa[:, o4:o4 + DH], wa[:, o4 + DH:o4 + 2 * DH])
            dve.tensor_add(wa[:, o4:o4 + DH], wa[:, o4:o4 + DH],
                           wa[:, o4 + 2 * DH:o4 + 3 * DH])
            dve.tensor_add(wa[:, o4:o4 + DH], wa[:, o4:o4 + DH],
                           wa[:, o4 + 3 * DH:o4 + 4 * DH])
            dve.tensor_copy(wob[:, oo:oo + DH], wa[:, o4:o4 + DH])
            s_dv.inc(dve.always.tensor_copy(wob[:, oo + DH:oo + 192],
                                     wkvs[:, 128 * dc:128 * (dc + 1)]))
        s_dv.wait(sync)
        s_st.dma(sync.dma_start(
            out=a1c.ap().rearrange("(dc p) k -> p dc k", p=128),
            in_=wob[:, :].rearrange("p (dc k) -> p dc k", k=192)))
        s_w.dma(sync.dma_start(out=wc[0:DH, 0:D], in_=wo1.ap()))
        s_w.wait(dve)
        s_dv.inc(dve.always.tensor_copy(wco[0:DH, 0:D], wc[0:DH, 0:D]))
        s_dv.wait(sync)
        s_st.dma(sync.dma_start(out=a2c.ap(), in_=wco[0:DH, 0:D]))
        pass
    wk_items = []
    for cs in range(FW_G // 2048):
        for blk in range(2):
            wk_items.append((fw1.ap()[128 * blk:128 * (blk + 1),
                                      2048 * cs:2048 * (cs + 1)],
                             fw1cAs[cs].ap()[128 * blk:128 * (blk + 1), :]))
    for blk in range(D // 128):
        wk_items.append((fgu.ap()[128 * blk:128 * (blk + 1), :],
                         fguc.ap()[128 * blk:128 * (blk + 1), :]))
    for blk in range(FI // NCORES // 128):
        wk_items.append((fdn.ap()[128 * blk:128 * (blk + 1), :],
                         fdnc.ap()[128 * blk:128 * (blk + 1), :]))
    NWI = len(wk_items)
    wk_gathers = [(2 * cs + 1, fw1cAs[cs], fw1gAs[cs])
                  for cs in range(FW_G // 2048)]
    wk_emitted = [False] * NWI
    wk_dvw = [0] * NWI
    wk_stw = [0] * NWI
    wk_swl = [0] * NWI
    dv_p0 = s_dv.n

    def wk_load(n):
        s_wg.wait(gps)
        if n == 0:
            s_dv.wait(gps, dv_p0)
        if n >= 2:
            s_dv.wait(gps, wk_dvw[n - 2])
        s_wg.dma(gps.dma_start(
            out=wc[:, (n % 2) * 2048:(n % 2 + 1) * 2048], in_=wk_items[n][0]))
        wk_swl[n] = s_wg.n

    def wk_pump(n):
        if n + 1 < NWI:
            wk_load(n + 1)
        s_wg.wait(dve, wk_swl[n])
        if n == 0:
            s_cc.wait(dve, cc_attnw)
        if n >= 2:
            s_sg.wait(dve, wk_stw[n - 2])
        s_dv.inc(dve.always.tensor_copy(wco[:, (n % 2) * 2048:(n % 2 + 1) * 2048],
                                 wc[:, (n % 2) * 2048:(n % 2 + 1) * 2048]))
        wk_dvw[n] = s_dv.n
        s_dv.wait(gps, wk_dvw[n])
        s_sg.wait(gps)
        s_sg.dma(gps.dma_start(out=wk_items[n][1],
                               in_=wco[:, (n % 2) * 2048:(n % 2 + 1) * 2048]))
        wk_stw[n] = s_sg.n
        wk_emitted[n] = True

    s_st.wait(gps)
    s_cc.inc(gps.always.collective_compute("AllGather", ALU.bypass, RG,
                                    [a1c.ap().opt()], [a1g.ap().opt()]))
    s_cc.inc(gps.always.collective_compute("AllGather", ALU.bypass, RG,
                                    [a2c.ap().opt()], [a2g.ap().opt()]))
    cc_attnw = s_cc.n
    cc_ffnw = [0]
    zz = wcctx.enter_context(nc.sbuf_tensor("zz", [128, 2052], F32))
    s_gp.inc(gps.always.memset(zz[:, :], 0.0))
    s_gp.wait(sync)
    zz_pend = list(range(cfg.KSEL // 128))

    syncpoint()

    def finish(ret=None):
        s_st.wait(sync)
        s_in.wait(sync)
        s_cc.wait(gps)
        bar()
        return nc

    if stop_after == "P0":
        return finish()

    with contextlib.ExitStack() as p3ctx:
        SB3 = lambda name, shape, dt=F32: p3ctx.enter_context(
            nc.sbuf_tensor(name, list(shape), dt))
        pscr = SB3("pscr", [128, 2048])
        stage = SB3("stage", [128, DC * Tc], BF16)
        p2ctx = contextlib.ExitStack()
        SB2 = lambda name, shape, dt=F32: p2ctx.enter_context(
            nc.sbuf_tensor(name, list(shape), dt))
        ln_t = SB2("ln_t", [128, NT * 2048], BF16)
        with nc.sbuf_tensor("xt", [128, 2 * 2048], F32) as xt, \
             nc.sbuf_tensor("sa", [128, 1], F32) as sa, \
             nc.sbuf_tensor("sb", [128, 1], F32) as sb, \
             nc.sbuf_tensor("sc2", [128, 1], F32) as sc2, \
             nc.psum_tensor("tp0", [128, 128], BF16) as tp0, \
             nc.psum_tensor("tp1", [128, 128], BF16) as tp1, \
             nc.psum_tensor("tp2", [128, 128], BF16) as tp2, \
             nc.psum_tensor("tp3", [128, 128], BF16) as tp3:
            tps = [tp0, tp1, tp2, tp3]
            xld = [0] * NT
            dvl = [0] * (NT * DC + 8)
            k = 0
            ln_done = [0] * NT

            def x_load(tb):
                s_x.wait(sync)
                s_dv.wait(sync)
                s_ac.wait(sync)
                s_x.dma(sync.dma_start(
                    out=xt[:, (tb % 2) * 2048:(tb % 2 + 1) * 2048],
                    in_=x_in.ap()[128 * tb:128 * (tb + 1), :]))
                xld[tb] = s_x.n

            x_load(0)
            for tb in range(NT):
                if tb + 1 < NT:
                    x_load(tb + 1)
                xa = xt[:, (tb % 2) * 2048:(tb % 2 + 1) * 2048]
                s_x.wait(dve, xld[tb])
                s_x.wait(act, xld[tb])
                ln_stats(xa, sa, sb, sc2, EPS_LN)
                s_dv.inc(dve.always.tensor_scalar(ln_t[:, 2048 * tb:2048 * (tb + 1)],
                                           xa, sa[:, :], sc2[:, :],
                                           ALU.subtract, ALU.mult))
                ln_done[tb] = s_dv.n
                s_dv.wait(pe, ln_done[tb])
                for dc in range(DC):
                    pp = tps[k % 4]
                    if k >= 4:
                        s_ac.wait(pe, dvl[k - 4])
                    s_pe.inc(pe.transpose(
                        pp[:, :], ln_t[:, 2048 * tb + 128 * dc:
                                       2048 * tb + 128 * (dc + 1)],
                        ey_bf[:, :]))
                    s_pe.wait(act)
                    s_ac.inc(act.always.copy(
                        stage[:, Tc * dc + 128 * tb:Tc * dc + 128 * (tb + 1)],
                        pp[:, :]))
                    dvl[k] = s_ac.n
                    k += 1
        if stop_after == "P2":
            s_dv.wait(sync)
            return finish()
        syncpoint()
        p2ctx.close()
        s_cc.wait(sync, cc_attnw)
        wq_t = SB3("wq_t", [128, 4 * DC * 128], BF16)
        wk_t = SB3("wk_t", [128, 4 * DC * 128], BF16)
        wv_t = SB3("wv_t", [128, DC * 512], BF16)
        WQW = 4 * DC * 128
        for hp in range(4):
            for g in range(2):
                h = 2 * hp + g
                base = hp * DC * 128 + 64 * g
                asrc = a1g.ap()[D * h:D * (h + 1), :].rearrange(
                    "(dc p) k -> p dc k", p=128)
                s_in.dma(sync.dma_start(
                    out=bass.AP(wq_t, base, [[WQW, 128], [128, DC], [1, 64]]),
                    in_=asrc[:, :, 0:DH]))
                s_in.dma(sync.dma_start(
                    out=bass.AP(wk_t, base, [[WQW, 128], [128, DC], [1, 64]]),
                    in_=asrc[:, :, DH:2 * DH]))
        for h in range(HE):
            s_in.dma(sync.dma_start(
                out=bass.AP(wv_t, 64 * h, [[DC * 512, 128], [512, DC], [1, 64]]),
                in_=a1g.ap()[D * h:D * (h + 1), 2 * DH:3 * DH]
                .rearrange("(dc p) k -> p dc k", p=128)))
        qkt_s = SB3("qkt_s", [128, 8 * Tc], BF16)
        vt_s = SB3("vt_s", [128, NT * 512], BF16)
        with nc.psum_tensor("pj_ps", [128, 512], F32) as pj_ps, \
             nc.psum_tensor("pj_ps2", [128, 512], F32) as pj_ps2, \
             nc.psum_tensor("pj_ps3", [128, 512], F32) as pj_ps3, \
             nc.psum_tensor("pj_ps4", [128, 512], F32) as pj_ps4:
            pjs = [pj_ps, pj_ps2, pj_ps3, pj_ps4]
            from concourse.aot_env import allow_ici_all_to_all
            s_in.wait(pe)
            s_dv.wait(pe)
            k = 0
            dv_base = s_dv.n
            for kind in range(2):
                wt = wq_t if kind == 0 else wk_t
                for hp in range(4):
                    for ns in range(NS):
                        pp = pjs[k % 4]
                        if k >= 4:
                            s_dv.wait(pe, dv_base + k - 3)
                        for dc in range(DC):
                            s_pe.inc(pe.matmul(
                                pp[:, :],
                                wt[:, (hp * DC + dc) * 128:
                                   (hp * DC + dc) * 128 + 128],
                                stage[:, Tc * dc + 512 * ns:
                                      Tc * dc + 512 * (ns + 1)],
                                start=(dc == 0), stop=(dc == DC - 1)))
                        s_pe.wait(dve)
                        s_dv.inc(dve.always.tensor_copy(
                            qkt_s[:, (kind * 4 + hp) * Tc + 512 * ns:
                                  (kind * 4 + hp) * Tc + 512 * (ns + 1)],
                            pp[:, :]))
                        k += 1
                s_dv.wait(sync)
                s_st.wait(sync)
                if kind == 0:
                    for hp in range(4):
                        s_st.dma(sync.dma_start(
                            out=qc.ap()[128 * hp:128 * (hp + 1), :],
                            in_=qkt_s[:, hp * Tc:(hp + 1) * Tc]))
                    s_st.wait(gps)
                    s_cc.inc(gps.always.collective_compute(
                        "AllGather", ALU.bypass, RG,
                        [qc.ap().opt()], [qg.ap().opt()]))
                else:
                    for m in range(cfg.NKC):
                        d_ = m % NCORES
                        half = m // NCORES
                        for hp in range(4):
                            s_st.dma(sync.dma_start(
                                out=ktc.ap()[512 * d_ + 128 * hp:
                                             512 * d_ + 128 * (hp + 1),
                                             128 * half:128 * (half + 1)],
                                in_=qkt_s[:, (4 + hp) * Tc + 128 * m:
                                          (4 + hp) * Tc + 128 * (m + 1)]))
                    s_st.wait(gps)
                    with allow_ici_all_to_all():
                        s_cc.inc(gps.always.collective_compute(
                            "AllToAll", ALU.bypass, RG,
                            [ktc.ap().opt()], [kta.ap().opt()]))
            for tb in range(NT):
                pp = pjs[k % 4]
                s_dv.wait(pe, dv_base + k - 3)
                for dc in range(DC):
                    s_pe.inc(pe.matmul(
                        pp[:, :],
                        stage[:, Tc * dc + 128 * tb:Tc * dc + 128 * (tb + 1)],
                        wv_t[:, 512 * dc:512 * (dc + 1)],
                        start=(dc == 0), stop=(dc == DC - 1)))
                s_pe.wait(dve)
                s_dv.inc(dve.always.tensor_copy(vt_s[:, 512 * tb:512 * (tb + 1)],
                                         pp[:, :]))
                k += 1
        s_dv.wait(sync)
        s_st.wait(sync)
        for m in range(cfg.NKC):
            d_ = m % NCORES
            half = m // NCORES
            s_st.dma(sync.dma_start(
                out=vtc.ap()[(Tc // NCORES) * d_ + 128 * half:
                             (Tc // NCORES) * d_ + 128 * (half + 1), :],
                in_=vt_s[:, 512 * m:512 * (m + 1)]))
        s_st.wait(gps)
        with allow_ici_all_to_all():
            s_cc.inc(gps.always.collective_compute("AllToAll", ALU.bypass, RG,
                                            [vtc.ap().opt()], [vta.ap().opt()]))
        cc_qkv = s_cc.n
    if stop_after == "P3":
        return finish()

    syncpoint()
    with contextlib.ExitStack() as p5ctx:
        SB5 = lambda name, shape, dt=F32: p5ctx.enter_context(
            nc.sbuf_tensor(name, list(shape), dt))
        ktx = SB5("ktx", [128, 4 * cfg.NKC * 128], BF16)
        vres = SB5("vres", [128, cfg.NKC * 512], BF16)
        s_cc.wait(sync, cc_qkv)
        NH2_ = cfg.NKC // 8
        for hp in range(4):
            s_ld.dma(sync.dma_start(
                out=ktx[:, hp * cfg.NKC * 128:(hp + 1) * cfg.NKC * 128]
                .rearrange("p (j2 half k) -> p j2 half k", half=NH2_, k=128),
                in_=kta.ap().rearrange("(j2 x p) (half k) -> x p j2 half k",
                                       x=4, p=128, k=128)[hp]))
        s_ld.dma(sync.dma_start(
            out=vres[:, :].rearrange("p (j2 half k) -> p j2 half k",
                                     half=NH2_, k=512),
            in_=vta.ap().rearrange("(j2 half p) k -> p j2 half k",
                                   half=NH2_, p=128)))
        ld_res = s_ld.n
        qbuf = SB5("qbuf", [128, 24 * QT], BF16)
        mbuf = SB5("mbuf", [128, 12 * QT], BF16)
        pT = SB5("pT", [128, 4 * QT], BF16)
        zst = SB5("zst", [128, 8 * 512])
        lstA = SB5("lstA", [1, 8 * 512])
        lstB = SB5("lstB", [1, 8 * 512])
        with nc.psum_tensor("s_ps", [128, 2 * QT], F32) as s_psA, \
             nc.psum_tensor("s_ps2", [128, 2 * QT], F32) as s_psB, \
             nc.psum_tensor("z_psa", [128, QT], F32) as z_psa, \
             nc.psum_tensor("z_psb", [128, QT], F32) as z_psb, \
             nc.psum_tensor("l_psa", [128, QT], F32) as l_psa, \
             nc.psum_tensor("l_psb", [128, QT], F32) as l_psb:

            def q_load(t, sl):
                o = (QT * t) // Tc
                col = QT * t - Tc * o
                for hp in range(4):
                    s_qp[sl].dma(sync.dma_start(
                        out=qbuf[:, (sl * 4 + hp) * QT:
                                 (sl * 4 + hp + 1) * QT],
                        in_=qg.ap()[512 * o + 128 * hp:512 * o + 128 * (hp + 1),
                                    col:col + QT]))

            def m_load(t, sl):
                base = len([s for s in cfg.slots if s[0] < t])
                for u in range(min(2, cfg.trips[t])):
                    s_qp[sl].dma(sync.dma_start(
                        out=mbuf[:, (sl * 2 + u) * QT:
                                 (sl * 2 + u + 1) * QT],
                        in_=dm_i.ap()[base + u, :, :]))

            NBLK = Tc // QT
            tseq = [t_ for b_ in range(NBLK) for t_ in range(b_, NQT, NBLK)]
            vbase = [0] * NQT
            for t_ in range(1, NQT):
                vbase[t_] = vbase[t_ - 1] + cfg.trips[t_ - 1]
            visits = []
            vend = []
            tslot = {}
            for si_, t_ in enumerate(tseq):
                trips_ = cfg.trips[t_]
                jm0_ = trips_ - min(2, trips_)
                tslot[t_] = si_ % 6
                for hp_ in range(4):
                    for j_ in range(trips_):
                        visits.append((t_, hp_, j_, trips_, jm0_,
                                       vbase[t_] + j_))
                vend.append(len(visits))
            NVIS = len(visits)
            seq_of = {t_: si_ for si_, t_ in enumerate(tseq)}
            ldcount = [0] * NQT
            for pf_ in range(min(5, len(tseq))):
                q_load(tseq[pf_], pf_)
                m_load(tseq[pf_], pf_)
                ldcount[tseq[pf_]] = s_qp[pf_].n
            av_hist = [0] * NVIS
            mk_hist = [0] * NVIS
            sc_hist = [0] * NVIS
            cc_zlb = [0] * NBLK
            wk_gat_next = [0]
            PUMP_EVERY = max(1, (2 * NVIS) // (3 * NWI))
            wk_load(0)

            qp_waited = set()

            def emit_scores(k):
                t_, hp_, j_, trips_, jm0_, v_ = visits[k]
                sp = s_psA if k % 2 == 0 else s_psB
                if t_ not in qp_waited:
                    s_qp[tslot[t_]].wait(pe, ldcount[t_])
                    qp_waited.add(t_)
                if k == 0:
                    s_ld.wait(pe)
                if k >= 2:
                    s_ex.wait(pe, k - 1)
                kof = (hp_ * cfg.NKC + j_) * 128
                qof = (tslot[t_] * 4 + hp_) * QT
                s_sc.inc(pe.matmul(sp[:, 0:QT], ktx[0:64, kof:kof + 128],
                                   qbuf[0:64, qof:qof + QT],
                                   start=True, stop=True, tile_position=(0, 0)))
                s_sc.inc(pe.matmul(sp[:, QT:2 * QT], ktx[64:128, kof:kof + 128],
                                   qbuf[64:128, qof:qof + QT],
                                   start=True, stop=True, tile_position=(64, 0)))
                sc_hist[k] = s_sc.n

            st_hist = {}

            def emit_evac(av_t, g_):
                sl_ = 512 * (g_ % 8)
                s_av.wait(dve, av_t)
                if g_ >= 8:
                    s_st.wait(dve, st_hist[g_ - 8])
                s_ev.inc(dve.tensor_copy(zst[0:64, sl_:sl_ + 512],
                                         z_psa[0:64, :]))
                s_ev.inc(dve.tensor_copy(zst[64:128, sl_:sl_ + 512],
                                         z_psb[64:128, :]))
                s_av.wait(act, av_t)
                if g_ >= 8:
                    s_st.wait(act, st_hist[g_ - 8])
                s_ev2.inc(act.copy(lstA[:, sl_:sl_ + 512], l_psa[0:1, :]))
                s_ev2.inc(act.copy(lstB[:, sl_:sl_ + 512], l_psb[64:65, :]))

            def emit_store(t_p, hp_p, g_):
                sl_ = 512 * (g_ % 8)
                s_ev.wait(sync)
                s_ev2.wait(sync)
                s_st.wait(sync)
                r_ = t_p // NBLK
                zt = zlcb[t_p % NBLK]
                s_st.dma(sync.dma_start(
                    out=zt.ap()[520 * r_ + 128 * hp_p:
                                520 * r_ + 128 * (hp_p + 1), :],
                    in_=zst[:, sl_:sl_ + 512]))
                s_st.dma(sync.dma_start(
                    out=zt.ap()[520 * r_ + 512 + 2 * hp_p:
                                520 * r_ + 513 + 2 * hp_p, :],
                    in_=lstA[:, sl_:sl_ + 512]))
                s_st.dma(sync.dma_start(
                    out=zt.ap()[520 * r_ + 513 + 2 * hp_p:
                                520 * r_ + 514 + 2 * hp_p, :],
                    in_=lstB[:, sl_:sl_ + 512]))
                st_hist[g_] = s_st.n
                if hp_p == 3 and t_p == NQT - NBLK + (t_p % NBLK):
                    b_ = t_p % NBLK
                    s_st.wait(gps)
                    s_cc.inc(gps.always.collective_compute(
                        "ReduceScatter", ALU.add, RG,
                        [zlcb[b_].ap().opt()], [zlrb[b_].ap().opt()]))
                    cc_zlb[b_] = s_cc.n
                    while wk_gat_next[0] < len(wk_gathers):
                        li, cin_, cout_ = wk_gathers[wk_gat_next[0]]
                        if not wk_emitted[li]:
                            break
                        s_sg.wait(gps, wk_stw[li])
                        s_cc.inc(gps.always.collective_compute(
                            "AllGather", ALU.bypass, RG,
                            [cin_.ap().opt()], [cout_.ap().opt()]))
                        cc_ffnw[0] = s_cc.n
                        wk_gat_next[0] += 1

            pend = None
            gcount = 0
            emit_scores(0)
            for k in range(NVIS):
                t, hp, j, trips, jmask0, v = visits[k]
                pj = k % 2
                sp = s_psA if pj == 0 else s_psB
                if j == 0 and hp == 0:
                    si = seq_of[t]
                    if zz_pend:
                        r_z = zz_pend.pop(0)
                        s_zz.dma(sync.dma_start(
                            out=selhc.ap()[128 * r_z:128 * (r_z + 1), :],
                            in_=zz[:, 0:2048]))
                        s_zz.dma(sync.dma_start(
                            out=selxc.ap()[128 * r_z:128 * (r_z + 1), :],
                            in_=zz[:, 0:4]))
                    if si + 5 < len(tseq):
                        tn = tseq[si + 5]
                        if si >= 1:
                            s_sc.wait(sync, 2 * vend[si - 1])
                            s_mk.wait(sync, mk_hist[vend[si - 1] - 1])
                        q_load(tn, (si + 5) % 6)
                        m_load(tn, (si + 5) % 6)
                        ldcount[tn] = s_qp[(si + 5) % 6].n
                if k + 1 < NVIS:
                    emit_scores(k + 1)
                s_sc.wait(act, sc_hist[k])
                if k >= 2:
                    s_av.wait(act, av_hist[k - 2])
                ei = s_ex.inc(act.activation(
                    pT[:, pj * 2 * QT:(pj + 1) * 2 * QT],
                    sp[:, :], AF.Exp, scale=0.125, bias=vb_s[:, v:v + 1]))
                if pend is not None:
                    emit_evac(pend[2], pend[3])
                    emit_store(pend[0], pend[1], pend[3])
                    pend = None
                masked = j >= jmask0
                if masked:
                    u = j - jmask0
                    mof = (tslot[t] * 2 + u) * QT
                    s_ex.wait(dve, k + 1)
                    s_qp[tslot[t]].wait(dve, ldcount[t])
                    s_mk.inc(dve.tensor_mul(
                        pT[:, pj * 2 * QT:pj * 2 * QT + QT],
                        pT[:, pj * 2 * QT:pj * 2 * QT + QT],
                        mbuf[:, mof:mof + QT]))
                    s_mk.inc(dve.tensor_mul(
                        pT[:, pj * 2 * QT + QT:(pj + 1) * 2 * QT],
                        pT[:, pj * 2 * QT + QT:(pj + 1) * 2 * QT],
                        mbuf[:, mof:mof + QT]))
                mk_hist[k] = s_mk.n
                if k % PUMP_EVERY == PUMP_EVERY - 1 and k // PUMP_EVERY < NWI:
                    wk_pump(k // PUMP_EVERY)
                s_ex.wait(pe, k + 1)
                if masked:
                    s_mk.wait(pe, mk_hist[k])
                ev0 = s_ev.n if (j == 0 and k > 0) else None
                ev2 = s_ev2.n if (j == 0 and k > 0) else None
                st_ = (j == 0)
                sp_ = (j == trips - 1)
                if ev0 is not None:
                    s_ev.wait(pe, ev0 - 1)
                s_av.inc(pe.matmul(z_psa[0:64, :],
                                   vres[:, 512 * j + 128 * hp:
                                        512 * j + 128 * hp + 64],
                                   pT[:, pj * 2 * QT:pj * 2 * QT + QT],
                                   start=st_, stop=sp_, tile_position=(0, 0)))
                if ev0 is not None:
                    s_ev.wait(pe, ev0)
                s_av.inc(pe.matmul(z_psb[64:128, :],
                                   vres[:, 512 * j + 128 * hp + 64:
                                        512 * j + 128 * (hp + 1)],
                                   pT[:, pj * 2 * QT + QT:(pj + 1) * 2 * QT],
                                   start=st_, stop=sp_, tile_position=(0, 64)))
                if ev0 is not None:
                    s_ev2.wait(pe, ev2 - 1)
                s_av.inc(pe.matmul(l_psa[0:1, :], on_bf[:, 0:1],
                                   pT[:, pj * 2 * QT:pj * 2 * QT + QT],
                                   start=st_, stop=sp_, tile_position=(0, 0)))
                if ev0 is not None:
                    s_ev2.wait(pe, ev2)
                s_av.inc(pe.matmul(l_psb[64:65, :], on_bf[:, 0:1],
                                   pT[:, pj * 2 * QT + QT:(pj + 1) * 2 * QT],
                                   start=st_, stop=sp_, tile_position=(0, 64)))
                av_hist[k] = s_av.n
                if j == trips - 1:
                    pend = (t, hp, av_hist[k], gcount)
                    gcount += 1
            emit_evac(pend[2], pend[3])
            emit_store(pend[0], pend[1], pend[3])
        s_st.wait(gps)
    wcctx.close()
    if stop_after == "P5":
        return finish()

    syncpoint()
    with contextlib.ExitStack() as p7ctx:
        SB7 = lambda name, shape, dt=F32: p7ctx.enter_context(
            nc.sbuf_tensor(name, list(shape), dt))
        zfm = SB7("zfm", [128, 4 * Tc], BF16)
        with nc.sbuf_tensor("zn_f", [128, Tc], F32) as zn_f, \
             nc.sbuf_tensor("li_ab", [2, Tc], F32) as li_ab, \
             nc.sbuf_tensor("rcp_t", [128, 512], F32) as rcp_t, \
             nc.psum_tensor("lb_ps", [128, 512], F32) as lb_ps:
            ldb = {}
            for hp in range(4):
                s_dv.wait(sync)
                for b_ in range(NZB):
                    if hp == 0:
                        s_cc.wait(sync, cc_zlb[b_])
                    s_ld.wait(sync)
                    s_ld.dma(sync.dma_start(
                        out=zn_f[:, 512 * b_:512 * (b_ + 1)],
                        in_=zlrb[b_].ap()[128 * hp:128 * (hp + 1), :]))
                    s_ld.dma(sync.dma_start(
                        out=li_ab[:, 512 * b_:512 * (b_ + 1)],
                        in_=zlrb[b_].ap()[512 + 2 * hp:514 + 2 * hp, :]))
                    ldb[b_] = s_ld.n
                for ns in range(NS):
                    s_ld.wait(pe, ldb[ns])
                    s_pe.inc(pe.matmul(lb_ps[:, :], selio[0:2, :],
                                       li_ab[:, 512 * ns:512 * (ns + 1)],
                                       start=True, stop=True))
                    s_pe.wait(dve)
                    s_ld.wait(dve, ldb[ns])
                    dve.reciprocal(rcp_t[:, :], lb_ps[:, :])
                    s_dv.inc(dve.always.tensor_mul(
                        zfm[:, hp * Tc + 512 * ns:hp * Tc + 512 * (ns + 1)],
                        zn_f[:, 512 * ns:512 * (ns + 1)], rcp_t[:, :]))
                    s_dv.wait(pe)
        s_dv.wait(sync)
        wo_t = SB7("wo_t", [128, 4 * DC * 128], BF16)
        for hp in range(4):
            s_in.dma(sync.dma_start(
                out=wo_t[:, hp * DC * 128:(hp + 1) * DC * 128],
                in_=a2g.ap()[128 * hp:128 * (hp + 1), :]))
        pscr = SB7("pscr7", [128, 2048])
        wr_b = SB7("wr_b", [128, 2048])
        srw_b = SB7("srw_b", [128, 1])
        with nc.sbuf_tensor("wr_r", [1, 2048], F32) as wr_r, \
             nc.psum_tensor("br_ps", [128, 512], F32) as br_ps:
            s_in.wait(sync)
            s_in.dma(sync.dma_start(out=wr_r[:, :], in_=wr_i.ap()[None, :]))
            s_in.wait(pe)
            for ns in range(4):
                s_pe.inc(pe.matmul(br_ps[:, :], on_f[0:1, :],
                                   wr_r[:, 512 * ns:512 * (ns + 1)],
                                   start=True, stop=True))
                s_pe.wait(dve)
                s_dv.inc(dve.always.tensor_copy(wr_b[:, 512 * ns:512 * (ns + 1)],
                                         br_ps[:, :]))
                s_dv.wait(pe)
        s_dv.inc(dve.always.reduce_sum(srw_b[:, :], wr_b[:, :], AX.X))
        with nc.sbuf_tensor("xt2", [128, 2 * 2048], F32) as xt2, \
             nc.sbuf_tensor("at2", [128, 2 * 2048], F32) as at2, \
             nc.sbuf_tensor("x1t", [128, 2 * 2048], F32) as x1t, \
             nc.psum_tensor("ao_psA", [128, 2048], F32) as ao_psA, \
             nc.psum_tensor("ao_psB", [128, 2048], F32) as ao_psB, \
             nc.sbuf_tensor("sa7", [128, 2], F32) as sa, \
             nc.sbuf_tensor("sb7", [128, 2], F32) as sb, \
             nc.sbuf_tensor("sc27", [128, 2], F32) as sc2:
            s_in.wait(pe)
            s_dv.wait(pe)
            xh7 = [0] * NT
            dv7 = [0] * NT
            st7 = [0] * NT
            pe7 = [0] * NT
            cta7 = [0] * NT

            def x7_fetch(tb):
                s_x.wait(sync)
                if tb >= 2:
                    s_dv.wait(sync, dv7[tb - 2])
                s_x.dma(sync.dma_start(
                    out=xt2[:, (tb % 2) * 2048:(tb % 2 + 1) * 2048],
                    in_=x_in.ap()[128 * tb:128 * (tb + 1), :]))
                xh7[tb] = s_x.n

            s_dv.wait(sync)
            s_ac.wait(sync)
            x7_fetch(0)
            x7_fetch(1)
            for tb in range(NT):
                sl = (tb % 2) * 2048
                c = tb % 2
                a2 = at2[:, sl:sl + 2048]
                x1s = x1t[:, sl:sl + 2048]
                sac = sa[:, c:c + 1]
                sbc = sb[:, c:c + 1]
                scc2 = sc2[:, c:c + 1]
                pp7 = ao_psA if tb % 2 == 0 else ao_psB
                if tb >= 2:
                    s_dv.wait(pe, cta7[tb - 2])
                for hp in range(4):
                    for dn in range(4):
                        s_pe.inc(pe.matmul(
                            pp7[:, 512 * dn:512 * (dn + 1)],
                            zfm[:, hp * Tc + 128 * tb:hp * Tc + 128 * (tb + 1)],
                            wo_t[:, hp * DC * 128 + 512 * dn:
                                 hp * DC * 128 + 512 * (dn + 1)],
                            start=(hp == 0), stop=(hp == 3)))
                pe7[tb] = s_pe.n
                s_pe.wait(dve, pe7[tb])
                s_dv.inc(dve.always.tensor_copy(a2, pp7[:, :]))
                cta = s_dv.n
                cta7[tb] = cta
                if tb >= 2:
                    s_dv.wait(act, dv7[tb - 2])
                s_dv.wait(act, cta)
                s_ac.inc(act.always.activation(pscr[:, :], a2, AF.Identity,
                                        accum_out=sac))
                s_ac.inc(act.always.activation(pscr[:, :], a2, AF.Square,
                                        accum_out=sbc))
                s_ac.wait(dve)
                dve.tensor_scalar_mul(sac, sac, 1.0 / D)
                dve.tensor_scalar_mul(sbc, sbc, 1.0 / D)
                dve.tensor_tensor(scc2, sac, sac, ALU.mult)
                s_dv.inc(dve.always.tensor_sub(sbc, sbc, scc2))
                s_dv.wait(act)
                s_ac.inc(act.always.activation(scc2, sbc, AF.Sqrt,
                                        bias=EPS_LN))
                s_ac.wait(dve)
                dve.reciprocal(scc2, scc2)
                dve.tensor_scalar(a2, a2, sac, scc2,
                                  ALU.subtract, ALU.mult)
                s_x.wait(dve, xh7[tb])
                if tb >= 2:
                    s_st.wait(dve, st7[tb - 2])
                s_dv.inc(dve.always.tensor_add(
                    x1s, xt2[:, sl:sl + 2048], a2))
                s_dv.wait(sync)
                s_st.wait(sync)
                s_st.dma(sync.dma_start(out=x1d.ap()[128 * tb:128 * (tb + 1), :],
                                        in_=x1s))
                st7[tb] = s_st.n
                ct2 = s_dv.n
                s_dv.wait(act, ct2)
                s_ac.inc(act.always.activation(pscr[:, :], x1s, AF.Identity,
                                        accum_out=sac))
                s_ac.inc(act.always.activation(pscr[:, :], x1s, AF.Square,
                                        accum_out=sbc))
                s_ac.wait(dve)
                dve.tensor_scalar_mul(m1_s[:, tb:tb + 1], sac, 1.0 / D)
                dve.tensor_scalar_mul(sbc, sbc, 1.0 / D)
                dve.tensor_tensor(scc2, m1_s[:, tb:tb + 1],
                                  m1_s[:, tb:tb + 1], ALU.mult)
                s_dv.inc(dve.always.tensor_sub(v1_s[:, tb:tb + 1], sbc, scc2))
                s_dv.wait(act)
                s_ac.inc(act.always.activation(sac, v1_s[:, tb:tb + 1],
                                        AF.Sqrt, bias=EPS_LN))
                s_ac.wait(dve)
                dve.reciprocal(r1_s[:, tb:tb + 1], sac)
                dve.tensor_mul(a2, x1s, wr_b[:, :])
                dve.reduce_sum(sac, a2, AX.X)
                dve.tensor_mul(sbc, m1_s[:, tb:tb + 1], srw_b[:, :])
                dve.tensor_sub(sac, sac, sbc)
                s_dv.inc(dve.always.tensor_mul(sc_s[:, tb:tb + 1], sac,
                                        r1_s[:, tb:tb + 1]))
                dv7[tb] = s_dv.n
                if tb + 2 < NT:
                    x7_fetch(tb + 2)
    if stop_after == "P7":
        return finish()

    syncpoint()
    with contextlib.ExitStack() as p8actx:
        SB8 = lambda name, shape, dt=F32: p8actx.enter_context(
            nc.sbuf_tensor(name, list(shape), dt))
        with nc.psum_tensor("sc_ps", [128, 128], F32) as sc_ps, \
             nc.sbuf_tensor("sc_t", [NT, 128], F32) as sc_t:
            s_dv.wait(pe)
            s_pe.inc(pe.transpose(sc_ps[0:NT, :], sc_s[:, :], ey_f[:, :]))
            s_pe.wait(dve)
            s_dv.inc(dve.always.tensor_copy(sc_t[:, :], sc_ps[0:NT, :]))
            s_dv.wait(sync)
            s_st.dma(sync.dma_start(
                out=scc.ap().rearrange("(tb p) one -> tb (p one)", p=128),
                in_=sc_t[:, :]))
        s_st.wait(gps)
        s_cc.inc(gps.always.collective_compute("AllGather", ALU.bypass, RG,
                                        [scc.ap().opt()], [scg.ap().opt()]))
        s_cc.wait(sync)
        s_st.wait(dve)
        s_st.wait(pe)
        th_lo = SB8("th_lo", [1, 1])
        th_sz = SB8("th_sz", [1, 1])
        thv = SB8("thv", [128, 1])
        cnt_s = SB8("cnt_s", [128, 1])
        nch = cfg.T // 2048
        with nc.sbuf_tensor("bcb", [128, nch * 2048], F32) as bcb, \
             nc.sbuf_tensor("bch", [128, nch * 2048], BF16) as bch, \
             nc.sbuf_tensor("cc1", [128, 1], F32) as cc1, \
             nc.sbuf_tensor("fl", [128, 1], F32) as fl, \
             nc.psum_tensor("tr1_ps", [128, 128], F32) as tr1_ps, \
             nc.sbuf_tensor("flt", [1, 128], F32) as flt, \
             nc.sbuf_tensor("mx", [1, 1], F32) as mx:
            for ch in range(nch):
                s_ld.dma(sync.dma_start(
                    out=bcb[:, 2048 * ch:2048 * (ch + 1)],
                    in_=bass.AP(scg, 2048 * ch, [[0, 128], [1, 2048]])))
            s_ld.wait(dve)
            dve.memset(th_lo[:, :], -16.0)
            s_dv.inc(dve.always.memset(th_sz[:, :], 32.0))
            for rnd in range(5):
                s_dv.wait(pe)
                s_pe.inc(pe.matmul(tr1_ps[:, 0:1], on_f[0:1, :], th_lo[:, :],
                                   start=True, stop=True))
                s_pe.inc(pe.matmul(tr1_ps[:, 1:2], on_f[0:1, :], th_sz[:, :],
                                   start=True, stop=True))
                s_pe.wait(dve)
                dve.tensor_scalar_mul(fl[:, :], tr1_ps[:, 1:2], 1.0 / 130.0)
                dve.tensor_mul(fl[:, :], fl[:, :], io1[:, :])
                dve.tensor_add(thv[:, :], fl[:, :], tr1_ps[:, 0:1])
                dve.tensor_scalar(bch[:, :], bcb[:, :], thv[:, :], None,
                                  ALU.is_gt, op1=ALU.add,
                                  accum_out=cc1[:, :])
                s_dv.inc(dve.always.tensor_copy(cnt_s[:, :], cc1[:, :]))
                dve.tensor_scalar(fl[:, :], cnt_s[:, :],
                                  float(cfg.KSEL) - 0.5, None, ALU.is_gt)
                s_dv.inc(dve.always.tensor_mul(fl[:, :], fl[:, :], io1[:, :]))
                s_dv.wait(pe)
                s_pe.inc(pe.transpose(tr1_ps[0:1, :], fl[:, :], ey_f[:, :]))
                s_pe.wait(dve)
                dve.tensor_copy(flt[:, :], tr1_ps[0:1, :])
                dve.reduce_max(mx[:, :], flt[:, :], AX.X)
                dve.tensor_scalar_mul(mx[:, :], mx[:, :], 1.0 / 130.0)
                dve.tensor_mul(mx[:, :], mx[:, :], th_sz[:, :])
                dve.tensor_add(th_lo[:, :], th_lo[:, :], mx[:, :])
                s_dv.inc(dve.always.tensor_scalar_mul(th_sz[:, :], th_sz[:, :],
                                               1.0 / 130.0))
        with nc.psum_tensor("px_ps", [128, 128], F32) as px_ps, \
             nc.sbuf_tensor("thb", [128, 1], F32) as thb, \
             nc.sbuf_tensor("rs1", [128, 1], F32) as rs1, \
             nc.sbuf_tensor("cs16", [16, 1], F32) as cs16, \
             nc.sbuf_tensor("pre16", [16, 1], F32) as pre16, \
             nc.sbuf_tensor("pre16t", [1, 16], F32) as pre16t, \
             nc.sbuf_tensor("preb", [128, NT], F32) as preb, \
             nc.sbuf_tensor("cnt1", [1, 1], F32) as cnt1, \
             nc.sbuf_tensor("cnt8", [8, 1], F32) as cnt8, \
             nc.sbuf_tensor("base1", [1, 1], F32) as base1, \
             nc.sbuf_tensor("pos_s", [128, NT], F32) as pos_s:
            s_dv.wait(pe)
            s_pe.inc(pe.matmul(px_ps[:, 0:1], on_f[0:1, :], th_lo[:, :],
                               start=True, stop=True))
            s_pe.wait(dve)
            s_dv.inc(dve.always.tensor_copy(thb[:, :], px_ps[:, 0:1]))
            dve.tensor_scalar(msk_s[:, :], sc_s[:, :], thb[:, :], None,
                              ALU.is_gt)
            s_dv.wait(act)
            s_ac.inc(act.always.activation(gate_s[:, :], sc_s[:, :], AF.Sigmoid))
            s_dv.inc(dve.always.reduce_sum(rs1[:, :], msk_s[:, :], AX.X))
            s_dv.wait(pe)
            s_pe.inc(pe.matmul(px_ps[:, 0:NT], u128[:, :], msk_s[:, :],
                               start=True, stop=True))
            s_pe.inc(pe.matmul(px_ps[0:NT, NT:NT + 1], msk_s[:, :], on_f[:, 0:1],
                               start=True, stop=True))
            s_pe.inc(pe.matmul(px_ps[0:1, NT + 1:NT + 2], rs1[:, :], on_f[:, 0:1],
                               start=True, stop=True))
            s_pe.wait(dve)
            dve.tensor_copy(pos_s[:, :], px_ps[:, 0:NT])
            dve.tensor_copy(cs16[0:NT, :], px_ps[0:NT, NT:NT + 1])
            s_dv.inc(dve.always.tensor_copy(cnt1[:, :], px_ps[0:1, NT + 1:NT + 2]))
            s_dv.wait(sync)
            s_st.dma(sync.dma_start(out=cntc.ap(), in_=cnt1[:, :]))
            s_st.wait(gps)
            s_cc.inc(gps.always.collective_compute("AllGather", ALU.bypass, RG,
                                            [cntc.ap().opt()], [cntg.ap().opt()]))
            s_cc.wait(sync)
            s_ld.dma(sync.dma_start(out=cnt8[:, :], in_=cntg.ap()))
            s_ld.wait(dve)
            s_dv.inc(dve.always.tensor_mul(cnt8[:, :], cnt8[:, :], mc_s[:, :]))
            s_dv.wait(pe)
            s_pe.inc(pe.matmul(px_ps[0:1, 0:1], cnt8[:, :], on_f[0:8, 0:1],
                               start=True, stop=True))
            s_pe.inc(pe.matmul(px_ps[0:NT, 1:2], u16[0:NT, 0:NT], cs16[0:NT, :],
                               start=True, stop=True))
            s_pe.wait(dve)
            dve.tensor_copy(base1[:, :], px_ps[0:1, 0:1])
            s_dv.inc(dve.always.tensor_copy(pre16[0:NT, :], px_ps[0:NT, 1:2]))
            s_dv.wait(pe)
            s_pe.inc(pe.transpose(px_ps[0:1, 0:NT], pre16[0:NT, :],
                                  ey_f[0:NT, 0:NT]))
            s_pe.wait(dve)
            s_dv.inc(dve.always.tensor_copy(pre16t[:, 0:NT], px_ps[0:1, 0:NT]))
            s_dv.wait(pe)
            s_pe.inc(pe.matmul(px_ps[:, 0:NT], on_f[0:1, :], pre16t[:, 0:NT],
                               start=True, stop=True))
            s_pe.inc(pe.matmul(px_ps[:, NT:NT + 1], on_f[0:1, :], base1[:, :],
                               start=True, stop=True))
            s_pe.wait(dve)
            dve.tensor_copy(preb[:, :], px_ps[:, 0:NT])
            s_dv.inc(dve.always.tensor_copy(thb[:, :], px_ps[:, NT:NT + 1]))
            dve.tensor_add(pos_s[:, :], pos_s[:, :], preb[:, :])
            dve.tensor_scalar(pos_s[:, :], pos_s[:, :], thb[:, :], None,
                              ALU.add)
            BIGI = 524288.0
            dve.tensor_scalar(pos_s[:, :], pos_s[:, :], -BIGI, None, ALU.add)
            dve.tensor_mul(pos_s[:, :], pos_s[:, :], msk_s[:, :])
            dve.tensor_scalar(pos_s[:, :], pos_s[:, :], BIGI, None, ALU.add)
            s_dv.inc(dve.always.tensor_copy(idx_s[:, :], pos_s[:, :]))
        with nc.sbuf_tensor("x1b", [128, 2 * 2048], F32) as x1b, \
             nc.sbuf_tensor("hrow", [128, 2 * 2048], F32) as hrow, \
             nc.sbuf_tensor("aux2", [128, 4], F32) as aux2:
            s_ac.wait(dve)
            s_pe.wait(sync)
            s_dv.wait(sync)
            xh9 = [0] * NT
            dvh9 = [0] * NT
            sch9 = [0] * NT

            def s9_fetch(tb):
                s_x.wait(sync)
                if tb >= 2:
                    s_dv.wait(sync, dvh9[tb - 2])
                s_x.dma(sync.dma_start(
                    out=x1b[:, (tb % 2) * 2048:(tb % 2 + 1) * 2048],
                    in_=x1d.ap()[128 * tb:128 * (tb + 1), :]))
                xh9[tb] = s_x.n

            s9_fetch(0)
            s9_fetch(1)
            for tb in range(NT):
                sl = (tb % 2) * 2048
                s_x.wait(dve, xh9[tb])
                if tb >= 2:
                    s_sca.wait(dve, sch9[tb - 2])
                dve.tensor_scalar(hrow[:, sl:sl + 2048],
                                  x1b[:, sl:sl + 2048], m1_s[:, tb:tb + 1],
                                  r1_s[:, tb:tb + 1], ALU.subtract, ALU.mult)
                dve.tensor_copy(aux2[:, (tb % 2) * 2:(tb % 2) * 2 + 1],
                                gt_s[:, tb:tb + 1])
                s_dv.inc(dve.always.tensor_copy(
                    aux2[:, (tb % 2) * 2 + 1:(tb % 2) * 2 + 2],
                    gate_s[:, tb:tb + 1]))
                dvh9[tb] = s_dv.n
                s_dv.wait(gps, dvh9[tb])
                if tb == 0:
                    s_zz.wait(gps)
                s_sca.wait(gps)
                s_sca.dma(gps.indirect_dma_start(
                    out=selhc.ap(),
                    out_offset=bass.IndirectOffsetOnAxis(
                        ap=idx_s[:, tb:tb + 1], axis=0),
                    in_=hrow[:, sl:sl + 2048], in_offset=None,
                    bounds_check=bnd_reg, oob_is_err=False))
                s_sca.dma(gps.indirect_dma_start(
                    out=selxc.ap(),
                    out_offset=bass.IndirectOffsetOnAxis(
                        ap=idx_s[:, tb:tb + 1], axis=0),
                    in_=aux2[:, (tb % 2) * 2:(tb % 2) * 2 + 2], in_offset=None,
                    bounds_check=bnd_reg, oob_is_err=False))
                sch9[tb] = s_sca.n
                if tb + 2 < NT:
                    s9_fetch(tb + 2)
        s_sca.wait(gps)
        s_cc.inc(gps.always.collective_compute("ReduceScatter", ALU.add, RG,
                                        [selhc.ap().opt()], [selhr.ap().opt()]))
        s_cc.inc(gps.always.collective_compute("ReduceScatter", ALU.add, RG,
                                        [selxc.ap().opt()], [selxr.ap().opt()]))
        cc_sel = s_cc.n
    if stop_after == "P8a":
        return finish()

    syncpoint()
    with contextlib.ExitStack() as p8ctx:
        SB8b = lambda name, shape, dt=F32: p8ctx.enter_context(
            nc.sbuf_tensor(name, list(shape), dt))
        s_cc.wait(sync, cc_sel)
        s_cc.wait(sync, cc_ffnw[0])
        sel_t = SB8b("sel_t", [128, NKT * 2048])
        posv = SB8b("posv", [1, KC])
        gat_v = SB8b("gat_v", [128, NKT])
        for kt in range(NKT):
            s_ld.dma(sync.dma_start(out=sel_t[:, 2048 * kt:2048 * (kt + 1)],
                                    in_=selhr.ap()[128 * kt:128 * (kt + 1), :]))
            s_ld.dma(sync.dma_start(out=gat_v[:, kt:kt + 1],
                                    in_=selxr.ap()[128 * kt:128 * (kt + 1),
                                                   1:2],
                                    allow_slow_non_contiguous=True))
        s_ld.dma(sync.dma_start(
            out=posv[:, :],
            in_=selxr.ap()[:, 0:1].rearrange("r one -> one r"),
            allow_slow_non_contiguous=True))
        s_ld.wait(sync)
        ln1_b = SB8b("ln1_b", [128, 2048])
        ln2_b = SB8b("ln2_b", [128, 2048])
        with nc.sbuf_tensor("lnra", [1, 2048], F32) as lnra, \
             nc.sbuf_tensor("lnrb", [1, 2048], F32) as lnrb, \
             nc.psum_tensor("lb2_ps", [128, 512], F32) as lb2_ps:
            s_in.dma(sync.dma_start(out=lnra[:, :], in_=ln1_i.ap()[None, :]))
            s_in.dma(sync.dma_start(out=lnrb[:, :], in_=ln2_i.ap()[None, :]))
            s_in.wait(pe)
            for (lr, dst) in [(lnra, ln1_b), (lnrb, ln2_b)]:
                for ns in range(4):
                    s_pe.inc(pe.matmul(lb2_ps[:, :], on_f[0:1, :],
                                       lr[:, 512 * ns:512 * (ns + 1)],
                                       start=True, stop=True))
                    s_pe.wait(dve)
                    s_dv.inc(dve.always.tensor_copy(dst[:, 512 * ns:512 * (ns + 1)],
                                             lb2_ps[:, :]))
                    s_dv.wait(pe)
        s_in.wait(sync)
        for eng in (dve, act, gps):
            s_in.wait(eng)
            s_pe.wait(eng)
        pscr = SB8b("pscr8b", [128, 2048])
        mid = contextlib.ExitStack()
        SBm = lambda name, shape, dt=F32: mid.enter_context(
            nc.sbuf_tensor(name, list(shape), dt))
        h1f = SBm("h1f", [128, DC * KC], BF16)
        with nc.sbuf_tensor("h1t", [128, 2048], BF16) as h1t, \
             nc.sbuf_tensor("ra", [128, 1], F32) as ra, \
             nc.psum_tensor("ht_ps0", [128, 128], BF16) as ht_ps0, \
             nc.psum_tensor("ht_ps1", [128, 128], BF16) as ht_ps1, \
             nc.psum_tensor("ht_ps2", [128, 128], BF16) as ht_ps2, \
             nc.psum_tensor("ht_ps3", [128, 128], BF16) as ht_ps3:
            htps = [ht_ps0, ht_ps1, ht_ps2, ht_ps3]
            kh = 0
            hdv = [0] * (NKT * DC)
            for kt in range(NKT):
                s_ld.wait(act)
                s_ld.wait(dve)
                s_ac.inc(act.always.activation(pscr[:, :],
                                        sel_t[:, 2048 * kt:2048 * (kt + 1)],
                                        AF.Square, accum_out=ra[:, :]))
                s_ac.wait(dve)
                s_dv.inc(dve.always.tensor_scalar_mul(ra[:, :], ra[:, :], 1.0 / D))
                s_dv.wait(act)
                s_ac.inc(act.always.activation(ra[:, :], ra[:, :], AF.Sqrt,
                                        bias=EPS_RMS))
                s_ac.wait(dve)
                dve.reciprocal(ra[:, :], ra[:, :])
                s_pe.wait(dve)
                s_dv.inc(dve.always.scalar_tensor_tensor(
                    h1t[:, :], sel_t[:, 2048 * kt:2048 * (kt + 1)], ra[:, :],
                    ln1_b[:, :], ALU.mult, ALU.mult))
                s_dv.wait(pe)
                for dc in range(DC):
                    pph = htps[kh % 4]
                    if kh >= 4:
                        s_ac.wait(pe, hdv[kh - 4])
                    s_pe.inc(pe.transpose(pph[:, :],
                                          h1t[:, 128 * dc:128 * (dc + 1)],
                                          ey_bf[:, :]))
                    s_pe.wait(act)
                    s_ac.inc(act.always.copy(
                        h1f[:, KC * dc + 128 * kt:KC * dc + 128 * (kt + 1)],
                        pph[:, :]))
                    hdv[kh] = s_ac.n
                    kh += 1
        cosf = SBm("cosf", [128, KC])
        sinf = SBm("sinf", [128, KC])
        sneg = SBm("sneg", [128, KC])
        with nc.psum_tensor("an_ps", [128, KC], F32) as an_ps, \
             nc.sbuf_tensor("ang", [128, KC], F32) as ang, \
             nc.sbuf_tensor("ang2", [128, KC], F32) as ang2, \
             nc.sbuf_tensor("nn", [128, KC], F32) as nn, \
             nc.sbuf_tensor("nni", [128, KC], I32) as nni:
            s_ld.wait(pe)
            s_ac.wait(pe)
            for ns in range(max(1, KC // 512)):
                w = min(512, KC)
                s_pe.inc(pe.matmul(an_ps[:, w * ns:w * (ns + 1)], ivf[:, :],
                                   posv[:, w * ns:w * (ns + 1)],
                                   start=True, stop=True))
            s_pe.wait(dve)
            s_dv.inc(dve.always.tensor_copy(ang[:, :], an_ps[:, :]))

            def range_reduce_sin(src_ap, out_ap):
                dve.tensor_scalar_mul(nn[:, :], src_ap, 1.0 / TWO_PI)
                dve.tensor_scalar(nn[:, :], nn[:, :], 0.5, None, ALU.add)
                dve.tensor_copy(nni[:, :], nn[:, :])
                dve.tensor_copy(nn[:, :], nni[:, :])
                dve.scalar_tensor_tensor(ang2[:, :], nn[:, :], -CW_C1, src_ap,
                                         ALU.mult, ALU.add)
                s_dv.inc(dve.always.scalar_tensor_tensor(ang2[:, :], nn[:, :], -CW_C2,
                                                  ang2[:, :], ALU.mult, ALU.add))
                s_dv.wait(act)
                s_ac.inc(act.always.activation(out_ap, ang2[:, :], AF.Sin))
                s_ac.wait(dve)

            range_reduce_sin(ang[:, :], sinf[:, :])
            dve.tensor_scalar(ang[:, :], ang[:, :], math.pi / 2.0, None,
                              ALU.add)
            range_reduce_sin(ang[:, :], cosf[:, :])
            dve.tensor_scalar_mul(sneg[0:64, :], sinf[0:64, :], -1.0)
            s_dv.inc(dve.always.tensor_copy(sneg[64:128, :], sinf[64:128, :]))
        fqt = SBm("fqt", [128, FH * KC], BF16)
        fkt = SBm("fkt", [128, HE * KC], BF16)
        fvt = SBm("fvt", [128, NKT * 1024], BF16)
        bq_s = SBm("bq_s", [128, DC])
        bkv_s = SBm("bkv_s", [128, 16])
        with nc.sbuf_tensor("btmp", [16, 128], F32) as btmp, \
             nc.psum_tensor("bt_ps", [128, 16], F32) as bt_ps:
            s_dv.wait(pe)
            s_ac.wait(pe)
            s_in.dma(sync.dma_start(
                out=btmp[:, :], in_=bq_i.ap().rearrange("(k p) -> k p", p=128)))
            s_in.wait(pe)
            s_pe.inc(pe.transpose(bt_ps[:, :], btmp[:, :], ey_f[0:16, 0:16]))
            s_pe.wait(dve)
            s_dv.inc(dve.always.tensor_copy(bq_s[:, :], bt_ps[:, :]))
            s_dv.wait(pe)
            s_pe.wait(sync)
            s_in.dma(sync.dma_start(
                out=btmp[0:8, :], in_=bk_i.ap().rearrange("(k p) -> k p", p=128)))
            s_in.wait(pe)
            s_pe.inc(pe.transpose(bt_ps[:, 0:8], btmp[0:8, :], ey_f[0:8, 0:8]))
            s_pe.wait(dve)
            s_dv.inc(dve.always.tensor_copy(bkv_s[:, 0:8], bt_ps[:, 0:8]))
            s_dv.wait(pe)
            s_pe.wait(sync)
            s_in.dma(sync.dma_start(
                out=btmp[0:8, :], in_=bv_i.ap().rearrange("(k p) -> k p", p=128)))
            s_in.wait(pe)
            s_pe.inc(pe.transpose(bt_ps[:, 0:8], btmp[0:8, :], ey_f[0:8, 0:8]))
            s_pe.wait(dve)
            s_dv.inc(dve.always.tensor_copy(bkv_s[:, 8:16], bt_ps[:, 0:8]))
            s_dv.wait(pe)
        with nc.sbuf_tensor("fw_t", [128, 2 * 2048], BF16) as fw_t, \
             nc.psum_tensor("fp_ps", [128, 1024], F32) as fp_ps, \
             nc.psum_tensor("fr_ps", [128, 128], BF16) as fr_ps, \
             nc.sbuf_tensor("prj", [128, KC], F32) as prj, \
             nc.sbuf_tensor("prh", [128, KC], F32) as prh, \
             nc.sbuf_tensor("sq_t", [128, KC], F32) as sq_t, \
             nc.sbuf_tensor("qsw", [128, KC], F32) as qsw, \
             nc.sbuf_tensor("vtmp", [128, KC], BF16) as vtmp:
            s_in.wait(dve)
            calls = ([(FW_WQ, h, bq_s, h, "q") for h in range(FH)]
                     + [(FW_WK, h, bkv_s, h, "k") for h in range(HE)]
                     + [(FW_WV, h, bkv_s, 8 + h, "v") for h in range(HE)])
            NCALL = len(calls)
            ldc = [0] * NCALL
            pe_after = [0] * NCALL
            dv_after = [0] * NCALL

            def load_call(n):
                col0, h = calls[n][0], calls[n][1]
                sl = (n % 2) * 2048
                s_ld.wait(sync)
                if n >= 2:
                    s_pe.wait(sync, pe_after[n - 2])
                gcol = col0 + 128 * h
                s_ld.dma(sync.dma_start(
                    out=fw_t[:, sl:sl + 2048].rearrange("p (dc k) -> p dc k",
                                                        dc=DC),
                    in_=fw1gAs[gcol // 2048]
                    .ap()[:, gcol % 2048:gcol % 2048 + 128]
                    .rearrange("(dc p) k -> p dc k", p=128)))
                ldc[n] = s_ld.n

            def rope_to(dst_ap):
                dve.tensor_copy(qsw[0:64, :], prj[64:128, :])
                dve.tensor_copy(qsw[64:128, :], prj[0:64, :])
                dve.tensor_mul(prh[:, :], prj[:, :], cosf[:, :])
                dve.tensor_mul(sq_t[:, :], qsw[:, :], sneg[:, :])
                s_dv.inc(dve.always.tensor_add(dst_ap, prh[:, :], sq_t[:, :]))

            s_in.wait(sync)
            s_pe.wait(sync)
            s_dv.wait(sync)
            load_call(0)
            for n in range(NCALL):
                col0, h, bias_col, bias_idx, kind = calls[n]
                if n + 1 < NCALL:
                    load_call(n + 1)
                sl = (n % 2) * 2048
                ps = fp_ps[:, (n % 2) * 512:(n % 2) * 512 + KC]
                s_ld.wait(pe, ldc[n])
                if n >= 2:
                    s_dv.wait(pe, dv_after[n - 2])
                for dc in range(DC):
                    s_pe.inc(pe.matmul(ps,
                                       fw_t[:, sl + 128 * dc:sl + 128 * (dc + 1)],
                                       h1f[:, KC * dc:KC * (dc + 1)],
                                       start=(dc == 0), stop=(dc == DC - 1)))
                pe_after[n] = s_pe.n
                s_pe.wait(dve, pe_after[n])
                s_dv.inc(dve.always.tensor_scalar(prj[:, :], ps,
                                           bias_col[:, bias_idx:bias_idx + 1],
                                           None, ALU.add))
                dv_after[n] = s_dv.n
                if kind == "q":
                    rope_to(fqt[:, KC * h:KC * (h + 1)])
                    if h == FH - 1:
                        s_dv.wait(sync)
                        s_st.wait(sync)
                        for h2 in range(FH):
                            s_st.dma(sync.dma_start(
                                out=fqc.ap()[128 * h2:128 * (h2 + 1), :],
                                in_=fqt[:, KC * h2:KC * (h2 + 1)]))
                        s_st.wait(gps)
                        s_cc.inc(gps.always.collective_compute(
                            "AllGather", ALU.bypass, RG,
                            [fqc.ap().opt()], [fqg.ap().opt()]))
                        cc_fq = s_cc.n
                elif kind == "k":
                    rope_to(fkt[:, KC * h:KC * (h + 1)])
                else:
                    s_pe.wait(dve)
                    s_dv.inc(dve.always.tensor_copy(vtmp[:, :], prj[:, :]))
                    s_dv.wait(pe)
                    for kt in range(NKT):
                        s_pe.inc(pe.transpose(fr_ps[:, :],
                                              vtmp[:, 128 * kt:128 * (kt + 1)],
                                              ey_bf[:, :]))
                        s_pe.wait(dve)
                        s_dv.inc(dve.always.tensor_copy(
                            fvt[:, 1024 * kt + 128 * h:
                                1024 * kt + 128 * (h + 1)],
                            fr_ps[:, :]))
                        s_dv.wait(pe)

        syncpoint()
        s_cc.wait(sync, cc_fq)
        fqh = SBm("fqh", [128, 2 * cfg.KSEL], BF16)
        amk = SBm("amk", [128, NKT * QH * QHW], BF16)
        for sidx in range(NKT * QH):
            s_ld.dma(sync.dma_start(out=amk[:, QHW * sidx:QHW * (sidx + 1)],
                                    in_=am_i.ap()[sidx, :, :]))
        s_ld.wait(sync)
        with nc.psum_tensor("fsc_ps", [128, 2 * QHW], F32) as fsc_ps, \
             nc.psum_tensor("fz_ps", [128, QHW], F32) as fz_ps, \
             nc.psum_tensor("fl_ps", [1, QHW], F32) as fl_ps, \
             nc.sbuf_tensor("fpT", [128, 2 * QHW], BF16) as fpT, \
             nc.sbuf_tensor("fzs", [128, QHW], F32) as fzs, \
             nc.sbuf_tensor("fls", [1, QHW], F32) as fls:

            def fq_load(h):
                s_fqp[h % 2].dma(sync.dma_start(
                    out=fqh[:, (h % 2) * cfg.KSEL:(h % 2 + 1) * cfg.KSEL]
                    .rearrange("p (o k) -> p o k", o=NCORES),
                    in_=fqg.ap().rearrange("(o hh p) k -> hh p o k",
                                           o=NCORES, p=128)[h]))

            fvisits = [(h, qh, jj) for h in range(FH) for qh in range(QH)
                       for jj in range(NKT)]
            NFV = len(fvisits)
            ex_base = s_ex.n
            fq_ld = [0] * FH
            fq_load(0)
            fq_ld[0] = s_fqp[0].n
            fmk_hist = [0] * NFV
            fav_hist = [0] * NFV
            fsc_hist = [0] * NFV
            fsc_end = []

            def femit_scores(k):
                h_, qh_, jj_ = fvisits[k]
                kvh_ = h_ // 2
                off = (k % 2) * QHW
                s_fqp[h_ % 2].wait(pe, fq_ld[h_])
                if k >= 2:
                    s_ex.wait(pe, ex_base + k - 1)
                for ns in range(QHW // 512):
                    s_sc.inc(pe.matmul(
                        fsc_ps[:, off + 512 * ns:off + 512 * (ns + 1)],
                        fkt[:, KC * kvh_ + 128 * jj_:KC * kvh_ + 128 * (jj_ + 1)],
                        fqh[:, (h_ % 2) * cfg.KSEL + QHW * qh_ + 512 * ns:
                            (h_ % 2) * cfg.KSEL + QHW * qh_ + 512 * (ns + 1)],
                        start=True, stop=True))
                fsc_hist[k] = s_sc.n

            def femit_evac(av_t):
                s_av.wait(act, av_t)
                s_st.wait(act)
                s_ev2.inc(act.copy(fzs[:, :], fz_ps[:, :]))
                s_ev2.inc(act.copy(fls[:, :], fl_ps[0:1, :]))

            def femit_store(h_p, qh_p):
                s_ev2.wait(sync)
                ft_ = fzlcA if h_p < FH // 2 else fzlcB
                hl_ = h_p % (FH // 2)
                for cons in range(cfg.NCONS):
                    r_ = (QHW * qh_p) // KC + cons
                    s_st.dma(sync.dma_start(
                        out=ft_.ap()[FZR * r_ + 128 * hl_:
                                     FZR * r_ + 128 * (hl_ + 1), :],
                        in_=fzs[:, KC * cons:KC * (cons + 1)]))
                    s_st.dma(sync.dma_start(
                        out=ft_.ap()[FZR * r_ + 1024 + hl_:
                                     FZR * r_ + 1025 + hl_, :],
                        in_=fls[:, KC * cons:KC * (cons + 1)]))
                if h_p == FH // 2 - 1 and qh_p == QH - 1:
                    s_st.wait(gps)
                    s_cc.inc(gps.always.collective_compute(
                        "ReduceScatter", ALU.add, RG,
                        [fzlcA.ap().opt()], [fzlrA.ap().opt()]))
                    cc_fzlh[0] = s_cc.n

            fpend = None
            cc_fzlh = [0, 0]
            femit_scores(0)
            for k in range(NFV):
                h, qh, jj = fvisits[k]
                kvh = h // 2
                off = (k % 2) * QHW
                if qh == 0 and jj == 0 and h + 1 < FH:
                    if h >= 1:
                        s_sc.wait(sync, fsc_end[h - 1])
                    fq_load(h + 1)
                    fq_ld[h + 1] = s_fqp[(h + 1) % 2].n
                if k + 1 < NFV:
                    femit_scores(k + 1)
                s_sc.wait(act, fsc_hist[k])
                if k >= 2:
                    s_av.wait(act, fav_hist[k - 2])
                s_ex.inc(act.activation(fpT[:, off:off + QHW],
                                        fsc_ps[:, off:off + QHW], AF.Exp,
                                        scale=1.0 / math.sqrt(FDH),
                                        bias=-C_FFN))
                if fpend is not None:
                    femit_evac(fpend[2])
                    femit_store(fpend[0], fpend[1])
                    fpend = None
                s_ex.wait(dve, s_ex.n)
                s_ld.wait(dve)
                s_mk.inc(dve.tensor_mul(
                    fpT[:, off:off + QHW], fpT[:, off:off + QHW],
                    amk[:, QHW * (jj * QH + qh):QHW * (jj * QH + qh + 1)]))
                fmk_hist[k] = s_mk.n
                s_mk.wait(pe, fmk_hist[k])
                fev0 = s_ev2.n if (jj == 0 and k > 0) else None
                st_ = (jj == 0)
                sp_ = (jj == NKT - 1)
                for ns in range(QHW // 512):
                    if fev0 is not None:
                        s_ev2.wait(pe, fev0 - 1)
                    s_av.inc(pe.matmul(
                        fz_ps[:, 512 * ns:512 * (ns + 1)],
                        fvt[:, 1024 * jj + 128 * kvh:1024 * jj + 128 * (kvh + 1)],
                        fpT[:, off + 512 * ns:off + 512 * (ns + 1)],
                        start=st_, stop=sp_))
                    if fev0 is not None:
                        s_ev2.wait(pe, fev0)
                        fev0 = None
                    s_av.inc(pe.matmul(
                        fl_ps[0:1, 512 * ns:512 * (ns + 1)], on_bf[:, 0:1],
                        fpT[:, off + 512 * ns:off + 512 * (ns + 1)],
                        start=st_, stop=sp_, tile_position=(0, 0)))
                fav_hist[k] = s_av.n
                if jj == NKT - 1:
                    fpend = (h, qh, fav_hist[k])
                    if qh == QH - 1:
                        fsc_end.append(s_sc.n)
            femit_evac(fpend[2])
            femit_store(fpend[0], fpend[1])
        s_st.wait(gps)
        s_cc.inc(gps.always.collective_compute("ReduceScatter", ALU.add, RG,
                                        [fzlcB.ap().opt()], [fzlrB.ap().opt()]))
        cc_fzlh[1] = s_cc.n
        mid.close()
        fzn = SB8b("fzn", [128, FH * KC], BF16)
        x1f = SB8b("x1f", [128, NKT * 2048])
        woctx = contextlib.ExitStack()
        fw_o = woctx.enter_context(nc.sbuf_tensor("fw_o", [128, 2 * 2048],
                                                  BF16))
        ldc4 = [0] * DC
        pe_af4 = [0] * DC

        def load_o(n):
            sl = (n % 2) * 2048
            s_ld.wait(sync)
            if n >= 2:
                s_pe.wait(sync, pe_af4[n - 2])
            s_ld.dma(sync.dma_start(
                out=fw_o[:, sl:sl + 2048].rearrange("p (zc k) -> p zc k",
                                                    zc=FH),
                in_=fw1gAs[2].ap()[0:2048, 128 * n:128 * (n + 1)]
                .rearrange("(zc p) k -> p zc k", p=128)))
            ldc4[n] = s_ld.n

        load_o(0)
        load_o(1)
        with nc.sbuf_tensor("fzl_t", [128, KC], F32) as fzl_t, \
             nc.sbuf_tensor("fli", [1, KC], F32) as fli, \
             nc.psum_tensor("fb_ps", [128, KC], F32) as fb_ps:
            for h in range(FH):
                fr_ = fzlrA if h < FH // 2 else fzlrB
                hl2 = h % (FH // 2)
                if h == 0:
                    s_cc.wait(sync, cc_fzlh[0])
                if h == FH // 2:
                    s_cc.wait(sync, cc_fzlh[1])
                s_dv.wait(sync)
                s_ld.wait(sync)
                s_ld.dma(sync.dma_start(
                    out=fzl_t[:, :],
                    in_=fr_.ap()[128 * hl2:128 * (hl2 + 1), :]))
                s_ld.dma(sync.dma_start(
                    out=fli[:, :],
                    in_=fr_.ap()[1024 + hl2:1025 + hl2, :]))
                s_ld.wait(sync)
                s_ld.wait(dve)
                s_dv.inc(dve.always.reciprocal(fli[:, :], fli[:, :]))
                s_dv.wait(pe)
                for ns in range(max(1, KC // 512)):
                    w = min(512, KC)
                    s_pe.inc(pe.matmul(fb_ps[:, w * ns:w * (ns + 1)],
                                       on_f[0:1, :], fli[:, w * ns:w * (ns + 1)],
                                       start=True, stop=True))
                s_pe.wait(dve)
                s_dv.inc(dve.always.tensor_mul(fzn[:, KC * h:KC * (h + 1)],
                                        fzl_t[:, :], fb_ps[:, :]))
                s_dv.wait(pe)
        with nc.psum_tensor("fo_ps", [128, KC], F32) as fo_ps, \
             nc.psum_tensor("fo_tr", [128, 128], F32) as fo_tr, \
             nc.sbuf_tensor("ao2", [128, KC], F32) as ao2:
            for dc in range(DC):
                sl = (dc % 2) * 2048
                s_ld.wait(pe, ldc4[dc])
                s_dv.wait(pe)
                for zc in range(FH):
                    s_pe.inc(pe.matmul(fo_ps[:, :],
                                       fw_o[:, sl + 128 * zc:sl + 128 * (zc + 1)],
                                       fzn[:, KC * zc:KC * (zc + 1)],
                                       start=(zc == 0), stop=(zc == FH - 1)))
                pe_af4[dc] = s_pe.n
                if dc + 2 < DC:
                    load_o(dc + 2)
                s_pe.wait(dve)
                s_dv.inc(dve.always.tensor_copy(ao2[:, :], fo_ps[:, :]))
                s_dv.wait(pe)
                for kt in range(NKT):
                    s_pe.inc(pe.transpose(fo_tr[:, :],
                                          ao2[:, 128 * kt:128 * (kt + 1)],
                                          ey_f[:, :]))
                    s_pe.wait(dve)
                    s_dv.inc(dve.always.tensor_copy(
                        x1f[:, 2048 * kt + 128 * dc:2048 * kt + 128 * (dc + 1)],
                        fo_tr[:, :]))
                    s_dv.wait(pe)
        s_ld.wait(sync)
        s_pe.wait(sync)
        woctx.close()
        for kt in range(NKT):
            s_dv.inc(dve.always.tensor_add(x1f[:, 2048 * kt:2048 * (kt + 1)],
                                    x1f[:, 2048 * kt:2048 * (kt + 1)],
                                    sel_t[:, 2048 * kt:2048 * (kt + 1)]))
        h2f = SB8b("h2f", [128, DC * KC], BF16)
        with nc.sbuf_tensor("h2t", [128, 2048], BF16) as h2t, \
             nc.sbuf_tensor("rb", [128, 1], F32) as rb, \
             nc.psum_tensor("h2_ps0", [128, 128], BF16) as h2_ps0, \
             nc.psum_tensor("h2_ps1", [128, 128], BF16) as h2_ps1, \
             nc.psum_tensor("h2_ps2", [128, 128], BF16) as h2_ps2, \
             nc.psum_tensor("h2_ps3", [128, 128], BF16) as h2_ps3:
            h2ps = [h2_ps0, h2_ps1, h2_ps2, h2_ps3]
            kh2 = 0
            h2dv = [0] * (NKT * DC)
            for kt in range(NKT):
                s_dv.wait(act)
                s_ac.inc(act.always.activation(pscr[:, :],
                                        x1f[:, 2048 * kt:2048 * (kt + 1)],
                                        AF.Square, accum_out=rb[:, :]))
                s_ac.wait(dve)
                s_dv.inc(dve.always.tensor_scalar_mul(rb[:, :], rb[:, :], 1.0 / D))
                s_dv.wait(act)
                s_ac.inc(act.always.activation(rb[:, :], rb[:, :], AF.Sqrt,
                                        bias=EPS_RMS))
                s_ac.wait(dve)
                dve.reciprocal(rb[:, :], rb[:, :])
                s_pe.wait(dve)
                s_dv.inc(dve.always.scalar_tensor_tensor(
                    h2t[:, :], x1f[:, 2048 * kt:2048 * (kt + 1)], rb[:, :],
                    ln2_b[:, :], ALU.mult, ALU.mult))
                s_dv.wait(pe)
                for dc in range(DC):
                    pph = h2ps[kh2 % 4]
                    if kh2 >= 4:
                        s_ac.wait(pe, h2dv[kh2 - 4])
                    s_pe.inc(pe.transpose(pph[:, :],
                                          h2t[:, 128 * dc:128 * (dc + 1)],
                                          ey_bf[:, :]))
                    s_pe.wait(act)
                    s_ac.inc(act.always.copy(
                        h2f[:, KC * dc + 128 * kt:KC * dc + 128 * (kt + 1)],
                        pph[:, :]))
                    h2dv[kh2] = s_ac.n
                    kh2 += 1
        s_dv.wait(sync)
        s_ac.wait(sync)
        s_st.dma(sync.dma_start(
            out=h2c.ap().rearrange("(dc p) t -> p dc t", p=128),
            in_=h2f[:, :].rearrange("p (dc t) -> p dc t", t=KC)))
        s_st.wait(gps)
        s_cc.inc(gps.always.collective_compute("AllGather", ALU.bypass, RG,
                                        [h2c.ap().opt()], [h2g.ap().opt()]))
        cc_h2 = s_cc.n
        ITN = FI // NCORES // 128
        KS = cfg.KSEL
        TB2 = KS // 512
        tsb = SB8b("tsb", [128, ITN * KS], BF16)
        with nc.sbuf_tensor("h2a", [128, DC * KS], BF16) as h2a, \
             nc.sbuf_tensor("wgu", [128, 2 * 2048], BF16) as wgu, \
             nc.psum_tensor("g_ps", [128, KS], F32) as g_ps, \
             nc.psum_tensor("u_ps", [128, KS], F32) as u_ps, \
             nc.sbuf_tensor("gsb", [128, KS], BF16) as gsb, \
             nc.sbuf_tensor("usb", [128, KS], BF16) as usb:
            s_dv.wait(sync)
            s_ac.wait(sync)
            bursts = [(kind, it) for it in range(ITN) for kind in ("g", "u")]
            NB = len(bursts)
            ldc2 = [0] * NB
            pe_af2 = [0] * NB

            def load_b(n):
                kind, it = bursts[n]
                base = 0 if kind == "g" else FI // NCORES
                sl = (n % 2) * 2048
                s_ld.wait(sync)
                if n >= 2:
                    s_pe.wait(sync, pe_af2[n - 2])
                s_ld.dma(sync.dma_start(
                    out=wgu[:, sl:sl + 2048].rearrange("p (dc k) -> p dc k",
                                                       dc=DC),
                    in_=fguc.ap()[:, base + 128 * it:base + 128 * (it + 1)]
                    .rearrange("(dc p) k -> p dc k", p=128)))
                ldc2[n] = s_ld.n

            load_b(0)
            load_b(1)
            s_cc.wait(sync, cc_h2)
            s_ld.wait(sync)
            for dc in range(DC):
                s_ld.dma(sync.dma_start(
                    out=h2a[:, KS * dc:KS * (dc + 1)]
                    .rearrange("p (o t) -> p o t", o=NCORES),
                    in_=h2g.ap().rearrange("(o dc2 p) t -> dc2 p o t",
                                           dc2=DC, p=128)[dc]))
            ldh2 = s_ld.n
            s_ld.wait(sync)
            s_ld.wait(pe, ldh2)
            pe_g = 0
            for n in range(NB):
                kind, it = bursts[n]
                sl = (n % 2) * 2048
                s_ld.wait(pe, ldc2[n])
                if kind == "g":
                    s_ac.wait(pe)
                    for dc in range(DC):
                        for tb in range(TB2):
                            s_pe.inc(pe.matmul(
                                g_ps[:, 512 * tb:512 * (tb + 1)],
                                wgu[:, sl + 128 * dc:sl + 128 * (dc + 1)],
                                h2a[:, KS * dc + 512 * tb:
                                    KS * dc + 512 * (tb + 1)],
                                start=(dc == 0), stop=(dc == DC - 1)))
                    pe_g = s_pe.n
                else:
                    s_dv.wait(pe)
                    for dc in range(DC):
                        for tb in range(TB2):
                            s_pe.inc(pe.matmul(
                                u_ps[:, 512 * tb:512 * (tb + 1)],
                                wgu[:, sl + 128 * dc:sl + 128 * (dc + 1)],
                                h2a[:, KS * dc + 512 * tb:
                                    KS * dc + 512 * (tb + 1)],
                                start=(dc == 0), stop=(dc == DC - 1)))
                    pe_u = s_pe.n
                    s_pe.wait(act, pe_g)
                    s_dv.wait(act)
                    s_ac.inc(act.always.activation(gsb[:, :], g_ps[:, :], AF.Silu))
                    s_pe.wait(dve, pe_u)
                    s_ac.wait(dve)
                    dve.tensor_copy(usb[:, :], u_ps[:, :])
                    s_dv.inc(dve.always.tensor_mul(tsb[:, KS * it:KS * (it + 1)],
                                            gsb[:, :], usb[:, :]))
                pe_af2[n] = s_pe.n
                if n + 2 < NB:
                    load_b(n + 2)
        gtd = SB8b("gtd", [128, NKT * 2048])
        with nc.sbuf_tensor("fdn_s", [128, ITN * 2048], BF16) as fdn_s, \
             nc.psum_tensor("dn_psA", [128, 2048], F32) as dn_psA, \
             nc.psum_tensor("dn_psB", [128, 2048], F32) as dn_psB, \
             nc.sbuf_tensor("dnb", [128, 2048], F32) as dnb, \
             nc.sbuf_tensor("dnb2", [128, 2048], F32) as dnb2:
            s_dv.wait(sync)
            s_ac.wait(sync)
            s_pe.wait(sync)
            s_ld.wait(sync)
            for it in range(ITN):
                s_ld.dma(sync.dma_start(
                    out=fdn_s[:, 2048 * it:2048 * (it + 1)],
                    in_=fdnc.ap()[128 * it:128 * (it + 1), :]))
            ldn = s_ld.n
            s_ld.wait(pe, ldn)
            s_dv.wait(pe)
            TT = KS // 128
            SPL = NKT >= 2
            if SPL:
                tts = ([t2 for t2 in range(TT) if t2 % NKT < NKT // 2]
                       + [t2 for t2 in range(TT) if t2 % NKT >= NKT // 2])
            else:
                tts = list(range(TT))
            dn_done = [0] * TT
            st_dn = [0] * TT
            cc_dnh = [0, 0]
            for nn_, tt in enumerate(tts):
                ps = dn_psA if nn_ % 2 == 0 else dn_psB
                bufd = dnb if nn_ % 2 == 0 else dnb2
                if nn_ >= 2:
                    s_ac.wait(pe, dn_done[nn_ - 2])
                for it in range(ITN):
                    for db in range(4):
                        s_pe.inc(pe.matmul(
                            ps[:, 512 * db:512 * (db + 1)],
                            tsb[:, KS * it + 128 * tt:KS * it + 128 * (tt + 1)],
                            fdn_s[:, 2048 * it + 512 * db:
                                  2048 * it + 512 * (db + 1)],
                            start=(it == 0), stop=(it == ITN - 1)))
                s_pe.wait(act)
                if nn_ >= 2:
                    s_st.wait(act, st_dn[nn_ - 2])
                s_ac.inc(act.always.copy(bufd[:, :], ps[:, :]))
                dn_done[nn_] = s_ac.n
                s_ac.wait(sync, dn_done[nn_])
                s_st.wait(sync)
                o_, j_ = tt // NKT, tt % NKT
                if SPL and j_ >= NKT // 2:
                    dt_, ro_ = dncB, (KC // 2) * o_ + 128 * (j_ - NKT // 2)
                else:
                    dt_, ro_ = dncA, (KC // 2 if SPL else KC) * o_ + 128 * j_
                s_st.dma(sync.dma_start(
                    out=dt_.ap()[ro_:ro_ + 128, :], in_=bufd[:, :]))
                st_dn[nn_] = s_st.n
                if SPL and nn_ == TT // 2 - 1:
                    s_st.wait(gps)
                    s_cc.inc(gps.always.collective_compute(
                        "ReduceScatter", ALU.add, RG,
                        [dncA.ap().opt()], [dnrA.ap().opt()]))
                    cc_dnh[0] = s_cc.n
        s_st.wait(gps)
        if SPL:
            s_cc.inc(gps.always.collective_compute(
                "ReduceScatter", ALU.add, RG,
                [dncB.ap().opt()], [dnrB.ap().opt()]))
            cc_dnh[1] = s_cc.n
        else:
            s_cc.inc(gps.always.collective_compute(
                "ReduceScatter", ALU.add, RG,
                [dncA.ap().opt()], [dnrA.ap().opt()]))
            cc_dnh[0] = cc_dnh[1] = s_cc.n
        for kt in range(NKT):
            s_cc.wait(sync, cc_dnh[1 if SPL and kt >= NKT // 2 else 0])
            if SPL and kt >= NKT // 2:
                src_, ro_ = dnrB, 128 * (kt - NKT // 2)
            else:
                src_, ro_ = dnrA, 128 * kt
            s_ld.dma(sync.dma_start(
                out=gtd[:, 2048 * kt:2048 * (kt + 1)],
                in_=src_.ap()[ro_:ro_ + 128, :]))
        s_ld.wait(sync)
        s_ld.wait(dve)
        for kt in range(NKT):
            dve.tensor_add(gtd[:, 2048 * kt:2048 * (kt + 1)],
                           gtd[:, 2048 * kt:2048 * (kt + 1)],
                           x1f[:, 2048 * kt:2048 * (kt + 1)])
            dve.tensor_sub(gtd[:, 2048 * kt:2048 * (kt + 1)],
                           gtd[:, 2048 * kt:2048 * (kt + 1)],
                           sel_t[:, 2048 * kt:2048 * (kt + 1)])
            dve.tensor_scalar(gtd[:, 2048 * kt:2048 * (kt + 1)],
                              gtd[:, 2048 * kt:2048 * (kt + 1)],
                              gat_v[:, kt:kt + 1], None, ALU.mult)
            s_dv.inc(dve.always.tensor_add(gtd[:, 2048 * kt:2048 * (kt + 1)],
                                    gtd[:, 2048 * kt:2048 * (kt + 1)],
                                    sel_t[:, 2048 * kt:2048 * (kt + 1)]))
            s_dv.wait(sync)
            s_st.dma(sync.dma_start(out=gatc.ap()[128 * kt:128 * (kt + 1), :],
                                    in_=gtd[:, 2048 * kt:2048 * (kt + 1)]))
        s_st.wait(gps)
        s_cc.inc(gps.always.collective_compute("AllGather", ALU.bypass, RG,
                                        [gatc.ap().opt()], [gatg.ap().opt()]))
        cc_gat = s_cc.n
    if stop_after == "P8b":
        return finish()

    syncpoint()
    with contextlib.ExitStack() as p9ctx:
        SB9 = lambda name, shape, dt=F32: p9ctx.enter_context(
            nc.sbuf_tensor(name, list(shape), dt))
        pscr = SB9("pscr9", [128, 2048])
        with nc.sbuf_tensor("k1", [128, 1], F32) as k1, \
             nc.sbuf_tensor("k2", [128, 1], F32) as k2:
            for tb in range(NT):
                dve.tensor_mul(k1[:, :], r1_s[:, tb:tb + 1], r1_s[:, tb:tb + 1])
                s_dv.inc(dve.always.tensor_mul(k1[:, :], k1[:, :], v1_s[:, tb:tb + 1]))
                s_dv.wait(act)
                s_ac.inc(act.always.activation(k2[:, :], k1[:, :], AF.Sqrt,
                                        bias=EPS_LN))
                s_ac.wait(dve)
                dve.reciprocal(k2[:, :], k2[:, :])
                dve.tensor_mul(k2[:, :], k2[:, :], r1_s[:, tb:tb + 1])
                dve.tensor_scalar(k1[:, :], msk_s[:, tb:tb + 1], -1.0, None,
                                  ALU.mult)
                dve.tensor_scalar(k1[:, :], k1[:, :], 1.0, None, ALU.add)
                dve.tensor_mul(k1[:, :], k1[:, :], k2[:, :])
                dve.tensor_scalar(ka_s[:, tb:tb + 1], k1[:, :], 1.0, None,
                                  ALU.add)
                s_dv.inc(dve.always.tensor_mul(kb_s[:, tb:tb + 1], m1_s[:, tb:tb + 1],
                                        k1[:, :]))
        with nc.sbuf_tensor("x1c", [128, 2 * 2048], F32) as x1c, \
             nc.sbuf_tensor("gg", [128, 2 * 2048], F32) as gg, \
             nc.sbuf_tensor("oo", [128, 2 * 2048], F32) as oo, \
             nc.sbuf_tensor("ga", [128, 1], F32) as ga, \
             nc.sbuf_tensor("gb", [128, 1], F32) as gb, \
             nc.sbuf_tensor("gc", [128, 1], F32) as gc:
            gat_h = [0] * NT
            x_h = [0] * NT
            dv_h = [0] * NT
            out_h = [0] * NT

            def p9_pre(tb):
                sl = (tb % 2) * 2048
                s_x.wait(sync)
                if tb >= 2:
                    s_dv.wait(sync, dv_h[tb - 2])
                s_x.dma(sync.dma_start(
                    out=x1c[:, sl:sl + 2048],
                    in_=x1d.ap()[128 * tb:128 * (tb + 1), :]))
                x_h[tb] = s_x.n
                if tb >= 2:
                    s_dv.wait(gps, dv_h[tb - 2])
                s_gp.inc(gps.always.memset(gg[:, sl:sl + 2048], 0.0))
                s_gp.wait(gps)

            def p9_gather(tb):
                sl = (tb % 2) * 2048
                s_gat.wait(gps)
                s_gat.dma(gps.indirect_dma_start(
                    out=gg[:, sl:sl + 2048], out_offset=None,
                    in_=gatg.ap(),
                    in_offset=bass.IndirectOffsetOnAxis(
                        ap=idx_s[:, tb:tb + 1], axis=0),
                    bounds_check=bnd_reg, oob_is_err=False))
                gat_h[tb] = s_gat.n

            def p9_fetch(tb):
                p9_pre(tb)
                p9_gather(tb)

            s_ac.wait(sync)
            s_dv.wait(sync)
            p9_pre(0)
            p9_pre(1)
            s_cc.wait(sync, cc_gat)
            s_cc.wait(gps, cc_gat)
            p9_gather(0)
            p9_gather(1)
            for tb in range(NT):
                sl = (tb % 2) * 2048
                s_gat.wait(dve, gat_h[tb])
                s_gat.wait(act, gat_h[tb])
                s_x.wait(dve, x_h[tb])
                if tb >= 1:
                    s_dv.wait(act, dv_h[tb - 1])
                s_ac.inc(act.always.activation(pscr[:, :], gg[:, sl:sl + 2048],
                                        AF.Identity, accum_out=ga[:, :]))
                s_ac.inc(act.always.activation(pscr[:, :], gg[:, sl:sl + 2048],
                                        AF.Square, accum_out=gb[:, :]))
                s_ac.wait(dve)
                dve.tensor_scalar_mul(ga[:, :], ga[:, :], 1.0 / D)
                dve.tensor_scalar_mul(gb[:, :], gb[:, :], 1.0 / D)
                dve.tensor_tensor(gc[:, :], ga[:, :], ga[:, :], ALU.mult)
                s_dv.inc(dve.always.tensor_sub(gb[:, :], gb[:, :], gc[:, :]))
                s_dv.wait(act)
                s_ac.inc(act.always.activation(gb[:, :], gb[:, :], AF.Sqrt,
                                        bias=EPS_LN))
                s_ac.wait(dve)
                dve.reciprocal(gb[:, :], gb[:, :])
                dve.tensor_mul(gb[:, :], gb[:, :], msk_s[:, tb:tb + 1])
                dve.tensor_scalar(gg[:, sl:sl + 2048], gg[:, sl:sl + 2048],
                                  ga[:, :], gb[:, :],
                                  ALU.subtract, ALU.mult)
                if tb >= 2:
                    s_out.wait(dve, out_h[tb - 2])
                dve.tensor_scalar(oo[:, sl:sl + 2048], x1c[:, sl:sl + 2048],
                                  ka_s[:, tb:tb + 1], kb_s[:, tb:tb + 1],
                                  ALU.mult, ALU.subtract)
                s_dv.inc(dve.always.tensor_add(
                    oo[:, sl:sl + 2048], oo[:, sl:sl + 2048],
                    gg[:, sl:sl + 2048]))
                dv_h[tb] = s_dv.n
                s_dv.wait(sync, dv_h[tb])
                s_out.wait(sync)
                s_out.dma(sync.dma_start(
                    out=out_t.ap()[128 * tb:128 * (tb + 1), :],
                    in_=oo[:, sl:sl + 2048]))
                out_h[tb] = s_out.n
                if tb + 2 < NT:
                    p9_fetch(tb + 2)
    s_out.wait(sync)
    bar()
    return nc


_CACHE = {}


def kernel(**inputs):
    T = inputs["L0_attn_normalized_resid_pre"].shape[1]
    cfg = CFG(T)
    if T not in _CACHE:
        _CACHE[T] = build(cfg)
    nc = _CACHE[T]
    in_maps = [shard_inputs(cfg, c, inputs) for c in range(NCORES)]
    res = run_bass_kernel_spmd(nc, in_maps, core_ids=list(range(NCORES)))
    out = np.concatenate([res.results[c]["out"] for c in range(NCORES)], axis=0)
    return out[None, :, :]
